# version 06
# baseline (speedup 1.0000x reference)
import os
import contextlib
import numpy as np
import concourse.bass as bass
import concourse.mybir as mybir
from concourse.bass_utils import run_bass_kernel_spmd

F32 = mybir.dt.float32
F32R = mybir.dt.float32r
AF = mybir.ActivationFunctionType
ALU = mybir.AluOpType
def MM(E, o, l, r, **kw):
    return E.matmul(o, l, r, **kw)

B, T, D = 2, 16384, 1024
NCORE = 8
P = (B * T) // NCORE
H, HS = 16, 64
TT1 = 256
C0 = 64
NCH1 = TT1 // C0
CB = 8

CV_MIUX, CV_LAM_W, CV_LAM_K, CV_LAM_V, CV_LAM_R, CV_LAM_G, CV_TDMIU, CV_U = range(8)
NCV1 = 8

ENGSEM = {"sy": "in", "po": "out", "te": "pe", "ve": "ve", "se": "se"}
ENGD = {"sy": 16, "po": 16, "te": 1, "ve": 1, "se": 1}
ALL_SEMS = ["in", "wl", "out", "cc", "pe", "ve", "se"]
DMA_SEMS = ("in", "wl", "out", "cc")


def nm(ap):
    t = getattr(ap, "tensor", None)
    if t is not None:
        return t.name
    n = getattr(ap, "name", None)
    if n is not None:
        return n
    return str(ap)


class Ctx:
    ENGINES = ("sy", "po", "te", "ve", "se")

    def __init__(self, nc, block):
        self.nc = nc
        self.block = block
        self.sems = {}
        self.cnt = {}
        self._guards = []
        for name in ALL_SEMS:
            g = nc.semaphore(name)
            h = g.__enter__()
            self._guards.append(g)
            self.sems[name] = h
            self.cnt[name] = 0
        self.waited_vals = {s: set() for s in ALL_SEMS}
        self.cur_wait = {e: {s: 0 for s in ALL_SEMS} for e in self.ENGINES}
        self.q = {e: [] for e in self.ENGINES}
        self.last_w = {}
        self.last_r = {}
        self.pending = {e: [] for e in self.ENGINES}

    def wait(self, eng, sem, val):
        if val is None or val <= 0:
            return
        if self.cur_wait[eng][sem] >= val:
            return
        self.cur_wait[eng][sem] = val
        self.waited_vals[sem].add(val)
        self.q[eng].append(("w", sem, val))

    def run(self, eng, f, reads=(), writes=(), inc=True, sem=None, d=None):
        if sem is None:
            sem = ENGSEM[eng]
        if d is None:
            d = ENGD[eng]
        if sem in DMA_SEMS:
            self.wait(eng, sem, self.cnt[sem])
        for ap in reads:
            lw = self.last_w.get(nm(ap))
            if lw:
                self.wait(eng, lw[0], lw[1])
        for ap in writes:
            n = nm(ap)
            lw = self.last_w.get(n)
            if lw:
                self.wait(eng, lw[0], lw[1])
            for s2, t2 in self.last_r.get(n, {}).items():
                self.wait(eng, s2, t2)
        if inc:
            self.cnt[sem] += d
            tok = self.cnt[sem]
            self.q[eng].append(("op", f, sem, tok, d))
            recs = self.pending[eng]
            self.pending[eng] = []
            recs.append((reads, writes))
            for rs, ws in recs:
                for ap in rs:
                    self.last_r.setdefault(nm(ap), {})[sem] = tok
                for ap in ws:
                    n = nm(ap)
                    self.last_w[n] = (sem, tok)
                    self.last_r[n] = {}
            return tok
        else:
            self.q[eng].append(("op", f, None, 0, 0))
            self.pending[eng].append((reads, writes))
            return None

    def full_barrier(self):
        for e in self.ENGINES:
            for sm in ALL_SEMS:
                self.wait(e, sm, self.cnt[sm])

    def barrier_token(self, sem):
        v = self.cnt[sem]
        if v > 0:
            self.waited_vals[sem].add(v)
        return v

    def flush(self):
        with self.nc.allow_low_precision(
                reason="tf32-class matmul operands; problem tolerance calibrated for bf16-class"):
            self._flush_inner()

    def _flush_inner(self):
        blk = self.block
        decs = {"sy": blk.sync, "po": blk.gpsimd, "te": blk.tensor,
                "ve": blk.vector, "se": blk.scalar}
        for name in self.ENGINES:
            items = self.q[name]
            if not items:
                continue
            prog = []
            pending_d = {s: 0 for s in ALL_SEMS}
            for it in items:
                if it[0] == "w":
                    prog.append(it)
                else:
                    _, f, sem, tok, d = it
                    if sem is not None and (sem in DMA_SEMS or tok in self.waited_vals[sem]):
                        dd = pending_d[sem] + d
                        pending_d[sem] = 0
                        prog.append(("op", f, sem, dd))
                    else:
                        if sem is not None:
                            pending_d[sem] += d
                        prog.append(("op", f, None, 0))
            sems = self.sems

            def body(E, prog=prog, sems=sems):
                for it in prog:
                    if it[0] == "w":
                        E.wait_ge(sems[it[1]], it[2])
                    else:
                        _, f, sem, dd = it
                        if sem is not None:
                            f(E).then_inc(sems[sem], dd)
                        else:
                            f(E)
            decs[name](body)
        self.q = {e: [] for e in self.ENGINES}



class _StopEmit(Exception):
    pass


def emit_phase1(ctx, io, p_tokens=P):
    stop_at = int(os.environ.get("P1_STOP", "99"))

    def ckpt(n):
        if stop_at <= n:
            raise _StopEmit()
    sub = int(os.environ.get("P1_SUB", "31"))
    nc = ctx.nc
    es1 = contextlib.ExitStack()

    def al(name, shape, dt):
        return es1.enter_context(nc.sbuf_tensor(name, shape, dt))

    def ap_(name, shape, dt):
        return es1.enter_context(nc.psum_tensor(name, shape, dt))

    s_alora = al("s_alora", [128, 1280], F32R)
    s_blora_a = al("s_blora_a", [96, 1024], F32R)
    s_blora_b = al("s_blora_b", [64, 1024], F32R)
    s_tda = al("s_tda", [128, 512], F32R)
    s_tdb = al("s_tdb", [64, 1024], F32R)
    s_cvec = al("s_cvec", [128, 8 * NCV1], F32)
    s_id = al("s_id", [128, 128], F32)
    s_m64 = al("s_m64", [64, 64], F32)
    s_i64 = al("s_i64", [64, 64], F32)
    s_dsel = al("s_dsel", [128, 128], F32)
    s_esel = al("s_esel", [16, 1024], F32)
    s_z64 = al("s_z64", [128, 64], F32)
    s_wbuf = al("s_wbuf", [128, 8192], F32R)

    s_x = al("s_x", [128, CB * (TT1 + 1)], F32)
    s_dxa = al("s_dxa", [128, CB * TT1], F32)
    s_mx = al("s_mx", [128, CB * TT1], F32R)
    s_lh_a = al("s_lh_a", [96, TT1], F32R)
    s_lh_b = al("s_lh_b", [64, TT1], F32R)
    s_t1 = al("s_t1", [64, TT1], F32R)
    s_k = al("s_k", [128, CB * TT1], F32)
    s_v = al("s_v", [128, CB * TT1], F32)
    s_r = al("s_r", [128, CB * TT1], F32)
    s_ew = al("s_ew", [128, CB * TT1], F32)
    s_g = al("s_g", [128, CB * TT1], F32)

    csc2 = [al("s_csc0", [128, 512], F32), al("s_csc1", [128, 512], F32)]
    e22 = [al("s_e20", [128, 512], F32), al("s_e21", [128, 512], F32)]
    et2 = [al("s_et0", [128, 512], F32), al("s_et1", [128, 512], F32)]
    e12 = [al("s_e10", [128, 512], F32), al("s_e11", [128, 512], F32)]
    e32 = [al("s_e30", [128, 512], F32), al("s_e31", [128, 512], F32)]
    kt2 = [al("s_kt0", [128, 512], F32), al("s_kt1", [128, 512], F32)]
    rt2 = [al("s_rt0", [128, 512], F32), al("s_rt1", [128, 512], F32)]
    kh2 = [al("s_kh0", [128, 512], F32), al("s_kh1", [128, 512], F32)]
    ncol2 = [al("s_ncol0", [128, 8], F32), al("s_ncol1", [128, 8], F32)]
    dtot2 = [al("s_dtot0", [128, 8], F32), al("s_dtot1", [128, 8], F32)]
    ecg2 = [al("s_ecg0", [128, 8], F32), al("s_ecg1", [128, 8], F32)]
    s_cg = [al("s_cg0", [128, 8], F32), al("s_cg1", [128, 8], F32)]
    s_dtS = al("s_dtS", [128, 8], F32)
    tmpA2 = [al("s_tmpA0", [128, 512], F32), al("s_tmpA1", [128, 512], F32)]
    tmpB2 = [al("s_tmpB0", [128, 512], F32), al("s_tmpB1", [128, 512], F32)]
    dvec2 = [al("s_dvec0", [16, 64], F32), al("s_dvec1", [16, 64], F32)]
    vk2 = [al("s_vk0", [64, 256], F32), al("s_vk1", [64, 256], F32)]
    s_atm = [al("s_atm%d" % i, [64, 64], F32) for i in range(4)]
    s_dg = [al("s_dg%d" % i, [64, 64], F32) for i in range(4)]
    s_S = [al("s_S0", [128, 512], F32), al("s_S1", [128, 512], F32)]
    s_ysum = al("s_ysum", [128, 64], F32)
    s_yl = al("s_yl", [128, CB * TT1], F32)
    s_rd = al("s_rd", [128, CB * TT1], F32)

    p_ab = [ap_("p_ab0", [64, 128], F32), ap_("p_ab1", [64, 128], F32)]
    p_y = ap_("p_y", [128, 64], F32)
    p_y2 = ap_("p_y2", [128, 64], F32)
    p_tr = ap_("p_tr", [64, 256], F32)
    p_sa = ap_("p_sa", [128, 64], F32)
    p_pr = [ap_("p_pr0", [128, 256], F32), ap_("p_pr1", [128, 256], F32)]

    def fm(t_sb, cb, lo, n):
        return t_sb[:, cb * TT1 + lo: cb * TT1 + lo + n]

    def xcur(cb, lo=0, n=TT1):
        return s_x[:, cb * (TT1 + 1) + 1 + lo: cb * (TT1 + 1) + 1 + lo + n]

    def dma_in(dst, src, sem="in"):
        return ctx.run("sy", lambda E, a=dst, b=src: E.dma_start(out=a, in_=b),
                       reads=(src,), writes=(dst,), sem=sem)

    def dma_out(dst, src):
        return ctx.run("po", lambda E, a=dst, b=src: E.dma_start(out=a, in_=b),
                       reads=(src,), writes=(dst,))

    dma_in(s_alora[:].rearrange("p (cb f) -> p cb f", f=160),
           io["A_lora"].rearrange("(cb p) f -> p cb f", p=128))
    for f in range(3):
        dma_in(s_blora_a[32 * f:32 * f + 32, :], io["B_lora"][f])
    for f in range(2):
        dma_in(s_blora_b[32 * f:32 * f + 32, :], io["B_lora"][3 + f])
    dma_in(s_tda[:].rearrange("p (cb r) -> p cb r", r=64),
           io["td_A"].rearrange("(cb p) r -> p cb r", p=128))
    dma_in(s_tdb[:], io["td_B"])
    dma_in(s_cvec[:], io["cvec"])
    dma_in(s_id[:], io["id128"])
    dma_in(s_m64[:], io["mask64s"])
    dma_in(s_i64[:], io["i64"])
    dma_in(s_dsel[:], io["dsel"])
    dma_in(s_esel[:], io["esel"])

    ctx.run("ve", lambda E, a=s_z64[:]: E.memset(a, 0.0), writes=(s_z64,))
    ctx.run("ve", lambda E, a=s_S[0][:]: E.memset(a, 0.0), writes=(s_S[0],))
    ctx.run("ve", lambda E, a=s_cg[0][:]: E.memset(a, 0.0), writes=(s_cg[0],))

    w_names = {"k": "Wk_t", "v": "Wv_t", "r": "Wr_t", "g": "Wg_t"}
    w_dst = {"k": s_k, "v": s_v, "r": s_r, "g": s_g}

    nt1 = p_tokens // TT1
    try:
      for t in range(nt1):
        t0 = t * TT1

        xin = io["xT"].rearrange("(cb p) tt -> p cb tt", p=128)[:, :, t0:t0 + TT1 + 1]
        xout = s_x[:].rearrange("p (cb tt) -> p cb tt", tt=TT1 + 1)
        dma_in(xout, xin)

        xv = s_x[:].rearrange("p (cb tt) -> p cb tt", tt=TT1 + 1)
        dxv = s_dxa[:].rearrange("p (cb tt) -> p cb tt", tt=TT1)
        ctx.run("ve", lambda E, o=dxv, a=xv[:, :, 0:TT1], b=xv[:, :, 1:TT1 + 1]:
                E.tensor_tensor(o, a, b, ALU.subtract),
                reads=(s_x,), writes=(s_dxa,))

        for cb in range(CB):
            ctx.run("ve", lambda E, o=fm(s_mx, cb, 0, TT1), a=fm(s_dxa, cb, 0, TT1),
                    s=s_cvec[:, 8 * CV_MIUX + cb: 8 * CV_MIUX + cb + 1], b=xcur(cb):
                    E.scalar_tensor_tensor(o, a, s, b, ALU.mult, ALU.add),
                    reads=(s_dxa, s_cvec, s_x), writes=(s_mx,))

        ckpt(1)
        lh_t = [s_lh_a, s_lh_a, s_lh_a, s_lh_b, s_lh_b]
        lh_slices = [s_lh_a[0:32, :], s_lh_a[32:64, :], s_lh_a[64:96, :],
                     s_lh_b[0:32, :], s_lh_b[32:64, :]]
        for f in range(5):
            pl = p_pr[f % 2][0:32, :]
            for cb in range(CB):
                ctx.run("te", lambda E, o=pl, l=s_alora[:, 160 * cb + 32 * f: 160 * cb + 32 * f + 32],
                        r=fm(s_mx, cb, 0, TT1), st=(cb == 0), sp=(cb == CB - 1):
                        MM(E, o, l, r, start=st, stop=sp),
                        reads=(s_alora, s_mx), writes=(p_pr[f % 2],),
                        inc=(cb == CB - 1))
            ctx.run("se", lambda E, o=lh_slices[f], i=pl: E.activation(o, i, AF.Tanh),
                    reads=(p_pr[f % 2],), writes=(lh_t[f],))

        ckpt(2)
        for fi, fname in enumerate(["w", "k", "v", "r", "g"]):
            if fi < 3:
                blora, rbase = s_blora_a, 32 * fi
            else:
                blora, rbase = s_blora_b, 32 * (fi - 3)
            lam = 8 * (CV_LAM_W + fi)
            for cb in range(CB):
                pm = p_pr[cb % 2]
                ctx.run("te", lambda E, o=pm[:], l=blora[rbase:rbase + 32, 128 * cb:128 * cb + 128],
                        r=lh_slices[fi]: MM(E, o, l, r, start=True, stop=True),
                        reads=(blora, lh_t[fi]), writes=(pm,))
                ctx.run("ve", lambda E, o=fm(s_mx, cb, 0, TT1), a=pm[:],
                        s=s_cvec[:, lam + cb: lam + cb + 1], b=fm(s_dxa, cb, 0, TT1):
                        E.scalar_tensor_tensor(o, a, s, b, ALU.add, ALU.mult),
                        reads=(pm, s_cvec, s_dxa), writes=(s_mx,))
                ctx.run("ve", lambda E, o=fm(s_mx, cb, 0, TT1), a=fm(s_mx, cb, 0, TT1), b=xcur(cb):
                        E.tensor_tensor(o, a, b, ALU.add),
                        reads=(s_mx, s_x), writes=(s_mx,))

            if fname == "w":
                pt = p_pr[0][0:64, :]
                for cb in range(CB):
                    ctx.run("te", lambda E, o=pt, l=s_tda[:, 64 * cb:64 * cb + 64],
                            r=fm(s_mx, cb, 0, TT1), st=(cb == 0), sp=(cb == CB - 1):
                            MM(E, o, l, r, start=st, stop=sp),
                            reads=(s_tda, s_mx), writes=(p_pr[0],),
                            inc=(cb == CB - 1))
                ctx.run("se", lambda E, o=s_t1[:], i=pt: E.activation(o, i, AF.Tanh),
                        reads=(p_pr[0],), writes=(s_t1,))
                for ob in range(CB):
                    pw = p_pr[1]
                    ctx.run("te", lambda E, o=pw[:], l=s_tdb[:, 128 * ob:128 * ob + 128],
                            r=s_t1[:]: MM(E, o, l, r, start=True, stop=True),
                            reads=(s_tdb, s_t1), writes=(pw,))
                    ctx.run("se", lambda E, o=fm(s_ew, ob, 0, TT1), i=pw[:],
                            bi=s_cvec[:, 8 * CV_TDMIU + ob: 8 * CV_TDMIU + ob + 1]:
                            E.activation(o, i, AF.Exp, bias=bi),
                            reads=(pw, s_cvec), writes=(s_ew,))
            else:
                wkey = w_names[fname]
                wout = s_wbuf[:].rearrange("p (cb co) -> p cb co", co=1024)
                win = io[wkey].rearrange("(cb p) co -> p cb co", p=128)
                dma_in(wout, win, sem="wl")
                dst = w_dst[fname]
                func = AF.Silu if fname == "g" else AF.Copy
                for ob in range(CB):
                    pp = p_pr[ob % 2]
                    for cb in range(CB):
                        ctx.run("te", lambda E, o=pp[:],
                                l=s_wbuf[:, 1024 * cb + 128 * ob: 1024 * cb + 128 * ob + 128],
                                r=fm(s_mx, cb, 0, TT1), st=(cb == 0), sp=(cb == CB - 1):
                                MM(E, o, l, r, start=st, stop=sp),
                                reads=(s_wbuf, s_mx), writes=(pp,),
                                inc=(cb == CB - 1))
                    ctx.run("se", lambda E, o=fm(dst, ob, 0, TT1), i=pp[:], fn=func:
                            E.activation(o, i, fn), reads=(pp,), writes=(dst,))

        ckpt(3)
        gout = io["gsil"].rearrange("(cb p) tt -> p cb tt", p=128)[:, :, t0:t0 + TT1]
        dma_out(gout, s_g[:].rearrange("p (cb tt) -> p cb tt", tt=TT1))

        for c in range(NCH1):
            lo = c * C0
            gidx = t * NCH1 + c
            par2 = gidx % 2
            s_csc, s_e2, s_et, s_e1, s_e3 = csc2[par2], e22[par2], et2[par2], e32[par2], e32[par2]
            s_e1 = e12[par2]
            s_kt, s_rt, s_kh = kt2[par2], rt2[par2], kh2[par2]
            s_ncol, s_dtot, s_ecg = ncol2[par2], dtot2[par2], ecg2[par2]
            s_tmpA, s_tmpB, s_dvec, s_vk = tmpA2[par2], tmpB2[par2], dvec2[par2], vk2[par2]
            cgA, cgB = s_cg[gidx % 2], s_cg[(gidx + 1) % 2]
            Sprev, Snew = s_S[gidx % 2], s_S[(gidx + 1) % 2]

            for cb in range(CB):
                ctx.run("ve", lambda E, o=s_csc[:, 64 * cb:64 * cb + 64],
                        a=fm(s_ew, cb, lo, C0), z=s_z64[:, 0:64]:
                        E.tensor_tensor_scan(o, a, z, 0.0, ALU.add, ALU.add),
                        reads=(s_ew, s_z64), writes=(s_csc,))
            ctx.run("se", lambda E, o=s_e2[:], i=s_csc[:]: E.activation(o, i, AF.Exp),
                    reads=(s_csc,), writes=(s_e2,))
            ewv = s_ew[:].rearrange("p (cb tt) -> p cb tt", tt=TT1)[:, :, lo:lo + C0]
            cscv = s_csc[:].rearrange("p (cb cc) -> p cb cc", cc=C0)
            etv = s_et[:].rearrange("p (cb cc) -> p cb cc", cc=C0)
            ctx.run("ve", lambda E, o=etv, a=ewv, b=cscv:
                    E.tensor_tensor(o, a, b, ALU.subtract),
                    reads=(s_ew, s_csc), writes=(s_et,))
            ctx.run("se", lambda E, o=s_e1[:], i=s_et[:]: E.activation(o, i, AF.Exp),
                    reads=(s_et,), writes=(s_e1,))
            for cb in range(CB):
                ctx.run("ve", lambda E, o=s_ncol[:, cb:cb + 1],
                        i=s_csc[:, 64 * cb + 63:64 * cb + 64]:
                        E.tensor_scalar(o, i, -1.0, None, ALU.mult),
                        reads=(s_csc,), writes=(s_ncol,))
                ctx.run("ve", lambda E, o=cgB[:, cb:cb + 1], a=cgA[:, cb:cb + 1],
                        b=s_csc[:, 64 * cb + 63:64 * cb + 64]:
                        E.tensor_tensor(o, a, b, ALU.add),
                        reads=(cgA, s_csc), writes=(cgB,))
            for cb in range(CB):
                ctx.run("se", lambda E, o=s_e3[:, 64 * cb:64 * cb + 64],
                        i=s_csc[:, 64 * cb:64 * cb + 64], bi=s_ncol[:, cb:cb + 1]:
                        E.activation(o, i, AF.Exp, bias=bi),
                        reads=(s_csc, s_ncol), writes=(s_e3,))
                ctx.run("se", lambda E, o=s_dtot[:, cb:cb + 1], i=s_ncol[:, cb:cb + 1]:
                        E.activation(o, i, AF.Exp), reads=(s_ncol,), writes=(s_dtot,))
                ctx.run("se", lambda E, o=s_ecg[:, cb:cb + 1], i=cgA[:, cb:cb + 1]:
                        E.activation(o, i, AF.Exp, scale=-1.0),
                        reads=(cgA,), writes=(s_ecg,))

            ckpt(4)
            kvv = s_k[:].rearrange("p (cb tt) -> p cb tt", tt=TT1)[:, :, lo:lo + C0]
            rvv = s_r[:].rearrange("p (cb tt) -> p cb tt", tt=TT1)[:, :, lo:lo + C0]
            ktv = s_kt[:].rearrange("p (cb cc) -> p cb cc", cc=C0)
            rtv = s_rt[:].rearrange("p (cb cc) -> p cb cc", cc=C0)
            khv = s_kh[:].rearrange("p (cb cc) -> p cb cc", cc=C0)
            e2v = s_e2[:].rearrange("p (cb cc) -> p cb cc", cc=C0)
            e1v = s_e1[:].rearrange("p (cb cc) -> p cb cc", cc=C0)
            e3v = s_e3[:].rearrange("p (cb cc) -> p cb cc", cc=C0)
            ctx.run("ve", lambda E, o=ktv, a=kvv, b=e2v: E.tensor_tensor(o, a, b, ALU.mult),
                    reads=(s_k, s_e2), writes=(s_kt,))
            ctx.run("ve", lambda E, o=rtv, a=rvv, b=e1v: E.tensor_tensor(o, a, b, ALU.mult),
                    reads=(s_r, s_e1), writes=(s_rt,))
            ctx.run("ve", lambda E, o=khv, a=kvv, b=e3v: E.tensor_tensor(o, a, b, ALU.mult),
                    reads=(s_k, s_e3), writes=(s_kh,))
            for cb in range(CB):
                ctx.run("ve", lambda E, o=fm(s_rd, cb, lo, C0), a=s_rt[:, 64 * cb:64 * cb + 64],
                        s=s_ecg[:, cb:cb + 1]: E.tensor_scalar(o, a, s, None, ALU.mult),
                        reads=(s_rt, s_ecg), writes=(s_rd,))
            for cb in range(CB):
                ctx.run("ve", lambda E, o=s_tmpA[:, 64 * cb:64 * cb + 64], a=fm(s_k, cb, lo, C0),
                        s=s_cvec[:, 8 * CV_U + cb: 8 * CV_U + cb + 1]:
                        E.tensor_scalar(o, a, s, None, ALU.mult),
                        reads=(s_k, s_cvec), writes=(s_tmpA,))
            tav = s_tmpA[:].rearrange("p (cb cc) -> p cb cc", cc=C0)
            tbv = s_tmpB[:].rearrange("p (cb cc) -> p cb cc", cc=C0)
            ctx.run("ve", lambda E, o=tbv, a=tav, b=rvv: E.tensor_tensor(o, a, b, ALU.mult),
                    reads=(s_tmpA, s_r), writes=(s_tmpB,))

            ckpt(5)
            for cb in range(CB):
                ctx.run("te", lambda E, o=p_y[0:16, :], l=s_dsel[:, 16 * cb:16 * cb + 16],
                        r=s_tmpB[:, 64 * cb:64 * cb + 64], st=(cb == 0), sp=(cb == CB - 1):
                        MM(E, o, l, r, start=st, stop=sp),
                        reads=(s_dsel, s_tmpB), writes=(p_y,),
                        inc=(cb == CB - 1))
            ctx.run("ve", lambda E, o=s_dvec[:], i=p_y[0:16, :]: E.tensor_copy(o, i),
                    reads=(p_y,), writes=(s_dvec,))

            ckpt(6)
            for cb in range(CB):
                base_c = 64 * cb
                if sub & 1:
                    ctx.run("te", lambda E, o=p_tr[:, 0:128], i=fm(s_v, cb, lo, C0), d=s_id[:]:
                            E.transpose(o, i, d), reads=(s_v, s_id), writes=(p_tr,))
                    ctx.run("te", lambda E, o=p_tr[:, 128:256],
                            i=s_kh[:, base_c:base_c + C0], d=s_id[:]:
                            E.transpose(o, i, d), reads=(s_kh, s_id), writes=(p_tr,))
                    ctx.run("ve", lambda E, o=s_vk[:], i=p_tr[:]: E.tensor_copy(o, i),
                            reads=(p_tr,), writes=(s_vk,))

                for half in range(2):
                    h = 2 * cb + half
                    hb = 64 * half
                    par = h % 2
                    if sub & 2:
                        ctx.run("te", lambda E, o=p_ab[par][:, 0:64],
                                l=s_kt[hb:hb + 64, base_c:base_c + C0],
                                r=s_rt[hb:hb + 64, base_c:base_c + C0]:
                                MM(E, o, l, r, start=True, stop=True),
                                reads=(s_kt, s_rt), writes=(p_ab[par],))
                        ctx.run("te", lambda E, o=p_ab[par][:, 64:128],
                                l=s_esel[:, 64 * h:64 * h + 64], r=s_dvec[:]:
                                MM(E, o, l, r, start=True, stop=True),
                                reads=(s_esel, s_dvec), writes=(p_ab[par],))
                        ctx.run("ve", lambda E, o=s_atm[par][:], a=p_ab[par][:, 0:64], b=s_m64[:]:
                                E.tensor_tensor(o, a, b, ALU.mult),
                                reads=(p_ab[par], s_m64), writes=(s_atm[par],))
                        ctx.run("ve", lambda E, o=s_dg[par][:], a=p_ab[par][:, 64:128], b=s_i64[:]:
                                E.tensor_tensor(o, a, b, ALU.mult),
                                reads=(p_ab[par], s_i64), writes=(s_dg[par],))
                    if sub & 4:
                        ctx.run("te", lambda E, o=p_y[hb:hb + 64, :], l=s_vk[0:64, hb:hb + 64],
                                r=s_atm[par][:]: MM(E, o, l, r, start=True, stop=False),
                                reads=(s_vk, s_atm[par]), writes=(p_y,), inc=False)
                        ctx.run("te", lambda E, o=p_y[hb:hb + 64, :], l=s_vk[0:64, hb:hb + 64],
                                r=s_dg[par][:]: MM(E, o, l, r, start=False, stop=True),
                                reads=(s_vk, s_dg[par]), writes=(p_y,))
                        ctx.run("te", lambda E, o=p_y2[hb:hb + 64, :],
                                l=Sprev[hb:hb + 64, base_c:base_c + 64],
                                r=s_rt[hb:hb + 64, base_c:base_c + C0]:
                                MM(E, o, l, r, start=True, stop=True),
                                reads=(Sprev, s_rt), writes=(p_y2,))
                    if sub & 8:
                        ctx.run("te", lambda E, o=p_sa[hb:hb + 64, :],
                                l=s_vk[0:64, 128 + hb:128 + hb + 64],
                                r=s_vk[0:64, hb:hb + 64]:
                                MM(E, o, l, r, start=True, stop=True),
                                reads=(s_vk,), writes=(p_sa,))
                        ctx.run("ve", lambda E, o=Snew[hb:hb + 64, base_c:base_c + 64],
                                a=Sprev[hb:hb + 64, base_c:base_c + 64],
                                s=s_dtot[hb:hb + 64, cb:cb + 1], b=p_sa[hb:hb + 64, :]:
                                E.scalar_tensor_tensor(o, a, s, b, ALU.mult, ALU.add),
                                reads=(Sprev, s_dtot, p_sa), writes=(Snew,))
                if sub & 4:
                    ctx.run("ve", lambda E, o=s_ysum[:], i=p_y[:]:
                            E.tensor_copy(o, i), reads=(p_y,), writes=(s_ysum,))
                    ctx.run("ve", lambda E, o=fm(s_yl, cb, lo, C0), a=s_ysum[:], b=p_y2[:]:
                            E.tensor_tensor(o, a, b, ALU.add),
                            reads=(s_ysum, p_y2), writes=(s_yl,))

        ckpt(8)
        ylout = io["ylocal"].rearrange("(cb p) tt -> p cb tt", p=128)[:, :, t0:t0 + TT1]
        rdout = io["RDs"].rearrange("(cb p) tt -> p cb tt", p=128)[:, :, t0:t0 + TT1]
        dma_out(ylout, s_yl[:].rearrange("p (cb tt) -> p cb tt", tt=TT1))
        dma_out(rdout, s_rd[:].rearrange("p (cb tt) -> p cb tt", tt=TT1))

    except _StopEmit:
        pass
    nch_total = nt1 * NCH1
    cg_fin = s_cg[nch_total % 2]
    S_fin = s_S[nch_total % 2]
    ctx.run("se", lambda E, o=s_dtS[:], i=cg_fin[:]:
            E.activation(o, i, AF.Exp, scale=-1.0), reads=(cg_fin,), writes=(s_dtS,))
    dma_out(io["summ0"], S_fin[:])
    dma_out(io["summD"], s_dtS[:])
    ctx.wait("po", "out", ctx.barrier_token("out"))
    ctx.full_barrier()
    es1.close()


def build_phase1(p_tokens=P):
    nc = bass.Bass("TRN2", target_bir_lowering=False)
    io = {}
    io["xT"] = nc.dram_tensor("xT", [D, p_tokens + 1], F32, kind="ExternalInput").ap()
    for n in ["Wk_t", "Wv_t", "Wr_t", "Wg_t"]:
        io[n] = nc.dram_tensor(n, [D, D], F32R, kind="ExternalInput").ap()
    io["A_lora"] = nc.dram_tensor("A_lora", [D, 160], F32R, kind="ExternalInput").ap()
    io["B_lora"] = nc.dram_tensor("B_lora", [5, 32, D], F32R, kind="ExternalInput").ap()
    io["td_A"] = nc.dram_tensor("td_A", [D, 64], F32R, kind="ExternalInput").ap()
    io["td_B"] = nc.dram_tensor("td_B", [64, D], F32R, kind="ExternalInput").ap()
    io["cvec"] = nc.dram_tensor("cvec", [128, 8 * NCV1], F32, kind="ExternalInput").ap()
    io["id128"] = nc.dram_tensor("id128", [128, 128], F32, kind="ExternalInput").ap()
    io["mask64s"] = nc.dram_tensor("mask64s", [64, 64], F32, kind="ExternalInput").ap()
    io["i64"] = nc.dram_tensor("i64", [64, 64], F32, kind="ExternalInput").ap()
    io["dsel"] = nc.dram_tensor("dsel", [128, 128], F32, kind="ExternalInput").ap()
    io["esel"] = nc.dram_tensor("esel", [16, 1024], F32, kind="ExternalInput").ap()
    io["ylocal"] = nc.dram_tensor("ylocal", [D, p_tokens], F32, kind="ExternalOutput").ap()
    io["RDs"] = nc.dram_tensor("RDs", [D, p_tokens], F32, kind="ExternalOutput").ap()
    io["gsil"] = nc.dram_tensor("gsil", [D, p_tokens], F32, kind="ExternalOutput").ap()
    io["summ0"] = nc.dram_tensor("summ0", [128, 512], F32, kind="ExternalOutput").ap()
    io["summD"] = nc.dram_tensor("summD", [128, 8], F32, kind="ExternalOutput").ap()

    with nc.Block() as block:
        ctx = Ctx(nc, block)
        emit_phase1(ctx, io, p_tokens)
        ctx.flush()
    return nc



def pack_col(v):
    return np.ascontiguousarray(np.asarray(v, np.float32).reshape(8, 128).T)


def stage_phase1_shared(inputs):
    shared = {}
    for wn, key in [("Wk_t", "L0_attn_Wk"), ("Wv_t", "L0_attn_Wv"),
                    ("Wr_t", "L0_attn_Wr"), ("Wg_t", "L0_attn_Wg")]:
        shared[wn] = np.ascontiguousarray(np.asarray(inputs[key], np.float32).T)
    shared["A_lora"] = np.ascontiguousarray(np.asarray(inputs["L0_attn_A_lora"], np.float32))
    shared["B_lora"] = np.ascontiguousarray(np.asarray(inputs["L0_attn_B_lora"], np.float32))
    shared["td_A"] = np.ascontiguousarray(np.asarray(inputs["L0_attn_td_A"], np.float32))
    shared["td_B"] = np.ascontiguousarray(np.asarray(inputs["L0_attn_td_B"], np.float32))
    cv = np.zeros((128, 8 * NCV1), np.float32)
    cv[:, 8 * CV_MIUX:8 * CV_MIUX + 8] = pack_col(inputs["L0_attn_miu_x"])
    lam = np.asarray(inputs["L0_attn_lambda_"], np.float32).reshape(5, D)
    for f in range(5):
        cv[:, 8 * (CV_LAM_W + f):8 * (CV_LAM_W + f) + 8] = pack_col(lam[f])
    cv[:, 8 * CV_TDMIU:8 * CV_TDMIU + 8] = pack_col(inputs["L0_attn_td_miu"])
    cv[:, 8 * CV_U:8 * CV_U + 8] = pack_col(np.asarray(inputs["L0_attn_u"], np.float32).reshape(D))
    shared["cvec"] = cv
    shared["id128"] = np.eye(128, dtype=np.float32)
    shared["mask64s"] = np.triu(np.ones((64, 64), np.float32), 1)
    shared["i64"] = np.eye(64, dtype=np.float32)
    dsel = np.zeros((128, 128), np.float32)
    for cb in range(8):
        dsel[0:64, 16 * cb + 2 * cb] = 1.0
        dsel[64:128, 16 * cb + 2 * cb + 1] = 1.0
    shared["dsel"] = dsel
    esel = np.zeros((16, 1024), np.float32)
    for h in range(16):
        esel[h, 64 * h:64 * h + 64] = 1.0
    shared["esel"] = esel
    return shared


def stage_phase1_inputs(inputs, p_tokens=P):
    x = np.asarray(inputs["L0_attn_x"], np.float32)
    shared = stage_phase1_shared(inputs)
    in_maps = []
    pp = p_tokens
    for core in range(NCORE):
        b, q = core // 4, core % 4
        sl = x[b, q * pp:(q + 1) * pp]
        if q == 0:
            prev = np.zeros((1, D), np.float32)
        else:
            prev = x[b, q * pp - 1:q * pp]
        xT = np.ascontiguousarray(np.concatenate([prev, sl], axis=0).T)
        m = dict(shared)
        m["xT"] = xT
        in_maps.append(m)
    return in_maps


def run_phase1(inputs):
    nc = build_phase1()
    in_maps = stage_phase1_inputs(inputs)
    res = run_bass_kernel_spmd(nc, in_maps, core_ids=list(range(NCORE)))
    return res.results


def _fold_masks(core):
    b, q = core // 4, core % 4
    fm = np.zeros((128, 16), np.float32)
    for r in range(8):
        a = 1.0 if (r // 4 == b and r % 4 < q) else 0.0
        fm[:, r] = a
        fm[:, 8 + r] = 1.0 - a
    return fm


def _kernel_unfused(**inputs):
    nc1 = build_phase1()
    maps1 = stage_phase1_inputs(inputs)
    res1 = run_bass_kernel_spmd(nc1, maps1, core_ids=list(range(NCORE))).results

    nc2 = build_phase2()
    sh2 = stage_phase2_shared(inputs)
    maps2 = []
    for core in range(NCORE):
        b, q = core // 4, core % 4
        m = dict(sh2)
        m["xT"] = maps1[core]["xT"]
        m["ylocal"] = np.ascontiguousarray(res1[core]["ylocal"])
        m["RDs"] = np.ascontiguousarray(res1[core]["RDs"])
        m["gsil"] = np.ascontiguousarray(res1[core]["gsil"])
        gS = np.zeros((128, 8 * 512), np.float32)
        gD = np.zeros((128, 64), np.float32)
        for p in range(NCORE):
            gS[:, 512 * p:512 * (p + 1)] = res1[p]["summ0"]
            gD[:, 8 * p:8 * (p + 1)] = res1[p]["summD"]
        m["gS"] = gS
        m["gD"] = gD
        m["fmask"] = _fold_masks(core)
        maps2.append(m)
    res2 = run_bass_kernel_spmd(nc2, maps2, core_ids=list(range(NCORE))).results

    nc3 = build_phase3()
    sh3 = stage_phase3_shared(inputs)
    maps3 = []
    for core in range(NCORE):
        b, q = core // 4, core % 4
        m = dict(sh3)
        m["x2s"] = np.ascontiguousarray(res2[core]["x2s"])
        m["qfs"] = np.ascontiguousarray(res2[core]["qfs"])
        m["kfs"] = np.ascontiguousarray(res2[core]["kfs"])
        m["vtm"] = np.ascontiguousarray(res2[core]["vtm"])
        gS = np.zeros((128, 8 * 512), np.float32)
        gZ = np.zeros((128, 64), np.float32)
        for p in range(NCORE):
            gS[:, 512 * p:512 * (p + 1)] = res2[p]["summ1"]
            gZ[:, 8 * p:8 * (p + 1)] = res2[p]["summZ"]
        m["gS1"] = gS
        m["gZ1"] = gZ
        m["fmask"] = _fold_masks(core)
        maps3.append(m)
    res3 = run_bass_kernel_spmd(nc3, maps3, core_ids=list(range(NCORE))).results

    out = np.empty((B, T, D), np.float32)
    for core in range(NCORE):
        b, q = core // 4, core % 4
        out[b, q * P:(q + 1) * P] = res3[core]["outT"].T
    return out



TT2 = 256
GN_EPS = 1e-5 * H
LN_EPS = 1e-5


def emit_phase2(ctx, io, p_tokens=P, gS_of=None, gD_of=None):
    nc = ctx.nc
    if gS_of is None:
        gS_of = lambda r: io["gS"][:, 512 * r:512 * (r + 1)]
        gD_of = lambda r: io["gD"][:, 8 * r:8 * (r + 1)]
    es2 = contextlib.ExitStack()

    def al(name, shape, dt):
        return es2.enter_context(nc.sbuf_tensor(name, shape, dt))

    def ap_(name, shape, dt):
        return es2.enter_context(nc.psum_tensor(name, shape, dt))

    s_cvec = al("q_cvec", [128, 8 * 4], F32)
    s_dsel = al("q_dsel", [128, 128], F32R)
    s_esel2 = al("q_esel2", [16, 1024], F32R)
    s_ones = al("q_ones", [128, 128], F32R)
    s_id = al("q_id", [128, 128], F32)
    s_S0 = al("q_S0", [128, 512], F32)
    s_gs = al("q_gs", [128, 512], F32)
    s_gd = al("q_gd", [128, 8], F32)
    s_fm = al("q_fm", [128, 16], F32)
    s_eps = al("q_eps", [128, 2], F32)
    s_deff = al("q_deff", [128, 8], F32)
    s_tmps = al("q_tmps", [128, 512], F32)

    s_M = al("q_M", [128, 8192], F32R)

    p_pr = [ap_("q_ppr0", [128, 512], F32), ap_("q_ppr1", [128, 512], F32)]
    p_aux = ap_("q_paux", [128, 512], F32)
    p_st = ap_("q_pst", [16, 512], F32)
    p_b1 = ap_("q_pb1", [128, 512], F32)
    p_b2 = ap_("q_pb2", [128, 512], F32)
    p_tr = ap_("q_ptr", [128, 512], F32)
    p_sloc = ap_("q_psloc", [128, 512], F32)

    def fm(t_sb, cb, lo, n, tt=TT2):
        return t_sb[:, cb * tt + lo: cb * tt + lo + n]

    def xcur(cb):
        return s_x[:, cb * (TT2 + 1) + 1: cb * (TT2 + 1) + 1 + TT2]

    def dma_in(dst, src, sem="in"):
        return ctx.run("sy", lambda E, a=dst, b=src: E.dma_start(out=a, in_=b),
                       reads=(src,), writes=(dst,), sem=sem)

    def dma_out(dst, src):
        return ctx.run("po", lambda E, a=dst, b=src: E.dma_start(out=a, in_=b),
                       reads=(src,), writes=(dst,))

    def TT_(eng, o, a, b, op, extra_reads=(), w=None):
        ctx.run(eng, lambda E, o=o, a=a, b=b, op=op: E.tensor_tensor(o, a, b, op),
                reads=(a, b) + tuple(extra_reads), writes=(w if w is not None else o,))

    dma_in(s_cvec[:], io["cvec2"])
    dma_in(s_dsel[:], io["dselr"])
    dma_in(s_esel2[:], io["esel2"])
    dma_in(s_ones[:], io["ones128"])
    dma_in(s_id[:], io["id128"])
    dma_in(s_fm[:], io["fmask"])
    dma_in(s_eps[:], io["epsc"])

    ctx.run("ve", lambda E, a=s_S0[:]: E.memset(a, 0.0), writes=(s_S0,))
    for j in range(NCORE):
        dma_in(s_gs[:], gS_of(j))
        dma_in(s_gd[:], gD_of(j))
        acol = s_fm[:, j:j + 1]
        amcol = s_fm[:, 8 + j:9 + j]
        for cb in range(CB):
            ctx.run("ve", lambda E, o=s_deff[:, cb:cb + 1], i=s_gd[:, cb:cb + 1],
                    sc=acol, b=amcol:
                    E.scalar_tensor_tensor(o, i, sc, b, ALU.mult, ALU.add),
                    reads=(s_gd, s_fm), writes=(s_deff,))
            ctx.run("ve", lambda E, o=s_tmps[:, 64 * cb:64 * cb + 64],
                    i=s_gs[:, 64 * cb:64 * cb + 64], sc=acol:
                    E.tensor_scalar(o, i, sc, None, ALU.mult),
                    reads=(s_gs, s_fm), writes=(s_tmps,))
            ctx.run("ve", lambda E, o=s_S0[:, 64 * cb:64 * cb + 64],
                    a=s_S0[:, 64 * cb:64 * cb + 64], sc=s_deff[:, cb:cb + 1],
                    b=s_tmps[:, 64 * cb:64 * cb + 64]:
                    E.scalar_tensor_tensor(o, a, sc, b, ALU.mult, ALU.add),
                    reads=(s_S0, s_deff, s_tmps), writes=(s_S0,))

    with (
        nc.sbuf_tensor("q_wr", [128, 8192], F32R) as s_wr,
        nc.sbuf_tensor("q_wi", [128, 8192], F32R) as s_wi,
        nc.sbuf_tensor("q_fb", [128, 8192], F32R) as s_fb,
        nc.sbuf_tensor("q_p1t", [128, 8192], F32R) as s_p1t,
        nc.sbuf_tensor("q_p2n", [128, 8192], F32R) as s_p2n,
    ):
        dma_in(s_wr[:].rearrange("p (cb co) -> p cb co", co=1024),
               io["wfr"].rearrange("(cb p) co -> p cb co", p=128))
        dma_in(s_wi[:].rearrange("p (cb co) -> p cb co", co=1024),
               io["wfi"].rearrange("(cb p) co -> p cb co", p=128))

        def mm_pass(wsrc, dst, accum_sign):
            for oc in range(CB):
                for hf in range(2):
                    pp = p_pr[hf]
                    for kc in range(CB):
                        ctx.run("te", lambda E, o=pp[:],
                                l=wsrc[:, 1024 * kc + 128 * oc: 1024 * kc + 128 * oc + 128],
                                r=s_fb[:, 1024 * kc + 512 * hf: 1024 * kc + 512 * hf + 512],
                                st=(kc == 0), sp=(kc == CB - 1):
                                MM(E, o, l, r, start=st, stop=sp),
                                reads=(wsrc, s_fb), writes=(pp,), inc=(kc == CB - 1))
                    dsl = dst[:, 1024 * oc + 512 * hf: 1024 * oc + 512 * hf + 512]
                    if accum_sign == 0:
                        ctx.run("ve", lambda E, o=dsl, i=pp[:]: E.tensor_copy(o, i),
                                reads=(pp,), writes=(dst,))
                    else:
                        sgn = float(accum_sign)
                        ctx.run("ve", lambda E, o=dsl, i=pp[:], a=dsl, sgn=sgn:
                                E.scalar_tensor_tensor(o, i, sgn, a, ALU.mult, ALU.add),
                                reads=(pp, dst), writes=(dst,))

        dma_in(s_fb[:].rearrange("p (cb co) -> p cb co", co=1024),
               io["Fr"].rearrange("(cb p) co -> p cb co", p=128), sem="wl")
        mm_pass(s_wr, s_p1t, 0)
        for oc in range(CB):
            for hf in range(2):
                pp = p_pr[hf]
                for kc in range(CB):
                    ctx.run("te", lambda E, o=pp[:],
                            l=s_wi[:, 1024 * kc + 128 * oc: 1024 * kc + 128 * oc + 128],
                            r=s_fb[:, 1024 * kc + 512 * hf: 1024 * kc + 512 * hf + 512],
                            st=(kc == 0), sp=(kc == CB - 1):
                            MM(E, o, l, r, start=st, stop=sp),
                            reads=(s_wi, s_fb), writes=(pp,), inc=(kc == CB - 1))
                dsl = s_p2n[:, 1024 * oc + 512 * hf: 1024 * oc + 512 * hf + 512]
                ctx.run("ve", lambda E, o=dsl, i=pp[:]:
                        E.tensor_scalar(o, i, -1.0, None, ALU.mult),
                        reads=(pp,), writes=(s_p2n,))
        dma_in(s_fb[:].rearrange("p (cb co) -> p cb co", co=1024),
               io["Fin"].rearrange("(cb p) co -> p cb co", p=128), sem="wl")
        mm_pass(s_wi, s_p1t, 1)
        dma_in(s_fb[:].rearrange("p (cb co) -> p cb co", co=1024),
               io["Fi"].rearrange("(cb p) co -> p cb co", p=128), sem="wl")
        mm_pass(s_wr, s_p2n, -1)
        dma_in(s_fb[:].rearrange("p (cb co) -> p cb co", co=1024),
               io["Grs"].rearrange("(cb p) co -> p cb co", p=128), sem="wl")
        for dc in range(CB):
            for hf in range(2):
                pp = p_pr[hf]
                for oc in range(CB):
                    ctx.run("te", lambda E, o=pp[:],
                            l=s_p1t[:, 1024 * oc + 128 * dc: 1024 * oc + 128 * dc + 128],
                            r=s_fb[:, 1024 * oc + 512 * hf: 1024 * oc + 512 * hf + 512],
                            st=(oc == 0), sp=(oc == CB - 1):
                            MM(E, o, l, r, start=st, stop=sp),
                            reads=(s_p1t, s_fb), writes=(pp,), inc=(oc == CB - 1))
                ctx.run("ve", lambda E, o=s_M[:, 1024 * dc + 512 * hf: 1024 * dc + 512 * hf + 512],
                        i=pp[:]: E.tensor_copy(o, i), reads=(pp,), writes=(s_M,))
        dma_in(s_fb[:].rearrange("p (cb co) -> p cb co", co=1024),
               io["Gis"].rearrange("(cb p) co -> p cb co", p=128), sem="wl")
        for dc in range(CB):
            for hf in range(2):
                pp = p_pr[hf]
                for oc in range(CB):
                    ctx.run("te", lambda E, o=pp[:],
                            l=s_p2n[:, 1024 * oc + 128 * dc: 1024 * oc + 128 * dc + 128],
                            r=s_fb[:, 1024 * oc + 512 * hf: 1024 * oc + 512 * hf + 512],
                            st=(oc == 0), sp=(oc == CB - 1):
                            MM(E, o, l, r, start=st, stop=sp),
                            reads=(s_p2n, s_fb), writes=(pp,), inc=(oc == CB - 1))
                msl = s_M[:, 1024 * dc + 512 * hf: 1024 * dc + 512 * hf + 512]
                ctx.run("ve", lambda E, o=msl, a=msl, b=pp[:]:
                        E.tensor_tensor(o, a, b, ALU.add),
                        reads=(s_M, pp), writes=(s_M,))
        ctx.full_barrier()

    s_wbuf = al("q_wbuf", [128, 8192], F32R)

    s_x = al("q_x", [128, CB * (TT2 + 1)], F32)
    s_yl = al("q_yl", [128, CB * TT2], F32)
    s_rd = al("q_rd", [128, CB * TT2], F32)
    s_g = al("q_g", [128, CB * TT2], F32)
    s_y = al("q_y", [128, CB * TT2], F32R)
    s_sq = al("q_sq", [128, CB * TT2], F32R)
    s_yg = al("q_yg", [128, CB * TT2], F32R)
    s_res = al("q_res", [128, CB * TT2], F32R)
    s_x1 = al("q_x1", [128, CB * TT2], F32R)
    s_x2 = al("q_x2", [128, CB * TT2], F32R)
    s_gt = al("q_gt", [128, CB * TT2], F32)
    s_qf = al("q_qf", [128, CB * TT2], F32)
    s_kf = al("q_kf", [128, CB * TT2], F32)
    s_vtm = al("q_vtm", [128, 2 * 1024], F32)
    s_kftm = al("q_kftm", [128, 128], F32)
    s_ln1 = al("q_ln1", [128, TT2], F32)
    s_tA = al("q_tA", [128, TT2], F32)
    s_tB = al("q_tB", [128, TT2], F32)
    s_zl = [al("q_zl0", [128, 8], F32), al("q_zl1", [128, 8], F32)]
    s_slp = [al("q_slp0", [128, 512], F32), al("q_slp1", [128, 512], F32)]
    s_stm = al("q_stm", [16, TT2], F32R)
    s_stq = al("q_stq", [16, TT2], F32R)
    s_str = al("q_str", [16, TT2], F32R)
    s_st2 = al("q_st2", [16, TT2], F32R)

    def emit_ln(src, dst, eps):
        for cb in range(CB):
            ctx.run("se", lambda E, o=fm(s_sq, cb, 0, TT2), i=fm(src, cb, 0, TT2):
                    E.activation(o, i, AF.Square), reads=(src,), writes=(s_sq,))
        for cb in range(CB):
            ctx.run("te", lambda E, o=p_st[0:1, 0:TT2], l=s_ones[:, 0:1],
                    r=fm(src, cb, 0, TT2), st=(cb == 0), sp=(cb == CB - 1):
                    MM(E, o, l, r, start=st, stop=sp),
                    reads=(s_ones, src), writes=(p_st,), inc=(cb == CB - 1))
        for cb in range(CB):
            ctx.run("te", lambda E, o=p_st[0:1, 256:256 + TT2], l=s_ones[:, 0:1],
                    r=fm(s_sq, cb, 0, TT2), st=(cb == 0), sp=(cb == CB - 1):
                    MM(E, o, l, r, start=st, stop=sp),
                    reads=(s_ones, s_sq), writes=(p_st,), inc=(cb == CB - 1))
        mrow = s_stm[0:1, :]
        qrow = s_stq[0:1, :]
        rrow = s_str[0:1, :]
        ctx.run("ve", lambda E, o=mrow, i=p_st[0:1, 0:TT2]:
                E.tensor_scalar(o, i, 1.0 / D, None, ALU.mult),
                reads=(p_st,), writes=(s_stm,))
        ctx.run("ve", lambda E, o=qrow, i=p_st[0:1, 256:256 + TT2]:
                E.tensor_scalar(o, i, 1.0 / D, None, ALU.mult),
                reads=(p_st,), writes=(s_stq,))
        ctx.run("ve", lambda E, o=rrow, a=mrow, b=mrow: E.tensor_tensor(o, a, b, ALU.mult),
                reads=(s_stm,), writes=(s_str,))
        ctx.run("ve", lambda E, o=qrow, a=qrow, b=rrow: E.tensor_tensor(o, a, b, ALU.subtract),
                reads=(s_stq, s_str), writes=(s_stq,))
        ctx.run("se", lambda E, o=rrow, i=qrow, e=s_eps[0:1, 1:2]:
                E.activation(o, i, AF.Ln, bias=e),
                reads=(s_stq, s_eps), writes=(s_str,))
        ctx.run("se", lambda E, o=rrow, i=rrow: E.activation(o, i, AF.Exp, scale=-0.5),
                reads=(s_str,), writes=(s_str,))
        ctx.run("ve", lambda E, o=mrow, a=mrow, b=rrow: E.tensor_tensor(o, a, b, ALU.mult),
                reads=(s_stm, s_str), writes=(s_stm,))
        ctx.run("te", lambda E, o=p_b1[:, 0:TT2], l=s_ones[0:1, :], r=rrow:
                MM(E, o, l, r, start=True, stop=True),
                reads=(s_ones, s_str), writes=(p_b1,))
        ctx.run("te", lambda E, o=p_b2[:, 0:TT2], l=s_ones[0:1, :], r=mrow:
                MM(E, o, l, r, start=True, stop=True),
                reads=(s_ones, s_stm), writes=(p_b2,))
        for cb in range(CB):
            ctx.run("ve", lambda E, o=s_ln1[:], a=fm(src, cb, 0, TT2), b=p_b1[:, 0:TT2]:
                    E.tensor_tensor(o, a, b, ALU.mult),
                    reads=(src, p_b1), writes=(s_ln1,))
            ctx.run("ve", lambda E, o=fm(dst, cb, 0, TT2), a=s_ln1[:], b=p_b2[:, 0:TT2]:
                    E.tensor_tensor(o, a, b, ALU.subtract),
                    reads=(s_ln1, p_b2), writes=(dst,))

    def emit_proj(wkey, src, evac, nobs=CB):
        wout = s_wbuf[:].rearrange("p (cb co) -> p cb co", co=1024)
        win = io[wkey].rearrange("(cb p) co -> p cb co", p=128)
        dma_in(wout, win, sem="wl")
        for ob in range(nobs):
            pp = p_pr[ob % 2]
            for cb in range(CB):
                ctx.run("te", lambda E, o=pp[:, 0:TT2],
                        l=s_wbuf[:, 1024 * cb + 128 * ob: 1024 * cb + 128 * ob + 128],
                        r=fm(src, cb, 0, TT2), st=(cb == 0), sp=(cb == CB - 1):
                        MM(E, o, l, r, start=st, stop=sp),
                        reads=(s_wbuf, src), writes=(pp,), inc=(cb == CB - 1))
            evac(ob, pp)

    nt = p_tokens // TT2
    for t in range(nt):
        t0 = t * TT2
        dma_in(s_x[:].rearrange("p (cb tt) -> p cb tt", tt=TT2 + 1),
               io["xT"].rearrange("(cb p) tt -> p cb tt", p=128)[:, :, t0:t0 + TT2 + 1])
        dma_in(s_yl[:].rearrange("p (cb tt) -> p cb tt", tt=TT2),
               io["ylocal"].rearrange("(cb p) tt -> p cb tt", p=128)[:, :, t0:t0 + TT2])
        dma_in(s_rd[:].rearrange("p (cb tt) -> p cb tt", tt=TT2),
               io["RDs"].rearrange("(cb p) tt -> p cb tt", p=128)[:, :, t0:t0 + TT2])
        dma_in(s_g[:].rearrange("p (cb tt) -> p cb tt", tt=TT2),
               io["gsil"].rearrange("(cb p) tt -> p cb tt", p=128)[:, :, t0:t0 + TT2])

        for cb in range(CB):
            for half in range(2):
                h = 2 * cb + half
                hb = 64 * half
                ctx.run("te", lambda E, o=p_aux[hb:hb + 64, 0:TT2],
                        l=s_S0[hb:hb + 64, 64 * cb:64 * cb + 64],
                        r=fm(s_rd, cb, 0, TT2)[hb:hb + 64, :] if False else s_rd[hb:hb + 64, cb * TT2: cb * TT2 + TT2]:
                        MM(E, o, l, r, start=True, stop=True),
                        reads=(s_S0, s_rd), writes=(p_aux,))
            TT_("ve", fm(s_y, cb, 0, TT2), fm(s_yl, cb, 0, TT2), p_aux[:, 0:TT2], ALU.add)

        for cb in range(CB):
            ctx.run("se", lambda E, o=fm(s_sq, cb, 0, TT2), i=fm(s_y, cb, 0, TT2):
                    E.activation(o, i, AF.Square), reads=(s_y,), writes=(s_sq,))
        for cb in range(CB):
            ctx.run("te", lambda E, o=p_st[0:16, 0:TT2], l=s_dsel[:, 16 * cb:16 * cb + 16],
                    r=fm(s_y, cb, 0, TT2), st=(cb == 0), sp=(cb == CB - 1):
                    MM(E, o, l, r, start=st, stop=sp),
                    reads=(s_dsel, s_y), writes=(p_st,), inc=(cb == CB - 1))
        for cb in range(CB):
            ctx.run("te", lambda E, o=p_st[0:16, 256:256 + TT2], l=s_dsel[:, 16 * cb:16 * cb + 16],
                    r=fm(s_sq, cb, 0, TT2), st=(cb == 0), sp=(cb == CB - 1):
                    MM(E, o, l, r, start=st, stop=sp),
                    reads=(s_dsel, s_sq), writes=(p_st,), inc=(cb == CB - 1))
        ctx.run("ve", lambda E, o=s_stm[:], i=p_st[0:16, 0:TT2]:
                E.tensor_scalar(o, i, 1.0 / HS, None, ALU.mult),
                reads=(p_st,), writes=(s_stm,))
        ctx.run("ve", lambda E, o=s_stq[:], i=p_st[0:16, 256:256 + TT2]:
                E.tensor_scalar(o, i, 1.0 / HS, None, ALU.mult),
                reads=(p_st,), writes=(s_stq,))
        TT_("ve", s_str[:], s_stm[:], s_stm[:], ALU.mult)
        TT_("ve", s_stq[:], s_stq[:], s_str[:], ALU.subtract)
        ctx.run("se", lambda E, o=s_str[:], i=s_stq[:], e=s_eps[0:16, 0:1]:
                E.activation(o, i, AF.Ln, bias=e), reads=(s_stq, s_eps), writes=(s_str,))
        ctx.run("se", lambda E, o=s_str[:], i=s_str[:]:
                E.activation(o, i, AF.Exp, scale=-0.5), reads=(s_str,), writes=(s_str,))
        TT_("ve", s_st2[:], s_stm[:], s_str[:], ALU.mult)
        for cb in range(CB):
            ctx.run("te", lambda E, o=p_b1[:, 0:TT2], l=s_esel2[:, 128 * cb:128 * cb + 128],
                    r=s_str[:]: MM(E, o, l, r, start=True, stop=True),
                    reads=(s_esel2, s_str), writes=(p_b1,))
            ctx.run("te", lambda E, o=p_b2[:, 0:TT2], l=s_esel2[:, 128 * cb:128 * cb + 128],
                    r=s_st2[:]: MM(E, o, l, r, start=True, stop=True),
                    reads=(s_esel2, s_st2), writes=(p_b2,))
            ctx.run("ve", lambda E, o=s_ln1[:], a=fm(s_y, cb, 0, TT2), b=p_b1[:, 0:TT2]:
                    E.tensor_tensor(o, a, b, ALU.mult), reads=(s_y, p_b1), writes=(s_ln1,))
            ctx.run("ve", lambda E, o=s_ln1[:], a=s_ln1[:], b=p_b2[:, 0:TT2]:
                    E.tensor_tensor(o, a, b, ALU.subtract), reads=(s_ln1, p_b2), writes=(s_ln1,))
            ctx.run("ve", lambda E, o=s_ln1[:], i=s_ln1[:],
                    g1=s_cvec[:, 0 + cb:0 + cb + 1], b1=s_cvec[:, 8 + cb:8 + cb + 1]:
                    E.tensor_scalar(o, i, g1, b1, ALU.mult, ALU.add),
                    reads=(s_ln1, s_cvec), writes=(s_ln1,))
            ctx.run("ve", lambda E, o=fm(s_yg, cb, 0, TT2), a=s_ln1[:], b=fm(s_g, cb, 0, TT2):
                    E.tensor_tensor(o, a, b, ALU.mult), reads=(s_ln1, s_g), writes=(s_yg,))

        def evac_wo(ob, pp):
            ctx.run("ve", lambda E, o=fm(s_res, ob, 0, TT2), a=xcur(ob), b=pp[:, 0:TT2]:
                    E.tensor_tensor(o, a, b, ALU.add), reads=(s_x, pp), writes=(s_res,))
        emit_proj("Wo_t", s_yg, evac_wo)
        emit_ln(s_res, s_x1, LN_EPS)

        for ob in range(CB):
            pp = p_pr[ob % 2]
            for cb in range(CB):
                ctx.run("te", lambda E, o=pp[:, 0:TT2],
                        l=s_M[:, 1024 * cb + 128 * ob: 1024 * cb + 128 * ob + 128],
                        r=fm(s_x1, cb, 0, TT2), st=(cb == 0), sp=(cb == CB - 1):
                        MM(E, o, l, r, start=st, stop=sp),
                        reads=(s_M, s_x1), writes=(pp,), inc=(cb == CB - 1))
            ctx.run("ve", lambda E, o=s_ln1[:], i=pp[:, 0:TT2],
                    bi=s_cvec[:, 16 + ob:16 + ob + 1]:
                    E.tensor_scalar(o, i, bi, None, ALU.add), reads=(pp, s_cvec), writes=(s_ln1,))
            ctx.run("ve", lambda E, o=fm(s_res, ob, 0, TT2), a=fm(s_x1, ob, 0, TT2), b=s_ln1[:]:
                    E.tensor_tensor(o, a, b, ALU.add), reads=(s_x1, s_ln1), writes=(s_res,))
        emit_ln(s_res, s_x2, LN_EPS)
        dma_out(io["x2s"].rearrange("(cb p) tt -> p cb tt", p=128)[:, :, t0:t0 + TT2],
                s_x2[:].rearrange("p (cb tt) -> p cb tt", tt=TT2))

        def evac_gt(ob, pp):
            ctx.run("se", lambda E, o=fm(s_gt, ob, 0, TT2), i=pp[:, 0:TT2],
                    bi=s_cvec[:, 24 + ob:24 + ob + 1]:
                    E.activation(o, i, AF.Sigmoid, bias=bi), reads=(pp, s_cvec), writes=(s_gt,))
        emit_proj("Wg1_t", s_x2, evac_gt)

        def evac_q(ob, pp):
            ctx.run("se", lambda E, o=s_tA[:], i=pp[:, 0:TT2]: E.activation(o, i, AF.Abs),
                    reads=(pp,), writes=(s_tA,))
            ctx.run("se", lambda E, o=s_tB[:], i=s_tA[:]: E.activation(o, i, AF.Exp, scale=-1.0),
                    reads=(s_tA,), writes=(s_tB,))
            ctx.run("se", lambda E, o=s_tA[:], i=s_tB[:]: E.activation(o, i, AF.Ln, bias=1.0),
                    reads=(s_tB,), writes=(s_tA,))
            ctx.run("se", lambda E, o=s_tB[:], i=pp[:, 0:TT2]: E.activation(o, i, AF.Relu),
                    reads=(pp,), writes=(s_tB,))
            ctx.run("ve", lambda E, o=s_tA[:], a=s_tA[:], b=s_tB[:]:
                    E.tensor_tensor(o, a, b, ALU.add), reads=(s_tA, s_tB), writes=(s_tA,))
            ctx.run("ve", lambda E, o=fm(s_qf, ob, 0, TT2), a=fm(s_gt, ob, 0, TT2),
                    b=s_tA[:]: E.scalar_tensor_tensor(o, a, 1e-6, b, ALU.add, ALU.mult),
                    reads=(s_gt, s_tA), writes=(s_qf,))
        emit_proj("Wq_t", s_x2, evac_q)
        dma_out(io["qfs"].rearrange("(cb p) tt -> p cb tt", p=128)[:, :, t0:t0 + TT2],
                s_qf[:].rearrange("p (cb tt) -> p cb tt", tt=TT2))

        def evac_k(ob, pp):
            ctx.run("se", lambda E, o=s_tA[:], i=pp[:, 0:TT2]: E.activation(o, i, AF.Abs),
                    reads=(pp,), writes=(s_tA,))
            ctx.run("se", lambda E, o=s_tB[:], i=s_tA[:]: E.activation(o, i, AF.Exp, scale=-1.0),
                    reads=(s_tA,), writes=(s_tB,))
            ctx.run("se", lambda E, o=s_tA[:], i=s_tB[:]: E.activation(o, i, AF.Ln, bias=1.0),
                    reads=(s_tB,), writes=(s_tA,))
            ctx.run("se", lambda E, o=s_tB[:], i=pp[:, 0:TT2]: E.activation(o, i, AF.Relu),
                    reads=(pp,), writes=(s_tB,))
            ctx.run("ve", lambda E, o=fm(s_kf, ob, 0, TT2), a=s_tA[:], b=s_tB[:]:
                    E.tensor_tensor(o, a, b, ALU.add), reads=(s_tA, s_tB), writes=(s_kf,))
        emit_proj("Wk1_t", s_x2, evac_k)
        dma_out(io["kfs"].rearrange("(cb p) tt -> p cb tt", p=128)[:, :, t0:t0 + TT2],
                s_kf[:].rearrange("p (cb tt) -> p cb tt", tt=TT2))

        wout = s_wbuf[:].rearrange("p (cb co) -> p cb co", co=1024)
        win = io["Wv1_t"].rearrange("(cb p) co -> p cb co", p=128)
        dma_in(wout, win, sem="wl")
        for tc in range(2):
            for hf in range(2):
                pp = p_pr[hf]
                for cb in range(CB):
                    ctx.run("te", lambda E, o=pp[:],
                            l=s_x2[:, cb * TT2 + 128 * tc: cb * TT2 + 128 * tc + 128],
                            r=s_wbuf[:].rearrange("p (cb co) -> p cb co", co=1024)[:, cb, 512 * hf:512 * hf + 512],
                            st=(cb == 0), sp=(cb == CB - 1):
                            MM(E, o, l, r, start=st, stop=sp),
                            reads=(s_x2, s_wbuf), writes=(pp,), inc=(cb == CB - 1))
                ctx.run("ve", lambda E, o=s_vtm[:, 1024 * tc + 512 * hf: 1024 * tc + 512 * hf + 512],
                        i=pp[:]: E.tensor_copy(o, i), reads=(pp,), writes=(s_vtm,))
        dma_out(io["vtm"].rearrange("(nt p) c -> p nt c", p=128)[:, 2 * t:2 * t + 2, :],
                s_vtm[:].rearrange("p (nt c) -> p nt c", c=1024))

        first_tile = (t == 0)
        if first_tile:
            ctx.run("ve", lambda E, a=s_slp[0][:]: E.memset(a, 0.0), writes=(s_slp[0],))
        for tc in range(2):
            step = 2 * t + tc
            SA, SB = s_slp[step % 2], s_slp[(step + 1) % 2]
            for cb in range(CB):
                ctx.run("te", lambda E, o=p_tr[:, 0:128],
                        i=s_kf[:, cb * TT2 + 128 * tc: cb * TT2 + 128 * tc + 128],
                        d=s_id[:]: E.transpose(o, i, d),
                        reads=(s_kf, s_id), writes=(p_tr,))
                ctx.run("ve", lambda E, o=s_kftm[:], i=p_tr[:, 0:128]: E.tensor_copy(o, i),
                        reads=(p_tr,), writes=(s_kftm,))
                for half in range(2):
                    h = 2 * cb + half
                    hb = 64 * half
                    ctx.run("te", lambda E, o=p_sloc[hb:hb + 64, 64 * cb:64 * cb + 64],
                            l=s_kftm[:, hb:hb + 64],
                            r=s_vtm[:, 1024 * tc + 64 * h: 1024 * tc + 64 * h + 64]:
                            MM(E, o, l, r, start=True, stop=True),
                            reads=(s_kftm, s_vtm), writes=(p_sloc,))
                    ctx.run("ve", lambda E, o=SB[hb:hb + 64, 64 * cb:64 * cb + 64],
                            a=SA[hb:hb + 64, 64 * cb:64 * cb + 64],
                            b=p_sloc[hb:hb + 64, 64 * cb:64 * cb + 64]:
                            E.tensor_tensor(o, a, b, ALU.add),
                            reads=(SA, p_sloc), writes=(SB,))
        zA = s_zl[t % 2]
        zB = s_zl[(t + 1) % 2]
        if first_tile:
            ctx.run("ve", lambda E, a=zA[:]: E.memset(a, 0.0), writes=(zA,))
        for cb in range(CB):
            ctx.run("ve", lambda E, o=s_tA[:, 0:1], i=fm(s_kf, cb, 0, TT2):
                    E.reduce_sum(o, i, axis=mybir.AxisListType.X),
                    reads=(s_kf,), writes=(s_tA,))
            ctx.run("ve", lambda E, o=zB[:, cb:cb + 1], a=zA[:, cb:cb + 1], b=s_tA[:, 0:1]:
                    E.tensor_tensor(o, a, b, ALU.add), reads=(zA, s_tA), writes=(zB,))

    s_sl = s_slp[(2 * nt) % 2]
    dma_out(io["summ1"], s_sl[:])
    zfin = s_zl[nt % 2]
    dma_out(io["summZ"], zfin[:])
    ctx.wait("po", "out", ctx.barrier_token("out"))
    ctx.full_barrier()
    es2.close()


def build_phase2(p_tokens=P):
    nc = bass.Bass("TRN2", target_bir_lowering=False)
    io = {}
    io["xT"] = nc.dram_tensor("xT", [D, p_tokens + 1], F32, kind="ExternalInput").ap()
    io["ylocal"] = nc.dram_tensor("ylocal", [D, p_tokens], F32, kind="ExternalInput").ap()
    io["RDs"] = nc.dram_tensor("RDs", [D, p_tokens], F32, kind="ExternalInput").ap()
    io["gsil"] = nc.dram_tensor("gsil", [D, p_tokens], F32, kind="ExternalInput").ap()
    io["gS"] = nc.dram_tensor("gS", [128, 8 * 512], F32, kind="ExternalInput").ap()
    io["gD"] = nc.dram_tensor("gD", [128, 64], F32, kind="ExternalInput").ap()
    io["fmask"] = nc.dram_tensor("fmask", [128, 16], F32, kind="ExternalInput").ap()
    for n in ["Wo_t", "Wq_t", "Wk1_t", "Wv1_t", "Wg1_t",
              "wfr", "wfi", "Fr", "Fi", "Fin", "Grs", "Gis"]:
        io[n] = nc.dram_tensor(n, [D, D], F32R, kind="ExternalInput").ap()
    io["cvec2"] = nc.dram_tensor("cvec2", [128, 32], F32, kind="ExternalInput").ap()
    io["dselr"] = nc.dram_tensor("dselr", [128, 128], F32R, kind="ExternalInput").ap()
    io["esel2"] = nc.dram_tensor("esel2", [16, 1024], F32R, kind="ExternalInput").ap()
    io["ones128"] = nc.dram_tensor("ones128", [128, 128], F32R, kind="ExternalInput").ap()
    io["id128"] = nc.dram_tensor("id128", [128, 128], F32, kind="ExternalInput").ap()
    io["epsc"] = nc.dram_tensor("epsc", [128, 2], F32, kind="ExternalInput").ap()
    io["x2s"] = nc.dram_tensor("x2s", [D, p_tokens], F32R, kind="ExternalOutput").ap()
    io["qfs"] = nc.dram_tensor("qfs", [D, p_tokens], F32, kind="ExternalOutput").ap()
    io["kfs"] = nc.dram_tensor("kfs", [D, p_tokens], F32, kind="ExternalOutput").ap()
    io["vtm"] = nc.dram_tensor("vtm", [p_tokens, D], F32, kind="ExternalOutput").ap()
    io["summ1"] = nc.dram_tensor("summ1", [128, 512], F32, kind="ExternalOutput").ap()
    io["summZ"] = nc.dram_tensor("summZ", [128, 8], F32, kind="ExternalOutput").ap()
    with nc.Block() as block:
        ctx = Ctx(nc, block)
        emit_phase2(ctx, io, p_tokens)
        ctx.flush()
    return nc


def stage_phase2_shared(inputs):
    sh = {}
    for wn, key in [("Wo_t", "L0_attn_Wo"), ("Wq_t", "L1_attn_Wq"),
                    ("Wk1_t", "L1_attn_Wk"), ("Wv1_t", "L1_attn_Wv"),
                    ("Wg1_t", "L1_attn_Wg")]:
        sh[wn] = np.ascontiguousarray(np.asarray(inputs[key], np.float32).T)
    sh["wfr"] = np.ascontiguousarray(np.asarray(inputs["L0_ffn_w_freq_real"], np.float32))
    sh["wfi"] = np.ascontiguousarray(np.asarray(inputs["L0_ffn_w_freq_imag"], np.float32))
    d = np.arange(D)
    ang = 2 * np.pi * np.outer(d, d) / D
    sh["Fr"] = np.cos(ang).astype(np.float32)
    sh["Fi"] = (-np.sin(ang)).astype(np.float32)
    sh["Fin"] = (np.sin(ang)).astype(np.float32)
    sh["Grs"] = (np.cos(ang) / D).astype(np.float32)
    sh["Gis"] = (np.sin(ang) / D).astype(np.float32)
    cv = np.zeros((128, 32), np.float32)
    cv[:, 0:8] = pack_col(inputs["L0_attn_gn_gamma"])
    cv[:, 8:16] = pack_col(inputs["L0_attn_gn_beta"])
    cv[:, 16:24] = pack_col(inputs["L0_ffn_bias"])
    cv[:, 24:32] = pack_col(inputs["L1_attn_bg"])
    sh["cvec2"] = cv
    dsel = np.zeros((128, 128), np.float32)
    for cb in range(8):
        dsel[0:64, 16 * cb + 2 * cb] = 1.0
        dsel[64:128, 16 * cb + 2 * cb + 1] = 1.0
    sh["dsel"] = dsel
    sh["dselr"] = dsel
    esel2 = np.zeros((16, 1024), np.float32)
    for cb in range(8):
        esel2[2 * cb, 128 * cb:128 * cb + 64] = 1.0
        esel2[2 * cb + 1, 128 * cb + 64:128 * cb + 128] = 1.0
    sh["esel2"] = esel2
    sh["ones128"] = np.ones((128, 128), np.float32)
    sh["id128"] = np.eye(128, dtype=np.float32)
    epsc = np.zeros((128, 2), np.float32)
    epsc[:, 0] = GN_EPS
    epsc[:, 1] = LN_EPS
    sh["epsc"] = epsc
    return sh



TT3 = 256
C1 = 128
TTM = 512


def emit_phase3(ctx, io, p_tokens=P, gS_of=None, gZ_of=None):
    nc = ctx.nc
    if gS_of is None:
        gS_of = lambda r: io["gS1"][:, 512 * r:512 * (r + 1)]
        gZ_of = lambda r: io["gZ1"][:, 8 * r:8 * (r + 1)]
    es3h = contextlib.ExitStack()

    def al(name, shape, dt):
        return es3h.enter_context(nc.sbuf_tensor(name, shape, dt))

    def ap_(name, shape, dt):
        return es3h.enter_context(nc.psum_tensor(name, shape, dt))

    s_dsel = al("r_dsel", [128, 128], F32)
    s_esel2 = al("r_esel2", [16, 1024], F32R)
    s_eselE = al("r_eselE", [16, 2048], F32R)
    s_ones = al("r_ones", [128, 128], F32R)
    s_id = al("r_id", [128, 128], F32)
    s_mki = al("r_mki", [128, 128], F32)
    s_eps = al("r_eps", [128, 2], F32)
    s_fm = al("r_fm", [128, 16], F32)
    s_gw = al("r_gw", [128, 128], F32)
    s_idg = al("r_idg", [128, 128], F32)
    s_gbb = al("r_gbb", [128, 16], F32)
    s_becs = al("r_becs", [128, 64 + 64 + 128 + 24], F32)
    s_b2row = al("r_b2row", [1, 1024], F32R)

    s_S0 = al("r_S0", [128, 512], F32)
    s_z0 = al("r_z0", [128, 8], F32)
    s_gs = al("r_gs", [128, 512], F32)
    s_gz = al("r_gz", [128, 8], F32)
    s_tmps = al("r_tmps", [128, 512], F32)

    def dma_in(dst, src, sem="in"):
        return ctx.run("sy", lambda E, a=dst, b=src: E.dma_start(out=a, in_=b),
                       reads=(src,), writes=(dst,), sem=sem)

    def dma_out(dst, src):
        return ctx.run("po", lambda E, a=dst, b=src: E.dma_start(out=a, in_=b),
                       reads=(src,), writes=(dst,))

    dma_in(s_dsel[:], io["dsel"])
    dma_in(s_esel2[:], io["esel2"])
    dma_in(s_eselE[:], io["eselE"])
    dma_in(s_ones[:], io["ones128"])
    dma_in(s_id[:], io["id128"])
    dma_in(s_mki[:], io["mask128i"])
    dma_in(s_eps[:], io["epsc"])
    dma_in(s_fm[:], io["fmask"])
    dma_in(s_gw[:].rearrange("p (cb e) -> p cb e", e=16),
           io["gwT"].rearrange("(cb p) e -> p cb e", p=128))
    dma_in(s_idg[:], io["id128f"])
    dma_in(s_becs[:, 0:64], io["be1c"])
    dma_in(s_becs[:, 64:128], io["be3c"])
    dma_in(s_becs[:, 128:256], io["be2c"])
    dma_in(s_becs[:, 256:280], io["bsc"])

    def be1col(e, ic):
        return s_becs[:, 4 * e + ic: 4 * e + ic + 1]

    def be3col(e, ic):
        return s_becs[:, 64 + 4 * e + ic: 64 + 4 * e + ic + 1]

    def bscol(j, cb):
        return s_becs[:, 256 + 8 * j + cb: 256 + 8 * j + cb + 1]

    ctx.run("ve", lambda E, a=s_S0[:]: E.memset(a, 0.0), writes=(s_S0,))
    ctx.run("ve", lambda E, a=s_z0[:]: E.memset(a, 0.0), writes=(s_z0,))
    for j in range(NCORE):
        dma_in(s_gs[:], gS_of(j))
        dma_in(s_gz[:], gZ_of(j))
        acol = s_fm[:, j:j + 1]
        ctx.run("ve", lambda E, o=s_S0[:], a=s_gs[:], sc=acol, b=s_S0[:]:
                E.scalar_tensor_tensor(o, a, sc, b, ALU.mult, ALU.add),
                reads=(s_gs, s_fm, s_S0), writes=(s_S0,))
        ctx.run("ve", lambda E, o=s_z0[:], a=s_gz[:], sc=acol, b=s_z0[:]:
                E.scalar_tensor_tensor(o, a, sc, b, ALU.mult, ALU.add),
                reads=(s_gz, s_fm, s_z0), writes=(s_z0,))

    p_misc = ap_("r_pmisc", [128, 512], F32)
    dma_in(s_gbb[:], io["gbb"])

    es3a = contextlib.ExitStack()

    def al3(name, shape, dt):
        return es3a.enter_context(nc.sbuf_tensor(name, shape, dt))
    s_wbuf = al3("r_wbuf2", [128, 8192], F32R)
    s_qf = al3("r_qf", [128, CB * TT3], F32)
    s_kf = al3("r_kf", [128, CB * TT3], F32)
    s_vtm = al3("r_vtm", [128, 2 * 1024], F32)
    s_x2 = al3("r_x2", [128, CB * TT3], F32R)
    s_att = al3("r_att", [128, CB * TT3], F32R)
    s_res = al3("r_res", [128, CB * TT3], F32R)
    s_x3 = al3("r_x3", [128, CB * TT3], F32)
    s_x3r = al3("r_x3r", [128, CB * TT3], F32R)
    s_sq = al3("r_sq", [128, CB * TT3], F32R)
    s_kcs = al3("r_kcs", [128, 1024], F32)
    s_kct = al3("r_kct", [128, 1024], F32)
    s_qk = al3("r_qk", [128, 1024], F32)
    s_kftm = al3("r_kftm", [128, 128], F32)
    s_atm1 = al3("r_atm1", [128, 128], F32)
    s_nA = al3("r_nA", [128, 128], F32)
    s_dr = al3("r_dr", [16, 128], F32)
    s_dr2 = al3("r_dr2", [16, 128], F32R)
    s_Sr = [al3("r_Sr0", [128, 512], F32), al3("r_Sr1", [128, 512], F32)]
    s_zr = [al3("r_zr0", [128, 8], F32), al3("r_zr1", [128, 8], F32)]
    s_ln1 = al3("r_ln1", [128, TT3], F32)
    s_stm = al3("r_stm", [16, TT3], F32R)
    s_stq = al3("r_stq", [16, TT3], F32R)
    s_str = al3("r_str", [16, TT3], F32R)
    s_z64b = al3("r_z64b", [128, 128], F32)

    p_at = ap_("r_pat", [128, 512], F32)
    p_n1 = ap_("r_pn1", [128, 512], F32)
    p_n2 = ap_("r_pn2", [128, 512], F32)
    p_b1 = ap_("r_pb1", [128, 512], F32)
    p_tr = ap_("r_ptr", [128, 512], F32)
    p_st = ap_("r_pst", [16, 512], F32)
    p_pr = [p_misc, ap_("r_ppr1", [128, 512], F32)]

    ctx.run("ve", lambda E, a=s_z64b[:]: E.memset(a, 0.0), writes=(s_z64b,))
    ctx.run("ve", lambda E, o=s_Sr[0][:], i=s_S0[:]: E.tensor_copy(o, i),
            reads=(s_S0,), writes=(s_Sr[0],))
    ctx.run("ve", lambda E, o=s_zr[0][:], i=s_z0[:]: E.tensor_copy(o, i),
            reads=(s_z0,), writes=(s_zr[0],))

    def fm3(t_sb, cb, lo, n, tt=TT3):
        return t_sb[:, cb * tt + lo: cb * tt + lo + n]

    def emit_ln3(src, dst):
        for cb in range(CB):
            ctx.run("se", lambda E, o=fm3(s_sq, cb, 0, TT3), i=fm3(src, cb, 0, TT3):
                    E.activation(o, i, AF.Square), reads=(src,), writes=(s_sq,))
        for cb in range(CB):
            ctx.run("te", lambda E, o=p_st[0:1, 0:TT3], l=s_ones[:, 0:1],
                    r=fm3(src, cb, 0, TT3), st=(cb == 0), sp=(cb == CB - 1):
                    MM(E, o, l, r, start=st, stop=sp),
                    reads=(s_ones, src), writes=(p_st,), inc=(cb == CB - 1))
        for cb in range(CB):
            ctx.run("te", lambda E, o=p_st[0:1, 256:256 + TT3], l=s_ones[:, 0:1],
                    r=fm3(s_sq, cb, 0, TT3), st=(cb == 0), sp=(cb == CB - 1):
                    MM(E, o, l, r, start=st, stop=sp),
                    reads=(s_ones, s_sq), writes=(p_st,), inc=(cb == CB - 1))
        mrow, qrow, rrow = s_stm[0:1, :], s_stq[0:1, :], s_str[0:1, :]
        ctx.run("ve", lambda E, o=mrow, i=p_st[0:1, 0:TT3]:
                E.tensor_scalar(o, i, 1.0 / D, None, ALU.mult), reads=(p_st,), writes=(s_stm,))
        ctx.run("ve", lambda E, o=qrow, i=p_st[0:1, 256:256 + TT3]:
                E.tensor_scalar(o, i, 1.0 / D, None, ALU.mult), reads=(p_st,), writes=(s_stq,))
        ctx.run("ve", lambda E, o=rrow, a=mrow, b=mrow: E.tensor_tensor(o, a, b, ALU.mult),
                reads=(s_stm,), writes=(s_str,))
        ctx.run("ve", lambda E, o=qrow, a=qrow, b=rrow: E.tensor_tensor(o, a, b, ALU.subtract),
                reads=(s_stq, s_str), writes=(s_stq,))
        ctx.run("se", lambda E, o=rrow, i=qrow, e=s_eps[0:1, 1:2]:
                E.activation(o, i, AF.Ln, bias=e), reads=(s_stq, s_eps), writes=(s_str,))
        ctx.run("se", lambda E, o=rrow, i=rrow: E.activation(o, i, AF.Exp, scale=-0.5),
                reads=(s_str,), writes=(s_str,))
        ctx.run("ve", lambda E, o=mrow, a=mrow, b=rrow: E.tensor_tensor(o, a, b, ALU.mult),
                reads=(s_stm, s_str), writes=(s_stm,))
        ctx.run("te", lambda E, o=p_b1[:, 0:TT3], l=s_ones[0:1, :], r=rrow:
                MM(E, o, l, r, start=True, stop=True), reads=(s_ones, s_str), writes=(p_b1,))
        ctx.run("te", lambda E, o=p_b1[:, 256:256 + TT3], l=s_ones[0:1, :], r=mrow:
                MM(E, o, l, r, start=True, stop=True), reads=(s_ones, s_stm), writes=(p_b1,))
        for cb in range(CB):
            ctx.run("ve", lambda E, o=s_ln1[:], a=fm3(src, cb, 0, TT3), b=p_b1[:, 0:TT3]:
                    E.tensor_tensor(o, a, b, ALU.mult), reads=(src, p_b1), writes=(s_ln1,))
            ctx.run("ve", lambda E, o=fm3(dst, cb, 0, TT3), a=s_ln1[:], b=p_b1[:, 256:256 + TT3]:
                    E.tensor_tensor(o, a, b, ALU.subtract), reads=(s_ln1, p_b1), writes=(dst,))

    dma_in(s_wbuf[:].rearrange("p (cb co) -> p cb co", co=1024),
           io["Wo1_t"].rearrange("(cb p) co -> p cb co", p=128), sem="wl")
    nt3 = p_tokens // TT3
    for t in range(nt3):
        t0 = t * TT3
        dma_in(s_qf[:].rearrange("p (cb tt) -> p cb tt", tt=TT3),
               io["qfs"].rearrange("(cb p) tt -> p cb tt", p=128)[:, :, t0:t0 + TT3])
        dma_in(s_kf[:].rearrange("p (cb tt) -> p cb tt", tt=TT3),
               io["kfs"].rearrange("(cb p) tt -> p cb tt", p=128)[:, :, t0:t0 + TT3])
        dma_in(s_x2[:].rearrange("p (cb tt) -> p cb tt", tt=TT3),
               io["x2s"].rearrange("(cb p) tt -> p cb tt", p=128)[:, :, t0:t0 + TT3])
        dma_in(s_vtm[:].rearrange("p (nt c) -> p nt c", c=1024),
               io["vtm"].rearrange("(nt p) c -> p nt c", p=128)[:, 2 * t:2 * t + 2, :])

        for kc in range(2):
            step = 2 * t + kc
            zA, zB = s_zr[step % 2], s_zr[(step + 1) % 2]
            SA, SB = s_Sr[step % 2], s_Sr[(step + 1) % 2]
            for cb in range(CB):
                ctx.run("ve", lambda E, o=s_kcs[:, 128 * cb:128 * cb + 128],
                        a=fm3(s_kf, cb, 128 * kc, C1), z=s_z64b[:]:
                        E.tensor_tensor_scan(o, a, z, 0.0, ALU.add, ALU.add),
                        reads=(s_kf, s_z64b), writes=(s_kcs,))
                ctx.run("ve", lambda E, o=s_kct[:, 128 * cb:128 * cb + 128],
                        i=s_kcs[:, 128 * cb:128 * cb + 128], sc=zA[:, cb:cb + 1]:
                        E.tensor_scalar(o, i, sc, None, ALU.add),
                        reads=(s_kcs, zA), writes=(s_kct,))
                ctx.run("ve", lambda E, o=zB[:, cb:cb + 1], a=zA[:, cb:cb + 1],
                        b=s_kcs[:, 128 * cb + 127:128 * cb + 128]:
                        E.tensor_tensor(o, a, b, ALU.add), reads=(zA, s_kcs), writes=(zB,))
            qfv = s_qf[:].rearrange("p (cb tt) -> p cb tt", tt=TT3)[:, :, 128 * kc:128 * kc + C1]
            ctx.run("ve", lambda E, o=s_qk[:].rearrange("p (cb c) -> p cb c", c=C1),
                    a=qfv, b=s_kct[:].rearrange("p (cb c) -> p cb c", c=C1):
                    E.tensor_tensor(o, a, b, ALU.mult), reads=(s_qf, s_kct), writes=(s_qk,))
            for cb in range(CB):
                ctx.run("te", lambda E, o=p_st[0:16, 0:C1], l=s_dsel[:, 16 * cb:16 * cb + 16],
                        r=s_qk[:, 128 * cb:128 * cb + 128], st=(cb == 0), sp=(cb == CB - 1):
                        MM(E, o, l, r, start=st, stop=sp),
                        reads=(s_dsel, s_qk), writes=(p_st,), inc=(cb == CB - 1))
            ctx.run("ve", lambda E, o=s_dr[:], i=p_st[0:16, 0:C1]:
                    E.tensor_scalar(o, i, 1e-6, None, ALU.add), reads=(p_st,), writes=(s_dr,))
            ctx.run("ve", lambda E, o=s_dr2[:], i=s_dr[:]: E.reciprocal(o, i),
                    reads=(s_dr,), writes=(s_dr2,))

            for cb in range(CB):
                ctx.run("te", lambda E, o=p_tr[:, 0:128],
                        i=fm3(s_kf, cb, 128 * kc, C1), d=s_id[:]:
                        E.transpose(o, i, d), reads=(s_kf, s_id), writes=(p_tr,))
                ctx.run("ve", lambda E, o=s_kftm[:], i=p_tr[:, 0:128]: E.tensor_copy(o, i),
                        reads=(p_tr,), writes=(s_kftm,))
                ctx.run("te", lambda E, o=p_b1[:, 0:C1], l=s_esel2[:, 128 * cb:128 * cb + 128],
                        r=s_dr2[:]: MM(E, o, l, r, start=True, stop=True),
                        reads=(s_esel2, s_dr2), writes=(p_b1,))
                for half in range(2):
                    h = 2 * cb + half
                    hb = 64 * half
                    ctx.run("te", lambda E, o=p_at[:, 0:C1],
                            l=fm3(s_kf, cb, 128 * kc, C1)[hb:hb + 64, :] if False else s_kf[hb:hb + 64, cb * TT3 + 128 * kc: cb * TT3 + 128 * kc + C1],
                            r=s_qf[hb:hb + 64, cb * TT3 + 128 * kc: cb * TT3 + 128 * kc + C1]:
                            MM(E, o, l, r, start=True, stop=True),
                            reads=(s_kf, s_qf), writes=(p_at,))
                    ctx.run("ve", lambda E, o=s_atm1[:], a=p_at[:, 0:C1], b=s_mki[:]:
                            E.tensor_tensor(o, a, b, ALU.mult),
                            reads=(p_at, s_mki), writes=(s_atm1,))
                    ctx.run("te", lambda E, o=p_n1[hb:hb + 64, 0:C1],
                            l=s_vtm[:, 1024 * kc + 64 * h: 1024 * kc + 64 * h + 64],
                            r=s_atm1[:]: MM(E, o, l, r, start=True, stop=True),
                            reads=(s_vtm, s_atm1), writes=(p_n1,))
                    ctx.run("te", lambda E, o=p_n2[hb:hb + 64, 0:C1],
                            l=SA[hb:hb + 64, 64 * cb:64 * cb + 64],
                            r=s_qf[hb:hb + 64, cb * TT3 + 128 * kc: cb * TT3 + 128 * kc + C1]:
                            MM(E, o, l, r, start=True, stop=True),
                            reads=(SA, s_qf), writes=(p_n2,))
                    ctx.run("te", lambda E, o=p_misc[hb:hb + 64, 256:256 + 64],
                            l=s_kftm[:, hb:hb + 64],
                            r=s_vtm[:, 1024 * kc + 64 * h: 1024 * kc + 64 * h + 64]:
                            MM(E, o, l, r, start=True, stop=True),
                            reads=(s_kftm, s_vtm), writes=(p_misc,))
                    ctx.run("ve", lambda E, o=SB[hb:hb + 64, 64 * cb:64 * cb + 64],
                            a=SA[hb:hb + 64, 64 * cb:64 * cb + 64],
                            b=p_misc[hb:hb + 64, 256:256 + 64]:
                            E.tensor_tensor(o, a, b, ALU.add),
                            reads=(SA, p_misc), writes=(SB,))
                ctx.run("ve", lambda E, o=s_nA[:, 0:C1], i=p_n1[:, 0:C1]: E.tensor_copy(o, i),
                        reads=(p_n1,), writes=(s_nA,))
                ctx.run("ve", lambda E, o=s_nA[:, 0:C1], a=s_nA[:, 0:C1], b=p_n2[:, 0:C1]:
                        E.tensor_tensor(o, a, b, ALU.add), reads=(s_nA, p_n2), writes=(s_nA,))
                ctx.run("ve", lambda E, o=fm3(s_att, cb, 128 * kc, C1), a=s_nA[:, 0:C1],
                        b=p_b1[:, 0:C1]: E.tensor_tensor(o, a, b, ALU.mult),
                        reads=(s_nA, p_b1), writes=(s_att,))

        for ob in range(CB):
            pp = p_pr[ob % 2]
            for cb in range(CB):
                ctx.run("te", lambda E, o=pp[:, 0:TT3],
                        l=s_wbuf[:, 1024 * cb + 128 * ob: 1024 * cb + 128 * ob + 128],
                        r=fm3(s_att, cb, 0, TT3), st=(cb == 0), sp=(cb == CB - 1):
                        MM(E, o, l, r, start=st, stop=sp),
                        reads=(s_wbuf, s_att), writes=(pp,), inc=(cb == CB - 1))
            ctx.run("ve", lambda E, o=fm3(s_res, ob, 0, TT3), a=fm3(s_x2, ob, 0, TT3),
                    b=pp[:, 0:TT3]: E.tensor_tensor(o, a, b, ALU.add),
                    reads=(s_x2, pp), writes=(s_res,))
        emit_ln3(s_res, s_x3)
        for cb in range(CB):
            ctx.run("ve", lambda E, o=fm3(s_x3r, cb, 0, TT3), i=fm3(s_x3, cb, 0, TT3):
                    E.tensor_copy(o, i), reads=(s_x3,), writes=(s_x3r,))
        dma_out(io["x3s"].rearrange("(cb p) tt -> p cb tt", p=128)[:, :, t0:t0 + TT3],
                s_x3[:].rearrange("p (cb tt) -> p cb tt", tt=TT3))
        dma_out(io["x3r"].rearrange("(cb p) tt -> p cb tt", p=128)[:, :, t0:t0 + TT3],
                s_x3r[:].rearrange("p (cb tt) -> p cb tt", tt=TT3))

    ctx.full_barrier()
    es3a.close()

    s_x3m = al("r_x3m", [128, CB * TTM], F32R)
    s_x3g = al("r_x3g", [128, CB * TTM], F32)
    s_ya = [al("r_ya0", [128, CB * TTM], F32), al("r_ya1", [128, CB * TTM], F32)]
    s_h1 = al("r_h1", [128, 2048], F32R)
    s_h3 = al("r_h3", [128, 2048], F32)
    s_zz = al("r_zz", [128, 4096], F32R)
    s_we1 = al("r_we1", [128, 4096], F32R)
    s_we3 = al("r_we3", [128, 4096], F32R)
    s_we2 = al("r_we2", [128, 4096], F32R)
    s_combT = al("r_combT", [16, TTM], F32R)
    s_cb1 = al("r_cb1", [128, 512], F32R)
    s_x4 = al("r_x4", [128, CB * TTM], F32)
    s_sq2 = al("r_sq2", [128, 2048], F32)
    s_onesf = al("r_onesf", [128, 128], F32)
    s_ln1m = al("r_ln1m", [128, 256], F32)
    s_stmm = al("r_stmm", [1, 512], F32)
    s_stqm = al("r_stqm", [1, 512], F32)
    s_strm = al("r_strm", [1, 512], F32)
    s_sc = al("r_sc", [128, 16], F32)
    s_se = al("r_se", [128, 16], F32)
    s_so = al("r_so", [128, 16], F32)
    s_sb = al("r_sb", [128, 16], F32)
    s_msk = al("r_msk", [128, 16], F32)
    s_c1 = al("r_c1", [128, 1], F32)
    s_c2 = al("r_c2", [128, 1], F32)
    s_c3 = al("r_c3", [128, 1], F32)
    s_gpA = al("r_gpA", [128, 4], F32)
    s_gpB = al("r_gpB", [128, 4], F32)
    s_kp = al("r_kp", [128, 4], F32)
    s_pen = al("r_pen", [128, 4], F32)

    def emit_ln_m(src, dst):
        for half in range(2):
            for cb in range(CB):
                ctx.run("te", lambda E, o=p_st[0:1, 0:256], l=s_onesf[:, 0:1],
                        r=src[:, cb * TTM + 256 * half: cb * TTM + 256 * half + 256],
                        st=(cb == 0), sp=(cb == CB - 1):
                        E.matmul(o, l, r, start=st, stop=sp),
                        reads=(s_onesf, src), writes=(p_st,), inc=(cb == CB - 1))
            for cb in range(CB):
                ctx.run("se", lambda E, o=s_sq2[:, 256 * cb:256 * cb + 256],
                        i=src[:, cb * TTM + 256 * half: cb * TTM + 256 * half + 256]:
                        E.activation(o, i, AF.Square), reads=(src,), writes=(s_sq2,))
            for cb in range(CB):
                ctx.run("te", lambda E, o=p_st[0:1, 256:512], l=s_onesf[:, 0:1],
                        r=s_sq2[:, 256 * cb:256 * cb + 256], st=(cb == 0), sp=(cb == CB - 1):
                        E.matmul(o, l, r, start=st, stop=sp),
                        reads=(s_onesf, s_sq2), writes=(p_st,), inc=(cb == CB - 1))
            mrow = s_stmm[0:1, 256 * half:256 * half + 256]
            qrow = s_stqm[0:1, 256 * half:256 * half + 256]
            rrow = s_strm[0:1, 256 * half:256 * half + 256]
            ctx.run("ve", lambda E, o=mrow, i=p_st[0:1, 0:256]:
                    E.tensor_scalar(o, i, 1.0 / D, None, ALU.mult), reads=(p_st,), writes=(s_stmm,))
            ctx.run("ve", lambda E, o=qrow, i=p_st[0:1, 256:512]:
                    E.tensor_scalar(o, i, 1.0 / D, None, ALU.mult), reads=(p_st,), writes=(s_stqm,))
            ctx.run("ve", lambda E, o=rrow, a=mrow, b=mrow: E.tensor_tensor(o, a, b, ALU.mult),
                    reads=(s_stmm,), writes=(s_strm,))
            ctx.run("ve", lambda E, o=qrow, a=qrow, b=rrow: E.tensor_tensor(o, a, b, ALU.subtract),
                    reads=(s_stqm, s_strm), writes=(s_stqm,))
            ctx.run("se", lambda E, o=rrow, i=qrow, e=s_eps[0:1, 1:2]:
                    E.activation(o, i, AF.Ln, bias=e), reads=(s_stqm, s_eps), writes=(s_strm,))
            ctx.run("se", lambda E, o=rrow, i=rrow: E.activation(o, i, AF.Exp, scale=-0.5),
                    reads=(s_strm,), writes=(s_strm,))
            ctx.run("ve", lambda E, o=mrow, a=mrow, b=rrow: E.tensor_tensor(o, a, b, ALU.mult),
                    reads=(s_stmm, s_strm), writes=(s_stmm,))
            ctx.run("te", lambda E, o=p_b1[:, 0:256], l=s_onesf[0:1, :], r=rrow:
                    E.matmul(o, l, r, start=True, stop=True), reads=(s_onesf, s_strm), writes=(p_b1,))
            ctx.run("te", lambda E, o=p_b1[:, 256:512], l=s_onesf[0:1, :], r=mrow:
                    E.matmul(o, l, r, start=True, stop=True), reads=(s_onesf, s_stmm), writes=(p_b1,))
            for cb in range(CB):
                sl = slice(cb * TTM + 256 * half, cb * TTM + 256 * half + 256)
                ctx.run("ve", lambda E, o=s_ln1m[:], a=src[:, sl], b=p_b1[:, 0:256]:
                        E.tensor_tensor(o, a, b, ALU.mult), reads=(src, p_b1), writes=(s_ln1m,))
                ctx.run("ve", lambda E, o=dst[:, sl], a=s_ln1m[:], b=p_b1[:, 256:512]:
                        E.tensor_tensor(o, a, b, ALU.subtract), reads=(s_ln1m, p_b1), writes=(dst,))

    dma_in(s_onesf[:], io["ones128f"])
    ntm = p_tokens // TTM
    for t in range(ntm):
        t0 = t * TTM
        dma_in(s_x3m[:].rearrange("p (cb tt) -> p cb tt", tt=TTM),
               io["x3r"].rearrange("(cb p) tt -> p cb tt", p=128)[:, :, t0:t0 + TTM])
        dma_in(s_x3g[:].rearrange("p (cb tt) -> p cb tt", tt=TTM),
               io["x3s"].rearrange("(cb p) tt -> p cb tt", p=128)[:, :, t0:t0 + TTM])

        for tc in range(4):
            for cb in range(CB):
                ctx.run("te", lambda E, o=p_misc[:, 0:16],
                        l=s_x3g[:, cb * TTM + 128 * tc: cb * TTM + 128 * tc + 128],
                        r=s_gw[:, 16 * cb:16 * cb + 16], st=(cb == 0), sp=(cb == CB - 1):
                        E.matmul(o, l, r, start=st, stop=sp),
                        reads=(s_x3g, s_gw), writes=(p_misc,), inc=(cb == CB - 1))
            ctx.run("ve", lambda E, o=s_sc[:], i=p_misc[:, 0:16]: E.tensor_copy(o, i),
                    reads=(p_misc,), writes=(s_sc,))
            ctx.run("ve", lambda E, o=s_c1[:], i=s_sc[:]:
                    E.reduce_max(o, i, axis=mybir.AxisListType.X), reads=(s_sc,), writes=(s_c1,))
            ctx.run("ve", lambda E, o=s_c2[:], i=s_c1[:]:
                    E.tensor_scalar(o, i, -1.0, None, ALU.mult), reads=(s_c1,), writes=(s_c2,))
            ctx.run("se", lambda E, o=s_se[:], i=s_sc[:], b=s_c2[:]:
                    E.activation(o, i, AF.Exp, bias=b), reads=(s_sc, s_c2), writes=(s_se,))
            ctx.run("ve", lambda E, o=s_c1[:], i=s_se[:]:
                    E.reduce_sum(o, i, axis=mybir.AxisListType.X), reads=(s_se,), writes=(s_c1,))
            ctx.run("ve", lambda E, o=s_c3[:], i=s_c1[:]: E.reciprocal(o, i),
                    reads=(s_c1,), writes=(s_c3,))
            ctx.run("ve", lambda E, o=s_so[:], i=s_se[:], sc=s_c3[:]:
                    E.tensor_scalar(o, i, sc, None, ALU.mult), reads=(s_se, s_c3), writes=(s_so,))
            ctx.run("ve", lambda E, o=s_sb[:], a=s_so[:], b=s_gbb[:]:
                    E.tensor_tensor(o, a, b, ALU.add), reads=(s_so, s_gbb), writes=(s_sb,))
            sgv = s_sb[:].rearrange("p (g e) -> p g e", e=4)
            pairs = [(0, 1), (0, 2), (0, 3), (1, 2), (1, 3), (2, 3)]
            for pi, (i_, j_) in enumerate(pairs):
                if pi == 0:
                    ctx.run("ve", lambda E, o=s_gpA[:], a=sgv[:, :, i_:i_ + 1],
                            b=sgv[:, :, j_:j_ + 1]: E.tensor_tensor(o, a, b, ALU.add),
                            reads=(s_sb,), writes=(s_gpA,))
                else:
                    ctx.run("ve", lambda E, o=s_gpB[:], a=sgv[:, :, i_:i_ + 1],
                            b=sgv[:, :, j_:j_ + 1]: E.tensor_tensor(o, a, b, ALU.add),
                            reads=(s_sb,), writes=(s_gpB,))
                    ctx.run("ve", lambda E, o=s_gpA[:], a=s_gpA[:], b=s_gpB[:]:
                            E.tensor_tensor(o, a, b, ALU.max), reads=(s_gpA, s_gpB), writes=(s_gpA,))
            a0, b0 = s_gpA[:, 0:1], s_gpA[:, 1:2]
            c0, d0 = s_gpA[:, 2:3], s_gpA[:, 3:4]
            ctx.run("ve", lambda E, o=s_c1[:], a=a0, b=b0: E.tensor_tensor(o, a, b, ALU.max),
                    reads=(s_gpA,), writes=(s_c1,))
            ctx.run("ve", lambda E, o=s_c2[:], a=c0, b=d0: E.tensor_tensor(o, a, b, ALU.max),
                    reads=(s_gpA,), writes=(s_c2,))
            ctx.run("ve", lambda E, o=s_c1[:], a=s_c1[:], b=s_c2[:]:
                    E.tensor_tensor(o, a, b, ALU.min), reads=(s_c1, s_c2), writes=(s_c1,))
            ctx.run("ve", lambda E, o=s_c2[:], a=a0, b=b0: E.tensor_tensor(o, a, b, ALU.min),
                    reads=(s_gpA,), writes=(s_c2,))
            ctx.run("ve", lambda E, o=s_c3[:], a=c0, b=d0: E.tensor_tensor(o, a, b, ALU.min),
                    reads=(s_gpA,), writes=(s_c3,))
            ctx.run("ve", lambda E, o=s_c2[:], a=s_c2[:], b=s_c3[:]:
                    E.tensor_tensor(o, a, b, ALU.max), reads=(s_c2, s_c3), writes=(s_c2,))
            ctx.run("ve", lambda E, o=s_c1[:], a=s_c1[:], b=s_c2[:]:
                    E.tensor_tensor(o, a, b, ALU.max), reads=(s_c1, s_c2), writes=(s_c1,))
            ctx.run("ve", lambda E, o=s_kp[:], i=s_gpA[:], sc=s_c1[:]:
                    E.tensor_scalar(o, i, sc, None, ALU.is_ge), reads=(s_gpA, s_c1), writes=(s_kp,))
            ctx.run("ve", lambda E, o=s_pen[:], i=s_kp[:]:
                    E.tensor_scalar(o, i, 1.0, 1e30, ALU.subtract, ALU.mult),
                    reads=(s_kp,), writes=(s_pen,))
            for g in range(4):
                ctx.run("ve", lambda E, o=s_msk[:, 4 * g:4 * g + 4], i=s_sb[:, 4 * g:4 * g + 4],
                        s1=s_kp[:, g:g + 1], s2=s_pen[:, g:g + 1]:
                        E.tensor_scalar(o, i, s1, s2, ALU.mult, ALU.add),
                        reads=(s_sb, s_kp, s_pen), writes=(s_msk,))
            ctx.run("ve", lambda E, o=s_c1[:], i=s_msk[:]:
                    E.reduce_max(o, i, axis=mybir.AxisListType.X), reads=(s_msk,), writes=(s_c1,))
            ctx.run("ve", lambda E, o=s_se[:], i=s_msk[:], sc=s_c1[:]:
                    E.tensor_scalar(o, i, sc, None, ALU.is_equal), reads=(s_msk, s_c1), writes=(s_se,))
            ctx.run("ve", lambda E, o=s_se[:], a=s_se[:], b=s_msk[:]:
                    E.scalar_tensor_tensor(o, a, -1e30, b, ALU.mult, ALU.add),
                    reads=(s_se, s_msk), writes=(s_se,))
            ctx.run("ve", lambda E, o=s_c1[:], i=s_se[:]:
                    E.reduce_max(o, i, axis=mybir.AxisListType.X), reads=(s_se,), writes=(s_c1,))
            ctx.run("ve", lambda E, o=s_msk[:], i=s_msk[:], sc=s_c1[:]:
                    E.tensor_scalar(o, i, sc, None, ALU.is_ge), reads=(s_msk, s_c1), writes=(s_msk,))
            ctx.run("ve", lambda E, o=s_so[:], a=s_so[:], b=s_msk[:]:
                    E.tensor_tensor(o, a, b, ALU.mult), reads=(s_so, s_msk), writes=(s_so,))
            ctx.run("te", lambda E, o=p_misc[0:16, 128:256], i=s_so[:], d=s_idg[:]:
                    E.transpose(o, i, d), reads=(s_so, s_idg), writes=(p_misc,))
            ctx.run("ve", lambda E, o=s_combT[0:16, 128 * tc:128 * tc + 128],
                    i=p_misc[0:16, 128:256]: E.tensor_copy(o, i), reads=(p_misc,), writes=(s_combT,))

        ya0 = s_ya[0]
        for ih in range(2):
            for wname, dst, bj, act in [("ws1", s_h1, 0, True), ("ws3", s_h3, 1, False)]:
                dma_in(s_we1[:].rearrange("p (cb i) -> p cb i", i=512),
                       io[wname].rearrange("(cb p) i -> p cb i", p=128)[:, :, 512 * ih:512 * ih + 512],
                       sem="wl")
                for ic in range(4):
                    pp = p_pr[ic % 2]
                    for cb in range(CB):
                        ctx.run("te", lambda E, o=pp[:, 0:512],
                                l=s_we1[:, 512 * cb + 128 * ic: 512 * cb + 128 * ic + 128],
                                r=s_x3m[:, cb * TTM: cb * TTM + 512],
                                st=(cb == 0), sp=(cb == CB - 1):
                                MM(E, o, l, r, start=st, stop=sp),
                                reads=(s_we1, s_x3m), writes=(pp,), inc=(cb == CB - 1))
                    oc = 4 * ih + ic
                    if act:
                        ctx.run("se", lambda E, o=s_h1[:, 512 * ic:512 * ic + 512],
                                i=pp[:, 0:512], b=bscol(bj, oc):
                                E.activation(o, i, AF.Silu, bias=b),
                                reads=(pp, s_becs), writes=(s_h1,))
                    else:
                        ctx.run("ve", lambda E, o=s_h3[:, 512 * ic:512 * ic + 512],
                                i=pp[:, 0:512], b=bscol(bj, oc):
                                E.tensor_scalar(o, i, b, None, ALU.add),
                                reads=(pp, s_becs), writes=(s_h3,))
            ctx.run("ve", lambda E, o=s_zz[:, 2048 * ih:2048 * ih + 2048],
                    a=s_h1[:], b=s_h3[:]: E.tensor_tensor(o, a, b, ALU.mult),
                    reads=(s_h1, s_h3), writes=(s_zz,))
        for dh in range(2):
            dma_in(s_we2[:].rearrange("p (cb i) -> p cb i", i=512),
                   io["ws2"].rearrange("(cb p) i -> p cb i", p=128)[:, :, 512 * dh:512 * dh + 512],
                   sem="wl")
            for ob in range(4):
                gob = 4 * dh + ob
                pp = p_pr[ob % 2]
                for ic in range(CB):
                    ctx.run("te", lambda E, o=pp[:, 0:512],
                            l=s_we2[:, 512 * ic + 128 * ob: 512 * ic + 128 * ob + 128],
                            r=s_zz[:, 512 * ic:512 * ic + 512],
                            st=(ic == 0), sp=(ic == CB - 1):
                            MM(E, o, l, r, start=st, stop=sp),
                            reads=(s_we2, s_zz), writes=(pp,), inc=(ic == CB - 1))
                ctx.run("ve", lambda E, o=ya0[:, gob * TTM: gob * TTM + 512],
                        i=pp[:, 0:512], b=bscol(2, gob):
                        E.tensor_scalar(o, i, b, None, ALU.add),
                        reads=(pp, s_becs), writes=(ya0,))

        for e in range(16):
            yaA = s_ya[e % 2]
            yaB = s_ya[(e + 1) % 2]
            dma_in(s_we1[:].rearrange("p (cb i) -> p cb i", i=512),
                   io["We1"][e].rearrange("(cb p) i -> p cb i", p=128), sem="wl")
            dma_in(s_we3[:].rearrange("p (cb i) -> p cb i", i=512),
                   io["We3"][e].rearrange("(cb p) i -> p cb i", p=128), sem="wl")
            dma_in(s_we2[:].rearrange("p (cb i) -> p cb i", i=1024),
                   io["We2"][e].rearrange("(cb p) i -> p cb i", p=128), sem="wl")
            dma_in(s_b2row[:], io["be2r"][e:e + 1, :])
            ctx.run("te", lambda E, o=p_b1[:, 0:512],
                    l=s_eselE[:, 128 * e:128 * e + 128],
                    r=s_combT[0:16, 0:512]:
                    MM(E, o, l, r, start=True, stop=True),
                    reads=(s_eselE, s_combT), writes=(p_b1,))
            ctx.run("ve", lambda E, o=s_cb1[:], i=p_b1[:, 0:512]: E.tensor_copy(o, i),
                    reads=(p_b1,), writes=(s_cb1,))
            for ic in range(4):
                pp = p_pr[ic % 2]
                for cb in range(CB):
                    ctx.run("te", lambda E, o=pp[:, 0:512],
                            l=s_we1[:, 512 * cb + 128 * ic: 512 * cb + 128 * ic + 128],
                            r=s_x3m[:, cb * TTM: cb * TTM + 512],
                            st=(cb == 0), sp=(cb == CB - 1):
                            MM(E, o, l, r, start=st, stop=sp),
                            reads=(s_we1, s_x3m), writes=(pp,), inc=(cb == CB - 1))
                ctx.run("se", lambda E, o=s_h1[:, 512 * ic:512 * ic + 512], i=pp[:, 0:512],
                        b=be1col(e, ic): E.activation(o, i, AF.Silu, bias=b),
                        reads=(pp, s_becs), writes=(s_h1,))
            for ic in range(4):
                pp = p_pr[ic % 2]
                for cb in range(CB):
                    ctx.run("te", lambda E, o=pp[:, 0:512],
                            l=s_we3[:, 512 * cb + 128 * ic: 512 * cb + 128 * ic + 128],
                            r=s_x3m[:, cb * TTM: cb * TTM + 512],
                            st=(cb == 0), sp=(cb == CB - 1):
                            MM(E, o, l, r, start=st, stop=sp),
                            reads=(s_we3, s_x3m), writes=(pp,), inc=(cb == CB - 1))
                ctx.run("ve", lambda E, o=s_h3[:, 512 * ic:512 * ic + 512], i=pp[:, 0:512],
                        b=be3col(e, ic): E.tensor_scalar(o, i, b, None, ALU.add),
                        reads=(pp, s_becs), writes=(s_h3,))
            ctx.run("ve", lambda E, o=s_h3[:], a=s_h1[:], b=s_h3[:]:
                    E.tensor_tensor(o, a, b, ALU.mult), reads=(s_h1, s_h3), writes=(s_h3,))
            for ic in range(4):
                ctx.run("ve", lambda E, o=s_h1[:, 512 * ic:512 * ic + 512],
                        a=s_h3[:, 512 * ic:512 * ic + 512], b=s_cb1[:]:
                        E.tensor_tensor(o, a, b, ALU.mult), reads=(s_h3, s_cb1), writes=(s_h1,))
            for ob in range(CB):
                pp = p_pr[ob % 2]
                for ic in range(4):
                    ctx.run("te", lambda E, o=pp[:, 0:512],
                            l=s_we2[:, 1024 * ic + 128 * ob: 1024 * ic + 128 * ob + 128],
                            r=s_h1[:, 512 * ic:512 * ic + 512],
                            st=(ic == 0), sp=False:
                            MM(E, o, l, r, start=st, stop=sp),
                            reads=(s_we2, s_h1), writes=(pp,), inc=False)
                ctx.run("te", lambda E, o=pp[:, 0:512],
                        l=s_b2row[0:1, 128 * ob:128 * ob + 128],
                        r=s_cb1[0:1, 0:512]:
                        MM(E, o, l, r, start=False, stop=True),
                        reads=(s_b2row, s_cb1), writes=(pp,))
                ctx.run("ve", lambda E,
                        o=yaB[:, ob * TTM: ob * TTM + 512],
                        a=yaA[:, ob * TTM: ob * TTM + 512],
                        b=pp[:, 0:512]: E.tensor_tensor(o, a, b, ALU.add),
                        reads=(yaA, pp), writes=(yaB,))

        yfin = s_ya[16 % 2]
        for cb in range(CB):
            ctx.run("ve", lambda E, o=yfin[:, cb * TTM:(cb + 1) * TTM],
                    a=yfin[:, cb * TTM:(cb + 1) * TTM],
                    b=s_x3g[:, cb * TTM:(cb + 1) * TTM]:
                    E.tensor_tensor(o, a, b, ALU.add), reads=(yfin, s_x3g), writes=(yfin,))
        emit_ln_m(yfin, s_x4)
        dma_out(io["outT"].rearrange("(cb p) tt -> p cb tt", p=128)[:, :, t0:t0 + TTM],
                s_x4[:].rearrange("p (cb tt) -> p cb tt", tt=TTM))

    ctx.wait("po", "out", ctx.barrier_token("out"))
    ctx.full_barrier()
    es3h.close()


def build_phase3(p_tokens=P):
    nc = bass.Bass("TRN2", target_bir_lowering=False)
    io = {}
    io["x2s"] = nc.dram_tensor("x2s", [D, p_tokens], F32R, kind="ExternalInput").ap()
    for n in ["qfs", "kfs"]:
        io[n] = nc.dram_tensor(n, [D, p_tokens], F32, kind="ExternalInput").ap()
    io["vtm"] = nc.dram_tensor("vtm", [p_tokens, D], F32, kind="ExternalInput").ap()
    io["gS1"] = nc.dram_tensor("gS1", [128, 8 * 512], F32, kind="ExternalInput").ap()
    io["gZ1"] = nc.dram_tensor("gZ1", [128, 64], F32, kind="ExternalInput").ap()
    io["fmask"] = nc.dram_tensor("fmask", [128, 16], F32, kind="ExternalInput").ap()
    io["Wo1_t"] = nc.dram_tensor("Wo1_t", [D, D], F32R, kind="ExternalInput").ap()
    io["gwT"] = nc.dram_tensor("gwT", [D, 16], F32, kind="ExternalInput").ap()
    io["gb"] = nc.dram_tensor("gb", [1, 16], F32, kind="ExternalInput").ap()
    io["gbb"] = nc.dram_tensor("gbb", [128, 16], F32, kind="ExternalInput").ap()
    io["id128f"] = nc.dram_tensor("id128f", [128, 128], F32, kind="ExternalInput").ap()
    io["ones128f"] = nc.dram_tensor("ones128f", [128, 128], F32, kind="ExternalInput").ap()
    io["We1"] = nc.dram_tensor("We1", [16, D, 512], F32R, kind="ExternalInput").ap()
    io["We3"] = nc.dram_tensor("We3", [16, D, 512], F32R, kind="ExternalInput").ap()
    io["We2"] = nc.dram_tensor("We2", [16, 512, D], F32R, kind="ExternalInput").ap()
    io["be1c"] = nc.dram_tensor("be1c", [128, 64], F32, kind="ExternalInput").ap()
    io["be3c"] = nc.dram_tensor("be3c", [128, 64], F32, kind="ExternalInput").ap()
    io["be2c"] = nc.dram_tensor("be2c", [128, 128], F32, kind="ExternalInput").ap()
    io["be2r"] = nc.dram_tensor("be2r", [16, D], F32R, kind="ExternalInput").ap()
    io["bsc"] = nc.dram_tensor("bsc", [128, 24], F32, kind="ExternalInput").ap()
    io["ws1"] = nc.dram_tensor("ws1", [D, D], F32R, kind="ExternalInput").ap()
    io["ws3"] = nc.dram_tensor("ws3", [D, D], F32R, kind="ExternalInput").ap()
    io["ws2"] = nc.dram_tensor("ws2", [D, D], F32R, kind="ExternalInput").ap()
    io["dsel"] = nc.dram_tensor("dsel", [128, 128], F32, kind="ExternalInput").ap()
    io["esel2"] = nc.dram_tensor("esel2", [16, 1024], F32R, kind="ExternalInput").ap()
    io["eselE"] = nc.dram_tensor("eselE", [16, 2048], F32R, kind="ExternalInput").ap()
    io["ones128"] = nc.dram_tensor("ones128", [128, 128], F32R, kind="ExternalInput").ap()
    io["id128"] = nc.dram_tensor("id128", [128, 128], F32, kind="ExternalInput").ap()
    io["mask128i"] = nc.dram_tensor("mask128i", [128, 128], F32, kind="ExternalInput").ap()
    io["epsc"] = nc.dram_tensor("epsc", [128, 2], F32, kind="ExternalInput").ap()
    io["x3s"] = nc.dram_tensor("x3s", [D, p_tokens], F32, kind="Internal").ap()
    io["x3r"] = nc.dram_tensor("x3r", [D, p_tokens], F32R, kind="Internal").ap()
    io["outT"] = nc.dram_tensor("outT", [D, p_tokens], F32, kind="ExternalOutput").ap()
    with nc.Block() as block:
        ctx = Ctx(nc, block)
        emit_phase3(ctx, io, p_tokens)
        ctx.flush()
    return nc


def pack_col_n(v, nch):
    v = np.asarray(v, np.float32).reshape(nch, 128).T
    return np.ascontiguousarray(v)


def stage_phase3_shared(inputs):
    sh = {}
    sh["Wo1_t"] = np.ascontiguousarray(np.asarray(inputs["L1_attn_Wo"], np.float32).T)
    sh["gwT"] = np.ascontiguousarray(np.asarray(inputs["L1_ffn_gate_w"], np.float32).T)
    sh["gb"] = np.ascontiguousarray(np.asarray(inputs["L1_ffn_gate_b"], np.float32).reshape(1, 16))
    sh["gbb"] = np.ascontiguousarray(np.repeat(sh["gb"], 128, axis=0))
    sh["id128f"] = np.eye(128, dtype=np.float32)
    sh["ones128f"] = np.ones((128, 128), np.float32)
    sh["We1"] = np.ascontiguousarray(np.asarray(inputs["L1_ffn_We1"], np.float32))
    sh["We3"] = np.ascontiguousarray(np.asarray(inputs["L1_ffn_We3"], np.float32))
    sh["We2"] = np.ascontiguousarray(np.asarray(inputs["L1_ffn_We2"], np.float32))
    be1 = np.asarray(inputs["L1_ffn_be1"], np.float32)
    be3 = np.asarray(inputs["L1_ffn_be3"], np.float32)
    be2 = np.asarray(inputs["L1_ffn_be2"], np.float32)
    b1c = np.zeros((128, 64), np.float32)
    b3c = np.zeros((128, 64), np.float32)
    b2c = np.zeros((128, 128), np.float32)
    for e in range(16):
        b1c[:, 4 * e:4 * e + 4] = pack_col_n(be1[e], 4)
        b3c[:, 4 * e:4 * e + 4] = pack_col_n(be3[e], 4)
        b2c[:, 8 * e:8 * e + 8] = pack_col_n(be2[e], 8)
    sh["be1c"] = b1c
    sh["be3c"] = b3c
    sh["be2c"] = b2c
    sh["be2r"] = np.ascontiguousarray(be2)
    bsc = np.zeros((128, 24), np.float32)
    bsc[:, 0:8] = pack_col_n(inputs["L1_ffn_bs1"], 8)
    bsc[:, 8:16] = pack_col_n(inputs["L1_ffn_bs3"], 8)
    bsc[:, 16:24] = pack_col_n(inputs["L1_ffn_bs2"], 8)
    sh["bsc"] = bsc
    sh["ws1"] = np.ascontiguousarray(np.asarray(inputs["L1_ffn_ws1"], np.float32))
    sh["ws3"] = np.ascontiguousarray(np.asarray(inputs["L1_ffn_ws3"], np.float32))
    sh["ws2"] = np.ascontiguousarray(np.asarray(inputs["L1_ffn_ws2"], np.float32))
    dsel = np.zeros((128, 128), np.float32)
    for cb in range(8):
        dsel[0:64, 16 * cb + 2 * cb] = 1.0
        dsel[64:128, 16 * cb + 2 * cb + 1] = 1.0
    sh["dsel"] = dsel
    esel2 = np.zeros((16, 1024), np.float32)
    for cb in range(8):
        esel2[2 * cb, 128 * cb:128 * cb + 64] = 1.0
        esel2[2 * cb + 1, 128 * cb + 64:128 * cb + 128] = 1.0
    sh["esel2"] = esel2
    eselE = np.zeros((16, 2048), np.float32)
    for e in range(16):
        eselE[e, 128 * e:128 * e + 128] = 1.0
    sh["eselE"] = eselE
    sh["ones128"] = np.ones((128, 128), np.float32)
    sh["id128"] = np.eye(128, dtype=np.float32)
    sh["mask128i"] = np.triu(np.ones((128, 128), np.float32), 0)
    epsc = np.zeros((128, 2), np.float32)
    epsc[:, 0] = GN_EPS
    epsc[:, 1] = LN_EPS
    sh["epsc"] = epsc
    return sh



def build_fused(p_tokens=P):
    nc = bass.Bass("TRN2", target_bir_lowering=False)
    io = {}
    io["xT"] = nc.dram_tensor("xT", [D, p_tokens + 1], F32, kind="ExternalInput").ap()
    for n in ["Wk_t", "Wv_t", "Wr_t", "Wg_t"]:
        io[n] = nc.dram_tensor(n, [D, D], F32R, kind="ExternalInput").ap()
    io["A_lora"] = nc.dram_tensor("A_lora", [D, 160], F32R, kind="ExternalInput").ap()
    io["B_lora"] = nc.dram_tensor("B_lora", [5, 32, D], F32R, kind="ExternalInput").ap()
    io["td_A"] = nc.dram_tensor("td_A", [D, 64], F32R, kind="ExternalInput").ap()
    io["td_B"] = nc.dram_tensor("td_B", [64, D], F32R, kind="ExternalInput").ap()
    io["cvec"] = nc.dram_tensor("cvec", [128, 8 * NCV1], F32, kind="ExternalInput").ap()
    io["id128"] = nc.dram_tensor("id128", [128, 128], F32, kind="ExternalInput").ap()
    io["mask64s"] = nc.dram_tensor("mask64s", [64, 64], F32, kind="ExternalInput").ap()
    io["i64"] = nc.dram_tensor("i64", [64, 64], F32, kind="ExternalInput").ap()
    io["dsel"] = nc.dram_tensor("dsel", [128, 128], F32, kind="ExternalInput").ap()
    io["esel"] = nc.dram_tensor("esel", [16, 1024], F32, kind="ExternalInput").ap()
    for n in ["Wo_t", "Wq_t", "Wk1_t", "Wv1_t", "Wg1_t",
              "wfr", "wfi", "Fr", "Fi", "Fin", "Grs", "Gis"]:
        io[n] = nc.dram_tensor(n, [D, D], F32R, kind="ExternalInput").ap()
    io["cvec2"] = nc.dram_tensor("cvec2", [128, 32], F32, kind="ExternalInput").ap()
    io["dselr"] = nc.dram_tensor("dselr", [128, 128], F32R, kind="ExternalInput").ap()
    io["esel2"] = nc.dram_tensor("esel2", [16, 1024], F32R, kind="ExternalInput").ap()
    io["ones128"] = nc.dram_tensor("ones128", [128, 128], F32R, kind="ExternalInput").ap()
    io["epsc"] = nc.dram_tensor("epsc", [128, 2], F32, kind="ExternalInput").ap()
    io["fmask"] = nc.dram_tensor("fmask", [128, 16], F32, kind="ExternalInput").ap()
    io["Wo1_t"] = nc.dram_tensor("Wo1_t", [D, D], F32R, kind="ExternalInput").ap()
    io["gwT"] = nc.dram_tensor("gwT", [D, 16], F32, kind="ExternalInput").ap()
    io["gb"] = nc.dram_tensor("gb", [1, 16], F32, kind="ExternalInput").ap()
    io["gbb"] = nc.dram_tensor("gbb", [128, 16], F32, kind="ExternalInput").ap()
    io["id128f"] = nc.dram_tensor("id128f", [128, 128], F32, kind="ExternalInput").ap()
    io["ones128f"] = nc.dram_tensor("ones128f", [128, 128], F32, kind="ExternalInput").ap()
    io["We1"] = nc.dram_tensor("We1", [16, D, 512], F32R, kind="ExternalInput").ap()
    io["We3"] = nc.dram_tensor("We3", [16, D, 512], F32R, kind="ExternalInput").ap()
    io["We2"] = nc.dram_tensor("We2", [16, 512, D], F32R, kind="ExternalInput").ap()
    io["be1c"] = nc.dram_tensor("be1c", [128, 64], F32, kind="ExternalInput").ap()
    io["be3c"] = nc.dram_tensor("be3c", [128, 64], F32, kind="ExternalInput").ap()
    io["be2c"] = nc.dram_tensor("be2c", [128, 128], F32, kind="ExternalInput").ap()
    io["be2r"] = nc.dram_tensor("be2r", [16, D], F32R, kind="ExternalInput").ap()
    io["bsc"] = nc.dram_tensor("bsc", [128, 24], F32, kind="ExternalInput").ap()
    io["ws1"] = nc.dram_tensor("ws1", [D, D], F32R, kind="ExternalInput").ap()
    io["ws3"] = nc.dram_tensor("ws3", [D, D], F32R, kind="ExternalInput").ap()
    io["ws2"] = nc.dram_tensor("ws2", [D, D], F32R, kind="ExternalInput").ap()
    io["eselE"] = nc.dram_tensor("eselE", [16, 2048], F32R, kind="ExternalInput").ap()
    io["mask128i"] = nc.dram_tensor("mask128i", [128, 128], F32, kind="ExternalInput").ap()
    for n in ["ylocal", "gsil", "x3s"]:
        io[n] = nc.dram_tensor(n, [D, p_tokens], F32, kind="Internal").ap()
    for n in ["RDs", "qfs", "kfs"]:
        io[n] = nc.dram_tensor(n, [D, p_tokens], F32, kind="Internal").ap()
    for n in ["x2s", "x3r"]:
        io[n] = nc.dram_tensor(n, [D, p_tokens], F32R, kind="Internal").ap()
    io["vtm"] = nc.dram_tensor("vtm", [p_tokens, D], F32, kind="Internal").ap()
    cc0_in = nc.dram_tensor("cc0_in", [128, 520], F32, kind="Internal").ap()
    cc0_out = nc.dram_tensor("cc0_out", [8 * 128, 520], F32, kind="Internal").ap()
    cc1_in = nc.dram_tensor("cc1_in", [128, 520], F32, kind="Internal").ap()
    cc1_out = nc.dram_tensor("cc1_out", [8 * 128, 520], F32, kind="Internal").ap()
    io["summ0"] = cc0_in[:, 0:512]
    io["summD"] = cc0_in[:, 512:520]
    io["summ1"] = cc1_in[:, 0:512]
    io["summZ"] = cc1_in[:, 512:520]
    io["outT"] = nc.dram_tensor("outT", [D, p_tokens], F32, kind="ExternalOutput").ap()

    with nc.Block() as block:
        ctx = Ctx(nc, block)
        emit_phase1(ctx, io, p_tokens)
        ctx.run("po", lambda E, a=cc0_in.opt(), b=cc0_out.opt(): E.collective_compute(
            "AllGather", ALU.bypass, replica_groups=[list(range(NCORE))],
            ins=[a], outs=[b]), reads=(cc0_in,), writes=(cc0_out,), sem="cc", d=1)
        emit_phase2(ctx, io, p_tokens,
                    gS_of=lambda r: cc0_out[128 * r:128 * (r + 1), 0:512],
                    gD_of=lambda r: cc0_out[128 * r:128 * (r + 1), 512:520])
        ctx.run("po", lambda E, a=cc1_in.opt(), b=cc1_out.opt(): E.collective_compute(
            "AllGather", ALU.bypass, replica_groups=[list(range(NCORE))],
            ins=[a], outs=[b]), reads=(cc1_in,), writes=(cc1_out,), sem="cc", d=1)
        emit_phase3(ctx, io, p_tokens,
                    gS_of=lambda r: cc1_out[128 * r:128 * (r + 1), 0:512],
                    gZ_of=lambda r: cc1_out[128 * r:128 * (r + 1), 512:520])
        ctx.flush()
    return nc


def stage_fused_inputs(inputs, p_tokens=P):
    maps1 = stage_phase1_inputs(inputs, p_tokens)
    sh2 = stage_phase2_shared(inputs)
    sh3 = stage_phase3_shared(inputs)
    in_maps = []
    for core in range(NCORE):
        m = {}
        m.update(maps1[core])
        m.update(sh2)
        m.update(sh3)
        m["fmask"] = _fold_masks(core)
        in_maps.append(m)
    return in_maps


def kernel(**inputs):
    nc = build_fused()
    in_maps = stage_fused_inputs(inputs)
    res = run_bass_kernel_spmd(nc, in_maps, core_ids=list(range(NCORE))).results
    out = np.empty((B, T, D), np.float32)
    for core in range(NCORE):
        b, q = core // 4, core % 4
        out[b, q * P:(q + 1) * P] = res[core]["outT"].T
    return out


# version 07
# speedup vs baseline: 1.0318x; 1.0318x over previous
import os
import contextlib
import numpy as np
import concourse.bass as bass
import concourse.mybir as mybir
from concourse.bass_utils import run_bass_kernel_spmd

F32 = mybir.dt.float32
F32R = mybir.dt.float32r
AF = mybir.ActivationFunctionType
ALU = mybir.AluOpType
def MM(E, o, l, r, **kw):
    return E.matmul(o, l, r, **kw)

B, T, D = 2, 16384, 1024
NCORE = 8
P = (B * T) // NCORE
H, HS = 16, 64
TT1 = 256
C0 = 64
NCH1 = TT1 // C0
CB = 8

CV_MIUX, CV_LAM_W, CV_LAM_K, CV_LAM_V, CV_LAM_R, CV_LAM_G, CV_TDMIU, CV_U = range(8)
NCV1 = 8

ENGSEM = {"sy": "in", "po": "out", "te": "pe", "ve": "ve", "se": "se"}
ENGD = {"sy": 16, "po": 16, "te": 1, "ve": 1, "se": 1}
ALL_SEMS = ["in", "wl", "out", "cc", "pe", "ve", "se"]
DMA_SEMS = ("in", "wl", "out", "cc")


def nm(ap):
    t = getattr(ap, "tensor", None)
    if t is not None:
        return t.name
    n = getattr(ap, "name", None)
    if n is not None:
        return n
    return str(ap)


class Ctx:
    ENGINES = ("sy", "po", "te", "ve", "se")

    def __init__(self, nc, block):
        self.nc = nc
        self.block = block
        self.sems = {}
        self.cnt = {}
        self._guards = []
        for name in ALL_SEMS:
            g = nc.semaphore(name)
            h = g.__enter__()
            self._guards.append(g)
            self.sems[name] = h
            self.cnt[name] = 0
        self.waited_vals = {s: set() for s in ALL_SEMS}
        self.cur_wait = {e: {s: 0 for s in ALL_SEMS} for e in self.ENGINES}
        self.q = {e: [] for e in self.ENGINES}
        self.last_w = {}
        self.last_r = {}
        self.pending = {e: [] for e in self.ENGINES}

    def wait(self, eng, sem, val):
        if val is None or val <= 0:
            return
        if self.cur_wait[eng][sem] >= val:
            return
        self.cur_wait[eng][sem] = val
        self.waited_vals[sem].add(val)
        self.q[eng].append(("w", sem, val))

    def run(self, eng, f, reads=(), writes=(), inc=True, sem=None, d=None):
        if sem is None:
            sem = ENGSEM[eng]
        if d is None:
            d = ENGD[eng]
        if sem in DMA_SEMS:
            self.wait(eng, sem, self.cnt[sem])
        for ap in reads:
            lw = self.last_w.get(nm(ap))
            if lw:
                self.wait(eng, lw[0], lw[1])
        for ap in writes:
            n = nm(ap)
            lw = self.last_w.get(n)
            if lw:
                self.wait(eng, lw[0], lw[1])
            for s2, t2 in self.last_r.get(n, {}).items():
                self.wait(eng, s2, t2)
        if inc:
            self.cnt[sem] += d
            tok = self.cnt[sem]
            self.q[eng].append(("op", f, sem, tok, d))
            recs = self.pending[eng]
            self.pending[eng] = []
            recs.append((reads, writes))
            for rs, ws in recs:
                for ap in rs:
                    self.last_r.setdefault(nm(ap), {})[sem] = tok
                for ap in ws:
                    n = nm(ap)
                    self.last_w[n] = (sem, tok)
                    self.last_r[n] = {}
            return tok
        else:
            self.q[eng].append(("op", f, None, 0, 0))
            self.pending[eng].append((reads, writes))
            return None

    def full_barrier(self):
        for e in self.ENGINES:
            for sm in ALL_SEMS:
                self.wait(e, sm, self.cnt[sm])

    def barrier_token(self, sem):
        v = self.cnt[sem]
        if v > 0:
            self.waited_vals[sem].add(v)
        return v

    def flush(self):
        with self.nc.allow_low_precision(
                reason="tf32-class matmul operands; problem tolerance calibrated for bf16-class"):
            self._flush_inner()

    def _flush_inner(self):
        blk = self.block
        decs = {"sy": blk.sync, "po": blk.gpsimd, "te": blk.tensor,
                "ve": blk.vector, "se": blk.scalar}
        for name in self.ENGINES:
            items = self.q[name]
            if not items:
                continue
            prog = []
            pending_d = {s: 0 for s in ALL_SEMS}
            for it in items:
                if it[0] == "w":
                    prog.append(it)
                else:
                    _, f, sem, tok, d = it
                    if sem is not None and (sem in DMA_SEMS or tok in self.waited_vals[sem]):
                        dd = pending_d[sem] + d
                        pending_d[sem] = 0
                        prog.append(("op", f, sem, dd))
                    else:
                        if sem is not None:
                            pending_d[sem] += d
                        prog.append(("op", f, None, 0))
            sems = self.sems

            def body(E, prog=prog, sems=sems):
                for it in prog:
                    if it[0] == "w":
                        E.wait_ge(sems[it[1]], it[2])
                    else:
                        _, f, sem, dd = it
                        if sem is not None:
                            f(E).then_inc(sems[sem], dd)
                        else:
                            f(E)
            decs[name](body)
        self.q = {e: [] for e in self.ENGINES}



class _StopEmit(Exception):
    pass


def emit_phase1(ctx, io, p_tokens=P):
    stop_at = int(os.environ.get("P1_STOP", "99"))

    def ckpt(n):
        if stop_at <= n:
            raise _StopEmit()
    sub = int(os.environ.get("P1_SUB", "31"))
    nc = ctx.nc
    es1 = contextlib.ExitStack()

    def al(name, shape, dt):
        return es1.enter_context(nc.sbuf_tensor(name, shape, dt))

    def ap_(name, shape, dt):
        return es1.enter_context(nc.psum_tensor(name, shape, dt))

    s_alora = al("s_alora", [128, 1280], F32R)
    s_blora_a = al("s_blora_a", [96, 1024], F32R)
    s_blora_b = al("s_blora_b", [64, 1024], F32R)
    s_tda = al("s_tda", [128, 512], F32R)
    s_tdb = al("s_tdb", [64, 1024], F32R)
    s_cvec = al("s_cvec", [128, 8 * NCV1], F32)
    s_id = al("s_id", [128, 128], F32)
    s_m64 = al("s_m64", [64, 64], F32)
    s_i64 = al("s_i64", [64, 64], F32)
    s_dsel = al("s_dsel", [128, 128], F32)
    s_esel = al("s_esel", [16, 1024], F32)
    s_z64 = al("s_z64", [128, 64], F32)
    s_wbuf = al("s_wbuf", [128, 8192], F32R)

    s_x = al("s_x", [128, CB * (TT1 + 1)], F32)
    s_dxa = al("s_dxa", [128, CB * TT1], F32)
    s_mx = al("s_mx", [128, CB * TT1], F32R)
    s_lh_a = al("s_lh_a", [96, TT1], F32R)
    s_lh_b = al("s_lh_b", [64, TT1], F32R)
    s_t1 = al("s_t1", [64, TT1], F32R)
    s_k = al("s_k", [128, CB * TT1], F32)
    s_v = al("s_v", [128, CB * TT1], F32)
    s_r = al("s_r", [128, CB * TT1], F32)
    s_ew = al("s_ew", [128, CB * TT1], F32)
    s_g = al("s_g", [128, CB * TT1], F32)

    csc2 = [al("s_csc0", [128, 512], F32), al("s_csc1", [128, 512], F32)]
    e22 = [al("s_e20", [128, 512], F32), al("s_e21", [128, 512], F32)]
    et2 = [al("s_et0", [128, 512], F32), al("s_et1", [128, 512], F32)]
    e12 = [al("s_e10", [128, 512], F32), al("s_e11", [128, 512], F32)]
    e32 = [al("s_e30", [128, 512], F32), al("s_e31", [128, 512], F32)]
    kt2 = [al("s_kt0", [128, 512], F32), al("s_kt1", [128, 512], F32)]
    rt2 = [al("s_rt0", [128, 512], F32), al("s_rt1", [128, 512], F32)]
    kh2 = [al("s_kh0", [128, 512], F32), al("s_kh1", [128, 512], F32)]
    ncol2 = [al("s_ncol0", [128, 8], F32), al("s_ncol1", [128, 8], F32)]
    dtot2 = [al("s_dtot0", [128, 8], F32), al("s_dtot1", [128, 8], F32)]
    ecg2 = [al("s_ecg0", [128, 8], F32), al("s_ecg1", [128, 8], F32)]
    s_cg = [al("s_cg0", [128, 8], F32), al("s_cg1", [128, 8], F32)]
    s_dtS = al("s_dtS", [128, 8], F32)
    tmpA2 = [al("s_tmpA0", [128, 512], F32), al("s_tmpA1", [128, 512], F32)]
    tmpB2 = [al("s_tmpB0", [128, 512], F32), al("s_tmpB1", [128, 512], F32)]
    dvec2 = [al("s_dvec0", [16, 64], F32), al("s_dvec1", [16, 64], F32)]
    vk2 = [al("s_vk0", [64, 256], F32), al("s_vk1", [64, 256], F32)]
    s_atm = [al("s_atm%d" % i, [64, 64], F32) for i in range(4)]
    s_dg = [al("s_dg%d" % i, [64, 64], F32) for i in range(4)]
    s_S = [al("s_S0", [128, 512], F32), al("s_S1", [128, 512], F32)]
    s_ysum = al("s_ysum", [128, 64], F32)
    s_yl = al("s_yl", [128, CB * TT1], F32)
    s_rd = al("s_rd", [128, CB * TT1], F32)

    p_ab = [ap_("p_ab0", [64, 128], F32), ap_("p_ab1", [64, 128], F32)]
    p_y = ap_("p_y", [128, 64], F32)
    p_y2 = ap_("p_y2", [128, 64], F32)
    p_tr = ap_("p_tr", [64, 256], F32)
    p_sa = ap_("p_sa", [128, 64], F32)
    p_pr = [ap_("p_pr0", [128, 256], F32), ap_("p_pr1", [128, 256], F32)]

    def fm(t_sb, cb, lo, n):
        return t_sb[:, cb * TT1 + lo: cb * TT1 + lo + n]

    def xcur(cb, lo=0, n=TT1):
        return s_x[:, cb * (TT1 + 1) + 1 + lo: cb * (TT1 + 1) + 1 + lo + n]

    def dma_in(dst, src, sem="in"):
        return ctx.run("sy", lambda E, a=dst, b=src: E.dma_start(out=a, in_=b),
                       reads=(src,), writes=(dst,), sem=sem)

    def dma_out(dst, src):
        return ctx.run("po", lambda E, a=dst, b=src: E.dma_start(out=a, in_=b),
                       reads=(src,), writes=(dst,))

    dma_in(s_alora[:].rearrange("p (cb f) -> p cb f", f=160),
           io["A_lora"].rearrange("(cb p) f -> p cb f", p=128))
    for f in range(3):
        dma_in(s_blora_a[32 * f:32 * f + 32, :], io["B_lora"][f])
    for f in range(2):
        dma_in(s_blora_b[32 * f:32 * f + 32, :], io["B_lora"][3 + f])
    dma_in(s_tda[:].rearrange("p (cb r) -> p cb r", r=64),
           io["td_A"].rearrange("(cb p) r -> p cb r", p=128))
    dma_in(s_tdb[:], io["td_B"])
    dma_in(s_cvec[:], io["cvec"])
    dma_in(s_id[:], io["id128"])
    dma_in(s_m64[:], io["mask64s"])
    dma_in(s_i64[:], io["i64"])
    dma_in(s_dsel[:], io["dsel"])
    dma_in(s_esel[:], io["esel"])

    ctx.run("ve", lambda E, a=s_z64[:]: E.memset(a, 0.0), writes=(s_z64,))
    ctx.run("ve", lambda E, a=s_S[0][:]: E.memset(a, 0.0), writes=(s_S[0],))
    ctx.run("ve", lambda E, a=s_cg[0][:]: E.memset(a, 0.0), writes=(s_cg[0],))

    w_names = {"k": "Wk_t", "v": "Wv_t", "r": "Wr_t", "g": "Wg_t"}
    w_dst = {"k": s_k, "v": s_v, "r": s_r, "g": s_g}

    nt1 = p_tokens // TT1
    try:
      for t in range(nt1):
        t0 = t * TT1

        xin = io["xT"].rearrange("(cb p) tt -> p cb tt", p=128)[:, :, t0:t0 + TT1 + 1]
        xout = s_x[:].rearrange("p (cb tt) -> p cb tt", tt=TT1 + 1)
        dma_in(xout, xin)

        xv = s_x[:].rearrange("p (cb tt) -> p cb tt", tt=TT1 + 1)
        dxv = s_dxa[:].rearrange("p (cb tt) -> p cb tt", tt=TT1)
        ctx.run("ve", lambda E, o=dxv, a=xv[:, :, 0:TT1], b=xv[:, :, 1:TT1 + 1]:
                E.tensor_tensor(o, a, b, ALU.subtract),
                reads=(s_x,), writes=(s_dxa,))

        for cb in range(CB):
            ctx.run("ve", lambda E, o=fm(s_mx, cb, 0, TT1), a=fm(s_dxa, cb, 0, TT1),
                    s=s_cvec[:, 8 * CV_MIUX + cb: 8 * CV_MIUX + cb + 1], b=xcur(cb):
                    E.scalar_tensor_tensor(o, a, s, b, ALU.mult, ALU.add),
                    reads=(s_dxa, s_cvec, s_x), writes=(s_mx,))

        ckpt(1)
        lh_t = [s_lh_a, s_lh_a, s_lh_a, s_lh_b, s_lh_b]
        lh_slices = [s_lh_a[0:32, :], s_lh_a[32:64, :], s_lh_a[64:96, :],
                     s_lh_b[0:32, :], s_lh_b[32:64, :]]
        for f in range(5):
            pl = p_pr[f % 2][0:32, :]
            for cb in range(CB):
                ctx.run("te", lambda E, o=pl, l=s_alora[:, 160 * cb + 32 * f: 160 * cb + 32 * f + 32],
                        r=fm(s_mx, cb, 0, TT1), st=(cb == 0), sp=(cb == CB - 1):
                        MM(E, o, l, r, start=st, stop=sp),
                        reads=(s_alora, s_mx), writes=(p_pr[f % 2],),
                        inc=(cb == CB - 1))
            ctx.run("se", lambda E, o=lh_slices[f], i=pl: E.activation(o, i, AF.Tanh),
                    reads=(p_pr[f % 2],), writes=(lh_t[f],))

        ckpt(2)
        for fi, fname in enumerate(["w", "k", "v", "r", "g"]):
            if fi < 3:
                blora, rbase = s_blora_a, 32 * fi
            else:
                blora, rbase = s_blora_b, 32 * (fi - 3)
            lam = 8 * (CV_LAM_W + fi)
            for cb in range(CB):
                pm = p_pr[cb % 2]
                ctx.run("te", lambda E, o=pm[:], l=blora[rbase:rbase + 32, 128 * cb:128 * cb + 128],
                        r=lh_slices[fi]: MM(E, o, l, r, start=True, stop=True),
                        reads=(blora, lh_t[fi]), writes=(pm,))
                ctx.run("ve", lambda E, o=fm(s_mx, cb, 0, TT1), a=pm[:],
                        s=s_cvec[:, lam + cb: lam + cb + 1], b=fm(s_dxa, cb, 0, TT1):
                        E.scalar_tensor_tensor(o, a, s, b, ALU.add, ALU.mult),
                        reads=(pm, s_cvec, s_dxa), writes=(s_mx,))
                ctx.run("ve", lambda E, o=fm(s_mx, cb, 0, TT1), a=fm(s_mx, cb, 0, TT1), b=xcur(cb):
                        E.tensor_tensor(o, a, b, ALU.add),
                        reads=(s_mx, s_x), writes=(s_mx,))

            if fname == "w":
                pt = p_pr[0][0:64, :]
                for cb in range(CB):
                    ctx.run("te", lambda E, o=pt, l=s_tda[:, 64 * cb:64 * cb + 64],
                            r=fm(s_mx, cb, 0, TT1), st=(cb == 0), sp=(cb == CB - 1):
                            MM(E, o, l, r, start=st, stop=sp),
                            reads=(s_tda, s_mx), writes=(p_pr[0],),
                            inc=(cb == CB - 1))
                ctx.run("se", lambda E, o=s_t1[:], i=pt: E.activation(o, i, AF.Tanh),
                        reads=(p_pr[0],), writes=(s_t1,))
                for ob in range(CB):
                    pw = p_pr[1]
                    ctx.run("te", lambda E, o=pw[:], l=s_tdb[:, 128 * ob:128 * ob + 128],
                            r=s_t1[:]: MM(E, o, l, r, start=True, stop=True),
                            reads=(s_tdb, s_t1), writes=(pw,))
                    ctx.run("se", lambda E, o=fm(s_ew, ob, 0, TT1), i=pw[:],
                            bi=s_cvec[:, 8 * CV_TDMIU + ob: 8 * CV_TDMIU + ob + 1]:
                            E.activation(o, i, AF.Exp, bias=bi),
                            reads=(pw, s_cvec), writes=(s_ew,))
            else:
                wkey = w_names[fname]
                wout = s_wbuf[:].rearrange("p (cb co) -> p cb co", co=1024)
                win = io[wkey].rearrange("(cb p) co -> p cb co", p=128)
                dma_in(wout, win, sem="wl")
                dst = w_dst[fname]
                func = AF.Silu if fname == "g" else AF.Copy
                for ob in range(CB):
                    pp = p_pr[ob % 2]
                    for cb in range(CB):
                        ctx.run("te", lambda E, o=pp[:],
                                l=s_wbuf[:, 1024 * cb + 128 * ob: 1024 * cb + 128 * ob + 128],
                                r=fm(s_mx, cb, 0, TT1), st=(cb == 0), sp=(cb == CB - 1):
                                MM(E, o, l, r, start=st, stop=sp),
                                reads=(s_wbuf, s_mx), writes=(pp,),
                                inc=(cb == CB - 1))
                    ctx.run("se", lambda E, o=fm(dst, ob, 0, TT1), i=pp[:], fn=func:
                            E.activation(o, i, fn), reads=(pp,), writes=(dst,))

        ckpt(3)
        gout = io["gsil"].rearrange("(cb p) tt -> p cb tt", p=128)[:, :, t0:t0 + TT1]
        dma_out(gout, s_g[:].rearrange("p (cb tt) -> p cb tt", tt=TT1))

        for c in range(NCH1):
            lo = c * C0
            gidx = t * NCH1 + c
            par2 = gidx % 2
            s_csc, s_e2, s_et, s_e1, s_e3 = csc2[par2], e22[par2], et2[par2], e32[par2], e32[par2]
            s_e1 = e12[par2]
            s_kt, s_rt, s_kh = kt2[par2], rt2[par2], kh2[par2]
            s_ncol, s_dtot, s_ecg = ncol2[par2], dtot2[par2], ecg2[par2]
            s_tmpA, s_tmpB, s_dvec, s_vk = tmpA2[par2], tmpB2[par2], dvec2[par2], vk2[par2]
            cgA, cgB = s_cg[gidx % 2], s_cg[(gidx + 1) % 2]
            Sprev, Snew = s_S[gidx % 2], s_S[(gidx + 1) % 2]

            for cb in range(CB):
                ctx.run("ve", lambda E, o=s_csc[:, 64 * cb:64 * cb + 64],
                        a=fm(s_ew, cb, lo, C0), z=s_z64[:, 0:64]:
                        E.tensor_tensor_scan(o, a, z, 0.0, ALU.add, ALU.add),
                        reads=(s_ew, s_z64), writes=(s_csc,))
            ctx.run("se", lambda E, o=s_e2[:], i=s_csc[:]: E.activation(o, i, AF.Exp),
                    reads=(s_csc,), writes=(s_e2,))
            ewv = s_ew[:].rearrange("p (cb tt) -> p cb tt", tt=TT1)[:, :, lo:lo + C0]
            cscv = s_csc[:].rearrange("p (cb cc) -> p cb cc", cc=C0)
            etv = s_et[:].rearrange("p (cb cc) -> p cb cc", cc=C0)
            ctx.run("ve", lambda E, o=etv, a=ewv, b=cscv:
                    E.tensor_tensor(o, a, b, ALU.subtract),
                    reads=(s_ew, s_csc), writes=(s_et,))
            ctx.run("se", lambda E, o=s_e1[:], i=s_et[:]: E.activation(o, i, AF.Exp),
                    reads=(s_et,), writes=(s_e1,))
            for cb in range(CB):
                ctx.run("ve", lambda E, o=s_ncol[:, cb:cb + 1],
                        i=s_csc[:, 64 * cb + 63:64 * cb + 64]:
                        E.tensor_scalar(o, i, -1.0, None, ALU.mult),
                        reads=(s_csc,), writes=(s_ncol,))
                ctx.run("ve", lambda E, o=cgB[:, cb:cb + 1], a=cgA[:, cb:cb + 1],
                        b=s_csc[:, 64 * cb + 63:64 * cb + 64]:
                        E.tensor_tensor(o, a, b, ALU.add),
                        reads=(cgA, s_csc), writes=(cgB,))
            for cb in range(CB):
                ctx.run("se", lambda E, o=s_e3[:, 64 * cb:64 * cb + 64],
                        i=s_csc[:, 64 * cb:64 * cb + 64], bi=s_ncol[:, cb:cb + 1]:
                        E.activation(o, i, AF.Exp, bias=bi),
                        reads=(s_csc, s_ncol), writes=(s_e3,))
                ctx.run("se", lambda E, o=s_dtot[:, cb:cb + 1], i=s_ncol[:, cb:cb + 1]:
                        E.activation(o, i, AF.Exp), reads=(s_ncol,), writes=(s_dtot,))
                ctx.run("se", lambda E, o=s_ecg[:, cb:cb + 1], i=cgA[:, cb:cb + 1]:
                        E.activation(o, i, AF.Exp, scale=-1.0),
                        reads=(cgA,), writes=(s_ecg,))

            ckpt(4)
            kvv = s_k[:].rearrange("p (cb tt) -> p cb tt", tt=TT1)[:, :, lo:lo + C0]
            rvv = s_r[:].rearrange("p (cb tt) -> p cb tt", tt=TT1)[:, :, lo:lo + C0]
            ktv = s_kt[:].rearrange("p (cb cc) -> p cb cc", cc=C0)
            rtv = s_rt[:].rearrange("p (cb cc) -> p cb cc", cc=C0)
            khv = s_kh[:].rearrange("p (cb cc) -> p cb cc", cc=C0)
            e2v = s_e2[:].rearrange("p (cb cc) -> p cb cc", cc=C0)
            e1v = s_e1[:].rearrange("p (cb cc) -> p cb cc", cc=C0)
            e3v = s_e3[:].rearrange("p (cb cc) -> p cb cc", cc=C0)
            ctx.run("ve", lambda E, o=ktv, a=kvv, b=e2v: E.tensor_tensor(o, a, b, ALU.mult),
                    reads=(s_k, s_e2), writes=(s_kt,))
            ctx.run("ve", lambda E, o=rtv, a=rvv, b=e1v: E.tensor_tensor(o, a, b, ALU.mult),
                    reads=(s_r, s_e1), writes=(s_rt,))
            ctx.run("ve", lambda E, o=khv, a=kvv, b=e3v: E.tensor_tensor(o, a, b, ALU.mult),
                    reads=(s_k, s_e3), writes=(s_kh,))
            for cb in range(CB):
                ctx.run("ve", lambda E, o=fm(s_rd, cb, lo, C0), a=s_rt[:, 64 * cb:64 * cb + 64],
                        s=s_ecg[:, cb:cb + 1]: E.tensor_scalar(o, a, s, None, ALU.mult),
                        reads=(s_rt, s_ecg), writes=(s_rd,))
            for cb in range(CB):
                ctx.run("ve", lambda E, o=s_tmpA[:, 64 * cb:64 * cb + 64], a=fm(s_k, cb, lo, C0),
                        s=s_cvec[:, 8 * CV_U + cb: 8 * CV_U + cb + 1]:
                        E.tensor_scalar(o, a, s, None, ALU.mult),
                        reads=(s_k, s_cvec), writes=(s_tmpA,))
            tav = s_tmpA[:].rearrange("p (cb cc) -> p cb cc", cc=C0)
            tbv = s_tmpB[:].rearrange("p (cb cc) -> p cb cc", cc=C0)
            ctx.run("ve", lambda E, o=tbv, a=tav, b=rvv: E.tensor_tensor(o, a, b, ALU.mult),
                    reads=(s_tmpA, s_r), writes=(s_tmpB,))

            ckpt(5)
            for cb in range(CB):
                ctx.run("te", lambda E, o=p_y[0:16, :], l=s_dsel[:, 16 * cb:16 * cb + 16],
                        r=s_tmpB[:, 64 * cb:64 * cb + 64], st=(cb == 0), sp=(cb == CB - 1):
                        MM(E, o, l, r, start=st, stop=sp),
                        reads=(s_dsel, s_tmpB), writes=(p_y,),
                        inc=(cb == CB - 1))
            ctx.run("ve", lambda E, o=s_dvec[:], i=p_y[0:16, :]: E.tensor_copy(o, i),
                    reads=(p_y,), writes=(s_dvec,))

            ckpt(6)
            for cb in range(CB):
                base_c = 64 * cb
                if sub & 1:
                    ctx.run("te", lambda E, o=p_tr[:, 0:128], i=fm(s_v, cb, lo, C0), d=s_id[:]:
                            E.transpose(o, i, d), reads=(s_v, s_id), writes=(p_tr,))
                    ctx.run("te", lambda E, o=p_tr[:, 128:256],
                            i=s_kh[:, base_c:base_c + C0], d=s_id[:]:
                            E.transpose(o, i, d), reads=(s_kh, s_id), writes=(p_tr,))
                    ctx.run("ve", lambda E, o=s_vk[:], i=p_tr[:]: E.tensor_copy(o, i),
                            reads=(p_tr,), writes=(s_vk,))

                for half in range(2):
                    h = 2 * cb + half
                    hb = 64 * half
                    par = h % 2
                    if sub & 2:
                        ctx.run("te", lambda E, o=p_ab[par][:, 0:64],
                                l=s_kt[hb:hb + 64, base_c:base_c + C0],
                                r=s_rt[hb:hb + 64, base_c:base_c + C0]:
                                MM(E, o, l, r, start=True, stop=True),
                                reads=(s_kt, s_rt), writes=(p_ab[par],))
                        ctx.run("te", lambda E, o=p_ab[par][:, 64:128],
                                l=s_esel[:, 64 * h:64 * h + 64], r=s_dvec[:]:
                                MM(E, o, l, r, start=True, stop=True),
                                reads=(s_esel, s_dvec), writes=(p_ab[par],))
                        ctx.run("ve", lambda E, o=s_atm[par][:], a=p_ab[par][:, 0:64], b=s_m64[:]:
                                E.tensor_tensor(o, a, b, ALU.mult),
                                reads=(p_ab[par], s_m64), writes=(s_atm[par],))
                        ctx.run("ve", lambda E, o=s_dg[par][:], a=p_ab[par][:, 64:128], b=s_i64[:]:
                                E.tensor_tensor(o, a, b, ALU.mult),
                                reads=(p_ab[par], s_i64), writes=(s_dg[par],))
                    if sub & 4:
                        ctx.run("te", lambda E, o=p_y[hb:hb + 64, :], l=s_vk[0:64, hb:hb + 64],
                                r=s_atm[par][:]: MM(E, o, l, r, start=True, stop=False),
                                reads=(s_vk, s_atm[par]), writes=(p_y,), inc=False)
                        ctx.run("te", lambda E, o=p_y[hb:hb + 64, :], l=s_vk[0:64, hb:hb + 64],
                                r=s_dg[par][:]: MM(E, o, l, r, start=False, stop=True),
                                reads=(s_vk, s_dg[par]), writes=(p_y,))
                        ctx.run("te", lambda E, o=p_y2[hb:hb + 64, :],
                                l=Sprev[hb:hb + 64, base_c:base_c + 64],
                                r=s_rt[hb:hb + 64, base_c:base_c + C0]:
                                MM(E, o, l, r, start=True, stop=True),
                                reads=(Sprev, s_rt), writes=(p_y2,))
                    if sub & 8:
                        ctx.run("te", lambda E, o=p_sa[hb:hb + 64, :],
                                l=s_vk[0:64, 128 + hb:128 + hb + 64],
                                r=s_vk[0:64, hb:hb + 64]:
                                MM(E, o, l, r, start=True, stop=True),
                                reads=(s_vk,), writes=(p_sa,))
                        ctx.run("ve", lambda E, o=Snew[hb:hb + 64, base_c:base_c + 64],
                                a=Sprev[hb:hb + 64, base_c:base_c + 64],
                                s=s_dtot[hb:hb + 64, cb:cb + 1], b=p_sa[hb:hb + 64, :]:
                                E.scalar_tensor_tensor(o, a, s, b, ALU.mult, ALU.add),
                                reads=(Sprev, s_dtot, p_sa), writes=(Snew,))
                if sub & 4:
                    ctx.run("ve", lambda E, o=s_ysum[:], i=p_y[:]:
                            E.tensor_copy(o, i), reads=(p_y,), writes=(s_ysum,))
                    ctx.run("ve", lambda E, o=fm(s_yl, cb, lo, C0), a=s_ysum[:], b=p_y2[:]:
                            E.tensor_tensor(o, a, b, ALU.add),
                            reads=(s_ysum, p_y2), writes=(s_yl,))

        ckpt(8)
        ylout = io["ylocal"].rearrange("(cb p) tt -> p cb tt", p=128)[:, :, t0:t0 + TT1]
        rdout = io["RDs"].rearrange("(cb p) tt -> p cb tt", p=128)[:, :, t0:t0 + TT1]
        dma_out(ylout, s_yl[:].rearrange("p (cb tt) -> p cb tt", tt=TT1))
        dma_out(rdout, s_rd[:].rearrange("p (cb tt) -> p cb tt", tt=TT1))

    except _StopEmit:
        pass
    nch_total = nt1 * NCH1
    cg_fin = s_cg[nch_total % 2]
    S_fin = s_S[nch_total % 2]
    ctx.run("se", lambda E, o=s_dtS[:], i=cg_fin[:]:
            E.activation(o, i, AF.Exp, scale=-1.0), reads=(cg_fin,), writes=(s_dtS,))
    dma_out(io["summ0"], S_fin[:])
    dma_out(io["summD"], s_dtS[:])
    ctx.wait("po", "out", ctx.barrier_token("out"))
    ctx.full_barrier()
    es1.close()


def build_phase1(p_tokens=P):
    nc = bass.Bass("TRN2", target_bir_lowering=False)
    io = {}
    io["xT"] = nc.dram_tensor("xT", [D, p_tokens + 1], F32, kind="ExternalInput").ap()
    for n in ["Wk_t", "Wv_t", "Wr_t", "Wg_t"]:
        io[n] = nc.dram_tensor(n, [D, D], F32R, kind="ExternalInput").ap()
    io["A_lora"] = nc.dram_tensor("A_lora", [D, 160], F32R, kind="ExternalInput").ap()
    io["B_lora"] = nc.dram_tensor("B_lora", [5, 32, D], F32R, kind="ExternalInput").ap()
    io["td_A"] = nc.dram_tensor("td_A", [D, 64], F32R, kind="ExternalInput").ap()
    io["td_B"] = nc.dram_tensor("td_B", [64, D], F32R, kind="ExternalInput").ap()
    io["cvec"] = nc.dram_tensor("cvec", [128, 8 * NCV1], F32, kind="ExternalInput").ap()
    io["id128"] = nc.dram_tensor("id128", [128, 128], F32, kind="ExternalInput").ap()
    io["mask64s"] = nc.dram_tensor("mask64s", [64, 64], F32, kind="ExternalInput").ap()
    io["i64"] = nc.dram_tensor("i64", [64, 64], F32, kind="ExternalInput").ap()
    io["dsel"] = nc.dram_tensor("dsel", [128, 128], F32, kind="ExternalInput").ap()
    io["esel"] = nc.dram_tensor("esel", [16, 1024], F32, kind="ExternalInput").ap()
    io["ylocal"] = nc.dram_tensor("ylocal", [D, p_tokens], F32, kind="ExternalOutput").ap()
    io["RDs"] = nc.dram_tensor("RDs", [D, p_tokens], F32, kind="ExternalOutput").ap()
    io["gsil"] = nc.dram_tensor("gsil", [D, p_tokens], F32, kind="ExternalOutput").ap()
    io["summ0"] = nc.dram_tensor("summ0", [128, 512], F32, kind="ExternalOutput").ap()
    io["summD"] = nc.dram_tensor("summD", [128, 8], F32, kind="ExternalOutput").ap()

    with nc.Block() as block:
        ctx = Ctx(nc, block)
        emit_phase1(ctx, io, p_tokens)
        ctx.flush()
    return nc



def pack_col(v):
    return np.ascontiguousarray(np.asarray(v, np.float32).reshape(8, 128).T)


def stage_phase1_shared(inputs):
    shared = {}
    for wn, key in [("Wk_t", "L0_attn_Wk"), ("Wv_t", "L0_attn_Wv"),
                    ("Wr_t", "L0_attn_Wr"), ("Wg_t", "L0_attn_Wg")]:
        shared[wn] = np.ascontiguousarray(np.asarray(inputs[key], np.float32).T)
    shared["A_lora"] = np.ascontiguousarray(np.asarray(inputs["L0_attn_A_lora"], np.float32))
    shared["B_lora"] = np.ascontiguousarray(np.asarray(inputs["L0_attn_B_lora"], np.float32))
    shared["td_A"] = np.ascontiguousarray(np.asarray(inputs["L0_attn_td_A"], np.float32))
    shared["td_B"] = np.ascontiguousarray(np.asarray(inputs["L0_attn_td_B"], np.float32))
    cv = np.zeros((128, 8 * NCV1), np.float32)
    cv[:, 8 * CV_MIUX:8 * CV_MIUX + 8] = pack_col(inputs["L0_attn_miu_x"])
    lam = np.asarray(inputs["L0_attn_lambda_"], np.float32).reshape(5, D)
    for f in range(5):
        cv[:, 8 * (CV_LAM_W + f):8 * (CV_LAM_W + f) + 8] = pack_col(lam[f])
    cv[:, 8 * CV_TDMIU:8 * CV_TDMIU + 8] = pack_col(inputs["L0_attn_td_miu"])
    cv[:, 8 * CV_U:8 * CV_U + 8] = pack_col(np.asarray(inputs["L0_attn_u"], np.float32).reshape(D))
    shared["cvec"] = cv
    shared["id128"] = np.eye(128, dtype=np.float32)
    shared["mask64s"] = np.triu(np.ones((64, 64), np.float32), 1)
    shared["i64"] = np.eye(64, dtype=np.float32)
    dsel = np.zeros((128, 128), np.float32)
    for cb in range(8):
        dsel[0:64, 16 * cb + 2 * cb] = 1.0
        dsel[64:128, 16 * cb + 2 * cb + 1] = 1.0
    shared["dsel"] = dsel
    esel = np.zeros((16, 1024), np.float32)
    for h in range(16):
        esel[h, 64 * h:64 * h + 64] = 1.0
    shared["esel"] = esel
    return shared


def stage_phase1_inputs(inputs, p_tokens=P):
    x = np.asarray(inputs["L0_attn_x"], np.float32)
    shared = stage_phase1_shared(inputs)
    in_maps = []
    pp = p_tokens
    for core in range(NCORE):
        b, q = core // 4, core % 4
        sl = x[b, q * pp:(q + 1) * pp]
        if q == 0:
            prev = np.zeros((1, D), np.float32)
        else:
            prev = x[b, q * pp - 1:q * pp]
        xT = np.ascontiguousarray(np.concatenate([prev, sl], axis=0).T)
        m = dict(shared)
        m["xT"] = xT
        in_maps.append(m)
    return in_maps


def run_phase1(inputs):
    nc = build_phase1()
    in_maps = stage_phase1_inputs(inputs)
    res = run_bass_kernel_spmd(nc, in_maps, core_ids=list(range(NCORE)))
    return res.results


def _fold_masks(core):
    b, q = core // 4, core % 4
    fm = np.zeros((128, 16), np.float32)
    for r in range(8):
        a = 1.0 if (r // 4 == b and r % 4 < q) else 0.0
        fm[:, r] = a
        fm[:, 8 + r] = 1.0 - a
    return fm


def _kernel_unfused(**inputs):
    nc1 = build_phase1()
    maps1 = stage_phase1_inputs(inputs)
    res1 = run_bass_kernel_spmd(nc1, maps1, core_ids=list(range(NCORE))).results

    nc2 = build_phase2()
    sh2 = stage_phase2_shared(inputs)
    maps2 = []
    for core in range(NCORE):
        b, q = core // 4, core % 4
        m = dict(sh2)
        m["xT"] = maps1[core]["xT"]
        m["ylocal"] = np.ascontiguousarray(res1[core]["ylocal"])
        m["RDs"] = np.ascontiguousarray(res1[core]["RDs"])
        m["gsil"] = np.ascontiguousarray(res1[core]["gsil"])
        gS = np.zeros((128, 8 * 512), np.float32)
        gD = np.zeros((128, 64), np.float32)
        for p in range(NCORE):
            gS[:, 512 * p:512 * (p + 1)] = res1[p]["summ0"]
            gD[:, 8 * p:8 * (p + 1)] = res1[p]["summD"]
        m["gS"] = gS
        m["gD"] = gD
        m["fmask"] = _fold_masks(core)
        maps2.append(m)
    res2 = run_bass_kernel_spmd(nc2, maps2, core_ids=list(range(NCORE))).results

    nc3 = build_phase3()
    sh3 = stage_phase3_shared(inputs)
    maps3 = []
    for core in range(NCORE):
        b, q = core // 4, core % 4
        m = dict(sh3)
        m["x2s"] = np.ascontiguousarray(res2[core]["x2s"])
        m["qfs"] = np.ascontiguousarray(res2[core]["qfs"])
        m["kfs"] = np.ascontiguousarray(res2[core]["kfs"])
        m["vtm"] = np.ascontiguousarray(res2[core]["vtm"])
        gS = np.zeros((128, 8 * 512), np.float32)
        gZ = np.zeros((128, 64), np.float32)
        for p in range(NCORE):
            gS[:, 512 * p:512 * (p + 1)] = res2[p]["summ1"]
            gZ[:, 8 * p:8 * (p + 1)] = res2[p]["summZ"]
        m["gS1"] = gS
        m["gZ1"] = gZ
        m["fmask"] = _fold_masks(core)
        maps3.append(m)
    res3 = run_bass_kernel_spmd(nc3, maps3, core_ids=list(range(NCORE))).results

    out = np.empty((B, T, D), np.float32)
    for core in range(NCORE):
        b, q = core // 4, core % 4
        out[b, q * P:(q + 1) * P] = res3[core]["outT"].T
    return out



TT2 = 256
GN_EPS = 1e-5 * H
LN_EPS = 1e-5


def emit_phase2(ctx, io, p_tokens=P, gS_of=None, gD_of=None):
    nc = ctx.nc
    if gS_of is None:
        gS_of = lambda r: io["gS"][:, 512 * r:512 * (r + 1)]
        gD_of = lambda r: io["gD"][:, 8 * r:8 * (r + 1)]
    es2 = contextlib.ExitStack()

    def al(name, shape, dt):
        return es2.enter_context(nc.sbuf_tensor(name, shape, dt))

    def ap_(name, shape, dt):
        return es2.enter_context(nc.psum_tensor(name, shape, dt))

    s_cvec = al("q_cvec", [128, 8 * 4], F32)
    s_dsel = al("q_dsel", [128, 128], F32R)
    s_esel2 = al("q_esel2", [16, 1024], F32R)
    s_ones = al("q_ones", [128, 128], F32R)
    s_id = al("q_id", [128, 128], F32)
    s_S0 = al("q_S0", [128, 512], F32)
    s_gs = al("q_gs", [128, 512], F32)
    s_gd = al("q_gd", [128, 8], F32)
    s_fm = al("q_fm", [128, 16], F32)
    s_eps = al("q_eps", [128, 2], F32)
    s_deff = al("q_deff", [128, 8], F32)
    s_tmps = al("q_tmps", [128, 512], F32)
    s_M = al("q_M", [128, 8192], F32R)

    p_pr = [ap_("q_ppr0", [128, 512], F32), ap_("q_ppr1", [128, 512], F32)]
    p_aux = ap_("q_paux", [128, 512], F32)
    p_st = ap_("q_pst", [16, 512], F32)
    p_b1 = ap_("q_pb1", [128, 512], F32)
    p_b2 = ap_("q_pb2", [128, 512], F32)
    p_tr = ap_("q_ptr", [128, 512], F32)
    p_sloc = ap_("q_psloc", [128, 512], F32)

    def dma_in(dst, src, sem="in"):
        return ctx.run("sy", lambda E, a=dst, b=src: E.dma_start(out=a, in_=b),
                       reads=(src,), writes=(dst,), sem=sem)

    def dma_out(dst, src):
        return ctx.run("po", lambda E, a=dst, b=src: E.dma_start(out=a, in_=b),
                       reads=(src,), writes=(dst,))

    def TT_(eng, o, a, b, op, extra_reads=(), w=None):
        ctx.run(eng, lambda E, o=o, a=a, b=b, op=op: E.tensor_tensor(o, a, b, op),
                reads=(a, b) + tuple(extra_reads), writes=(w if w is not None else o,))

    dma_in(s_cvec[:], io["cvec2"])
    dma_in(s_dsel[:], io["dselr"])
    dma_in(s_esel2[:], io["esel2"])
    dma_in(s_ones[:], io["ones128"])
    dma_in(s_id[:], io["id128"])
    dma_in(s_fm[:], io["fmask"])
    dma_in(s_eps[:], io["epsc"])

    ctx.run("ve", lambda E, a=s_S0[:]: E.memset(a, 0.0), writes=(s_S0,))
    for j in range(NCORE):
        dma_in(s_gs[:], gS_of(j))
        dma_in(s_gd[:], gD_of(j))
        acol = s_fm[:, j:j + 1]
        amcol = s_fm[:, 8 + j:9 + j]
        for cb in range(CB):
            ctx.run("ve", lambda E, o=s_deff[:, cb:cb + 1], i=s_gd[:, cb:cb + 1],
                    sc=acol, b=amcol:
                    E.scalar_tensor_tensor(o, i, sc, b, ALU.mult, ALU.add),
                    reads=(s_gd, s_fm), writes=(s_deff,))
            ctx.run("ve", lambda E, o=s_tmps[:, 64 * cb:64 * cb + 64],
                    i=s_gs[:, 64 * cb:64 * cb + 64], sc=acol:
                    E.tensor_scalar(o, i, sc, None, ALU.mult),
                    reads=(s_gs, s_fm), writes=(s_tmps,))
            ctx.run("ve", lambda E, o=s_S0[:, 64 * cb:64 * cb + 64],
                    a=s_S0[:, 64 * cb:64 * cb + 64], sc=s_deff[:, cb:cb + 1],
                    b=s_tmps[:, 64 * cb:64 * cb + 64]:
                    E.scalar_tensor_tensor(o, a, sc, b, ALU.mult, ALU.add),
                    reads=(s_S0, s_deff, s_tmps), writes=(s_S0,))

    with (
        nc.sbuf_tensor("q_wr", [128, 8192], F32R) as s_wr,
        nc.sbuf_tensor("q_wi", [128, 8192], F32R) as s_wi,
        nc.sbuf_tensor("q_fb", [128, 8192], F32R) as s_fb,
        nc.sbuf_tensor("q_p1t", [128, 8192], F32R) as s_p1t,
        nc.sbuf_tensor("q_p2n", [128, 8192], F32R) as s_p2n,
    ):
        dma_in(s_wr[:].rearrange("p (cb co) -> p cb co", co=1024),
               io["wfr"].rearrange("(cb p) co -> p cb co", p=128))
        dma_in(s_wi[:].rearrange("p (cb co) -> p cb co", co=1024),
               io["wfi"].rearrange("(cb p) co -> p cb co", p=128))

        def mm_pass(wsrc, dst, accum_sign):
            for oc in range(CB):
                for hf in range(2):
                    pp = p_pr[hf]
                    for kc in range(CB):
                        ctx.run("te", lambda E, o=pp[:],
                                l=wsrc[:, 1024 * kc + 128 * oc: 1024 * kc + 128 * oc + 128],
                                r=s_fb[:, 1024 * kc + 512 * hf: 1024 * kc + 512 * hf + 512],
                                st=(kc == 0), sp=(kc == CB - 1):
                                MM(E, o, l, r, start=st, stop=sp),
                                reads=(wsrc, s_fb), writes=(pp,), inc=(kc == CB - 1))
                    dsl = dst[:, 1024 * oc + 512 * hf: 1024 * oc + 512 * hf + 512]
                    if accum_sign == 0:
                        ctx.run("ve", lambda E, o=dsl, i=pp[:]: E.tensor_copy(o, i),
                                reads=(pp,), writes=(dst,))
                    else:
                        sgn = float(accum_sign)
                        ctx.run("ve", lambda E, o=dsl, i=pp[:], a=dsl, sgn=sgn:
                                E.scalar_tensor_tensor(o, i, sgn, a, ALU.mult, ALU.add),
                                reads=(pp, dst), writes=(dst,))

        dma_in(s_fb[:].rearrange("p (cb co) -> p cb co", co=1024),
               io["Fr"].rearrange("(cb p) co -> p cb co", p=128), sem="wl")
        mm_pass(s_wr, s_p1t, 0)
        for oc in range(CB):
            for hf in range(2):
                pp = p_pr[hf]
                for kc in range(CB):
                    ctx.run("te", lambda E, o=pp[:],
                            l=s_wi[:, 1024 * kc + 128 * oc: 1024 * kc + 128 * oc + 128],
                            r=s_fb[:, 1024 * kc + 512 * hf: 1024 * kc + 512 * hf + 512],
                            st=(kc == 0), sp=(kc == CB - 1):
                            MM(E, o, l, r, start=st, stop=sp),
                            reads=(s_wi, s_fb), writes=(pp,), inc=(kc == CB - 1))
                dsl = s_p2n[:, 1024 * oc + 512 * hf: 1024 * oc + 512 * hf + 512]
                ctx.run("ve", lambda E, o=dsl, i=pp[:]:
                        E.tensor_scalar(o, i, -1.0, None, ALU.mult),
                        reads=(pp,), writes=(s_p2n,))
        dma_in(s_fb[:].rearrange("p (cb co) -> p cb co", co=1024),
               io["Fin"].rearrange("(cb p) co -> p cb co", p=128), sem="wl")
        mm_pass(s_wi, s_p1t, 1)
        dma_in(s_fb[:].rearrange("p (cb co) -> p cb co", co=1024),
               io["Fi"].rearrange("(cb p) co -> p cb co", p=128), sem="wl")
        mm_pass(s_wr, s_p2n, -1)
        dma_in(s_fb[:].rearrange("p (cb co) -> p cb co", co=1024),
               io["Grs"].rearrange("(cb p) co -> p cb co", p=128), sem="wl")
        for dc in range(CB):
            for hf in range(2):
                pp = p_pr[hf]
                for oc in range(CB):
                    ctx.run("te", lambda E, o=pp[:],
                            l=s_p1t[:, 1024 * oc + 128 * dc: 1024 * oc + 128 * dc + 128],
                            r=s_fb[:, 1024 * oc + 512 * hf: 1024 * oc + 512 * hf + 512],
                            st=(oc == 0), sp=(oc == CB - 1):
                            MM(E, o, l, r, start=st, stop=sp),
                            reads=(s_p1t, s_fb), writes=(pp,), inc=(oc == CB - 1))
                ctx.run("ve", lambda E, o=s_M[:, 1024 * dc + 512 * hf: 1024 * dc + 512 * hf + 512],
                        i=pp[:]: E.tensor_copy(o, i), reads=(pp,), writes=(s_M,))
        dma_in(s_fb[:].rearrange("p (cb co) -> p cb co", co=1024),
               io["Gis"].rearrange("(cb p) co -> p cb co", p=128), sem="wl")
        for dc in range(CB):
            for hf in range(2):
                pp = p_pr[hf]
                for oc in range(CB):
                    ctx.run("te", lambda E, o=pp[:],
                            l=s_p2n[:, 1024 * oc + 128 * dc: 1024 * oc + 128 * dc + 128],
                            r=s_fb[:, 1024 * oc + 512 * hf: 1024 * oc + 512 * hf + 512],
                            st=(oc == 0), sp=(oc == CB - 1):
                            MM(E, o, l, r, start=st, stop=sp),
                            reads=(s_p2n, s_fb), writes=(pp,), inc=(oc == CB - 1))
                msl = s_M[:, 1024 * dc + 512 * hf: 1024 * dc + 512 * hf + 512]
                ctx.run("ve", lambda E, o=msl, a=msl, b=pp[:]:
                        E.tensor_tensor(o, a, b, ALU.add),
                        reads=(s_M, pp), writes=(s_M,))
        ctx.full_barrier()

    nt = p_tokens // TT2

    with contextlib.ExitStack() as esA:
        def alA(name, shape, dt):
            return esA.enter_context(nc.sbuf_tensor(name, shape, dt))
        s_wo = alA("q_wo", [128, 8192], F32R)
        s_x = alA("q_x", [128, CB * (TT2 + 1)], F32)
        s_yl = alA("q_yl", [128, CB * TT2], F32)
        s_rd = alA("q_rd", [128, CB * TT2], F32)
        s_g = alA("q_g", [128, CB * TT2], F32)
        s_y = alA("q_y", [128, CB * TT2], F32R)
        s_sq = alA("q_sq", [128, CB * TT2], F32R)
        s_yg = alA("q_yg", [128, CB * TT2], F32R)
        s_res = alA("q_res", [128, CB * TT2], F32R)
        s_x1 = alA("q_x1", [128, CB * TT2], F32R)
        s_x2 = alA("q_x2", [128, CB * TT2], F32R)
        s_ln1 = alA("q_ln1", [128, TT2], F32)
        s_stm = alA("q_stm", [16, TT2], F32R)
        s_stq = alA("q_stq", [16, TT2], F32R)
        s_str = alA("q_str", [16, TT2], F32R)
        s_st2 = alA("q_st2", [16, TT2], F32R)

        def fm(t_sb, cb, lo, n, tt=TT2):
            return t_sb[:, cb * tt + lo: cb * tt + lo + n]

        def xcur(cb):
            return s_x[:, cb * (TT2 + 1) + 1: cb * (TT2 + 1) + 1 + TT2]

        def emit_ln(src, dst, eps):
            for cb in range(CB):
                ctx.run("se", lambda E, o=fm(s_sq, cb, 0, TT2), i=fm(src, cb, 0, TT2):
                        E.activation(o, i, AF.Square), reads=(src,), writes=(s_sq,))
            for cb in range(CB):
                ctx.run("te", lambda E, o=p_st[0:1, 0:TT2], l=s_ones[:, 0:1],
                        r=fm(src, cb, 0, TT2), st=(cb == 0), sp=(cb == CB - 1):
                        MM(E, o, l, r, start=st, stop=sp),
                        reads=(s_ones, src), writes=(p_st,), inc=(cb == CB - 1))
            for cb in range(CB):
                ctx.run("te", lambda E, o=p_st[0:1, 256:256 + TT2], l=s_ones[:, 0:1],
                        r=fm(s_sq, cb, 0, TT2), st=(cb == 0), sp=(cb == CB - 1):
                        MM(E, o, l, r, start=st, stop=sp),
                        reads=(s_ones, s_sq), writes=(p_st,), inc=(cb == CB - 1))
            mrow = s_stm[0:1, :]
            qrow = s_stq[0:1, :]
            rrow = s_str[0:1, :]
            ctx.run("ve", lambda E, o=mrow, i=p_st[0:1, 0:TT2]:
                    E.tensor_scalar(o, i, 1.0 / D, None, ALU.mult),
                    reads=(p_st,), writes=(s_stm,))
            ctx.run("ve", lambda E, o=qrow, i=p_st[0:1, 256:256 + TT2]:
                    E.tensor_scalar(o, i, 1.0 / D, None, ALU.mult),
                    reads=(p_st,), writes=(s_stq,))
            ctx.run("ve", lambda E, o=rrow, a=mrow, b=mrow: E.tensor_tensor(o, a, b, ALU.mult),
                    reads=(s_stm,), writes=(s_str,))
            ctx.run("ve", lambda E, o=qrow, a=qrow, b=rrow: E.tensor_tensor(o, a, b, ALU.subtract),
                    reads=(s_stq, s_str), writes=(s_stq,))
            ctx.run("se", lambda E, o=rrow, i=qrow, e=s_eps[0:1, 1:2]:
                    E.activation(o, i, AF.Ln, bias=e),
                    reads=(s_stq, s_eps), writes=(s_str,))
            ctx.run("se", lambda E, o=rrow, i=rrow: E.activation(o, i, AF.Exp, scale=-0.5),
                    reads=(s_str,), writes=(s_str,))
            ctx.run("ve", lambda E, o=mrow, a=mrow, b=rrow: E.tensor_tensor(o, a, b, ALU.mult),
                    reads=(s_stm, s_str), writes=(s_stm,))
            ctx.run("te", lambda E, o=p_b1[:, 0:TT2], l=s_ones[0:1, :], r=rrow:
                    MM(E, o, l, r, start=True, stop=True),
                    reads=(s_ones, s_str), writes=(p_b1,))
            ctx.run("te", lambda E, o=p_b2[:, 0:TT2], l=s_ones[0:1, :], r=mrow:
                    MM(E, o, l, r, start=True, stop=True),
                    reads=(s_ones, s_stm), writes=(p_b2,))
            for cb in range(CB):
                ctx.run("ve", lambda E, o=s_ln1[:], a=fm(src, cb, 0, TT2), b=p_b1[:, 0:TT2]:
                        E.tensor_tensor(o, a, b, ALU.mult),
                        reads=(src, p_b1), writes=(s_ln1,))
                ctx.run("ve", lambda E, o=fm(dst, cb, 0, TT2), a=s_ln1[:], b=p_b2[:, 0:TT2]:
                        E.tensor_tensor(o, a, b, ALU.subtract),
                        reads=(s_ln1, p_b2), writes=(dst,))

        dma_in(s_wo[:].rearrange("p (cb co) -> p cb co", co=1024),
               io["Wo_t"].rearrange("(cb p) co -> p cb co", p=128), sem="wl")
        for t in range(nt):
            t0 = t * TT2
            dma_in(s_x[:].rearrange("p (cb tt) -> p cb tt", tt=TT2 + 1),
                   io["xT"].rearrange("(cb p) tt -> p cb tt", p=128)[:, :, t0:t0 + TT2 + 1])
            dma_in(s_yl[:].rearrange("p (cb tt) -> p cb tt", tt=TT2),
                   io["ylocal"].rearrange("(cb p) tt -> p cb tt", p=128)[:, :, t0:t0 + TT2])
            dma_in(s_rd[:].rearrange("p (cb tt) -> p cb tt", tt=TT2),
                   io["RDs"].rearrange("(cb p) tt -> p cb tt", p=128)[:, :, t0:t0 + TT2])
            dma_in(s_g[:].rearrange("p (cb tt) -> p cb tt", tt=TT2),
                   io["gsil"].rearrange("(cb p) tt -> p cb tt", p=128)[:, :, t0:t0 + TT2])

            for cb in range(CB):
                for half in range(2):
                    h = 2 * cb + half
                    hb = 64 * half
                    ctx.run("te", lambda E, o=p_aux[hb:hb + 64, 0:TT2],
                            l=s_S0[hb:hb + 64, 64 * cb:64 * cb + 64],
                            r=s_rd[hb:hb + 64, cb * TT2: cb * TT2 + TT2]:
                            E.matmul(o, l, r, start=True, stop=True),
                            reads=(s_S0, s_rd), writes=(p_aux,))
                TT_("ve", fm(s_y, cb, 0, TT2), fm(s_yl, cb, 0, TT2), p_aux[:, 0:TT2], ALU.add)

            for cb in range(CB):
                ctx.run("se", lambda E, o=fm(s_sq, cb, 0, TT2), i=fm(s_y, cb, 0, TT2):
                        E.activation(o, i, AF.Square), reads=(s_y,), writes=(s_sq,))
            for cb in range(CB):
                ctx.run("te", lambda E, o=p_st[0:16, 0:TT2], l=s_dsel[:, 16 * cb:16 * cb + 16],
                        r=fm(s_y, cb, 0, TT2), st=(cb == 0), sp=(cb == CB - 1):
                        MM(E, o, l, r, start=st, stop=sp),
                        reads=(s_dsel, s_y), writes=(p_st,), inc=(cb == CB - 1))
            for cb in range(CB):
                ctx.run("te", lambda E, o=p_st[0:16, 256:256 + TT2], l=s_dsel[:, 16 * cb:16 * cb + 16],
                        r=fm(s_sq, cb, 0, TT2), st=(cb == 0), sp=(cb == CB - 1):
                        MM(E, o, l, r, start=st, stop=sp),
                        reads=(s_dsel, s_sq), writes=(p_st,), inc=(cb == CB - 1))
            ctx.run("ve", lambda E, o=s_stm[:], i=p_st[0:16, 0:TT2]:
                    E.tensor_scalar(o, i, 1.0 / HS, None, ALU.mult),
                    reads=(p_st,), writes=(s_stm,))
            ctx.run("ve", lambda E, o=s_stq[:], i=p_st[0:16, 256:256 + TT2]:
                    E.tensor_scalar(o, i, 1.0 / HS, None, ALU.mult),
                    reads=(p_st,), writes=(s_stq,))
            TT_("ve", s_str[:], s_stm[:], s_stm[:], ALU.mult)
            TT_("ve", s_stq[:], s_stq[:], s_str[:], ALU.subtract)
            ctx.run("se", lambda E, o=s_str[:], i=s_stq[:], e=s_eps[0:16, 0:1]:
                    E.activation(o, i, AF.Ln, bias=e), reads=(s_stq, s_eps), writes=(s_str,))
            ctx.run("se", lambda E, o=s_str[:], i=s_str[:]:
                    E.activation(o, i, AF.Exp, scale=-0.5), reads=(s_str,), writes=(s_str,))
            TT_("ve", s_st2[:], s_stm[:], s_str[:], ALU.mult)
            for cb in range(CB):
                ctx.run("te", lambda E, o=p_b1[:, 0:TT2], l=s_esel2[:, 128 * cb:128 * cb + 128],
                        r=s_str[:]: MM(E, o, l, r, start=True, stop=True),
                        reads=(s_esel2, s_str), writes=(p_b1,))
                ctx.run("te", lambda E, o=p_b2[:, 0:TT2], l=s_esel2[:, 128 * cb:128 * cb + 128],
                        r=s_st2[:]: MM(E, o, l, r, start=True, stop=True),
                        reads=(s_esel2, s_st2), writes=(p_b2,))
                ctx.run("ve", lambda E, o=s_ln1[:], a=fm(s_y, cb, 0, TT2), b=p_b1[:, 0:TT2]:
                        E.tensor_tensor(o, a, b, ALU.mult), reads=(s_y, p_b1), writes=(s_ln1,))
                ctx.run("ve", lambda E, o=s_ln1[:], a=s_ln1[:], b=p_b2[:, 0:TT2]:
                        E.tensor_tensor(o, a, b, ALU.subtract), reads=(s_ln1, p_b2), writes=(s_ln1,))
                ctx.run("ve", lambda E, o=s_ln1[:], i=s_ln1[:],
                        g1=s_cvec[:, 0 + cb:0 + cb + 1], b1=s_cvec[:, 8 + cb:8 + cb + 1]:
                        E.tensor_scalar(o, i, g1, b1, ALU.mult, ALU.add),
                        reads=(s_ln1, s_cvec), writes=(s_ln1,))
                ctx.run("ve", lambda E, o=fm(s_yg, cb, 0, TT2), a=s_ln1[:], b=fm(s_g, cb, 0, TT2):
                        E.tensor_tensor(o, a, b, ALU.mult), reads=(s_ln1, s_g), writes=(s_yg,))

            for ob in range(CB):
                pp = p_pr[ob % 2]
                for cb in range(CB):
                    ctx.run("te", lambda E, o=pp[:, 0:TT2],
                            l=s_wo[:, 1024 * cb + 128 * ob: 1024 * cb + 128 * ob + 128],
                            r=fm(s_yg, cb, 0, TT2), st=(cb == 0), sp=(cb == CB - 1):
                            MM(E, o, l, r, start=st, stop=sp),
                            reads=(s_wo, s_yg), writes=(pp,), inc=(cb == CB - 1))
                ctx.run("ve", lambda E, o=fm(s_res, ob, 0, TT2), a=xcur(ob), b=pp[:, 0:TT2]:
                        E.tensor_tensor(o, a, b, ALU.add), reads=(s_x, pp), writes=(s_res,))
            emit_ln(s_res, s_x1, LN_EPS)

            for ob in range(CB):
                pp = p_pr[ob % 2]
                for cb in range(CB):
                    ctx.run("te", lambda E, o=pp[:, 0:TT2],
                            l=s_M[:, 1024 * cb + 128 * ob: 1024 * cb + 128 * ob + 128],
                            r=fm(s_x1, cb, 0, TT2), st=(cb == 0), sp=(cb == CB - 1):
                            MM(E, o, l, r, start=st, stop=sp),
                            reads=(s_M, s_x1), writes=(pp,), inc=(cb == CB - 1))
                ctx.run("ve", lambda E, o=s_ln1[:], i=pp[:, 0:TT2],
                        bi=s_cvec[:, 16 + ob:16 + ob + 1]:
                        E.tensor_scalar(o, i, bi, None, ALU.add), reads=(pp, s_cvec), writes=(s_ln1,))
                ctx.run("ve", lambda E, o=fm(s_res, ob, 0, TT2), a=fm(s_x1, ob, 0, TT2), b=s_ln1[:]:
                        E.tensor_tensor(o, a, b, ALU.add), reads=(s_x1, s_ln1), writes=(s_res,))
            emit_ln(s_res, s_x2, LN_EPS)
            dma_out(io["x2s"].rearrange("(cb p) tt -> p cb tt", p=128)[:, :, t0:t0 + TT2],
                    s_x2[:].rearrange("p (cb tt) -> p cb tt", tt=TT2))
        ctx.full_barrier()

    with contextlib.ExitStack() as esB:
        def alB(name, shape, dt):
            return esB.enter_context(nc.sbuf_tensor(name, shape, dt))
        s_wg = alB("q_wg", [128, 8192], F32R)
        s_wq = alB("q_wq", [128, 8192], F32R)
        s_x2 = alB("q_x2b", [128, CB * TT2], F32R)
        s_gt = alB("q_gt", [128, CB * TT2], F32)
        s_qf = alB("q_qf", [128, CB * TT2], F32)
        s_tA = alB("q_tA", [128, TT2], F32)
        s_tB = alB("q_tB", [128, TT2], F32)

        def fm(t_sb, cb, lo, n, tt=TT2):
            return t_sb[:, cb * tt + lo: cb * tt + lo + n]

        dma_in(s_wg[:].rearrange("p (cb co) -> p cb co", co=1024),
               io["Wg1_t"].rearrange("(cb p) co -> p cb co", p=128), sem="wl")
        dma_in(s_wq[:].rearrange("p (cb co) -> p cb co", co=1024),
               io["Wq_t"].rearrange("(cb p) co -> p cb co", p=128), sem="wl")
        for t in range(nt):
            t0 = t * TT2
            dma_in(s_x2[:].rearrange("p (cb tt) -> p cb tt", tt=TT2),
                   io["x2s"].rearrange("(cb p) tt -> p cb tt", p=128)[:, :, t0:t0 + TT2])
            for ob in range(CB):
                pp = p_pr[ob % 2]
                for cb in range(CB):
                    ctx.run("te", lambda E, o=pp[:, 0:TT2],
                            l=s_wg[:, 1024 * cb + 128 * ob: 1024 * cb + 128 * ob + 128],
                            r=fm(s_x2, cb, 0, TT2), st=(cb == 0), sp=(cb == CB - 1):
                            MM(E, o, l, r, start=st, stop=sp),
                            reads=(s_wg, s_x2), writes=(pp,), inc=(cb == CB - 1))
                ctx.run("se", lambda E, o=fm(s_gt, ob, 0, TT2), i=pp[:, 0:TT2],
                        bi=s_cvec[:, 24 + ob:24 + ob + 1]:
                        E.activation(o, i, AF.Sigmoid, bias=bi), reads=(pp, s_cvec), writes=(s_gt,))
            for ob in range(CB):
                pp = p_pr[ob % 2]
                for cb in range(CB):
                    ctx.run("te", lambda E, o=pp[:, 0:TT2],
                            l=s_wq[:, 1024 * cb + 128 * ob: 1024 * cb + 128 * ob + 128],
                            r=fm(s_x2, cb, 0, TT2), st=(cb == 0), sp=(cb == CB - 1):
                            MM(E, o, l, r, start=st, stop=sp),
                            reads=(s_wq, s_x2), writes=(pp,), inc=(cb == CB - 1))
                ctx.run("se", lambda E, o=s_tA[:], i=pp[:, 0:TT2]: E.activation(o, i, AF.Abs),
                        reads=(pp,), writes=(s_tA,))
                ctx.run("se", lambda E, o=s_tB[:], i=s_tA[:]: E.activation(o, i, AF.Exp, scale=-1.0),
                        reads=(s_tA,), writes=(s_tB,))
                ctx.run("se", lambda E, o=s_tA[:], i=s_tB[:]: E.activation(o, i, AF.Ln, bias=1.0),
                        reads=(s_tB,), writes=(s_tA,))
                ctx.run("se", lambda E, o=s_tB[:], i=pp[:, 0:TT2]: E.activation(o, i, AF.Relu),
                        reads=(pp,), writes=(s_tB,))
                ctx.run("ve", lambda E, o=s_tA[:], a=s_tA[:], b=s_tB[:]:
                        E.tensor_tensor(o, a, b, ALU.add), reads=(s_tA, s_tB), writes=(s_tA,))
                ctx.run("ve", lambda E, o=fm(s_qf, ob, 0, TT2), a=fm(s_gt, ob, 0, TT2),
                        b=s_tA[:]: E.scalar_tensor_tensor(o, a, 1e-6, b, ALU.add, ALU.mult),
                        reads=(s_gt, s_tA), writes=(s_qf,))
            dma_out(io["qfs"].rearrange("(cb p) tt -> p cb tt", p=128)[:, :, t0:t0 + TT2],
                    s_qf[:].rearrange("p (cb tt) -> p cb tt", tt=TT2))
        ctx.full_barrier()

    with contextlib.ExitStack() as esD:
        def alD(name, shape, dt):
            return esD.enter_context(nc.sbuf_tensor(name, shape, dt))
        s_wk = alD("q_wk", [128, 8192], F32R)
        s_wv = alD("q_wv", [128, 8192], F32R)
        s_x2 = alD("q_x2d", [128, CB * TT2], F32R)
        s_kf = alD("q_kf", [128, CB * TT2], F32)
        s_vtm = alD("q_vtm", [128, 2 * 1024], F32)
        s_kftm = alD("q_kftm", [128, 128], F32)
        s_tA = alD("q_tAd", [128, TT2], F32)
        s_tB = alD("q_tBd", [128, TT2], F32)
        s_zl = [alD("q_zl0", [128, 8], F32), alD("q_zl1", [128, 8], F32)]
        s_slp = [alD("q_slp0", [128, 512], F32), alD("q_slp1", [128, 512], F32)]

        def fm(t_sb, cb, lo, n, tt=TT2):
            return t_sb[:, cb * tt + lo: cb * tt + lo + n]

        dma_in(s_wk[:].rearrange("p (cb co) -> p cb co", co=1024),
               io["Wk1_t"].rearrange("(cb p) co -> p cb co", p=128), sem="wl")
        dma_in(s_wv[:].rearrange("p (cb co) -> p cb co", co=1024),
               io["Wv1_t"].rearrange("(cb p) co -> p cb co", p=128), sem="wl")
        for t in range(nt):
            t0 = t * TT2
            dma_in(s_x2[:].rearrange("p (cb tt) -> p cb tt", tt=TT2),
                   io["x2s"].rearrange("(cb p) tt -> p cb tt", p=128)[:, :, t0:t0 + TT2])
            for ob in range(CB):
                pp = p_pr[ob % 2]
                for cb in range(CB):
                    ctx.run("te", lambda E, o=pp[:, 0:TT2],
                            l=s_wk[:, 1024 * cb + 128 * ob: 1024 * cb + 128 * ob + 128],
                            r=fm(s_x2, cb, 0, TT2), st=(cb == 0), sp=(cb == CB - 1):
                            MM(E, o, l, r, start=st, stop=sp),
                            reads=(s_wk, s_x2), writes=(pp,), inc=(cb == CB - 1))
                ctx.run("se", lambda E, o=s_tA[:], i=pp[:, 0:TT2]: E.activation(o, i, AF.Abs),
                        reads=(pp,), writes=(s_tA,))
                ctx.run("se", lambda E, o=s_tB[:], i=s_tA[:]: E.activation(o, i, AF.Exp, scale=-1.0),
                        reads=(s_tA,), writes=(s_tB,))
                ctx.run("se", lambda E, o=s_tA[:], i=s_tB[:]: E.activation(o, i, AF.Ln, bias=1.0),
                        reads=(s_tB,), writes=(s_tA,))
                ctx.run("se", lambda E, o=s_tB[:], i=pp[:, 0:TT2]: E.activation(o, i, AF.Relu),
                        reads=(pp,), writes=(s_tB,))
                ctx.run("ve", lambda E, o=fm(s_kf, ob, 0, TT2), a=s_tA[:], b=s_tB[:]:
                        E.tensor_tensor(o, a, b, ALU.add), reads=(s_tA, s_tB), writes=(s_kf,))
            dma_out(io["kfs"].rearrange("(cb p) tt -> p cb tt", p=128)[:, :, t0:t0 + TT2],
                    s_kf[:].rearrange("p (cb tt) -> p cb tt", tt=TT2))
            for tc in range(2):
                for hf in range(2):
                    pp = p_pr[hf]
                    for cb in range(CB):
                        ctx.run("te", lambda E, o=pp[:],
                                l=s_x2[:, cb * TT2 + 128 * tc: cb * TT2 + 128 * tc + 128],
                                r=s_wv[:].rearrange("p (cb co) -> p cb co", co=1024)[:, cb, 512 * hf:512 * hf + 512],
                                st=(cb == 0), sp=(cb == CB - 1):
                                MM(E, o, l, r, start=st, stop=sp),
                                reads=(s_x2, s_wv), writes=(pp,), inc=(cb == CB - 1))
                    ctx.run("ve", lambda E, o=s_vtm[:, 1024 * tc + 512 * hf: 1024 * tc + 512 * hf + 512],
                            i=pp[:]: E.tensor_copy(o, i), reads=(pp,), writes=(s_vtm,))
            dma_out(io["vtm"].rearrange("(nt p) c -> p nt c", p=128)[:, 2 * t:2 * t + 2, :],
                    s_vtm[:].rearrange("p (nt c) -> p nt c", c=1024))
            first_tile = (t == 0)
            if first_tile:
                ctx.run("ve", lambda E, a=s_slp[0][:]: E.memset(a, 0.0), writes=(s_slp[0],))
            for tc in range(2):
                step = 2 * t + tc
                SA, SB = s_slp[step % 2], s_slp[(step + 1) % 2]
                for cb in range(CB):
                    ctx.run("te", lambda E, o=p_tr[:, 0:128],
                            i=s_kf[:, cb * TT2 + 128 * tc: cb * TT2 + 128 * tc + 128],
                            d=s_id[:]: E.transpose(o, i, d),
                            reads=(s_kf, s_id), writes=(p_tr,))
                    ctx.run("ve", lambda E, o=s_kftm[:], i=p_tr[:, 0:128]: E.tensor_copy(o, i),
                            reads=(p_tr,), writes=(s_kftm,))
                    for half in range(2):
                        h = 2 * cb + half
                        hb = 64 * half
                        ctx.run("te", lambda E, o=p_sloc[hb:hb + 64, 64 * cb:64 * cb + 64],
                                l=s_kftm[:, hb:hb + 64],
                                r=s_vtm[:, 1024 * tc + 64 * h: 1024 * tc + 64 * h + 64]:
                                E.matmul(o, l, r, start=True, stop=True),
                                reads=(s_kftm, s_vtm), writes=(p_sloc,))
                        ctx.run("ve", lambda E, o=SB[hb:hb + 64, 64 * cb:64 * cb + 64],
                                a=SA[hb:hb + 64, 64 * cb:64 * cb + 64],
                                b=p_sloc[hb:hb + 64, 64 * cb:64 * cb + 64]:
                                E.tensor_tensor(o, a, b, ALU.add),
                                reads=(SA, p_sloc), writes=(SB,))
            zA = s_zl[t % 2]
            zB = s_zl[(t + 1) % 2]
            if first_tile:
                ctx.run("ve", lambda E, a=zA[:]: E.memset(a, 0.0), writes=(zA,))
            for cb in range(CB):
                ctx.run("ve", lambda E, o=s_tA[:, 0:1], i=fm(s_kf, cb, 0, TT2):
                        E.reduce_sum(o, i, axis=mybir.AxisListType.X),
                        reads=(s_kf,), writes=(s_tA,))
                ctx.run("ve", lambda E, o=zB[:, cb:cb + 1], a=zA[:, cb:cb + 1], b=s_tA[:, 0:1]:
                        E.tensor_tensor(o, a, b, ALU.add), reads=(zA, s_tA), writes=(zB,))

        s_sl = s_slp[(2 * nt) % 2]
        dma_out(io["summ1"], s_sl[:])
        zfin = s_zl[nt % 2]
        dma_out(io["summZ"], zfin[:])
        ctx.wait("po", "out", ctx.barrier_token("out"))
        ctx.full_barrier()
    ctx.full_barrier()
    es2.close()


def build_phase2(p_tokens=P):
    nc = bass.Bass("TRN2", target_bir_lowering=False)
    io = {}
    io["xT"] = nc.dram_tensor("xT", [D, p_tokens + 1], F32, kind="ExternalInput").ap()
    io["ylocal"] = nc.dram_tensor("ylocal", [D, p_tokens], F32, kind="ExternalInput").ap()
    io["RDs"] = nc.dram_tensor("RDs", [D, p_tokens], F32, kind="ExternalInput").ap()
    io["gsil"] = nc.dram_tensor("gsil", [D, p_tokens], F32, kind="ExternalInput").ap()
    io["gS"] = nc.dram_tensor("gS", [128, 8 * 512], F32, kind="ExternalInput").ap()
    io["gD"] = nc.dram_tensor("gD", [128, 64], F32, kind="ExternalInput").ap()
    io["fmask"] = nc.dram_tensor("fmask", [128, 16], F32, kind="ExternalInput").ap()
    for n in ["Wo_t", "Wq_t", "Wk1_t", "Wv1_t", "Wg1_t",
              "wfr", "wfi", "Fr", "Fi", "Fin", "Grs", "Gis"]:
        io[n] = nc.dram_tensor(n, [D, D], F32R, kind="ExternalInput").ap()
    io["cvec2"] = nc.dram_tensor("cvec2", [128, 32], F32, kind="ExternalInput").ap()
    io["dselr"] = nc.dram_tensor("dselr", [128, 128], F32R, kind="ExternalInput").ap()
    io["esel2"] = nc.dram_tensor("esel2", [16, 1024], F32R, kind="ExternalInput").ap()
    io["ones128"] = nc.dram_tensor("ones128", [128, 128], F32R, kind="ExternalInput").ap()
    io["id128"] = nc.dram_tensor("id128", [128, 128], F32, kind="ExternalInput").ap()
    io["epsc"] = nc.dram_tensor("epsc", [128, 2], F32, kind="ExternalInput").ap()
    io["x2s"] = nc.dram_tensor("x2s", [D, p_tokens], F32R, kind="ExternalOutput").ap()
    io["qfs"] = nc.dram_tensor("qfs", [D, p_tokens], F32, kind="ExternalOutput").ap()
    io["kfs"] = nc.dram_tensor("kfs", [D, p_tokens], F32, kind="ExternalOutput").ap()
    io["vtm"] = nc.dram_tensor("vtm", [p_tokens, D], F32, kind="ExternalOutput").ap()
    io["summ1"] = nc.dram_tensor("summ1", [128, 512], F32, kind="ExternalOutput").ap()
    io["summZ"] = nc.dram_tensor("summZ", [128, 8], F32, kind="ExternalOutput").ap()
    with nc.Block() as block:
        ctx = Ctx(nc, block)
        emit_phase2(ctx, io, p_tokens)
        ctx.flush()
    return nc


def stage_phase2_shared(inputs):
    sh = {}
    for wn, key in [("Wo_t", "L0_attn_Wo"), ("Wq_t", "L1_attn_Wq"),
                    ("Wk1_t", "L1_attn_Wk"), ("Wv1_t", "L1_attn_Wv"),
                    ("Wg1_t", "L1_attn_Wg")]:
        sh[wn] = np.ascontiguousarray(np.asarray(inputs[key], np.float32).T)
    sh["wfr"] = np.ascontiguousarray(np.asarray(inputs["L0_ffn_w_freq_real"], np.float32))
    sh["wfi"] = np.ascontiguousarray(np.asarray(inputs["L0_ffn_w_freq_imag"], np.float32))
    d = np.arange(D)
    ang = 2 * np.pi * np.outer(d, d) / D
    sh["Fr"] = np.cos(ang).astype(np.float32)
    sh["Fi"] = (-np.sin(ang)).astype(np.float32)
    sh["Fin"] = (np.sin(ang)).astype(np.float32)
    sh["Grs"] = (np.cos(ang) / D).astype(np.float32)
    sh["Gis"] = (np.sin(ang) / D).astype(np.float32)
    cv = np.zeros((128, 32), np.float32)
    cv[:, 0:8] = pack_col(inputs["L0_attn_gn_gamma"])
    cv[:, 8:16] = pack_col(inputs["L0_attn_gn_beta"])
    cv[:, 16:24] = pack_col(inputs["L0_ffn_bias"])
    cv[:, 24:32] = pack_col(inputs["L1_attn_bg"])
    sh["cvec2"] = cv
    dsel = np.zeros((128, 128), np.float32)
    for cb in range(8):
        dsel[0:64, 16 * cb + 2 * cb] = 1.0
        dsel[64:128, 16 * cb + 2 * cb + 1] = 1.0
    sh["dsel"] = dsel
    sh["dselr"] = dsel
    esel2 = np.zeros((16, 1024), np.float32)
    for cb in range(8):
        esel2[2 * cb, 128 * cb:128 * cb + 64] = 1.0
        esel2[2 * cb + 1, 128 * cb + 64:128 * cb + 128] = 1.0
    sh["esel2"] = esel2
    sh["ones128"] = np.ones((128, 128), np.float32)
    sh["id128"] = np.eye(128, dtype=np.float32)
    epsc = np.zeros((128, 2), np.float32)
    epsc[:, 0] = GN_EPS
    epsc[:, 1] = LN_EPS
    sh["epsc"] = epsc
    return sh



TT3 = 256
C1 = 128
TTM = 512


def emit_phase3(ctx, io, p_tokens=P, gS_of=None, gZ_of=None):
    nc = ctx.nc
    if gS_of is None:
        gS_of = lambda r: io["gS1"][:, 512 * r:512 * (r + 1)]
        gZ_of = lambda r: io["gZ1"][:, 8 * r:8 * (r + 1)]
    es3h = contextlib.ExitStack()

    def al(name, shape, dt):
        return es3h.enter_context(nc.sbuf_tensor(name, shape, dt))

    def ap_(name, shape, dt):
        return es3h.enter_context(nc.psum_tensor(name, shape, dt))

    s_dsel = al("r_dsel", [128, 128], F32)
    s_esel2 = al("r_esel2", [16, 1024], F32R)
    s_eselE = al("r_eselE", [16, 2048], F32R)
    s_ones = al("r_ones", [128, 128], F32R)
    s_id = al("r_id", [128, 128], F32)
    s_mki = al("r_mki", [128, 128], F32)
    s_eps = al("r_eps", [128, 2], F32)
    s_fm = al("r_fm", [128, 16], F32)
    s_gw = al("r_gw", [128, 128], F32)
    s_idg = al("r_idg", [128, 128], F32)
    s_gbb = al("r_gbb", [128, 16], F32)
    s_becs = al("r_becs", [128, 64 + 64 + 128 + 24], F32)
    s_b2row = al("r_b2row", [1, 1024], F32R)

    s_S0 = al("r_S0", [128, 512], F32)
    s_z0 = al("r_z0", [128, 8], F32)
    s_gs = al("r_gs", [128, 512], F32)
    s_gz = al("r_gz", [128, 8], F32)
    s_tmps = al("r_tmps", [128, 512], F32)

    def dma_in(dst, src, sem="in"):
        return ctx.run("sy", lambda E, a=dst, b=src: E.dma_start(out=a, in_=b),
                       reads=(src,), writes=(dst,), sem=sem)

    def dma_out(dst, src):
        return ctx.run("po", lambda E, a=dst, b=src: E.dma_start(out=a, in_=b),
                       reads=(src,), writes=(dst,))

    dma_in(s_dsel[:], io["dsel"])
    dma_in(s_esel2[:], io["esel2"])
    dma_in(s_eselE[:], io["eselE"])
    dma_in(s_ones[:], io["ones128"])
    dma_in(s_id[:], io["id128"])
    dma_in(s_mki[:], io["mask128i"])
    dma_in(s_eps[:], io["epsc"])
    dma_in(s_fm[:], io["fmask"])
    dma_in(s_gw[:].rearrange("p (cb e) -> p cb e", e=16),
           io["gwT"].rearrange("(cb p) e -> p cb e", p=128))
    dma_in(s_idg[:], io["id128f"])
    dma_in(s_becs[:, 0:64], io["be1c"])
    dma_in(s_becs[:, 64:128], io["be3c"])
    dma_in(s_becs[:, 128:256], io["be2c"])
    dma_in(s_becs[:, 256:280], io["bsc"])

    def be1col(e, ic):
        return s_becs[:, 4 * e + ic: 4 * e + ic + 1]

    def be3col(e, ic):
        return s_becs[:, 64 + 4 * e + ic: 64 + 4 * e + ic + 1]

    def bscol(j, cb):
        return s_becs[:, 256 + 8 * j + cb: 256 + 8 * j + cb + 1]

    ctx.run("ve", lambda E, a=s_S0[:]: E.memset(a, 0.0), writes=(s_S0,))
    ctx.run("ve", lambda E, a=s_z0[:]: E.memset(a, 0.0), writes=(s_z0,))
    for j in range(NCORE):
        dma_in(s_gs[:], gS_of(j))
        dma_in(s_gz[:], gZ_of(j))
        acol = s_fm[:, j:j + 1]
        ctx.run("ve", lambda E, o=s_S0[:], a=s_gs[:], sc=acol, b=s_S0[:]:
                E.scalar_tensor_tensor(o, a, sc, b, ALU.mult, ALU.add),
                reads=(s_gs, s_fm, s_S0), writes=(s_S0,))
        ctx.run("ve", lambda E, o=s_z0[:], a=s_gz[:], sc=acol, b=s_z0[:]:
                E.scalar_tensor_tensor(o, a, sc, b, ALU.mult, ALU.add),
                reads=(s_gz, s_fm, s_z0), writes=(s_z0,))

    p_misc = ap_("r_pmisc", [128, 512], F32)
    dma_in(s_gbb[:], io["gbb"])

    es3a = contextlib.ExitStack()

    def al3(name, shape, dt):
        return es3a.enter_context(nc.sbuf_tensor(name, shape, dt))
    s_wbuf = al3("r_wbuf2", [128, 8192], F32R)
    s_qf = al3("r_qf", [128, CB * TT3], F32)
    s_kf = al3("r_kf", [128, CB * TT3], F32)
    s_vtm = al3("r_vtm", [128, 2 * 1024], F32)
    s_x2 = al3("r_x2", [128, CB * TT3], F32R)
    s_att = al3("r_att", [128, CB * TT3], F32R)
    s_res = al3("r_res", [128, CB * TT3], F32R)
    s_x3 = al3("r_x3", [128, CB * TT3], F32)
    s_x3r = al3("r_x3r", [128, CB * TT3], F32R)
    s_sq = al3("r_sq", [128, CB * TT3], F32R)
    s_kcs = al3("r_kcs", [128, 1024], F32)
    s_kct = al3("r_kct", [128, 1024], F32)
    s_qk = al3("r_qk", [128, 1024], F32)
    s_kftm = al3("r_kftm", [128, 128], F32)
    s_atm1 = al3("r_atm1", [128, 128], F32)
    s_nA = al3("r_nA", [128, 128], F32)
    s_dr = al3("r_dr", [16, 128], F32)
    s_dr2 = al3("r_dr2", [16, 128], F32R)
    s_Sr = [al3("r_Sr0", [128, 512], F32), al3("r_Sr1", [128, 512], F32)]
    s_zr = [al3("r_zr0", [128, 8], F32), al3("r_zr1", [128, 8], F32)]
    s_ln1 = al3("r_ln1", [128, TT3], F32)
    s_stm = al3("r_stm", [16, TT3], F32R)
    s_stq = al3("r_stq", [16, TT3], F32R)
    s_str = al3("r_str", [16, TT3], F32R)
    s_z64b = al3("r_z64b", [128, 128], F32)

    p_at = ap_("r_pat", [128, 512], F32)
    p_n1 = ap_("r_pn1", [128, 512], F32)
    p_n2 = ap_("r_pn2", [128, 512], F32)
    p_b1 = ap_("r_pb1", [128, 512], F32)
    p_tr = ap_("r_ptr", [128, 512], F32)
    p_st = ap_("r_pst", [16, 512], F32)
    p_pr = [p_misc, ap_("r_ppr1", [128, 512], F32)]

    ctx.run("ve", lambda E, a=s_z64b[:]: E.memset(a, 0.0), writes=(s_z64b,))
    ctx.run("ve", lambda E, o=s_Sr[0][:], i=s_S0[:]: E.tensor_copy(o, i),
            reads=(s_S0,), writes=(s_Sr[0],))
    ctx.run("ve", lambda E, o=s_zr[0][:], i=s_z0[:]: E.tensor_copy(o, i),
            reads=(s_z0,), writes=(s_zr[0],))

    def fm3(t_sb, cb, lo, n, tt=TT3):
        return t_sb[:, cb * tt + lo: cb * tt + lo + n]

    def emit_ln3(src, dst):
        for cb in range(CB):
            ctx.run("se", lambda E, o=fm3(s_sq, cb, 0, TT3), i=fm3(src, cb, 0, TT3):
                    E.activation(o, i, AF.Square), reads=(src,), writes=(s_sq,))
        for cb in range(CB):
            ctx.run("te", lambda E, o=p_st[0:1, 0:TT3], l=s_ones[:, 0:1],
                    r=fm3(src, cb, 0, TT3), st=(cb == 0), sp=(cb == CB - 1):
                    MM(E, o, l, r, start=st, stop=sp),
                    reads=(s_ones, src), writes=(p_st,), inc=(cb == CB - 1))
        for cb in range(CB):
            ctx.run("te", lambda E, o=p_st[0:1, 256:256 + TT3], l=s_ones[:, 0:1],
                    r=fm3(s_sq, cb, 0, TT3), st=(cb == 0), sp=(cb == CB - 1):
                    MM(E, o, l, r, start=st, stop=sp),
                    reads=(s_ones, s_sq), writes=(p_st,), inc=(cb == CB - 1))
        mrow, qrow, rrow = s_stm[0:1, :], s_stq[0:1, :], s_str[0:1, :]
        ctx.run("ve", lambda E, o=mrow, i=p_st[0:1, 0:TT3]:
                E.tensor_scalar(o, i, 1.0 / D, None, ALU.mult), reads=(p_st,), writes=(s_stm,))
        ctx.run("ve", lambda E, o=qrow, i=p_st[0:1, 256:256 + TT3]:
                E.tensor_scalar(o, i, 1.0 / D, None, ALU.mult), reads=(p_st,), writes=(s_stq,))
        ctx.run("ve", lambda E, o=rrow, a=mrow, b=mrow: E.tensor_tensor(o, a, b, ALU.mult),
                reads=(s_stm,), writes=(s_str,))
        ctx.run("ve", lambda E, o=qrow, a=qrow, b=rrow: E.tensor_tensor(o, a, b, ALU.subtract),
                reads=(s_stq, s_str), writes=(s_stq,))
        ctx.run("se", lambda E, o=rrow, i=qrow, e=s_eps[0:1, 1:2]:
                E.activation(o, i, AF.Ln, bias=e), reads=(s_stq, s_eps), writes=(s_str,))
        ctx.run("se", lambda E, o=rrow, i=rrow: E.activation(o, i, AF.Exp, scale=-0.5),
                reads=(s_str,), writes=(s_str,))
        ctx.run("ve", lambda E, o=mrow, a=mrow, b=rrow: E.tensor_tensor(o, a, b, ALU.mult),
                reads=(s_stm, s_str), writes=(s_stm,))
        ctx.run("te", lambda E, o=p_b1[:, 0:TT3], l=s_ones[0:1, :], r=rrow:
                MM(E, o, l, r, start=True, stop=True), reads=(s_ones, s_str), writes=(p_b1,))
        ctx.run("te", lambda E, o=p_b1[:, 256:256 + TT3], l=s_ones[0:1, :], r=mrow:
                MM(E, o, l, r, start=True, stop=True), reads=(s_ones, s_stm), writes=(p_b1,))
        for cb in range(CB):
            ctx.run("ve", lambda E, o=s_ln1[:], a=fm3(src, cb, 0, TT3), b=p_b1[:, 0:TT3]:
                    E.tensor_tensor(o, a, b, ALU.mult), reads=(src, p_b1), writes=(s_ln1,))
            ctx.run("ve", lambda E, o=fm3(dst, cb, 0, TT3), a=s_ln1[:], b=p_b1[:, 256:256 + TT3]:
                    E.tensor_tensor(o, a, b, ALU.subtract), reads=(s_ln1, p_b1), writes=(dst,))

    dma_in(s_wbuf[:].rearrange("p (cb co) -> p cb co", co=1024),
           io["Wo1_t"].rearrange("(cb p) co -> p cb co", p=128), sem="wl")
    nt3 = p_tokens // TT3
    for t in range(nt3):
        t0 = t * TT3
        dma_in(s_qf[:].rearrange("p (cb tt) -> p cb tt", tt=TT3),
               io["qfs"].rearrange("(cb p) tt -> p cb tt", p=128)[:, :, t0:t0 + TT3])
        dma_in(s_kf[:].rearrange("p (cb tt) -> p cb tt", tt=TT3),
               io["kfs"].rearrange("(cb p) tt -> p cb tt", p=128)[:, :, t0:t0 + TT3])
        dma_in(s_x2[:].rearrange("p (cb tt) -> p cb tt", tt=TT3),
               io["x2s"].rearrange("(cb p) tt -> p cb tt", p=128)[:, :, t0:t0 + TT3])
        dma_in(s_vtm[:].rearrange("p (nt c) -> p nt c", c=1024),
               io["vtm"].rearrange("(nt p) c -> p nt c", p=128)[:, 2 * t:2 * t + 2, :])

        for kc in range(2):
            step = 2 * t + kc
            zA, zB = s_zr[step % 2], s_zr[(step + 1) % 2]
            SA, SB = s_Sr[step % 2], s_Sr[(step + 1) % 2]
            for cb in range(CB):
                ctx.run("ve", lambda E, o=s_kcs[:, 128 * cb:128 * cb + 128],
                        a=fm3(s_kf, cb, 128 * kc, C1), z=s_z64b[:]:
                        E.tensor_tensor_scan(o, a, z, 0.0, ALU.add, ALU.add),
                        reads=(s_kf, s_z64b), writes=(s_kcs,))
                ctx.run("ve", lambda E, o=s_kct[:, 128 * cb:128 * cb + 128],
                        i=s_kcs[:, 128 * cb:128 * cb + 128], sc=zA[:, cb:cb + 1]:
                        E.tensor_scalar(o, i, sc, None, ALU.add),
                        reads=(s_kcs, zA), writes=(s_kct,))
                ctx.run("ve", lambda E, o=zB[:, cb:cb + 1], a=zA[:, cb:cb + 1],
                        b=s_kcs[:, 128 * cb + 127:128 * cb + 128]:
                        E.tensor_tensor(o, a, b, ALU.add), reads=(zA, s_kcs), writes=(zB,))
            qfv = s_qf[:].rearrange("p (cb tt) -> p cb tt", tt=TT3)[:, :, 128 * kc:128 * kc + C1]
            ctx.run("ve", lambda E, o=s_qk[:].rearrange("p (cb c) -> p cb c", c=C1),
                    a=qfv, b=s_kct[:].rearrange("p (cb c) -> p cb c", c=C1):
                    E.tensor_tensor(o, a, b, ALU.mult), reads=(s_qf, s_kct), writes=(s_qk,))
            for cb in range(CB):
                ctx.run("te", lambda E, o=p_st[0:16, 0:C1], l=s_dsel[:, 16 * cb:16 * cb + 16],
                        r=s_qk[:, 128 * cb:128 * cb + 128], st=(cb == 0), sp=(cb == CB - 1):
                        MM(E, o, l, r, start=st, stop=sp),
                        reads=(s_dsel, s_qk), writes=(p_st,), inc=(cb == CB - 1))
            ctx.run("ve", lambda E, o=s_dr[:], i=p_st[0:16, 0:C1]:
                    E.tensor_scalar(o, i, 1e-6, None, ALU.add), reads=(p_st,), writes=(s_dr,))
            ctx.run("ve", lambda E, o=s_dr2[:], i=s_dr[:]: E.reciprocal(o, i),
                    reads=(s_dr,), writes=(s_dr2,))

            for cb in range(CB):
                ctx.run("te", lambda E, o=p_tr[:, 0:128],
                        i=fm3(s_kf, cb, 128 * kc, C1), d=s_id[:]:
                        E.transpose(o, i, d), reads=(s_kf, s_id), writes=(p_tr,))
                ctx.run("ve", lambda E, o=s_kftm[:], i=p_tr[:, 0:128]: E.tensor_copy(o, i),
                        reads=(p_tr,), writes=(s_kftm,))
                ctx.run("te", lambda E, o=p_b1[:, 0:C1], l=s_esel2[:, 128 * cb:128 * cb + 128],
                        r=s_dr2[:]: MM(E, o, l, r, start=True, stop=True),
                        reads=(s_esel2, s_dr2), writes=(p_b1,))
                for half in range(2):
                    h = 2 * cb + half
                    hb = 64 * half
                    ctx.run("te", lambda E, o=p_at[:, 0:C1],
                            l=fm3(s_kf, cb, 128 * kc, C1)[hb:hb + 64, :] if False else s_kf[hb:hb + 64, cb * TT3 + 128 * kc: cb * TT3 + 128 * kc + C1],
                            r=s_qf[hb:hb + 64, cb * TT3 + 128 * kc: cb * TT3 + 128 * kc + C1]:
                            MM(E, o, l, r, start=True, stop=True),
                            reads=(s_kf, s_qf), writes=(p_at,))
                    ctx.run("ve", lambda E, o=s_atm1[:], a=p_at[:, 0:C1], b=s_mki[:]:
                            E.tensor_tensor(o, a, b, ALU.mult),
                            reads=(p_at, s_mki), writes=(s_atm1,))
                    ctx.run("te", lambda E, o=p_n1[hb:hb + 64, 0:C1],
                            l=s_vtm[:, 1024 * kc + 64 * h: 1024 * kc + 64 * h + 64],
                            r=s_atm1[:]: MM(E, o, l, r, start=True, stop=True),
                            reads=(s_vtm, s_atm1), writes=(p_n1,))
                    ctx.run("te", lambda E, o=p_n2[hb:hb + 64, 0:C1],
                            l=SA[hb:hb + 64, 64 * cb:64 * cb + 64],
                            r=s_qf[hb:hb + 64, cb * TT3 + 128 * kc: cb * TT3 + 128 * kc + C1]:
                            MM(E, o, l, r, start=True, stop=True),
                            reads=(SA, s_qf), writes=(p_n2,))
                    ctx.run("te", lambda E, o=p_misc[hb:hb + 64, 256:256 + 64],
                            l=s_kftm[:, hb:hb + 64],
                            r=s_vtm[:, 1024 * kc + 64 * h: 1024 * kc + 64 * h + 64]:
                            MM(E, o, l, r, start=True, stop=True),
                            reads=(s_kftm, s_vtm), writes=(p_misc,))
                    ctx.run("ve", lambda E, o=SB[hb:hb + 64, 64 * cb:64 * cb + 64],
                            a=SA[hb:hb + 64, 64 * cb:64 * cb + 64],
                            b=p_misc[hb:hb + 64, 256:256 + 64]:
                            E.tensor_tensor(o, a, b, ALU.add),
                            reads=(SA, p_misc), writes=(SB,))
                ctx.run("ve", lambda E, o=s_nA[:, 0:C1], i=p_n1[:, 0:C1]: E.tensor_copy(o, i),
                        reads=(p_n1,), writes=(s_nA,))
                ctx.run("ve", lambda E, o=s_nA[:, 0:C1], a=s_nA[:, 0:C1], b=p_n2[:, 0:C1]:
                        E.tensor_tensor(o, a, b, ALU.add), reads=(s_nA, p_n2), writes=(s_nA,))
                ctx.run("ve", lambda E, o=fm3(s_att, cb, 128 * kc, C1), a=s_nA[:, 0:C1],
                        b=p_b1[:, 0:C1]: E.tensor_tensor(o, a, b, ALU.mult),
                        reads=(s_nA, p_b1), writes=(s_att,))

        for ob in range(CB):
            pp = p_pr[ob % 2]
            for cb in range(CB):
                ctx.run("te", lambda E, o=pp[:, 0:TT3],
                        l=s_wbuf[:, 1024 * cb + 128 * ob: 1024 * cb + 128 * ob + 128],
                        r=fm3(s_att, cb, 0, TT3), st=(cb == 0), sp=(cb == CB - 1):
                        MM(E, o, l, r, start=st, stop=sp),
                        reads=(s_wbuf, s_att), writes=(pp,), inc=(cb == CB - 1))
            ctx.run("ve", lambda E, o=fm3(s_res, ob, 0, TT3), a=fm3(s_x2, ob, 0, TT3),
                    b=pp[:, 0:TT3]: E.tensor_tensor(o, a, b, ALU.add),
                    reads=(s_x2, pp), writes=(s_res,))
        emit_ln3(s_res, s_x3)
        for cb in range(CB):
            ctx.run("ve", lambda E, o=fm3(s_x3r, cb, 0, TT3), i=fm3(s_x3, cb, 0, TT3):
                    E.tensor_copy(o, i), reads=(s_x3,), writes=(s_x3r,))
        dma_out(io["x3s"].rearrange("(cb p) tt -> p cb tt", p=128)[:, :, t0:t0 + TT3],
                s_x3[:].rearrange("p (cb tt) -> p cb tt", tt=TT3))
        dma_out(io["x3r"].rearrange("(cb p) tt -> p cb tt", p=128)[:, :, t0:t0 + TT3],
                s_x3r[:].rearrange("p (cb tt) -> p cb tt", tt=TT3))

    ctx.full_barrier()
    es3a.close()

    s_x3m = al("r_x3m", [128, CB * TTM], F32R)
    s_x3g = al("r_x3g", [128, CB * TTM], F32)
    s_ya = [al("r_ya0", [128, CB * TTM], F32), al("r_ya1", [128, CB * TTM], F32)]
    s_h1 = al("r_h1", [128, 2048], F32R)
    s_h3 = al("r_h3", [128, 2048], F32)
    s_zz = al("r_zz", [128, 4096], F32R)
    s_we1 = al("r_we1", [128, 4096], F32R)
    s_we3 = al("r_we3", [128, 4096], F32R)
    s_we2 = al("r_we2", [128, 4096], F32R)
    s_combT = al("r_combT", [16, TTM], F32R)
    s_cb1 = al("r_cb1", [128, 512], F32R)
    s_x4 = al("r_x4", [128, CB * TTM], F32)
    s_sq2 = al("r_sq2", [128, 2048], F32)
    s_onesf = al("r_onesf", [128, 128], F32)
    s_ln1m = al("r_ln1m", [128, 256], F32)
    s_stmm = al("r_stmm", [1, 512], F32)
    s_stqm = al("r_stqm", [1, 512], F32)
    s_strm = al("r_strm", [1, 512], F32)
    s_sc = al("r_sc", [128, 16], F32)
    s_se = al("r_se", [128, 16], F32)
    s_so = al("r_so", [128, 16], F32)
    s_sb = al("r_sb", [128, 16], F32)
    s_msk = al("r_msk", [128, 16], F32)
    s_c1 = al("r_c1", [128, 1], F32)
    s_c2 = al("r_c2", [128, 1], F32)
    s_c3 = al("r_c3", [128, 1], F32)
    s_gpA = al("r_gpA", [128, 4], F32)
    s_gpB = al("r_gpB", [128, 4], F32)
    s_kp = al("r_kp", [128, 4], F32)
    s_pen = al("r_pen", [128, 4], F32)

    def emit_ln_m(src, dst):
        for half in range(2):
            for cb in range(CB):
                ctx.run("te", lambda E, o=p_st[0:1, 0:256], l=s_onesf[:, 0:1],
                        r=src[:, cb * TTM + 256 * half: cb * TTM + 256 * half + 256],
                        st=(cb == 0), sp=(cb == CB - 1):
                        E.matmul(o, l, r, start=st, stop=sp),
                        reads=(s_onesf, src), writes=(p_st,), inc=(cb == CB - 1))
            for cb in range(CB):
                ctx.run("se", lambda E, o=s_sq2[:, 256 * cb:256 * cb + 256],
                        i=src[:, cb * TTM + 256 * half: cb * TTM + 256 * half + 256]:
                        E.activation(o, i, AF.Square), reads=(src,), writes=(s_sq2,))
            for cb in range(CB):
                ctx.run("te", lambda E, o=p_st[0:1, 256:512], l=s_onesf[:, 0:1],
                        r=s_sq2[:, 256 * cb:256 * cb + 256], st=(cb == 0), sp=(cb == CB - 1):
                        E.matmul(o, l, r, start=st, stop=sp),
                        reads=(s_onesf, s_sq2), writes=(p_st,), inc=(cb == CB - 1))
            mrow = s_stmm[0:1, 256 * half:256 * half + 256]
            qrow = s_stqm[0:1, 256 * half:256 * half + 256]
            rrow = s_strm[0:1, 256 * half:256 * half + 256]
            ctx.run("ve", lambda E, o=mrow, i=p_st[0:1, 0:256]:
                    E.tensor_scalar(o, i, 1.0 / D, None, ALU.mult), reads=(p_st,), writes=(s_stmm,))
            ctx.run("ve", lambda E, o=qrow, i=p_st[0:1, 256:512]:
                    E.tensor_scalar(o, i, 1.0 / D, None, ALU.mult), reads=(p_st,), writes=(s_stqm,))
            ctx.run("ve", lambda E, o=rrow, a=mrow, b=mrow: E.tensor_tensor(o, a, b, ALU.mult),
                    reads=(s_stmm,), writes=(s_strm,))
            ctx.run("ve", lambda E, o=qrow, a=qrow, b=rrow: E.tensor_tensor(o, a, b, ALU.subtract),
                    reads=(s_stqm, s_strm), writes=(s_stqm,))
            ctx.run("se", lambda E, o=rrow, i=qrow, e=s_eps[0:1, 1:2]:
                    E.activation(o, i, AF.Ln, bias=e), reads=(s_stqm, s_eps), writes=(s_strm,))
            ctx.run("se", lambda E, o=rrow, i=rrow: E.activation(o, i, AF.Exp, scale=-0.5),
                    reads=(s_strm,), writes=(s_strm,))
            ctx.run("ve", lambda E, o=mrow, a=mrow, b=rrow: E.tensor_tensor(o, a, b, ALU.mult),
                    reads=(s_stmm, s_strm), writes=(s_stmm,))
            ctx.run("te", lambda E, o=p_b1[:, 0:256], l=s_onesf[0:1, :], r=rrow:
                    E.matmul(o, l, r, start=True, stop=True), reads=(s_onesf, s_strm), writes=(p_b1,))
            ctx.run("te", lambda E, o=p_b1[:, 256:512], l=s_onesf[0:1, :], r=mrow:
                    E.matmul(o, l, r, start=True, stop=True), reads=(s_onesf, s_stmm), writes=(p_b1,))
            for cb in range(CB):
                sl = slice(cb * TTM + 256 * half, cb * TTM + 256 * half + 256)
                ctx.run("ve", lambda E, o=s_ln1m[:], a=src[:, sl], b=p_b1[:, 0:256]:
                        E.tensor_tensor(o, a, b, ALU.mult), reads=(src, p_b1), writes=(s_ln1m,))
                ctx.run("ve", lambda E, o=dst[:, sl], a=s_ln1m[:], b=p_b1[:, 256:512]:
                        E.tensor_tensor(o, a, b, ALU.subtract), reads=(s_ln1m, p_b1), writes=(dst,))

    dma_in(s_onesf[:], io["ones128f"])
    ntm = p_tokens // TTM
    for t in range(ntm):
        t0 = t * TTM
        dma_in(s_x3m[:].rearrange("p (cb tt) -> p cb tt", tt=TTM),
               io["x3r"].rearrange("(cb p) tt -> p cb tt", p=128)[:, :, t0:t0 + TTM])
        dma_in(s_x3g[:].rearrange("p (cb tt) -> p cb tt", tt=TTM),
               io["x3s"].rearrange("(cb p) tt -> p cb tt", p=128)[:, :, t0:t0 + TTM])

        for tc in range(4):
            for cb in range(CB):
                ctx.run("te", lambda E, o=p_misc[:, 0:16],
                        l=s_x3g[:, cb * TTM + 128 * tc: cb * TTM + 128 * tc + 128],
                        r=s_gw[:, 16 * cb:16 * cb + 16], st=(cb == 0), sp=(cb == CB - 1):
                        E.matmul(o, l, r, start=st, stop=sp),
                        reads=(s_x3g, s_gw), writes=(p_misc,), inc=(cb == CB - 1))
            ctx.run("ve", lambda E, o=s_sc[:], i=p_misc[:, 0:16]: E.tensor_copy(o, i),
                    reads=(p_misc,), writes=(s_sc,))
            ctx.run("ve", lambda E, o=s_c1[:], i=s_sc[:]:
                    E.reduce_max(o, i, axis=mybir.AxisListType.X), reads=(s_sc,), writes=(s_c1,))
            ctx.run("ve", lambda E, o=s_c2[:], i=s_c1[:]:
                    E.tensor_scalar(o, i, -1.0, None, ALU.mult), reads=(s_c1,), writes=(s_c2,))
            ctx.run("se", lambda E, o=s_se[:], i=s_sc[:], b=s_c2[:]:
                    E.activation(o, i, AF.Exp, bias=b), reads=(s_sc, s_c2), writes=(s_se,))
            ctx.run("ve", lambda E, o=s_c1[:], i=s_se[:]:
                    E.reduce_sum(o, i, axis=mybir.AxisListType.X), reads=(s_se,), writes=(s_c1,))
            ctx.run("ve", lambda E, o=s_c3[:], i=s_c1[:]: E.reciprocal(o, i),
                    reads=(s_c1,), writes=(s_c3,))
            ctx.run("ve", lambda E, o=s_so[:], i=s_se[:], sc=s_c3[:]:
                    E.tensor_scalar(o, i, sc, None, ALU.mult), reads=(s_se, s_c3), writes=(s_so,))
            ctx.run("ve", lambda E, o=s_sb[:], a=s_so[:], b=s_gbb[:]:
                    E.tensor_tensor(o, a, b, ALU.add), reads=(s_so, s_gbb), writes=(s_sb,))
            sgv = s_sb[:].rearrange("p (g e) -> p g e", e=4)
            pairs = [(0, 1), (0, 2), (0, 3), (1, 2), (1, 3), (2, 3)]
            for pi, (i_, j_) in enumerate(pairs):
                if pi == 0:
                    ctx.run("ve", lambda E, o=s_gpA[:], a=sgv[:, :, i_:i_ + 1],
                            b=sgv[:, :, j_:j_ + 1]: E.tensor_tensor(o, a, b, ALU.add),
                            reads=(s_sb,), writes=(s_gpA,))
                else:
                    ctx.run("ve", lambda E, o=s_gpB[:], a=sgv[:, :, i_:i_ + 1],
                            b=sgv[:, :, j_:j_ + 1]: E.tensor_tensor(o, a, b, ALU.add),
                            reads=(s_sb,), writes=(s_gpB,))
                    ctx.run("ve", lambda E, o=s_gpA[:], a=s_gpA[:], b=s_gpB[:]:
                            E.tensor_tensor(o, a, b, ALU.max), reads=(s_gpA, s_gpB), writes=(s_gpA,))
            a0, b0 = s_gpA[:, 0:1], s_gpA[:, 1:2]
            c0, d0 = s_gpA[:, 2:3], s_gpA[:, 3:4]
            ctx.run("ve", lambda E, o=s_c1[:], a=a0, b=b0: E.tensor_tensor(o, a, b, ALU.max),
                    reads=(s_gpA,), writes=(s_c1,))
            ctx.run("ve", lambda E, o=s_c2[:], a=c0, b=d0: E.tensor_tensor(o, a, b, ALU.max),
                    reads=(s_gpA,), writes=(s_c2,))
            ctx.run("ve", lambda E, o=s_c1[:], a=s_c1[:], b=s_c2[:]:
                    E.tensor_tensor(o, a, b, ALU.min), reads=(s_c1, s_c2), writes=(s_c1,))
            ctx.run("ve", lambda E, o=s_c2[:], a=a0, b=b0: E.tensor_tensor(o, a, b, ALU.min),
                    reads=(s_gpA,), writes=(s_c2,))
            ctx.run("ve", lambda E, o=s_c3[:], a=c0, b=d0: E.tensor_tensor(o, a, b, ALU.min),
                    reads=(s_gpA,), writes=(s_c3,))
            ctx.run("ve", lambda E, o=s_c2[:], a=s_c2[:], b=s_c3[:]:
                    E.tensor_tensor(o, a, b, ALU.max), reads=(s_c2, s_c3), writes=(s_c2,))
            ctx.run("ve", lambda E, o=s_c1[:], a=s_c1[:], b=s_c2[:]:
                    E.tensor_tensor(o, a, b, ALU.max), reads=(s_c1, s_c2), writes=(s_c1,))
            ctx.run("ve", lambda E, o=s_kp[:], i=s_gpA[:], sc=s_c1[:]:
                    E.tensor_scalar(o, i, sc, None, ALU.is_ge), reads=(s_gpA, s_c1), writes=(s_kp,))
            ctx.run("ve", lambda E, o=s_pen[:], i=s_kp[:]:
                    E.tensor_scalar(o, i, 1.0, 1e30, ALU.subtract, ALU.mult),
                    reads=(s_kp,), writes=(s_pen,))
            for g in range(4):
                ctx.run("ve", lambda E, o=s_msk[:, 4 * g:4 * g + 4], i=s_sb[:, 4 * g:4 * g + 4],
                        s1=s_kp[:, g:g + 1], s2=s_pen[:, g:g + 1]:
                        E.tensor_scalar(o, i, s1, s2, ALU.mult, ALU.add),
                        reads=(s_sb, s_kp, s_pen), writes=(s_msk,))
            ctx.run("ve", lambda E, o=s_c1[:], i=s_msk[:]:
                    E.reduce_max(o, i, axis=mybir.AxisListType.X), reads=(s_msk,), writes=(s_c1,))
            ctx.run("ve", lambda E, o=s_se[:], i=s_msk[:], sc=s_c1[:]:
                    E.tensor_scalar(o, i, sc, None, ALU.is_equal), reads=(s_msk, s_c1), writes=(s_se,))
            ctx.run("ve", lambda E, o=s_se[:], a=s_se[:], b=s_msk[:]:
                    E.scalar_tensor_tensor(o, a, -1e30, b, ALU.mult, ALU.add),
                    reads=(s_se, s_msk), writes=(s_se,))
            ctx.run("ve", lambda E, o=s_c1[:], i=s_se[:]:
                    E.reduce_max(o, i, axis=mybir.AxisListType.X), reads=(s_se,), writes=(s_c1,))
            ctx.run("ve", lambda E, o=s_msk[:], i=s_msk[:], sc=s_c1[:]:
                    E.tensor_scalar(o, i, sc, None, ALU.is_ge), reads=(s_msk, s_c1), writes=(s_msk,))
            ctx.run("ve", lambda E, o=s_so[:], a=s_so[:], b=s_msk[:]:
                    E.tensor_tensor(o, a, b, ALU.mult), reads=(s_so, s_msk), writes=(s_so,))
            ctx.run("te", lambda E, o=p_misc[0:16, 128:256], i=s_so[:], d=s_idg[:]:
                    E.transpose(o, i, d), reads=(s_so, s_idg), writes=(p_misc,))
            ctx.run("ve", lambda E, o=s_combT[0:16, 128 * tc:128 * tc + 128],
                    i=p_misc[0:16, 128:256]: E.tensor_copy(o, i), reads=(p_misc,), writes=(s_combT,))

        ya0 = s_ya[0]
        for ih in range(2):
            for wname, dst, bj, act in [("ws1", s_h1, 0, True), ("ws3", s_h3, 1, False)]:
                dma_in(s_we1[:].rearrange("p (cb i) -> p cb i", i=512),
                       io[wname].rearrange("(cb p) i -> p cb i", p=128)[:, :, 512 * ih:512 * ih + 512],
                       sem="wl")
                for ic in range(4):
                    pp = p_pr[ic % 2]
                    for cb in range(CB):
                        ctx.run("te", lambda E, o=pp[:, 0:512],
                                l=s_we1[:, 512 * cb + 128 * ic: 512 * cb + 128 * ic + 128],
                                r=s_x3m[:, cb * TTM: cb * TTM + 512],
                                st=(cb == 0), sp=(cb == CB - 1):
                                MM(E, o, l, r, start=st, stop=sp),
                                reads=(s_we1, s_x3m), writes=(pp,), inc=(cb == CB - 1))
                    oc = 4 * ih + ic
                    if act:
                        ctx.run("se", lambda E, o=s_h1[:, 512 * ic:512 * ic + 512],
                                i=pp[:, 0:512], b=bscol(bj, oc):
                                E.activation(o, i, AF.Silu, bias=b),
                                reads=(pp, s_becs), writes=(s_h1,))
                    else:
                        ctx.run("ve", lambda E, o=s_h3[:, 512 * ic:512 * ic + 512],
                                i=pp[:, 0:512], b=bscol(bj, oc):
                                E.tensor_scalar(o, i, b, None, ALU.add),
                                reads=(pp, s_becs), writes=(s_h3,))
            ctx.run("ve", lambda E, o=s_zz[:, 2048 * ih:2048 * ih + 2048],
                    a=s_h1[:], b=s_h3[:]: E.tensor_tensor(o, a, b, ALU.mult),
                    reads=(s_h1, s_h3), writes=(s_zz,))
        for dh in range(2):
            dma_in(s_we2[:].rearrange("p (cb i) -> p cb i", i=512),
                   io["ws2"].rearrange("(cb p) i -> p cb i", p=128)[:, :, 512 * dh:512 * dh + 512],
                   sem="wl")
            for ob in range(4):
                gob = 4 * dh + ob
                pp = p_pr[ob % 2]
                for ic in range(CB):
                    ctx.run("te", lambda E, o=pp[:, 0:512],
                            l=s_we2[:, 512 * ic + 128 * ob: 512 * ic + 128 * ob + 128],
                            r=s_zz[:, 512 * ic:512 * ic + 512],
                            st=(ic == 0), sp=(ic == CB - 1):
                            MM(E, o, l, r, start=st, stop=sp),
                            reads=(s_we2, s_zz), writes=(pp,), inc=(ic == CB - 1))
                ctx.run("ve", lambda E, o=ya0[:, gob * TTM: gob * TTM + 512],
                        i=pp[:, 0:512], b=bscol(2, gob):
                        E.tensor_scalar(o, i, b, None, ALU.add),
                        reads=(pp, s_becs), writes=(ya0,))

        for e in range(16):
            yaA = s_ya[e % 2]
            yaB = s_ya[(e + 1) % 2]
            dma_in(s_we1[:].rearrange("p (cb i) -> p cb i", i=512),
                   io["We1"][e].rearrange("(cb p) i -> p cb i", p=128), sem="wl")
            dma_in(s_we3[:].rearrange("p (cb i) -> p cb i", i=512),
                   io["We3"][e].rearrange("(cb p) i -> p cb i", p=128), sem="wl")
            dma_in(s_we2[:].rearrange("p (cb i) -> p cb i", i=1024),
                   io["We2"][e].rearrange("(cb p) i -> p cb i", p=128), sem="wl")
            dma_in(s_b2row[:], io["be2r"][e:e + 1, :])
            ctx.run("te", lambda E, o=p_b1[:, 0:512],
                    l=s_eselE[:, 128 * e:128 * e + 128],
                    r=s_combT[0:16, 0:512]:
                    MM(E, o, l, r, start=True, stop=True),
                    reads=(s_eselE, s_combT), writes=(p_b1,))
            ctx.run("ve", lambda E, o=s_cb1[:], i=p_b1[:, 0:512]: E.tensor_copy(o, i),
                    reads=(p_b1,), writes=(s_cb1,))
            for ic in range(4):
                pp = p_pr[ic % 2]
                for cb in range(CB):
                    ctx.run("te", lambda E, o=pp[:, 0:512],
                            l=s_we1[:, 512 * cb + 128 * ic: 512 * cb + 128 * ic + 128],
                            r=s_x3m[:, cb * TTM: cb * TTM + 512],
                            st=(cb == 0), sp=(cb == CB - 1):
                            MM(E, o, l, r, start=st, stop=sp),
                            reads=(s_we1, s_x3m), writes=(pp,), inc=(cb == CB - 1))
                ctx.run("se", lambda E, o=s_h1[:, 512 * ic:512 * ic + 512], i=pp[:, 0:512],
                        b=be1col(e, ic): E.activation(o, i, AF.Silu, bias=b),
                        reads=(pp, s_becs), writes=(s_h1,))
            for ic in range(4):
                pp = p_pr[ic % 2]
                for cb in range(CB):
                    ctx.run("te", lambda E, o=pp[:, 0:512],
                            l=s_we3[:, 512 * cb + 128 * ic: 512 * cb + 128 * ic + 128],
                            r=s_x3m[:, cb * TTM: cb * TTM + 512],
                            st=(cb == 0), sp=(cb == CB - 1):
                            MM(E, o, l, r, start=st, stop=sp),
                            reads=(s_we3, s_x3m), writes=(pp,), inc=(cb == CB - 1))
                ctx.run("ve", lambda E, o=s_h3[:, 512 * ic:512 * ic + 512], i=pp[:, 0:512],
                        b=be3col(e, ic): E.tensor_scalar(o, i, b, None, ALU.add),
                        reads=(pp, s_becs), writes=(s_h3,))
            ctx.run("ve", lambda E, o=s_h3[:], a=s_h1[:], b=s_h3[:]:
                    E.tensor_tensor(o, a, b, ALU.mult), reads=(s_h1, s_h3), writes=(s_h3,))
            for ic in range(4):
                ctx.run("ve", lambda E, o=s_h1[:, 512 * ic:512 * ic + 512],
                        a=s_h3[:, 512 * ic:512 * ic + 512], b=s_cb1[:]:
                        E.tensor_tensor(o, a, b, ALU.mult), reads=(s_h3, s_cb1), writes=(s_h1,))
            for ob in range(CB):
                pp = p_pr[ob % 2]
                for ic in range(4):
                    ctx.run("te", lambda E, o=pp[:, 0:512],
                            l=s_we2[:, 1024 * ic + 128 * ob: 1024 * ic + 128 * ob + 128],
                            r=s_h1[:, 512 * ic:512 * ic + 512],
                            st=(ic == 0), sp=False:
                            MM(E, o, l, r, start=st, stop=sp),
                            reads=(s_we2, s_h1), writes=(pp,), inc=False)
                ctx.run("te", lambda E, o=pp[:, 0:512],
                        l=s_b2row[0:1, 128 * ob:128 * ob + 128],
                        r=s_cb1[0:1, 0:512]:
                        MM(E, o, l, r, start=False, stop=True),
                        reads=(s_b2row, s_cb1), writes=(pp,))
                ctx.run("ve", lambda E,
                        o=yaB[:, ob * TTM: ob * TTM + 512],
                        a=yaA[:, ob * TTM: ob * TTM + 512],
                        b=pp[:, 0:512]: E.tensor_tensor(o, a, b, ALU.add),
                        reads=(yaA, pp), writes=(yaB,))

        yfin = s_ya[16 % 2]
        for cb in range(CB):
            ctx.run("ve", lambda E, o=yfin[:, cb * TTM:(cb + 1) * TTM],
                    a=yfin[:, cb * TTM:(cb + 1) * TTM],
                    b=s_x3g[:, cb * TTM:(cb + 1) * TTM]:
                    E.tensor_tensor(o, a, b, ALU.add), reads=(yfin, s_x3g), writes=(yfin,))
        emit_ln_m(yfin, s_x4)
        dma_out(io["outT"].rearrange("(cb p) tt -> p cb tt", p=128)[:, :, t0:t0 + TTM],
                s_x4[:].rearrange("p (cb tt) -> p cb tt", tt=TTM))

    ctx.wait("po", "out", ctx.barrier_token("out"))
    ctx.full_barrier()
    es3h.close()


def build_phase3(p_tokens=P):
    nc = bass.Bass("TRN2", target_bir_lowering=False)
    io = {}
    io["x2s"] = nc.dram_tensor("x2s", [D, p_tokens], F32R, kind="ExternalInput").ap()
    for n in ["qfs", "kfs"]:
        io[n] = nc.dram_tensor(n, [D, p_tokens], F32, kind="ExternalInput").ap()
    io["vtm"] = nc.dram_tensor("vtm", [p_tokens, D], F32, kind="ExternalInput").ap()
    io["gS1"] = nc.dram_tensor("gS1", [128, 8 * 512], F32, kind="ExternalInput").ap()
    io["gZ1"] = nc.dram_tensor("gZ1", [128, 64], F32, kind="ExternalInput").ap()
    io["fmask"] = nc.dram_tensor("fmask", [128, 16], F32, kind="ExternalInput").ap()
    io["Wo1_t"] = nc.dram_tensor("Wo1_t", [D, D], F32R, kind="ExternalInput").ap()
    io["gwT"] = nc.dram_tensor("gwT", [D, 16], F32, kind="ExternalInput").ap()
    io["gb"] = nc.dram_tensor("gb", [1, 16], F32, kind="ExternalInput").ap()
    io["gbb"] = nc.dram_tensor("gbb", [128, 16], F32, kind="ExternalInput").ap()
    io["id128f"] = nc.dram_tensor("id128f", [128, 128], F32, kind="ExternalInput").ap()
    io["ones128f"] = nc.dram_tensor("ones128f", [128, 128], F32, kind="ExternalInput").ap()
    io["We1"] = nc.dram_tensor("We1", [16, D, 512], F32R, kind="ExternalInput").ap()
    io["We3"] = nc.dram_tensor("We3", [16, D, 512], F32R, kind="ExternalInput").ap()
    io["We2"] = nc.dram_tensor("We2", [16, 512, D], F32R, kind="ExternalInput").ap()
    io["be1c"] = nc.dram_tensor("be1c", [128, 64], F32, kind="ExternalInput").ap()
    io["be3c"] = nc.dram_tensor("be3c", [128, 64], F32, kind="ExternalInput").ap()
    io["be2c"] = nc.dram_tensor("be2c", [128, 128], F32, kind="ExternalInput").ap()
    io["be2r"] = nc.dram_tensor("be2r", [16, D], F32R, kind="ExternalInput").ap()
    io["bsc"] = nc.dram_tensor("bsc", [128, 24], F32, kind="ExternalInput").ap()
    io["ws1"] = nc.dram_tensor("ws1", [D, D], F32R, kind="ExternalInput").ap()
    io["ws3"] = nc.dram_tensor("ws3", [D, D], F32R, kind="ExternalInput").ap()
    io["ws2"] = nc.dram_tensor("ws2", [D, D], F32R, kind="ExternalInput").ap()
    io["dsel"] = nc.dram_tensor("dsel", [128, 128], F32, kind="ExternalInput").ap()
    io["esel2"] = nc.dram_tensor("esel2", [16, 1024], F32R, kind="ExternalInput").ap()
    io["eselE"] = nc.dram_tensor("eselE", [16, 2048], F32R, kind="ExternalInput").ap()
    io["ones128"] = nc.dram_tensor("ones128", [128, 128], F32R, kind="ExternalInput").ap()
    io["id128"] = nc.dram_tensor("id128", [128, 128], F32, kind="ExternalInput").ap()
    io["mask128i"] = nc.dram_tensor("mask128i", [128, 128], F32, kind="ExternalInput").ap()
    io["epsc"] = nc.dram_tensor("epsc", [128, 2], F32, kind="ExternalInput").ap()
    io["x3s"] = nc.dram_tensor("x3s", [D, p_tokens], F32, kind="Internal").ap()
    io["x3r"] = nc.dram_tensor("x3r", [D, p_tokens], F32R, kind="Internal").ap()
    io["outT"] = nc.dram_tensor("outT", [D, p_tokens], F32, kind="ExternalOutput").ap()
    with nc.Block() as block:
        ctx = Ctx(nc, block)
        emit_phase3(ctx, io, p_tokens)
        ctx.flush()
    return nc


def pack_col_n(v, nch):
    v = np.asarray(v, np.float32).reshape(nch, 128).T
    return np.ascontiguousarray(v)


def stage_phase3_shared(inputs):
    sh = {}
    sh["Wo1_t"] = np.ascontiguousarray(np.asarray(inputs["L1_attn_Wo"], np.float32).T)
    sh["gwT"] = np.ascontiguousarray(np.asarray(inputs["L1_ffn_gate_w"], np.float32).T)
    sh["gb"] = np.ascontiguousarray(np.asarray(inputs["L1_ffn_gate_b"], np.float32).reshape(1, 16))
    sh["gbb"] = np.ascontiguousarray(np.repeat(sh["gb"], 128, axis=0))
    sh["id128f"] = np.eye(128, dtype=np.float32)
    sh["ones128f"] = np.ones((128, 128), np.float32)
    sh["We1"] = np.ascontiguousarray(np.asarray(inputs["L1_ffn_We1"], np.float32))
    sh["We3"] = np.ascontiguousarray(np.asarray(inputs["L1_ffn_We3"], np.float32))
    sh["We2"] = np.ascontiguousarray(np.asarray(inputs["L1_ffn_We2"], np.float32))
    be1 = np.asarray(inputs["L1_ffn_be1"], np.float32)
    be3 = np.asarray(inputs["L1_ffn_be3"], np.float32)
    be2 = np.asarray(inputs["L1_ffn_be2"], np.float32)
    b1c = np.zeros((128, 64), np.float32)
    b3c = np.zeros((128, 64), np.float32)
    b2c = np.zeros((128, 128), np.float32)
    for e in range(16):
        b1c[:, 4 * e:4 * e + 4] = pack_col_n(be1[e], 4)
        b3c[:, 4 * e:4 * e + 4] = pack_col_n(be3[e], 4)
        b2c[:, 8 * e:8 * e + 8] = pack_col_n(be2[e], 8)
    sh["be1c"] = b1c
    sh["be3c"] = b3c
    sh["be2c"] = b2c
    sh["be2r"] = np.ascontiguousarray(be2)
    bsc = np.zeros((128, 24), np.float32)
    bsc[:, 0:8] = pack_col_n(inputs["L1_ffn_bs1"], 8)
    bsc[:, 8:16] = pack_col_n(inputs["L1_ffn_bs3"], 8)
    bsc[:, 16:24] = pack_col_n(inputs["L1_ffn_bs2"], 8)
    sh["bsc"] = bsc
    sh["ws1"] = np.ascontiguousarray(np.asarray(inputs["L1_ffn_ws1"], np.float32))
    sh["ws3"] = np.ascontiguousarray(np.asarray(inputs["L1_ffn_ws3"], np.float32))
    sh["ws2"] = np.ascontiguousarray(np.asarray(inputs["L1_ffn_ws2"], np.float32))
    dsel = np.zeros((128, 128), np.float32)
    for cb in range(8):
        dsel[0:64, 16 * cb + 2 * cb] = 1.0
        dsel[64:128, 16 * cb + 2 * cb + 1] = 1.0
    sh["dsel"] = dsel
    esel2 = np.zeros((16, 1024), np.float32)
    for cb in range(8):
        esel2[2 * cb, 128 * cb:128 * cb + 64] = 1.0
        esel2[2 * cb + 1, 128 * cb + 64:128 * cb + 128] = 1.0
    sh["esel2"] = esel2
    eselE = np.zeros((16, 2048), np.float32)
    for e in range(16):
        eselE[e, 128 * e:128 * e + 128] = 1.0
    sh["eselE"] = eselE
    sh["ones128"] = np.ones((128, 128), np.float32)
    sh["id128"] = np.eye(128, dtype=np.float32)
    sh["mask128i"] = np.triu(np.ones((128, 128), np.float32), 0)
    epsc = np.zeros((128, 2), np.float32)
    epsc[:, 0] = GN_EPS
    epsc[:, 1] = LN_EPS
    sh["epsc"] = epsc
    return sh



def build_fused(p_tokens=P):
    nc = bass.Bass("TRN2", target_bir_lowering=False)
    io = {}
    io["xT"] = nc.dram_tensor("xT", [D, p_tokens + 1], F32, kind="ExternalInput").ap()
    for n in ["Wk_t", "Wv_t", "Wr_t", "Wg_t"]:
        io[n] = nc.dram_tensor(n, [D, D], F32R, kind="ExternalInput").ap()
    io["A_lora"] = nc.dram_tensor("A_lora", [D, 160], F32R, kind="ExternalInput").ap()
    io["B_lora"] = nc.dram_tensor("B_lora", [5, 32, D], F32R, kind="ExternalInput").ap()
    io["td_A"] = nc.dram_tensor("td_A", [D, 64], F32R, kind="ExternalInput").ap()
    io["td_B"] = nc.dram_tensor("td_B", [64, D], F32R, kind="ExternalInput").ap()
    io["cvec"] = nc.dram_tensor("cvec", [128, 8 * NCV1], F32, kind="ExternalInput").ap()
    io["id128"] = nc.dram_tensor("id128", [128, 128], F32, kind="ExternalInput").ap()
    io["mask64s"] = nc.dram_tensor("mask64s", [64, 64], F32, kind="ExternalInput").ap()
    io["i64"] = nc.dram_tensor("i64", [64, 64], F32, kind="ExternalInput").ap()
    io["dsel"] = nc.dram_tensor("dsel", [128, 128], F32, kind="ExternalInput").ap()
    io["esel"] = nc.dram_tensor("esel", [16, 1024], F32, kind="ExternalInput").ap()
    for n in ["Wo_t", "Wq_t", "Wk1_t", "Wv1_t", "Wg1_t",
              "wfr", "wfi", "Fr", "Fi", "Fin", "Grs", "Gis"]:
        io[n] = nc.dram_tensor(n, [D, D], F32R, kind="ExternalInput").ap()
    io["cvec2"] = nc.dram_tensor("cvec2", [128, 32], F32, kind="ExternalInput").ap()
    io["dselr"] = nc.dram_tensor("dselr", [128, 128], F32R, kind="ExternalInput").ap()
    io["esel2"] = nc.dram_tensor("esel2", [16, 1024], F32R, kind="ExternalInput").ap()
    io["ones128"] = nc.dram_tensor("ones128", [128, 128], F32R, kind="ExternalInput").ap()
    io["epsc"] = nc.dram_tensor("epsc", [128, 2], F32, kind="ExternalInput").ap()
    io["fmask"] = nc.dram_tensor("fmask", [128, 16], F32, kind="ExternalInput").ap()
    io["Wo1_t"] = nc.dram_tensor("Wo1_t", [D, D], F32R, kind="ExternalInput").ap()
    io["gwT"] = nc.dram_tensor("gwT", [D, 16], F32, kind="ExternalInput").ap()
    io["gb"] = nc.dram_tensor("gb", [1, 16], F32, kind="ExternalInput").ap()
    io["gbb"] = nc.dram_tensor("gbb", [128, 16], F32, kind="ExternalInput").ap()
    io["id128f"] = nc.dram_tensor("id128f", [128, 128], F32, kind="ExternalInput").ap()
    io["ones128f"] = nc.dram_tensor("ones128f", [128, 128], F32, kind="ExternalInput").ap()
    io["We1"] = nc.dram_tensor("We1", [16, D, 512], F32R, kind="ExternalInput").ap()
    io["We3"] = nc.dram_tensor("We3", [16, D, 512], F32R, kind="ExternalInput").ap()
    io["We2"] = nc.dram_tensor("We2", [16, 512, D], F32R, kind="ExternalInput").ap()
    io["be1c"] = nc.dram_tensor("be1c", [128, 64], F32, kind="ExternalInput").ap()
    io["be3c"] = nc.dram_tensor("be3c", [128, 64], F32, kind="ExternalInput").ap()
    io["be2c"] = nc.dram_tensor("be2c", [128, 128], F32, kind="ExternalInput").ap()
    io["be2r"] = nc.dram_tensor("be2r", [16, D], F32R, kind="ExternalInput").ap()
    io["bsc"] = nc.dram_tensor("bsc", [128, 24], F32, kind="ExternalInput").ap()
    io["ws1"] = nc.dram_tensor("ws1", [D, D], F32R, kind="ExternalInput").ap()
    io["ws3"] = nc.dram_tensor("ws3", [D, D], F32R, kind="ExternalInput").ap()
    io["ws2"] = nc.dram_tensor("ws2", [D, D], F32R, kind="ExternalInput").ap()
    io["eselE"] = nc.dram_tensor("eselE", [16, 2048], F32R, kind="ExternalInput").ap()
    io["mask128i"] = nc.dram_tensor("mask128i", [128, 128], F32, kind="ExternalInput").ap()
    for n in ["ylocal", "gsil", "x3s"]:
        io[n] = nc.dram_tensor(n, [D, p_tokens], F32, kind="Internal").ap()
    for n in ["RDs", "qfs", "kfs"]:
        io[n] = nc.dram_tensor(n, [D, p_tokens], F32, kind="Internal").ap()
    for n in ["x2s", "x3r"]:
        io[n] = nc.dram_tensor(n, [D, p_tokens], F32R, kind="Internal").ap()
    io["vtm"] = nc.dram_tensor("vtm", [p_tokens, D], F32, kind="Internal").ap()
    cc0_in = nc.dram_tensor("cc0_in", [128, 520], F32, kind="Internal").ap()
    cc0_out = nc.dram_tensor("cc0_out", [8 * 128, 520], F32, kind="Internal").ap()
    cc1_in = nc.dram_tensor("cc1_in", [128, 520], F32, kind="Internal").ap()
    cc1_out = nc.dram_tensor("cc1_out", [8 * 128, 520], F32, kind="Internal").ap()
    io["summ0"] = cc0_in[:, 0:512]
    io["summD"] = cc0_in[:, 512:520]
    io["summ1"] = cc1_in[:, 0:512]
    io["summZ"] = cc1_in[:, 512:520]
    io["outT"] = nc.dram_tensor("outT", [D, p_tokens], F32, kind="ExternalOutput").ap()

    with nc.Block() as block:
        ctx = Ctx(nc, block)
        emit_phase1(ctx, io, p_tokens)
        ctx.run("po", lambda E, a=cc0_in.opt(), b=cc0_out.opt(): E.collective_compute(
            "AllGather", ALU.bypass, replica_groups=[list(range(NCORE))],
            ins=[a], outs=[b]), reads=(cc0_in,), writes=(cc0_out,), sem="cc", d=1)
        emit_phase2(ctx, io, p_tokens,
                    gS_of=lambda r: cc0_out[128 * r:128 * (r + 1), 0:512],
                    gD_of=lambda r: cc0_out[128 * r:128 * (r + 1), 512:520])
        ctx.run("po", lambda E, a=cc1_in.opt(), b=cc1_out.opt(): E.collective_compute(
            "AllGather", ALU.bypass, replica_groups=[list(range(NCORE))],
            ins=[a], outs=[b]), reads=(cc1_in,), writes=(cc1_out,), sem="cc", d=1)
        emit_phase3(ctx, io, p_tokens,
                    gS_of=lambda r: cc1_out[128 * r:128 * (r + 1), 0:512],
                    gZ_of=lambda r: cc1_out[128 * r:128 * (r + 1), 512:520])
        ctx.flush()
    return nc


def stage_fused_inputs(inputs, p_tokens=P):
    maps1 = stage_phase1_inputs(inputs, p_tokens)
    sh2 = stage_phase2_shared(inputs)
    sh3 = stage_phase3_shared(inputs)
    in_maps = []
    for core in range(NCORE):
        m = {}
        m.update(maps1[core])
        m.update(sh2)
        m.update(sh3)
        m["fmask"] = _fold_masks(core)
        in_maps.append(m)
    return in_maps


def kernel(**inputs):
    nc = build_fused()
    in_maps = stage_fused_inputs(inputs)
    res = run_bass_kernel_spmd(nc, in_maps, core_ids=list(range(NCORE))).results
    out = np.empty((B, T, D), np.float32)
    for core in range(NCORE):
        b, q = core // 4, core % 4
        out[b, q * P:(q + 1) * P] = res[core]["outT"].T
    return out


# version 09
# speedup vs baseline: 1.0540x; 1.0215x over previous
import os
import contextlib
import numpy as np
import concourse.bass as bass
import concourse.mybir as mybir
from concourse.bass_utils import run_bass_kernel_spmd

F32 = mybir.dt.float32
F32R = mybir.dt.float32r
AF = mybir.ActivationFunctionType
ALU = mybir.AluOpType
def MM(E, o, l, r, **kw):
    return E.matmul(o, l, r, **kw)

B, T, D = 2, 16384, 1024
NCORE = 8
P = (B * T) // NCORE
H, HS = 16, 64
TT1 = 256
C0 = 64
NCH1 = TT1 // C0
CB = 8

CV_MIUX, CV_LAM_W, CV_LAM_K, CV_LAM_V, CV_LAM_R, CV_LAM_G, CV_TDMIU, CV_U = range(8)
NCV1 = 8

ENGSEM = {"sy": "in", "po": "out", "te": "pe", "ve": "ve", "se": "se"}
ENGD = {"sy": 16, "po": 16, "te": 1, "ve": 1, "se": 1}
ALL_SEMS = ["in", "wl", "out", "cc", "pe", "ve", "se"]
DMA_SEMS = ("in", "wl", "out", "cc")


def nm(ap):
    t = getattr(ap, "tensor", None)
    if t is not None:
        return t.name
    n = getattr(ap, "name", None)
    if n is not None:
        return n
    return str(ap)


class Ctx:
    ENGINES = ("sy", "po", "te", "ve", "se")

    def __init__(self, nc, block):
        self.nc = nc
        self.block = block
        self.sems = {}
        self.cnt = {}
        self._guards = []
        for name in ALL_SEMS:
            g = nc.semaphore(name)
            h = g.__enter__()
            self._guards.append(g)
            self.sems[name] = h
            self.cnt[name] = 0
        self.waited_vals = {s: set() for s in ALL_SEMS}
        self.cur_wait = {e: {s: 0 for s in ALL_SEMS} for e in self.ENGINES}
        self.q = {e: [] for e in self.ENGINES}
        self.last_w = {}
        self.last_r = {}
        self.pending = {e: [] for e in self.ENGINES}

    def wait(self, eng, sem, val):
        if val is None or val <= 0:
            return
        if self.cur_wait[eng][sem] >= val:
            return
        self.cur_wait[eng][sem] = val
        self.waited_vals[sem].add(val)
        self.q[eng].append(("w", sem, val))

    def run(self, eng, f, reads=(), writes=(), inc=True, sem=None, d=None):
        if sem is None:
            sem = ENGSEM[eng]
        if d is None:
            d = ENGD[eng]
        if sem in DMA_SEMS:
            self.wait(eng, sem, self.cnt[sem])
        for ap in reads:
            lw = self.last_w.get(nm(ap))
            if lw:
                self.wait(eng, lw[0], lw[1])
        for ap in writes:
            n = nm(ap)
            lw = self.last_w.get(n)
            if lw:
                self.wait(eng, lw[0], lw[1])
            for s2, t2 in self.last_r.get(n, {}).items():
                self.wait(eng, s2, t2)
        if inc:
            self.cnt[sem] += d
            tok = self.cnt[sem]
            self.q[eng].append(("op", f, sem, tok, d))
            recs = self.pending[eng]
            self.pending[eng] = []
            recs.append((reads, writes))
            for rs, ws in recs:
                for ap in rs:
                    self.last_r.setdefault(nm(ap), {})[sem] = tok
                for ap in ws:
                    n = nm(ap)
                    self.last_w[n] = (sem, tok)
                    self.last_r[n] = {}
            return tok
        else:
            self.q[eng].append(("op", f, None, 0, 0))
            self.pending[eng].append((reads, writes))
            return None

    def full_barrier(self):
        for e in self.ENGINES:
            for sm in ALL_SEMS:
                self.wait(e, sm, self.cnt[sm])

    def barrier_token(self, sem):
        v = self.cnt[sem]
        if v > 0:
            self.waited_vals[sem].add(v)
        return v

    def flush(self):
        with self.nc.allow_low_precision(
                reason="tf32-class matmul operands; problem tolerance calibrated for bf16-class"):
            self._flush_inner()

    def _flush_inner(self):
        blk = self.block
        decs = {"sy": blk.sync, "po": blk.gpsimd, "te": blk.tensor,
                "ve": blk.vector, "se": blk.scalar}
        for name in self.ENGINES:
            items = self.q[name]
            if not items:
                continue
            prog = []
            pending_d = {s: 0 for s in ALL_SEMS}
            for it in items:
                if it[0] == "w":
                    prog.append(it)
                else:
                    _, f, sem, tok, d = it
                    if sem is not None and (sem in DMA_SEMS or tok in self.waited_vals[sem]):
                        dd = pending_d[sem] + d
                        pending_d[sem] = 0
                        prog.append(("op", f, sem, dd))
                    else:
                        if sem is not None:
                            pending_d[sem] += d
                        prog.append(("op", f, None, 0))
            sems = self.sems

            def body(E, prog=prog, sems=sems):
                for it in prog:
                    if it[0] == "w":
                        E.wait_ge(sems[it[1]], it[2])
                    else:
                        _, f, sem, dd = it
                        if sem is not None:
                            f(E).then_inc(sems[sem], dd)
                        else:
                            f(E)
            decs[name](body)
        self.q = {e: [] for e in self.ENGINES}



class _StopEmit(Exception):
    pass


def emit_phase1(ctx, io, p_tokens=P):
    stop_at = int(os.environ.get("P1_STOP", "99"))

    def ckpt(n):
        if stop_at <= n:
            raise _StopEmit()
    sub = int(os.environ.get("P1_SUB", "31"))
    nc = ctx.nc
    es1 = contextlib.ExitStack()

    def al(name, shape, dt):
        return es1.enter_context(nc.sbuf_tensor(name, shape, dt))

    def ap_(name, shape, dt):
        return es1.enter_context(nc.psum_tensor(name, shape, dt))

    s_alora = al("s_alora", [128, 1280], F32R)
    s_blora_a = al("s_blora_a", [96, 1024], F32R)
    s_blora_b = al("s_blora_b", [64, 1024], F32R)
    s_tda = al("s_tda", [128, 512], F32R)
    s_tdb = al("s_tdb", [64, 1024], F32R)
    s_cvec = al("s_cvec", [128, 8 * NCV1], F32)
    s_id = al("s_id", [128, 128], F32)
    s_m64 = al("s_m64", [64, 1024], F32)
    s_dsel = al("s_dsel", [128, 128], F32)
    s_esel = al("s_esel", [16, 1024], F32)
    s_z64 = al("s_z64", [128, 64], F32)
    s_wbuf = al("s_wbuf", [128, 8192], F32R)

    s_x = al("s_x", [128, CB * (TT1 + 1)], F32)
    s_dxa = al("s_dxa", [128, CB * TT1], F32)
    s_mx = al("s_mx", [128, CB * TT1], F32R)
    s_lh_a = al("s_lh_a", [96, TT1], F32R)
    s_lh_b = al("s_lh_b", [64, TT1], F32R)
    s_t1 = al("s_t1", [64, TT1], F32R)
    s_k = al("s_k", [128, CB * TT1], F32)
    s_v = al("s_v", [128, CB * TT1], F32)
    s_r = al("s_r", [128, CB * TT1], F32)
    s_ew = al("s_ew", [128, CB * TT1], F32)
    s_g = al("s_g", [128, CB * TT1], F32)

    csc2 = [al("s_csc0", [128, 512], F32), al("s_csc1", [128, 512], F32)]
    e22 = [al("s_e20", [128, 512], F32), al("s_e21", [128, 512], F32)]
    et2 = [al("s_et0", [128, 512], F32), al("s_et1", [128, 512], F32)]
    e12 = [al("s_e10", [128, 512], F32), al("s_e11", [128, 512], F32)]
    e32 = [al("s_e30", [128, 512], F32), al("s_e31", [128, 512], F32)]
    kt2 = [al("s_kt0", [128, 512], F32), al("s_kt1", [128, 512], F32)]
    rt2 = [al("s_rt0", [128, 512], F32), al("s_rt1", [128, 512], F32)]
    kh2 = [al("s_kh0", [128, 512], F32), al("s_kh1", [128, 512], F32)]
    ncol2 = [al("s_ncol0", [128, 8], F32), al("s_ncol1", [128, 8], F32)]
    dtot2 = [al("s_dtot0", [128, 8], F32), al("s_dtot1", [128, 8], F32)]
    ecg2 = [al("s_ecg0", [128, 8], F32), al("s_ecg1", [128, 8], F32)]
    s_cg = [al("s_cg0", [128, 8], F32), al("s_cg1", [128, 8], F32)]
    s_dtS = al("s_dtS", [128, 8], F32)
    tmpA2 = [al("s_tmpA0", [128, 512], F32), al("s_tmpA1", [128, 512], F32)]
    tmpB2 = [al("s_tmpB0", [128, 512], F32), al("s_tmpB1", [128, 512], F32)]
    dvec2 = [al("s_dvec0", [16, 64], F32), al("s_dvec1", [16, 64], F32)]
    s_vkw = [al("s_vkA", [64, 1024], F32), al("s_vkB", [64, 1024], F32)]
    s_abw = [al("s_abA", [64, 1024], F32), al("s_abB", [64, 1024], F32)]
    s_S = [al("s_S0", [128, 512], F32), al("s_S1", [128, 512], F32)]
    s_yl = al("s_yl", [128, CB * TT1], F32)
    s_rd = al("s_rd", [128, CB * TT1], F32)

    p_trw = ap_("p_trw", [64, 1024], F32)
    p_abw = ap_("p_abw", [64, 1024], F32)
    p_yw = ap_("p_yw", [128, 512], F32)
    p_y2w = ap_("p_y2w", [128, 512], F32)
    p_pr = [ap_("p_pr0", [128, 256], F32), ap_("p_pr1", [128, 256], F32)]

    def fm(t_sb, cb, lo, n):
        return t_sb[:, cb * TT1 + lo: cb * TT1 + lo + n]

    def xcur(cb, lo=0, n=TT1):
        return s_x[:, cb * (TT1 + 1) + 1 + lo: cb * (TT1 + 1) + 1 + lo + n]

    def dma_in(dst, src, sem="in"):
        return ctx.run("sy", lambda E, a=dst, b=src: E.dma_start(out=a, in_=b),
                       reads=(src,), writes=(dst,), sem=sem)

    def dma_out(dst, src):
        return ctx.run("po", lambda E, a=dst, b=src: E.dma_start(out=a, in_=b),
                       reads=(src,), writes=(dst,))

    dma_in(s_alora[:].rearrange("p (cb f) -> p cb f", f=160),
           io["A_lora"].rearrange("(cb p) f -> p cb f", p=128))
    for f in range(3):
        dma_in(s_blora_a[32 * f:32 * f + 32, :], io["B_lora"][f])
    for f in range(2):
        dma_in(s_blora_b[32 * f:32 * f + 32, :], io["B_lora"][3 + f])
    dma_in(s_tda[:].rearrange("p (cb r) -> p cb r", r=64),
           io["td_A"].rearrange("(cb p) r -> p cb r", p=128))
    dma_in(s_tdb[:], io["td_B"])
    dma_in(s_cvec[:], io["cvec"])
    dma_in(s_id[:], io["id128"])
    dma_in(s_m64[:], io["mask64s"])
    dma_in(s_dsel[:], io["dsel"])
    dma_in(s_esel[:], io["esel"])

    ctx.run("ve", lambda E, a=s_z64[:]: E.memset(a, 0.0), writes=(s_z64,))
    ctx.run("ve", lambda E, a=s_S[0][:]: E.memset(a, 0.0), writes=(s_S[0],))
    ctx.run("ve", lambda E, a=s_cg[0][:]: E.memset(a, 0.0), writes=(s_cg[0],))

    w_names = {"k": "Wk_t", "v": "Wv_t", "r": "Wr_t", "g": "Wg_t"}
    w_dst = {"k": s_k, "v": s_v, "r": s_r, "g": s_g}

    nt1 = p_tokens // TT1
    try:
      for t in range(nt1):
        t0 = t * TT1

        xin = io["xT"].rearrange("(cb p) tt -> p cb tt", p=128)[:, :, t0:t0 + TT1 + 1]
        xout = s_x[:].rearrange("p (cb tt) -> p cb tt", tt=TT1 + 1)
        dma_in(xout, xin)

        xv = s_x[:].rearrange("p (cb tt) -> p cb tt", tt=TT1 + 1)
        dxv = s_dxa[:].rearrange("p (cb tt) -> p cb tt", tt=TT1)
        ctx.run("ve", lambda E, o=dxv, a=xv[:, :, 0:TT1], b=xv[:, :, 1:TT1 + 1]:
                E.tensor_tensor(o, a, b, ALU.subtract),
                reads=(s_x,), writes=(s_dxa,))

        for cb in range(CB):
            ctx.run("ve", lambda E, o=fm(s_mx, cb, 0, TT1), a=fm(s_dxa, cb, 0, TT1),
                    s=s_cvec[:, 8 * CV_MIUX + cb: 8 * CV_MIUX + cb + 1], b=xcur(cb):
                    E.scalar_tensor_tensor(o, a, s, b, ALU.mult, ALU.add),
                    reads=(s_dxa, s_cvec, s_x), writes=(s_mx,))

        ckpt(1)
        lh_t = [s_lh_a, s_lh_a, s_lh_a, s_lh_b, s_lh_b]
        lh_slices = [s_lh_a[0:32, :], s_lh_a[32:64, :], s_lh_a[64:96, :],
                     s_lh_b[0:32, :], s_lh_b[32:64, :]]
        for f in range(5):
            pl = p_pr[f % 2][0:32, :]
            for cb in range(CB):
                ctx.run("te", lambda E, o=pl, l=s_alora[:, 160 * cb + 32 * f: 160 * cb + 32 * f + 32],
                        r=fm(s_mx, cb, 0, TT1), st=(cb == 0), sp=(cb == CB - 1):
                        MM(E, o, l, r, start=st, stop=sp),
                        reads=(s_alora, s_mx), writes=(p_pr[f % 2],),
                        inc=(cb == CB - 1))
            ctx.run("se", lambda E, o=lh_slices[f], i=pl: E.activation(o, i, AF.Tanh),
                    reads=(p_pr[f % 2],), writes=(lh_t[f],))

        ckpt(2)
        for fi, fname in enumerate(["w", "k", "v", "r", "g"]):
            if fi < 3:
                blora, rbase = s_blora_a, 32 * fi
            else:
                blora, rbase = s_blora_b, 32 * (fi - 3)
            lam = 8 * (CV_LAM_W + fi)
            for cb in range(CB):
                pm = p_pr[cb % 2]
                ctx.run("te", lambda E, o=pm[:], l=blora[rbase:rbase + 32, 128 * cb:128 * cb + 128],
                        r=lh_slices[fi]: MM(E, o, l, r, start=True, stop=True),
                        reads=(blora, lh_t[fi]), writes=(pm,))
                ctx.run("ve", lambda E, o=fm(s_mx, cb, 0, TT1), a=pm[:],
                        s=s_cvec[:, lam + cb: lam + cb + 1], b=fm(s_dxa, cb, 0, TT1):
                        E.scalar_tensor_tensor(o, a, s, b, ALU.add, ALU.mult),
                        reads=(pm, s_cvec, s_dxa), writes=(s_mx,))
                ctx.run("ve", lambda E, o=fm(s_mx, cb, 0, TT1), a=fm(s_mx, cb, 0, TT1), b=xcur(cb):
                        E.tensor_tensor(o, a, b, ALU.add),
                        reads=(s_mx, s_x), writes=(s_mx,))

            if fname == "w":
                pt = p_pr[0][0:64, :]
                for cb in range(CB):
                    ctx.run("te", lambda E, o=pt, l=s_tda[:, 64 * cb:64 * cb + 64],
                            r=fm(s_mx, cb, 0, TT1), st=(cb == 0), sp=(cb == CB - 1):
                            MM(E, o, l, r, start=st, stop=sp),
                            reads=(s_tda, s_mx), writes=(p_pr[0],),
                            inc=(cb == CB - 1))
                ctx.run("se", lambda E, o=s_t1[:], i=pt: E.activation(o, i, AF.Tanh),
                        reads=(p_pr[0],), writes=(s_t1,))
                for ob in range(CB):
                    pw = p_pr[1]
                    ctx.run("te", lambda E, o=pw[:], l=s_tdb[:, 128 * ob:128 * ob + 128],
                            r=s_t1[:]: MM(E, o, l, r, start=True, stop=True),
                            reads=(s_tdb, s_t1), writes=(pw,))
                    ctx.run("se", lambda E, o=fm(s_ew, ob, 0, TT1), i=pw[:],
                            bi=s_cvec[:, 8 * CV_TDMIU + ob: 8 * CV_TDMIU + ob + 1]:
                            E.activation(o, i, AF.Exp, bias=bi),
                            reads=(pw, s_cvec), writes=(s_ew,))
            else:
                wkey = w_names[fname]
                wout = s_wbuf[:].rearrange("p (cb co) -> p cb co", co=1024)
                win = io[wkey].rearrange("(cb p) co -> p cb co", p=128)
                dma_in(wout, win, sem="wl")
                dst = w_dst[fname]
                func = AF.Silu if fname == "g" else AF.Copy
                for ob in range(CB):
                    pp = p_pr[ob % 2]
                    for cb in range(CB):
                        ctx.run("te", lambda E, o=pp[:],
                                l=s_wbuf[:, 1024 * cb + 128 * ob: 1024 * cb + 128 * ob + 128],
                                r=fm(s_mx, cb, 0, TT1), st=(cb == 0), sp=(cb == CB - 1):
                                MM(E, o, l, r, start=st, stop=sp),
                                reads=(s_wbuf, s_mx), writes=(pp,),
                                inc=(cb == CB - 1))
                    ctx.run("se", lambda E, o=fm(dst, ob, 0, TT1), i=pp[:], fn=func:
                            E.activation(o, i, fn), reads=(pp,), writes=(dst,))

        ckpt(3)
        gout = io["gsil"].rearrange("(cb p) tt -> p cb tt", p=128)[:, :, t0:t0 + TT1]
        dma_out(gout, s_g[:].rearrange("p (cb tt) -> p cb tt", tt=TT1))

        def emit_prep(c):
            lo = c * C0
            gidx = t * NCH1 + c
            par2 = gidx % 2
            s_csc, s_e2, s_et, s_e1, s_e3 = csc2[par2], e22[par2], et2[par2], e32[par2], e32[par2]
            s_e1 = e12[par2]
            s_kt, s_rt, s_kh = kt2[par2], rt2[par2], kh2[par2]
            s_ncol, s_dtot, s_ecg = ncol2[par2], dtot2[par2], ecg2[par2]
            s_tmpA, s_tmpB, s_dvec = tmpA2[par2], tmpB2[par2], dvec2[par2]
            cgA, cgB = s_cg[gidx % 2], s_cg[(gidx + 1) % 2]
            Sprev, Snew = s_S[gidx % 2], s_S[(gidx + 1) % 2]

            for cb in range(CB):
                ctx.run("ve", lambda E, o=s_csc[:, 64 * cb:64 * cb + 64],
                        a=fm(s_ew, cb, lo, C0), z=s_z64[:, 0:64]:
                        E.tensor_tensor_scan(o, a, z, 0.0, ALU.add, ALU.add),
                        reads=(s_ew, s_z64), writes=(s_csc,))
            ctx.run("se", lambda E, o=s_e2[:], i=s_csc[:]: E.activation(o, i, AF.Exp),
                    reads=(s_csc,), writes=(s_e2,))
            ewv = s_ew[:].rearrange("p (cb tt) -> p cb tt", tt=TT1)[:, :, lo:lo + C0]
            cscv = s_csc[:].rearrange("p (cb cc) -> p cb cc", cc=C0)
            etv = s_et[:].rearrange("p (cb cc) -> p cb cc", cc=C0)
            ctx.run("ve", lambda E, o=etv, a=ewv, b=cscv:
                    E.tensor_tensor(o, a, b, ALU.subtract),
                    reads=(s_ew, s_csc), writes=(s_et,))
            ctx.run("se", lambda E, o=s_e1[:], i=s_et[:]: E.activation(o, i, AF.Exp),
                    reads=(s_et,), writes=(s_e1,))
            for cb in range(CB):
                ctx.run("ve", lambda E, o=s_ncol[:, cb:cb + 1],
                        i=s_csc[:, 64 * cb + 63:64 * cb + 64]:
                        E.tensor_scalar(o, i, -1.0, None, ALU.mult),
                        reads=(s_csc,), writes=(s_ncol,))
                ctx.run("ve", lambda E, o=cgB[:, cb:cb + 1], a=cgA[:, cb:cb + 1],
                        b=s_csc[:, 64 * cb + 63:64 * cb + 64]:
                        E.tensor_tensor(o, a, b, ALU.add),
                        reads=(cgA, s_csc), writes=(cgB,))
            for cb in range(CB):
                ctx.run("se", lambda E, o=s_e3[:, 64 * cb:64 * cb + 64],
                        i=s_csc[:, 64 * cb:64 * cb + 64], bi=s_ncol[:, cb:cb + 1]:
                        E.activation(o, i, AF.Exp, bias=bi),
                        reads=(s_csc, s_ncol), writes=(s_e3,))
                ctx.run("se", lambda E, o=s_dtot[:, cb:cb + 1], i=s_ncol[:, cb:cb + 1]:
                        E.activation(o, i, AF.Exp), reads=(s_ncol,), writes=(s_dtot,))
                ctx.run("se", lambda E, o=s_ecg[:, cb:cb + 1], i=cgA[:, cb:cb + 1]:
                        E.activation(o, i, AF.Exp, scale=-1.0),
                        reads=(cgA,), writes=(s_ecg,))

            ckpt(4)
            kvv = s_k[:].rearrange("p (cb tt) -> p cb tt", tt=TT1)[:, :, lo:lo + C0]
            rvv = s_r[:].rearrange("p (cb tt) -> p cb tt", tt=TT1)[:, :, lo:lo + C0]
            ktv = s_kt[:].rearrange("p (cb cc) -> p cb cc", cc=C0)
            rtv = s_rt[:].rearrange("p (cb cc) -> p cb cc", cc=C0)
            khv = s_kh[:].rearrange("p (cb cc) -> p cb cc", cc=C0)
            e2v = s_e2[:].rearrange("p (cb cc) -> p cb cc", cc=C0)
            e1v = s_e1[:].rearrange("p (cb cc) -> p cb cc", cc=C0)
            e3v = s_e3[:].rearrange("p (cb cc) -> p cb cc", cc=C0)
            ctx.run("ve", lambda E, o=ktv, a=kvv, b=e2v: E.tensor_tensor(o, a, b, ALU.mult),
                    reads=(s_k, s_e2), writes=(s_kt,))
            ctx.run("ve", lambda E, o=rtv, a=rvv, b=e1v: E.tensor_tensor(o, a, b, ALU.mult),
                    reads=(s_r, s_e1), writes=(s_rt,))
            ctx.run("ve", lambda E, o=khv, a=kvv, b=e3v: E.tensor_tensor(o, a, b, ALU.mult),
                    reads=(s_k, s_e3), writes=(s_kh,))
            for cb in range(CB):
                ctx.run("ve", lambda E, o=fm(s_rd, cb, lo, C0), a=s_rt[:, 64 * cb:64 * cb + 64],
                        s=s_ecg[:, cb:cb + 1]: E.tensor_scalar(o, a, s, None, ALU.mult),
                        reads=(s_rt, s_ecg), writes=(s_rd,))
            for cb in range(CB):
                ctx.run("ve", lambda E, o=s_tmpA[:, 64 * cb:64 * cb + 64], a=fm(s_k, cb, lo, C0),
                        s=s_cvec[:, 8 * CV_U + cb: 8 * CV_U + cb + 1]:
                        E.tensor_scalar(o, a, s, None, ALU.mult),
                        reads=(s_k, s_cvec), writes=(s_tmpA,))
            tav = s_tmpA[:].rearrange("p (cb cc) -> p cb cc", cc=C0)
            tbv = s_tmpB[:].rearrange("p (cb cc) -> p cb cc", cc=C0)
            ctx.run("ve", lambda E, o=tbv, a=tav, b=rvv: E.tensor_tensor(o, a, b, ALU.mult),
                    reads=(s_tmpA, s_r), writes=(s_tmpB,))



        def emit_heads(c):
            lo = c * C0
            gidx = t * NCH1 + c
            par2 = gidx % 2
            s_kt, s_rt, s_kh = kt2[par2], rt2[par2], kh2[par2]
            s_dtot = dtot2[par2]
            s_tmpB, s_dvec = tmpB2[par2], dvec2[par2]
            Sprev, Snew = s_S[gidx % 2], s_S[(gidx + 1) % 2]
            ckpt(5)
            for cb in range(CB):
                ctx.run("te", lambda E, o=p_y2w[0:16, 0:64], l=s_dsel[:, 16 * cb:16 * cb + 16],
                        r=s_tmpB[:, 64 * cb:64 * cb + 64], st=(cb == 0), sp=(cb == CB - 1):
                        MM(E, o, l, r, start=st, stop=sp),
                        reads=(s_dsel, s_tmpB), writes=(p_y2w,),
                        inc=(cb == CB - 1))
            ctx.run("ve", lambda E, o=s_dvec[:], i=p_y2w[0:16, 0:64]: E.tensor_copy(o, i),
                    reads=(p_y2w,), writes=(s_dvec,))

            ckpt(6)
            for bi in range(2):
                for cj in range(4):
                    cb = 4 * bi + cj
                    ctx.run("te", lambda E, o=p_trw[0:64, 256 * cj:256 * cj + 128],
                            i=fm(s_v, cb, lo, C0), d=s_id[:]:
                            E.transpose(o, i, d), reads=(s_v, s_id), writes=(p_trw,),
                            inc=False)
                    ctx.run("te", lambda E, o=p_trw[0:64, 256 * cj + 128:256 * cj + 256],
                            i=s_kh[:, 64 * cb:64 * cb + 64], d=s_id[:]:
                            E.transpose(o, i, d), reads=(s_kh, s_id), writes=(p_trw,))
                ctx.run("ve", lambda E, o=s_vkw[bi][:], i=p_trw[0:64, :]: E.tensor_copy(o, i),
                        reads=(p_trw,), writes=(s_vkw[bi],))
            for bi in range(2):
                for cj in range(4):
                    cb = 4 * bi + cj
                    base_c = 64 * cb
                    for half in range(2):
                        h = 2 * cb + half
                        hb = 64 * half
                        c0 = 256 * cj + 128 * half
                        ctx.run("te", lambda E, o=p_abw[0:64, c0:c0 + 64],
                                l=s_kt[hb:hb + 64, base_c:base_c + C0],
                                r=s_rt[hb:hb + 64, base_c:base_c + C0]:
                                MM(E, o, l, r, start=True, stop=True),
                                reads=(s_kt, s_rt), writes=(p_abw,), inc=False)
                        ctx.run("te", lambda E, o=p_abw[0:64, c0 + 64:c0 + 128],
                                l=s_esel[:, 64 * h:64 * h + 64], r=s_dvec[:]:
                                MM(E, o, l, r, start=True, stop=True),
                                reads=(s_esel, s_dvec), writes=(p_abw,))
                ctx.run("ve", lambda E, o=s_abw[bi][:], a=p_abw[0:64, :], b=s_m64[:]:
                        E.tensor_tensor(o, a, b, ALU.mult),
                        reads=(p_abw, s_m64), writes=(s_abw[bi],))
            for cb in range(CB):
                bi, cj = cb // 4, cb % 4
                base_c = 64 * cb
                vkX, abX = s_vkw[bi], s_abw[bi]
                for half in range(2):
                    hb = 64 * half
                    c0 = 256 * cj + 128 * half
                    ctx.run("te", lambda E, o=p_yw[hb:hb + 64, base_c:base_c + 64],
                            l=vkX[0:64, 256 * cj + hb:256 * cj + hb + 64],
                            r=abX[0:64, c0:c0 + 64]:
                            E.matmul(o, l, r, start=True, stop=False),
                            reads=(vkX, abX), writes=(p_yw,), inc=False)
                    ctx.run("te", lambda E, o=p_yw[hb:hb + 64, base_c:base_c + 64],
                            l=vkX[0:64, 256 * cj + hb:256 * cj + hb + 64],
                            r=abX[0:64, c0 + 64:c0 + 128]:
                            E.matmul(o, l, r, start=False, stop=True),
                            reads=(vkX, abX), writes=(p_yw,))
                    ctx.run("te", lambda E, o=p_y2w[hb:hb + 64, base_c:base_c + 64],
                            l=Sprev[hb:hb + 64, base_c:base_c + 64],
                            r=s_rt[hb:hb + 64, base_c:base_c + C0]:
                            E.matmul(o, l, r, start=True, stop=True),
                            reads=(Sprev, s_rt), writes=(p_y2w,))
                    ctx.run("te", lambda E, o=p_pr[bi][hb:hb + 64, 64 * cj:64 * cj + 64],
                            l=vkX[0:64, 256 * cj + 128 + hb:256 * cj + 128 + hb + 64],
                            r=vkX[0:64, 256 * cj + hb:256 * cj + hb + 64]:
                            E.matmul(o, l, r, start=True, stop=True),
                            reads=(vkX,), writes=(p_pr[bi],))
            ylv = s_yl[:].rearrange("p (cb tt) -> p cb tt", tt=TT1)[:, :, lo:lo + C0]
            ctx.run("ve", lambda E, o=ylv, i=p_yw[:].rearrange("p (cb c) -> p cb c", c=64):
                    E.tensor_copy(o, i), reads=(p_yw,), writes=(s_yl,))
            ctx.run("ve", lambda E, o=ylv, a=ylv,
                    b=p_y2w[:].rearrange("p (cb c) -> p cb c", c=64):
                    E.tensor_tensor(o, a, b, ALU.add),
                    reads=(s_yl, p_y2w), writes=(s_yl,))
            for cb in range(CB):
                bi, cj = cb // 4, cb % 4
                ctx.run("ve", lambda E, o=Snew[:, 64 * cb:64 * cb + 64],
                        a=Sprev[:, 64 * cb:64 * cb + 64],
                        sc=s_dtot[:, cb:cb + 1],
                        b=p_pr[bi][:, 64 * cj:64 * cj + 64]:
                        E.scalar_tensor_tensor(o, a, sc, b, ALU.mult, ALU.add),
                        reads=(Sprev, s_dtot, p_pr[bi]), writes=(Snew,))



        emit_prep(0)
        emit_prep(1)
        for c in range(NCH1):
            emit_heads(c)
            if c + 2 < NCH1:
                emit_prep(c + 2)

        ckpt(8)
        ylout = io["ylocal"].rearrange("(cb p) tt -> p cb tt", p=128)[:, :, t0:t0 + TT1]
        rdout = io["RDs"].rearrange("(cb p) tt -> p cb tt", p=128)[:, :, t0:t0 + TT1]
        dma_out(ylout, s_yl[:].rearrange("p (cb tt) -> p cb tt", tt=TT1))
        dma_out(rdout, s_rd[:].rearrange("p (cb tt) -> p cb tt", tt=TT1))

    except _StopEmit:
        pass
    nch_total = nt1 * NCH1
    cg_fin = s_cg[nch_total % 2]
    S_fin = s_S[nch_total % 2]
    ctx.run("se", lambda E, o=s_dtS[:], i=cg_fin[:]:
            E.activation(o, i, AF.Exp, scale=-1.0), reads=(cg_fin,), writes=(s_dtS,))
    dma_out(io["summ0"], S_fin[:])
    dma_out(io["summD"], s_dtS[:])
    ctx.wait("po", "out", ctx.barrier_token("out"))
    ctx.full_barrier()
    es1.close()


def build_phase1(p_tokens=P):
    nc = bass.Bass("TRN2", target_bir_lowering=False)
    io = {}
    io["xT"] = nc.dram_tensor("xT", [D, p_tokens + 1], F32, kind="ExternalInput").ap()
    for n in ["Wk_t", "Wv_t", "Wr_t", "Wg_t"]:
        io[n] = nc.dram_tensor(n, [D, D], F32R, kind="ExternalInput").ap()
    io["A_lora"] = nc.dram_tensor("A_lora", [D, 160], F32R, kind="ExternalInput").ap()
    io["B_lora"] = nc.dram_tensor("B_lora", [5, 32, D], F32R, kind="ExternalInput").ap()
    io["td_A"] = nc.dram_tensor("td_A", [D, 64], F32R, kind="ExternalInput").ap()
    io["td_B"] = nc.dram_tensor("td_B", [64, D], F32R, kind="ExternalInput").ap()
    io["cvec"] = nc.dram_tensor("cvec", [128, 8 * NCV1], F32, kind="ExternalInput").ap()
    io["id128"] = nc.dram_tensor("id128", [128, 128], F32, kind="ExternalInput").ap()
    io["mask64s"] = nc.dram_tensor("mask64s", [64, 1024], F32, kind="ExternalInput").ap()
    io["i64"] = nc.dram_tensor("i64", [64, 64], F32, kind="ExternalInput").ap()
    io["dsel"] = nc.dram_tensor("dsel", [128, 128], F32, kind="ExternalInput").ap()
    io["esel"] = nc.dram_tensor("esel", [16, 1024], F32, kind="ExternalInput").ap()
    io["ylocal"] = nc.dram_tensor("ylocal", [D, p_tokens], F32, kind="ExternalOutput").ap()
    io["RDs"] = nc.dram_tensor("RDs", [D, p_tokens], F32, kind="ExternalOutput").ap()
    io["gsil"] = nc.dram_tensor("gsil", [D, p_tokens], F32, kind="ExternalOutput").ap()
    io["summ0"] = nc.dram_tensor("summ0", [128, 512], F32, kind="ExternalOutput").ap()
    io["summD"] = nc.dram_tensor("summD", [128, 8], F32, kind="ExternalOutput").ap()

    with nc.Block() as block:
        ctx = Ctx(nc, block)
        emit_phase1(ctx, io, p_tokens)
        ctx.flush()
    return nc



def pack_col(v):
    return np.ascontiguousarray(np.asarray(v, np.float32).reshape(8, 128).T)


def stage_phase1_shared(inputs):
    shared = {}
    for wn, key in [("Wk_t", "L0_attn_Wk"), ("Wv_t", "L0_attn_Wv"),
                    ("Wr_t", "L0_attn_Wr"), ("Wg_t", "L0_attn_Wg")]:
        shared[wn] = np.ascontiguousarray(np.asarray(inputs[key], np.float32).T)
    shared["A_lora"] = np.ascontiguousarray(np.asarray(inputs["L0_attn_A_lora"], np.float32))
    shared["B_lora"] = np.ascontiguousarray(np.asarray(inputs["L0_attn_B_lora"], np.float32))
    shared["td_A"] = np.ascontiguousarray(np.asarray(inputs["L0_attn_td_A"], np.float32))
    shared["td_B"] = np.ascontiguousarray(np.asarray(inputs["L0_attn_td_B"], np.float32))
    cv = np.zeros((128, 8 * NCV1), np.float32)
    cv[:, 8 * CV_MIUX:8 * CV_MIUX + 8] = pack_col(inputs["L0_attn_miu_x"])
    lam = np.asarray(inputs["L0_attn_lambda_"], np.float32).reshape(5, D)
    for f in range(5):
        cv[:, 8 * (CV_LAM_W + f):8 * (CV_LAM_W + f) + 8] = pack_col(lam[f])
    cv[:, 8 * CV_TDMIU:8 * CV_TDMIU + 8] = pack_col(inputs["L0_attn_td_miu"])
    cv[:, 8 * CV_U:8 * CV_U + 8] = pack_col(np.asarray(inputs["L0_attn_u"], np.float32).reshape(D))
    shared["cvec"] = cv
    shared["id128"] = np.eye(128, dtype=np.float32)
    shared["mask64s"] = np.tile(np.hstack([np.triu(np.ones((64, 64), np.float32), 1),
                                       np.eye(64, dtype=np.float32)]), (1, 8))
    shared["i64"] = np.eye(64, dtype=np.float32)
    dsel = np.zeros((128, 128), np.float32)
    for cb in range(8):
        dsel[0:64, 16 * cb + 2 * cb] = 1.0
        dsel[64:128, 16 * cb + 2 * cb + 1] = 1.0
    shared["dsel"] = dsel
    esel = np.zeros((16, 1024), np.float32)
    for h in range(16):
        esel[h, 64 * h:64 * h + 64] = 1.0
    shared["esel"] = esel
    return shared


def stage_phase1_inputs(inputs, p_tokens=P):
    x = np.asarray(inputs["L0_attn_x"], np.float32)
    shared = stage_phase1_shared(inputs)
    in_maps = []
    pp = p_tokens
    for core in range(NCORE):
        b, q = core // 4, core % 4
        sl = x[b, q * pp:(q + 1) * pp]
        if q == 0:
            prev = np.zeros((1, D), np.float32)
        else:
            prev = x[b, q * pp - 1:q * pp]
        xT = np.ascontiguousarray(np.concatenate([prev, sl], axis=0).T)
        m = dict(shared)
        m["xT"] = xT
        in_maps.append(m)
    return in_maps


def run_phase1(inputs):
    nc = build_phase1()
    in_maps = stage_phase1_inputs(inputs)
    res = run_bass_kernel_spmd(nc, in_maps, core_ids=list(range(NCORE)))
    return res.results


def _fold_masks(core):
    b, q = core // 4, core % 4
    fm = np.zeros((128, 16), np.float32)
    for r in range(8):
        a = 1.0 if (r // 4 == b and r % 4 < q) else 0.0
        fm[:, r] = a
        fm[:, 8 + r] = 1.0 - a
    return fm


def _kernel_unfused(**inputs):
    nc1 = build_phase1()
    maps1 = stage_phase1_inputs(inputs)
    res1 = run_bass_kernel_spmd(nc1, maps1, core_ids=list(range(NCORE))).results

    nc2 = build_phase2()
    sh2 = stage_phase2_shared(inputs)
    maps2 = []
    for core in range(NCORE):
        b, q = core // 4, core % 4
        m = dict(sh2)
        m["xT"] = maps1[core]["xT"]
        m["ylocal"] = np.ascontiguousarray(res1[core]["ylocal"])
        m["RDs"] = np.ascontiguousarray(res1[core]["RDs"])
        m["gsil"] = np.ascontiguousarray(res1[core]["gsil"])
        gS = np.zeros((128, 8 * 512), np.float32)
        gD = np.zeros((128, 64), np.float32)
        for p in range(NCORE):
            gS[:, 512 * p:512 * (p + 1)] = res1[p]["summ0"]
            gD[:, 8 * p:8 * (p + 1)] = res1[p]["summD"]
        m["gS"] = gS
        m["gD"] = gD
        m["fmask"] = _fold_masks(core)
        maps2.append(m)
    res2 = run_bass_kernel_spmd(nc2, maps2, core_ids=list(range(NCORE))).results

    nc3 = build_phase3()
    sh3 = stage_phase3_shared(inputs)
    maps3 = []
    for core in range(NCORE):
        b, q = core // 4, core % 4
        m = dict(sh3)
        m["x2s"] = np.ascontiguousarray(res2[core]["x2s"])
        m["qfs"] = np.ascontiguousarray(res2[core]["qfs"])
        m["kfs"] = np.ascontiguousarray(res2[core]["kfs"])
        m["vtm"] = np.ascontiguousarray(res2[core]["vtm"])
        gS = np.zeros((128, 8 * 512), np.float32)
        gZ = np.zeros((128, 64), np.float32)
        for p in range(NCORE):
            gS[:, 512 * p:512 * (p + 1)] = res2[p]["summ1"]
            gZ[:, 8 * p:8 * (p + 1)] = res2[p]["summZ"]
        m["gS1"] = gS
        m["gZ1"] = gZ
        m["fmask"] = _fold_masks(core)
        maps3.append(m)
    res3 = run_bass_kernel_spmd(nc3, maps3, core_ids=list(range(NCORE))).results

    out = np.empty((B, T, D), np.float32)
    for core in range(NCORE):
        b, q = core // 4, core % 4
        out[b, q * P:(q + 1) * P] = res3[core]["outT"].T
    return out



TT2 = 256
GN_EPS = 1e-5 * H
LN_EPS = 1e-5


def emit_phase2(ctx, io, p_tokens=P, gS_of=None, gD_of=None):
    nc = ctx.nc
    if gS_of is None:
        gS_of = lambda r: io["gS"][:, 512 * r:512 * (r + 1)]
        gD_of = lambda r: io["gD"][:, 8 * r:8 * (r + 1)]
    es2 = contextlib.ExitStack()

    def al(name, shape, dt):
        return es2.enter_context(nc.sbuf_tensor(name, shape, dt))

    def ap_(name, shape, dt):
        return es2.enter_context(nc.psum_tensor(name, shape, dt))

    s_cvec = al("q_cvec", [128, 8 * 4], F32)
    s_dsel = al("q_dsel", [128, 128], F32R)
    s_esel2 = al("q_esel2", [16, 1024], F32R)
    s_ones = al("q_ones", [128, 128], F32R)
    s_id = al("q_id", [128, 128], F32)
    s_S0 = al("q_S0", [128, 512], F32)
    s_gs = al("q_gs", [128, 512], F32)
    s_gd = al("q_gd", [128, 8], F32)
    s_fm = al("q_fm", [128, 16], F32)
    s_eps = al("q_eps", [128, 2], F32)
    s_deff = al("q_deff", [128, 8], F32)
    s_tmps = al("q_tmps", [128, 512], F32)
    s_M = al("q_M", [128, 8192], F32R)

    p_pr = [ap_("q_ppr0", [128, 512], F32), ap_("q_ppr1", [128, 512], F32)]
    p_aux = ap_("q_paux", [128, 512], F32)
    p_st = ap_("q_pst", [16, 512], F32)
    p_b1 = ap_("q_pb1", [128, 512], F32)
    p_b2 = ap_("q_pb2", [128, 512], F32)
    p_tr = ap_("q_ptr", [128, 512], F32)
    p_sloc = ap_("q_psloc", [128, 512], F32)

    def dma_in(dst, src, sem="in"):
        return ctx.run("sy", lambda E, a=dst, b=src: E.dma_start(out=a, in_=b),
                       reads=(src,), writes=(dst,), sem=sem)

    def dma_out(dst, src):
        return ctx.run("po", lambda E, a=dst, b=src: E.dma_start(out=a, in_=b),
                       reads=(src,), writes=(dst,))

    def TT_(eng, o, a, b, op, extra_reads=(), w=None):
        ctx.run(eng, lambda E, o=o, a=a, b=b, op=op: E.tensor_tensor(o, a, b, op),
                reads=(a, b) + tuple(extra_reads), writes=(w if w is not None else o,))

    dma_in(s_cvec[:], io["cvec2"])
    dma_in(s_dsel[:], io["dselr"])
    dma_in(s_esel2[:], io["esel2"])
    dma_in(s_ones[:], io["ones128"])
    dma_in(s_id[:], io["id128"])
    dma_in(s_fm[:], io["fmask"])
    dma_in(s_eps[:], io["epsc"])

    ctx.run("ve", lambda E, a=s_S0[:]: E.memset(a, 0.0), writes=(s_S0,))
    for j in range(NCORE):
        dma_in(s_gs[:], gS_of(j))
        dma_in(s_gd[:], gD_of(j))
        acol = s_fm[:, j:j + 1]
        amcol = s_fm[:, 8 + j:9 + j]
        for cb in range(CB):
            ctx.run("ve", lambda E, o=s_deff[:, cb:cb + 1], i=s_gd[:, cb:cb + 1],
                    sc=acol, b=amcol:
                    E.scalar_tensor_tensor(o, i, sc, b, ALU.mult, ALU.add),
                    reads=(s_gd, s_fm), writes=(s_deff,))
            ctx.run("ve", lambda E, o=s_tmps[:, 64 * cb:64 * cb + 64],
                    i=s_gs[:, 64 * cb:64 * cb + 64], sc=acol:
                    E.tensor_scalar(o, i, sc, None, ALU.mult),
                    reads=(s_gs, s_fm), writes=(s_tmps,))
            ctx.run("ve", lambda E, o=s_S0[:, 64 * cb:64 * cb + 64],
                    a=s_S0[:, 64 * cb:64 * cb + 64], sc=s_deff[:, cb:cb + 1],
                    b=s_tmps[:, 64 * cb:64 * cb + 64]:
                    E.scalar_tensor_tensor(o, a, sc, b, ALU.mult, ALU.add),
                    reads=(s_S0, s_deff, s_tmps), writes=(s_S0,))

    with (
        nc.sbuf_tensor("q_wr", [128, 8192], F32R) as s_wr,
        nc.sbuf_tensor("q_wi", [128, 8192], F32R) as s_wi,
        nc.sbuf_tensor("q_fb", [128, 8192], F32R) as s_fb,
        nc.sbuf_tensor("q_p1t", [128, 8192], F32R) as s_p1t,
        nc.sbuf_tensor("q_p2n", [128, 8192], F32R) as s_p2n,
    ):
        dma_in(s_wr[:].rearrange("p (cb co) -> p cb co", co=1024),
               io["wfr"].rearrange("(cb p) co -> p cb co", p=128))
        dma_in(s_wi[:].rearrange("p (cb co) -> p cb co", co=1024),
               io["wfi"].rearrange("(cb p) co -> p cb co", p=128))

        def mm_pass(wsrc, dst, accum_sign):
            for oc in range(CB):
                for hf in range(2):
                    pp = p_pr[hf]
                    for kc in range(CB):
                        ctx.run("te", lambda E, o=pp[:],
                                l=wsrc[:, 1024 * kc + 128 * oc: 1024 * kc + 128 * oc + 128],
                                r=s_fb[:, 1024 * kc + 512 * hf: 1024 * kc + 512 * hf + 512],
                                st=(kc == 0), sp=(kc == CB - 1):
                                MM(E, o, l, r, start=st, stop=sp),
                                reads=(wsrc, s_fb), writes=(pp,), inc=(kc == CB - 1))
                    dsl = dst[:, 1024 * oc + 512 * hf: 1024 * oc + 512 * hf + 512]
                    if accum_sign == 0:
                        ctx.run("ve", lambda E, o=dsl, i=pp[:]: E.tensor_copy(o, i),
                                reads=(pp,), writes=(dst,))
                    else:
                        sgn = float(accum_sign)
                        ctx.run("ve", lambda E, o=dsl, i=pp[:], a=dsl, sgn=sgn:
                                E.scalar_tensor_tensor(o, i, sgn, a, ALU.mult, ALU.add),
                                reads=(pp, dst), writes=(dst,))

        dma_in(s_fb[:].rearrange("p (cb co) -> p cb co", co=1024),
               io["Fr"].rearrange("(cb p) co -> p cb co", p=128), sem="wl")
        mm_pass(s_wr, s_p1t, 0)
        for oc in range(CB):
            for hf in range(2):
                pp = p_pr[hf]
                for kc in range(CB):
                    ctx.run("te", lambda E, o=pp[:],
                            l=s_wi[:, 1024 * kc + 128 * oc: 1024 * kc + 128 * oc + 128],
                            r=s_fb[:, 1024 * kc + 512 * hf: 1024 * kc + 512 * hf + 512],
                            st=(kc == 0), sp=(kc == CB - 1):
                            MM(E, o, l, r, start=st, stop=sp),
                            reads=(s_wi, s_fb), writes=(pp,), inc=(kc == CB - 1))
                dsl = s_p2n[:, 1024 * oc + 512 * hf: 1024 * oc + 512 * hf + 512]
                ctx.run("ve", lambda E, o=dsl, i=pp[:]:
                        E.tensor_scalar(o, i, -1.0, None, ALU.mult),
                        reads=(pp,), writes=(s_p2n,))
        dma_in(s_fb[:].rearrange("p (cb co) -> p cb co", co=1024),
               io["Fin"].rearrange("(cb p) co -> p cb co", p=128), sem="wl")
        mm_pass(s_wi, s_p1t, 1)
        dma_in(s_fb[:].rearrange("p (cb co) -> p cb co", co=1024),
               io["Fi"].rearrange("(cb p) co -> p cb co", p=128), sem="wl")
        mm_pass(s_wr, s_p2n, -1)
        dma_in(s_fb[:].rearrange("p (cb co) -> p cb co", co=1024),
               io["Grs"].rearrange("(cb p) co -> p cb co", p=128), sem="wl")
        for dc in range(CB):
            for hf in range(2):
                pp = p_pr[hf]
                for oc in range(CB):
                    ctx.run("te", lambda E, o=pp[:],
                            l=s_p1t[:, 1024 * oc + 128 * dc: 1024 * oc + 128 * dc + 128],
                            r=s_fb[:, 1024 * oc + 512 * hf: 1024 * oc + 512 * hf + 512],
                            st=(oc == 0), sp=(oc == CB - 1):
                            MM(E, o, l, r, start=st, stop=sp),
                            reads=(s_p1t, s_fb), writes=(pp,), inc=(oc == CB - 1))
                ctx.run("ve", lambda E, o=s_M[:, 1024 * dc + 512 * hf: 1024 * dc + 512 * hf + 512],
                        i=pp[:]: E.tensor_copy(o, i), reads=(pp,), writes=(s_M,))
        dma_in(s_fb[:].rearrange("p (cb co) -> p cb co", co=1024),
               io["Gis"].rearrange("(cb p) co -> p cb co", p=128), sem="wl")
        for dc in range(CB):
            for hf in range(2):
                pp = p_pr[hf]
                for oc in range(CB):
                    ctx.run("te", lambda E, o=pp[:],
                            l=s_p2n[:, 1024 * oc + 128 * dc: 1024 * oc + 128 * dc + 128],
                            r=s_fb[:, 1024 * oc + 512 * hf: 1024 * oc + 512 * hf + 512],
                            st=(oc == 0), sp=(oc == CB - 1):
                            MM(E, o, l, r, start=st, stop=sp),
                            reads=(s_p2n, s_fb), writes=(pp,), inc=(oc == CB - 1))
                msl = s_M[:, 1024 * dc + 512 * hf: 1024 * dc + 512 * hf + 512]
                ctx.run("ve", lambda E, o=msl, a=msl, b=pp[:]:
                        E.tensor_tensor(o, a, b, ALU.add),
                        reads=(s_M, pp), writes=(s_M,))
        ctx.full_barrier()

    nt = p_tokens // TT2

    with contextlib.ExitStack() as esA:
        def alA(name, shape, dt):
            return esA.enter_context(nc.sbuf_tensor(name, shape, dt))
        s_wo = alA("q_wo", [128, 8192], F32R)
        s_x = alA("q_x", [128, CB * (TT2 + 1)], F32)
        s_yl = alA("q_yl", [128, CB * TT2], F32)
        s_rd = alA("q_rd", [128, CB * TT2], F32)
        s_g = alA("q_g", [128, CB * TT2], F32)
        s_y = alA("q_y", [128, CB * TT2], F32R)
        s_sq = alA("q_sq", [128, CB * TT2], F32R)
        s_yg = alA("q_yg", [128, CB * TT2], F32R)
        s_res = alA("q_res", [128, CB * TT2], F32R)
        s_x1 = alA("q_x1", [128, CB * TT2], F32R)
        s_x2 = alA("q_x2", [128, CB * TT2], F32R)
        s_ln1 = alA("q_ln1", [128, TT2], F32)
        s_stm = alA("q_stm", [16, TT2], F32R)
        s_stq = alA("q_stq", [16, TT2], F32R)
        s_str = alA("q_str", [16, TT2], F32R)
        s_st2 = alA("q_st2", [16, TT2], F32R)

        def fm(t_sb, cb, lo, n, tt=TT2):
            return t_sb[:, cb * tt + lo: cb * tt + lo + n]

        def xcur(cb):
            return s_x[:, cb * (TT2 + 1) + 1: cb * (TT2 + 1) + 1 + TT2]

        def emit_ln(src, dst, eps):
            for cb in range(CB):
                ctx.run("se", lambda E, o=fm(s_sq, cb, 0, TT2), i=fm(src, cb, 0, TT2):
                        E.activation(o, i, AF.Square), reads=(src,), writes=(s_sq,))
            for cb in range(CB):
                ctx.run("te", lambda E, o=p_st[0:1, 0:TT2], l=s_ones[:, 0:1],
                        r=fm(src, cb, 0, TT2), st=(cb == 0), sp=(cb == CB - 1):
                        MM(E, o, l, r, start=st, stop=sp),
                        reads=(s_ones, src), writes=(p_st,), inc=(cb == CB - 1))
            for cb in range(CB):
                ctx.run("te", lambda E, o=p_st[0:1, 256:256 + TT2], l=s_ones[:, 0:1],
                        r=fm(s_sq, cb, 0, TT2), st=(cb == 0), sp=(cb == CB - 1):
                        MM(E, o, l, r, start=st, stop=sp),
                        reads=(s_ones, s_sq), writes=(p_st,), inc=(cb == CB - 1))
            mrow = s_stm[0:1, :]
            qrow = s_stq[0:1, :]
            rrow = s_str[0:1, :]
            ctx.run("ve", lambda E, o=mrow, i=p_st[0:1, 0:TT2]:
                    E.tensor_scalar(o, i, 1.0 / D, None, ALU.mult),
                    reads=(p_st,), writes=(s_stm,))
            ctx.run("ve", lambda E, o=qrow, i=p_st[0:1, 256:256 + TT2]:
                    E.tensor_scalar(o, i, 1.0 / D, None, ALU.mult),
                    reads=(p_st,), writes=(s_stq,))
            ctx.run("ve", lambda E, o=rrow, a=mrow, b=mrow: E.tensor_tensor(o, a, b, ALU.mult),
                    reads=(s_stm,), writes=(s_str,))
            ctx.run("ve", lambda E, o=qrow, a=qrow, b=rrow: E.tensor_tensor(o, a, b, ALU.subtract),
                    reads=(s_stq, s_str), writes=(s_stq,))
            ctx.run("se", lambda E, o=rrow, i=qrow, e=s_eps[0:1, 1:2]:
                    E.activation(o, i, AF.Ln, bias=e),
                    reads=(s_stq, s_eps), writes=(s_str,))
            ctx.run("se", lambda E, o=rrow, i=rrow: E.activation(o, i, AF.Exp, scale=-0.5),
                    reads=(s_str,), writes=(s_str,))
            ctx.run("ve", lambda E, o=mrow, a=mrow, b=rrow: E.tensor_tensor(o, a, b, ALU.mult),
                    reads=(s_stm, s_str), writes=(s_stm,))
            ctx.run("te", lambda E, o=p_b1[:, 0:TT2], l=s_ones[0:1, :], r=rrow:
                    MM(E, o, l, r, start=True, stop=True),
                    reads=(s_ones, s_str), writes=(p_b1,))
            ctx.run("te", lambda E, o=p_b2[:, 0:TT2], l=s_ones[0:1, :], r=mrow:
                    MM(E, o, l, r, start=True, stop=True),
                    reads=(s_ones, s_stm), writes=(p_b2,))
            for cb in range(CB):
                ctx.run("ve", lambda E, o=s_ln1[:], a=fm(src, cb, 0, TT2), b=p_b1[:, 0:TT2]:
                        E.tensor_tensor(o, a, b, ALU.mult),
                        reads=(src, p_b1), writes=(s_ln1,))
                ctx.run("ve", lambda E, o=fm(dst, cb, 0, TT2), a=s_ln1[:], b=p_b2[:, 0:TT2]:
                        E.tensor_tensor(o, a, b, ALU.subtract),
                        reads=(s_ln1, p_b2), writes=(dst,))

        dma_in(s_wo[:].rearrange("p (cb co) -> p cb co", co=1024),
               io["Wo_t"].rearrange("(cb p) co -> p cb co", p=128), sem="wl")
        for t in range(nt):
            t0 = t * TT2
            dma_in(s_x[:].rearrange("p (cb tt) -> p cb tt", tt=TT2 + 1),
                   io["xT"].rearrange("(cb p) tt -> p cb tt", p=128)[:, :, t0:t0 + TT2 + 1])
            dma_in(s_yl[:].rearrange("p (cb tt) -> p cb tt", tt=TT2),
                   io["ylocal"].rearrange("(cb p) tt -> p cb tt", p=128)[:, :, t0:t0 + TT2])
            dma_in(s_rd[:].rearrange("p (cb tt) -> p cb tt", tt=TT2),
                   io["RDs"].rearrange("(cb p) tt -> p cb tt", p=128)[:, :, t0:t0 + TT2])
            dma_in(s_g[:].rearrange("p (cb tt) -> p cb tt", tt=TT2),
                   io["gsil"].rearrange("(cb p) tt -> p cb tt", p=128)[:, :, t0:t0 + TT2])

            for cb in range(CB):
                for half in range(2):
                    h = 2 * cb + half
                    hb = 64 * half
                    ctx.run("te", lambda E, o=p_aux[hb:hb + 64, 0:TT2],
                            l=s_S0[hb:hb + 64, 64 * cb:64 * cb + 64],
                            r=s_rd[hb:hb + 64, cb * TT2: cb * TT2 + TT2]:
                            E.matmul(o, l, r, start=True, stop=True),
                            reads=(s_S0, s_rd), writes=(p_aux,))
                TT_("ve", fm(s_y, cb, 0, TT2), fm(s_yl, cb, 0, TT2), p_aux[:, 0:TT2], ALU.add)

            for cb in range(CB):
                ctx.run("se", lambda E, o=fm(s_sq, cb, 0, TT2), i=fm(s_y, cb, 0, TT2):
                        E.activation(o, i, AF.Square), reads=(s_y,), writes=(s_sq,))
            for cb in range(CB):
                ctx.run("te", lambda E, o=p_st[0:16, 0:TT2], l=s_dsel[:, 16 * cb:16 * cb + 16],
                        r=fm(s_y, cb, 0, TT2), st=(cb == 0), sp=(cb == CB - 1):
                        MM(E, o, l, r, start=st, stop=sp),
                        reads=(s_dsel, s_y), writes=(p_st,), inc=(cb == CB - 1))
            for cb in range(CB):
                ctx.run("te", lambda E, o=p_st[0:16, 256:256 + TT2], l=s_dsel[:, 16 * cb:16 * cb + 16],
                        r=fm(s_sq, cb, 0, TT2), st=(cb == 0), sp=(cb == CB - 1):
                        MM(E, o, l, r, start=st, stop=sp),
                        reads=(s_dsel, s_sq), writes=(p_st,), inc=(cb == CB - 1))
            ctx.run("ve", lambda E, o=s_stm[:], i=p_st[0:16, 0:TT2]:
                    E.tensor_scalar(o, i, 1.0 / HS, None, ALU.mult),
                    reads=(p_st,), writes=(s_stm,))
            ctx.run("ve", lambda E, o=s_stq[:], i=p_st[0:16, 256:256 + TT2]:
                    E.tensor_scalar(o, i, 1.0 / HS, None, ALU.mult),
                    reads=(p_st,), writes=(s_stq,))
            TT_("ve", s_str[:], s_stm[:], s_stm[:], ALU.mult)
            TT_("ve", s_stq[:], s_stq[:], s_str[:], ALU.subtract)
            ctx.run("se", lambda E, o=s_str[:], i=s_stq[:], e=s_eps[0:16, 0:1]:
                    E.activation(o, i, AF.Ln, bias=e), reads=(s_stq, s_eps), writes=(s_str,))
            ctx.run("se", lambda E, o=s_str[:], i=s_str[:]:
                    E.activation(o, i, AF.Exp, scale=-0.5), reads=(s_str,), writes=(s_str,))
            TT_("ve", s_st2[:], s_stm[:], s_str[:], ALU.mult)
            for cb in range(CB):
                ctx.run("te", lambda E, o=p_b1[:, 0:TT2], l=s_esel2[:, 128 * cb:128 * cb + 128],
                        r=s_str[:]: MM(E, o, l, r, start=True, stop=True),
                        reads=(s_esel2, s_str), writes=(p_b1,))
                ctx.run("te", lambda E, o=p_b2[:, 0:TT2], l=s_esel2[:, 128 * cb:128 * cb + 128],
                        r=s_st2[:]: MM(E, o, l, r, start=True, stop=True),
                        reads=(s_esel2, s_st2), writes=(p_b2,))
                ctx.run("ve", lambda E, o=s_ln1[:], a=fm(s_y, cb, 0, TT2), b=p_b1[:, 0:TT2]:
                        E.tensor_tensor(o, a, b, ALU.mult), reads=(s_y, p_b1), writes=(s_ln1,))
                ctx.run("ve", lambda E, o=s_ln1[:], a=s_ln1[:], b=p_b2[:, 0:TT2]:
                        E.tensor_tensor(o, a, b, ALU.subtract), reads=(s_ln1, p_b2), writes=(s_ln1,))
                ctx.run("ve", lambda E, o=s_ln1[:], i=s_ln1[:],
                        g1=s_cvec[:, 0 + cb:0 + cb + 1], b1=s_cvec[:, 8 + cb:8 + cb + 1]:
                        E.tensor_scalar(o, i, g1, b1, ALU.mult, ALU.add),
                        reads=(s_ln1, s_cvec), writes=(s_ln1,))
                ctx.run("ve", lambda E, o=fm(s_yg, cb, 0, TT2), a=s_ln1[:], b=fm(s_g, cb, 0, TT2):
                        E.tensor_tensor(o, a, b, ALU.mult), reads=(s_ln1, s_g), writes=(s_yg,))

            for ob in range(CB):
                pp = p_pr[ob % 2]
                for cb in range(CB):
                    ctx.run("te", lambda E, o=pp[:, 0:TT2],
                            l=s_wo[:, 1024 * cb + 128 * ob: 1024 * cb + 128 * ob + 128],
                            r=fm(s_yg, cb, 0, TT2), st=(cb == 0), sp=(cb == CB - 1):
                            MM(E, o, l, r, start=st, stop=sp),
                            reads=(s_wo, s_yg), writes=(pp,), inc=(cb == CB - 1))
                ctx.run("ve", lambda E, o=fm(s_res, ob, 0, TT2), a=xcur(ob), b=pp[:, 0:TT2]:
                        E.tensor_tensor(o, a, b, ALU.add), reads=(s_x, pp), writes=(s_res,))
            emit_ln(s_res, s_x1, LN_EPS)

            for ob in range(CB):
                pp = p_pr[ob % 2]
                for cb in range(CB):
                    ctx.run("te", lambda E, o=pp[:, 0:TT2],
                            l=s_M[:, 1024 * cb + 128 * ob: 1024 * cb + 128 * ob + 128],
                            r=fm(s_x1, cb, 0, TT2), st=(cb == 0), sp=(cb == CB - 1):
                            MM(E, o, l, r, start=st, stop=sp),
                            reads=(s_M, s_x1), writes=(pp,), inc=(cb == CB - 1))
                ctx.run("ve", lambda E, o=s_ln1[:], i=pp[:, 0:TT2],
                        bi=s_cvec[:, 16 + ob:16 + ob + 1]:
                        E.tensor_scalar(o, i, bi, None, ALU.add), reads=(pp, s_cvec), writes=(s_ln1,))
                ctx.run("ve", lambda E, o=fm(s_res, ob, 0, TT2), a=fm(s_x1, ob, 0, TT2), b=s_ln1[:]:
                        E.tensor_tensor(o, a, b, ALU.add), reads=(s_x1, s_ln1), writes=(s_res,))
            emit_ln(s_res, s_x2, LN_EPS)
            dma_out(io["x2s"].rearrange("(cb p) tt -> p cb tt", p=128)[:, :, t0:t0 + TT2],
                    s_x2[:].rearrange("p (cb tt) -> p cb tt", tt=TT2))
        ctx.full_barrier()

    with contextlib.ExitStack() as esB:
        def alB(name, shape, dt):
            return esB.enter_context(nc.sbuf_tensor(name, shape, dt))
        s_wg = alB("q_wg", [128, 8192], F32R)
        s_wq = alB("q_wq", [128, 8192], F32R)
        s_x2 = alB("q_x2b", [128, CB * TT2], F32R)
        s_gt = alB("q_gt", [128, CB * TT2], F32)
        s_qf = alB("q_qf", [128, CB * TT2], F32)
        s_tA = alB("q_tA", [128, TT2], F32)
        s_tB = alB("q_tB", [128, TT2], F32)

        def fm(t_sb, cb, lo, n, tt=TT2):
            return t_sb[:, cb * tt + lo: cb * tt + lo + n]

        dma_in(s_wg[:].rearrange("p (cb co) -> p cb co", co=1024),
               io["Wg1_t"].rearrange("(cb p) co -> p cb co", p=128), sem="wl")
        dma_in(s_wq[:].rearrange("p (cb co) -> p cb co", co=1024),
               io["Wq_t"].rearrange("(cb p) co -> p cb co", p=128), sem="wl")
        for t in range(nt):
            t0 = t * TT2
            dma_in(s_x2[:].rearrange("p (cb tt) -> p cb tt", tt=TT2),
                   io["x2s"].rearrange("(cb p) tt -> p cb tt", p=128)[:, :, t0:t0 + TT2])
            for ob in range(CB):
                pp = p_pr[ob % 2]
                for cb in range(CB):
                    ctx.run("te", lambda E, o=pp[:, 0:TT2],
                            l=s_wg[:, 1024 * cb + 128 * ob: 1024 * cb + 128 * ob + 128],
                            r=fm(s_x2, cb, 0, TT2), st=(cb == 0), sp=(cb == CB - 1):
                            MM(E, o, l, r, start=st, stop=sp),
                            reads=(s_wg, s_x2), writes=(pp,), inc=(cb == CB - 1))
                ctx.run("se", lambda E, o=fm(s_gt, ob, 0, TT2), i=pp[:, 0:TT2],
                        bi=s_cvec[:, 24 + ob:24 + ob + 1]:
                        E.activation(o, i, AF.Sigmoid, bias=bi), reads=(pp, s_cvec), writes=(s_gt,))
            for ob in range(CB):
                pp = p_pr[ob % 2]
                for cb in range(CB):
                    ctx.run("te", lambda E, o=pp[:, 0:TT2],
                            l=s_wq[:, 1024 * cb + 128 * ob: 1024 * cb + 128 * ob + 128],
                            r=fm(s_x2, cb, 0, TT2), st=(cb == 0), sp=(cb == CB - 1):
                            MM(E, o, l, r, start=st, stop=sp),
                            reads=(s_wq, s_x2), writes=(pp,), inc=(cb == CB - 1))
                ctx.run("se", lambda E, o=s_tA[:], i=pp[:, 0:TT2]: E.activation(o, i, AF.Abs),
                        reads=(pp,), writes=(s_tA,))
                ctx.run("se", lambda E, o=s_tB[:], i=s_tA[:]: E.activation(o, i, AF.Exp, scale=-1.0),
                        reads=(s_tA,), writes=(s_tB,))
                ctx.run("se", lambda E, o=s_tA[:], i=s_tB[:]: E.activation(o, i, AF.Ln, bias=1.0),
                        reads=(s_tB,), writes=(s_tA,))
                ctx.run("se", lambda E, o=s_tB[:], i=pp[:, 0:TT2]: E.activation(o, i, AF.Relu),
                        reads=(pp,), writes=(s_tB,))
                ctx.run("ve", lambda E, o=s_tA[:], a=s_tA[:], b=s_tB[:]:
                        E.tensor_tensor(o, a, b, ALU.add), reads=(s_tA, s_tB), writes=(s_tA,))
                ctx.run("ve", lambda E, o=fm(s_qf, ob, 0, TT2), a=fm(s_gt, ob, 0, TT2),
                        b=s_tA[:]: E.scalar_tensor_tensor(o, a, 1e-6, b, ALU.add, ALU.mult),
                        reads=(s_gt, s_tA), writes=(s_qf,))
            dma_out(io["qfs"].rearrange("(cb p) tt -> p cb tt", p=128)[:, :, t0:t0 + TT2],
                    s_qf[:].rearrange("p (cb tt) -> p cb tt", tt=TT2))
        ctx.full_barrier()

    with contextlib.ExitStack() as esD:
        def alD(name, shape, dt):
            return esD.enter_context(nc.sbuf_tensor(name, shape, dt))
        s_wk = alD("q_wk", [128, 8192], F32R)
        s_wv = alD("q_wv", [128, 8192], F32R)
        s_x2 = alD("q_x2d", [128, CB * TT2], F32R)
        s_kf = alD("q_kf", [128, CB * TT2], F32)
        s_vtm = alD("q_vtm", [128, 2 * 1024], F32)
        s_kftm = alD("q_kftm", [128, 128], F32)
        s_tA = alD("q_tAd", [128, TT2], F32)
        s_tB = alD("q_tBd", [128, TT2], F32)
        s_zl = [alD("q_zl0", [128, 8], F32), alD("q_zl1", [128, 8], F32)]
        s_slp = [alD("q_slp0", [128, 512], F32), alD("q_slp1", [128, 512], F32)]

        def fm(t_sb, cb, lo, n, tt=TT2):
            return t_sb[:, cb * tt + lo: cb * tt + lo + n]

        dma_in(s_wk[:].rearrange("p (cb co) -> p cb co", co=1024),
               io["Wk1_t"].rearrange("(cb p) co -> p cb co", p=128), sem="wl")
        dma_in(s_wv[:].rearrange("p (cb co) -> p cb co", co=1024),
               io["Wv1_t"].rearrange("(cb p) co -> p cb co", p=128), sem="wl")
        for t in range(nt):
            t0 = t * TT2
            dma_in(s_x2[:].rearrange("p (cb tt) -> p cb tt", tt=TT2),
                   io["x2s"].rearrange("(cb p) tt -> p cb tt", p=128)[:, :, t0:t0 + TT2])
            for ob in range(CB):
                pp = p_pr[ob % 2]
                for cb in range(CB):
                    ctx.run("te", lambda E, o=pp[:, 0:TT2],
                            l=s_wk[:, 1024 * cb + 128 * ob: 1024 * cb + 128 * ob + 128],
                            r=fm(s_x2, cb, 0, TT2), st=(cb == 0), sp=(cb == CB - 1):
                            MM(E, o, l, r, start=st, stop=sp),
                            reads=(s_wk, s_x2), writes=(pp,), inc=(cb == CB - 1))
                ctx.run("se", lambda E, o=s_tA[:], i=pp[:, 0:TT2]: E.activation(o, i, AF.Abs),
                        reads=(pp,), writes=(s_tA,))
                ctx.run("se", lambda E, o=s_tB[:], i=s_tA[:]: E.activation(o, i, AF.Exp, scale=-1.0),
                        reads=(s_tA,), writes=(s_tB,))
                ctx.run("se", lambda E, o=s_tA[:], i=s_tB[:]: E.activation(o, i, AF.Ln, bias=1.0),
                        reads=(s_tB,), writes=(s_tA,))
                ctx.run("se", lambda E, o=s_tB[:], i=pp[:, 0:TT2]: E.activation(o, i, AF.Relu),
                        reads=(pp,), writes=(s_tB,))
                ctx.run("ve", lambda E, o=fm(s_kf, ob, 0, TT2), a=s_tA[:], b=s_tB[:]:
                        E.tensor_tensor(o, a, b, ALU.add), reads=(s_tA, s_tB), writes=(s_kf,))
            dma_out(io["kfs"].rearrange("(cb p) tt -> p cb tt", p=128)[:, :, t0:t0 + TT2],
                    s_kf[:].rearrange("p (cb tt) -> p cb tt", tt=TT2))
            for tc in range(2):
                for hf in range(2):
                    pp = p_pr[hf]
                    for cb in range(CB):
                        ctx.run("te", lambda E, o=pp[:],
                                l=s_x2[:, cb * TT2 + 128 * tc: cb * TT2 + 128 * tc + 128],
                                r=s_wv[:].rearrange("p (cb co) -> p cb co", co=1024)[:, cb, 512 * hf:512 * hf + 512],
                                st=(cb == 0), sp=(cb == CB - 1):
                                MM(E, o, l, r, start=st, stop=sp),
                                reads=(s_x2, s_wv), writes=(pp,), inc=(cb == CB - 1))
                    ctx.run("ve", lambda E, o=s_vtm[:, 1024 * tc + 512 * hf: 1024 * tc + 512 * hf + 512],
                            i=pp[:]: E.tensor_copy(o, i), reads=(pp,), writes=(s_vtm,))
            dma_out(io["vtm"].rearrange("(nt p) c -> p nt c", p=128)[:, 2 * t:2 * t + 2, :],
                    s_vtm[:].rearrange("p (nt c) -> p nt c", c=1024))
            first_tile = (t == 0)
            if first_tile:
                ctx.run("ve", lambda E, a=s_slp[0][:]: E.memset(a, 0.0), writes=(s_slp[0],))
            for tc in range(2):
                step = 2 * t + tc
                SA, SB = s_slp[step % 2], s_slp[(step + 1) % 2]
                for cb in range(CB):
                    ctx.run("te", lambda E, o=p_tr[:, 0:128],
                            i=s_kf[:, cb * TT2 + 128 * tc: cb * TT2 + 128 * tc + 128],
                            d=s_id[:]: E.transpose(o, i, d),
                            reads=(s_kf, s_id), writes=(p_tr,))
                    ctx.run("ve", lambda E, o=s_kftm[:], i=p_tr[:, 0:128]: E.tensor_copy(o, i),
                            reads=(p_tr,), writes=(s_kftm,))
                    for half in range(2):
                        h = 2 * cb + half
                        hb = 64 * half
                        ctx.run("te", lambda E, o=p_sloc[hb:hb + 64, 64 * cb:64 * cb + 64],
                                l=s_kftm[:, hb:hb + 64],
                                r=s_vtm[:, 1024 * tc + 64 * h: 1024 * tc + 64 * h + 64]:
                                E.matmul(o, l, r, start=True, stop=True),
                                reads=(s_kftm, s_vtm), writes=(p_sloc,))
                        ctx.run("ve", lambda E, o=SB[hb:hb + 64, 64 * cb:64 * cb + 64],
                                a=SA[hb:hb + 64, 64 * cb:64 * cb + 64],
                                b=p_sloc[hb:hb + 64, 64 * cb:64 * cb + 64]:
                                E.tensor_tensor(o, a, b, ALU.add),
                                reads=(SA, p_sloc), writes=(SB,))
            zA = s_zl[t % 2]
            zB = s_zl[(t + 1) % 2]
            if first_tile:
                ctx.run("ve", lambda E, a=zA[:]: E.memset(a, 0.0), writes=(zA,))
            for cb in range(CB):
                ctx.run("ve", lambda E, o=s_tA[:, 0:1], i=fm(s_kf, cb, 0, TT2):
                        E.reduce_sum(o, i, axis=mybir.AxisListType.X),
                        reads=(s_kf,), writes=(s_tA,))
                ctx.run("ve", lambda E, o=zB[:, cb:cb + 1], a=zA[:, cb:cb + 1], b=s_tA[:, 0:1]:
                        E.tensor_tensor(o, a, b, ALU.add), reads=(zA, s_tA), writes=(zB,))

        s_sl = s_slp[(2 * nt) % 2]
        dma_out(io["summ1"], s_sl[:])
        zfin = s_zl[nt % 2]
        dma_out(io["summZ"], zfin[:])
        ctx.wait("po", "out", ctx.barrier_token("out"))
        ctx.full_barrier()
    ctx.full_barrier()
    es2.close()


def build_phase2(p_tokens=P):
    nc = bass.Bass("TRN2", target_bir_lowering=False)
    io = {}
    io["xT"] = nc.dram_tensor("xT", [D, p_tokens + 1], F32, kind="ExternalInput").ap()
    io["ylocal"] = nc.dram_tensor("ylocal", [D, p_tokens], F32, kind="ExternalInput").ap()
    io["RDs"] = nc.dram_tensor("RDs", [D, p_tokens], F32, kind="ExternalInput").ap()
    io["gsil"] = nc.dram_tensor("gsil", [D, p_tokens], F32, kind="ExternalInput").ap()
    io["gS"] = nc.dram_tensor("gS", [128, 8 * 512], F32, kind="ExternalInput").ap()
    io["gD"] = nc.dram_tensor("gD", [128, 64], F32, kind="ExternalInput").ap()
    io["fmask"] = nc.dram_tensor("fmask", [128, 16], F32, kind="ExternalInput").ap()
    for n in ["Wo_t", "Wq_t", "Wk1_t", "Wv1_t", "Wg1_t",
              "wfr", "wfi", "Fr", "Fi", "Fin", "Grs", "Gis"]:
        io[n] = nc.dram_tensor(n, [D, D], F32R, kind="ExternalInput").ap()
    io["cvec2"] = nc.dram_tensor("cvec2", [128, 32], F32, kind="ExternalInput").ap()
    io["dselr"] = nc.dram_tensor("dselr", [128, 128], F32R, kind="ExternalInput").ap()
    io["esel2"] = nc.dram_tensor("esel2", [16, 1024], F32R, kind="ExternalInput").ap()
    io["ones128"] = nc.dram_tensor("ones128", [128, 128], F32R, kind="ExternalInput").ap()
    io["id128"] = nc.dram_tensor("id128", [128, 128], F32, kind="ExternalInput").ap()
    io["epsc"] = nc.dram_tensor("epsc", [128, 2], F32, kind="ExternalInput").ap()
    io["x2s"] = nc.dram_tensor("x2s", [D, p_tokens], F32R, kind="ExternalOutput").ap()
    io["qfs"] = nc.dram_tensor("qfs", [D, p_tokens], F32, kind="ExternalOutput").ap()
    io["kfs"] = nc.dram_tensor("kfs", [D, p_tokens], F32, kind="ExternalOutput").ap()
    io["vtm"] = nc.dram_tensor("vtm", [p_tokens, D], F32, kind="ExternalOutput").ap()
    io["summ1"] = nc.dram_tensor("summ1", [128, 512], F32, kind="ExternalOutput").ap()
    io["summZ"] = nc.dram_tensor("summZ", [128, 8], F32, kind="ExternalOutput").ap()
    with nc.Block() as block:
        ctx = Ctx(nc, block)
        emit_phase2(ctx, io, p_tokens)
        ctx.flush()
    return nc


def stage_phase2_shared(inputs):
    sh = {}
    for wn, key in [("Wo_t", "L0_attn_Wo"), ("Wq_t", "L1_attn_Wq"),
                    ("Wk1_t", "L1_attn_Wk"), ("Wv1_t", "L1_attn_Wv"),
                    ("Wg1_t", "L1_attn_Wg")]:
        sh[wn] = np.ascontiguousarray(np.asarray(inputs[key], np.float32).T)
    sh["wfr"] = np.ascontiguousarray(np.asarray(inputs["L0_ffn_w_freq_real"], np.float32))
    sh["wfi"] = np.ascontiguousarray(np.asarray(inputs["L0_ffn_w_freq_imag"], np.float32))
    d = np.arange(D)
    ang = 2 * np.pi * np.outer(d, d) / D
    sh["Fr"] = np.cos(ang).astype(np.float32)
    sh["Fi"] = (-np.sin(ang)).astype(np.float32)
    sh["Fin"] = (np.sin(ang)).astype(np.float32)
    sh["Grs"] = (np.cos(ang) / D).astype(np.float32)
    sh["Gis"] = (np.sin(ang) / D).astype(np.float32)
    cv = np.zeros((128, 32), np.float32)
    cv[:, 0:8] = pack_col(inputs["L0_attn_gn_gamma"])
    cv[:, 8:16] = pack_col(inputs["L0_attn_gn_beta"])
    cv[:, 16:24] = pack_col(inputs["L0_ffn_bias"])
    cv[:, 24:32] = pack_col(inputs["L1_attn_bg"])
    sh["cvec2"] = cv
    dsel = np.zeros((128, 128), np.float32)
    for cb in range(8):
        dsel[0:64, 16 * cb + 2 * cb] = 1.0
        dsel[64:128, 16 * cb + 2 * cb + 1] = 1.0
    sh["dsel"] = dsel
    sh["dselr"] = dsel
    esel2 = np.zeros((16, 1024), np.float32)
    for cb in range(8):
        esel2[2 * cb, 128 * cb:128 * cb + 64] = 1.0
        esel2[2 * cb + 1, 128 * cb + 64:128 * cb + 128] = 1.0
    sh["esel2"] = esel2
    sh["ones128"] = np.ones((128, 128), np.float32)
    sh["id128"] = np.eye(128, dtype=np.float32)
    epsc = np.zeros((128, 2), np.float32)
    epsc[:, 0] = GN_EPS
    epsc[:, 1] = LN_EPS
    sh["epsc"] = epsc
    return sh



TT3 = 256
C1 = 128
TTM = 512


def emit_phase3(ctx, io, p_tokens=P, gS_of=None, gZ_of=None):
    nc = ctx.nc
    if gS_of is None:
        gS_of = lambda r: io["gS1"][:, 512 * r:512 * (r + 1)]
        gZ_of = lambda r: io["gZ1"][:, 8 * r:8 * (r + 1)]
    es3h = contextlib.ExitStack()

    def al(name, shape, dt):
        return es3h.enter_context(nc.sbuf_tensor(name, shape, dt))

    def ap_(name, shape, dt):
        return es3h.enter_context(nc.psum_tensor(name, shape, dt))

    s_dsel = al("r_dsel", [128, 128], F32)
    s_esel2 = al("r_esel2", [16, 1024], F32R)
    s_eselE = al("r_eselE", [16, 2048], F32R)
    s_ones = al("r_ones", [128, 128], F32R)
    s_id = al("r_id", [128, 128], F32)
    s_mki = al("r_mki", [128, 128], F32)
    s_eps = al("r_eps", [128, 2], F32)
    s_fm = al("r_fm", [128, 16], F32)
    s_gw = al("r_gw", [128, 128], F32)
    s_idg = al("r_idg", [128, 128], F32)
    s_gbb = al("r_gbb", [128, 16], F32)
    s_becs = al("r_becs", [128, 64 + 64 + 128 + 24], F32)
    s_b2row = al("r_b2row", [1, 1024], F32R)

    s_S0 = al("r_S0", [128, 512], F32)
    s_z0 = al("r_z0", [128, 8], F32)
    s_gs = al("r_gs", [128, 512], F32)
    s_gz = al("r_gz", [128, 8], F32)
    s_tmps = al("r_tmps", [128, 512], F32)

    def dma_in(dst, src, sem="in"):
        return ctx.run("sy", lambda E, a=dst, b=src: E.dma_start(out=a, in_=b),
                       reads=(src,), writes=(dst,), sem=sem)

    def dma_out(dst, src):
        return ctx.run("po", lambda E, a=dst, b=src: E.dma_start(out=a, in_=b),
                       reads=(src,), writes=(dst,))

    dma_in(s_dsel[:], io["dsel"])
    dma_in(s_esel2[:], io["esel2"])
    dma_in(s_eselE[:], io["eselE"])
    dma_in(s_ones[:], io["ones128"])
    dma_in(s_id[:], io["id128"])
    dma_in(s_mki[:], io["mask128i"])
    dma_in(s_eps[:], io["epsc"])
    dma_in(s_fm[:], io["fmask"])
    dma_in(s_gw[:].rearrange("p (cb e) -> p cb e", e=16),
           io["gwT"].rearrange("(cb p) e -> p cb e", p=128))
    dma_in(s_idg[:], io["id128f"])
    dma_in(s_becs[:, 0:64], io["be1c"])
    dma_in(s_becs[:, 64:128], io["be3c"])
    dma_in(s_becs[:, 128:256], io["be2c"])
    dma_in(s_becs[:, 256:280], io["bsc"])

    def be1col(e, ic):
        return s_becs[:, 4 * e + ic: 4 * e + ic + 1]

    def be3col(e, ic):
        return s_becs[:, 64 + 4 * e + ic: 64 + 4 * e + ic + 1]

    def bscol(j, cb):
        return s_becs[:, 256 + 8 * j + cb: 256 + 8 * j + cb + 1]

    ctx.run("ve", lambda E, a=s_S0[:]: E.memset(a, 0.0), writes=(s_S0,))
    ctx.run("ve", lambda E, a=s_z0[:]: E.memset(a, 0.0), writes=(s_z0,))
    for j in range(NCORE):
        dma_in(s_gs[:], gS_of(j))
        dma_in(s_gz[:], gZ_of(j))
        acol = s_fm[:, j:j + 1]
        ctx.run("ve", lambda E, o=s_S0[:], a=s_gs[:], sc=acol, b=s_S0[:]:
                E.scalar_tensor_tensor(o, a, sc, b, ALU.mult, ALU.add),
                reads=(s_gs, s_fm, s_S0), writes=(s_S0,))
        ctx.run("ve", lambda E, o=s_z0[:], a=s_gz[:], sc=acol, b=s_z0[:]:
                E.scalar_tensor_tensor(o, a, sc, b, ALU.mult, ALU.add),
                reads=(s_gz, s_fm, s_z0), writes=(s_z0,))

    p_misc = ap_("r_pmisc", [128, 512], F32)
    dma_in(s_gbb[:], io["gbb"])

    es3a = contextlib.ExitStack()

    def al3(name, shape, dt):
        return es3a.enter_context(nc.sbuf_tensor(name, shape, dt))
    s_wbuf = al3("r_wbuf2", [128, 8192], F32R)
    s_qf = al3("r_qf", [128, CB * TT3], F32)
    s_kf = al3("r_kf", [128, CB * TT3], F32)
    s_vtm = al3("r_vtm", [128, 2 * 1024], F32)
    s_x2 = al3("r_x2", [128, CB * TT3], F32R)
    s_att = al3("r_att", [128, CB * TT3], F32R)
    s_res = al3("r_res", [128, CB * TT3], F32R)
    s_x3 = al3("r_x3", [128, CB * TT3], F32)
    s_x3r = al3("r_x3r", [128, CB * TT3], F32R)
    s_sq = al3("r_sq", [128, CB * TT3], F32R)
    s_kcs = al3("r_kcs", [128, 1024], F32)
    s_kct = al3("r_kct", [128, 1024], F32)
    s_qk = al3("r_qk", [128, 1024], F32)
    s_kftm = al3("r_kftm", [128, 128], F32)
    s_atm1 = al3("r_atm1", [128, 128], F32)
    s_nA = al3("r_nA", [128, 128], F32)
    s_dr = al3("r_dr", [16, 128], F32)
    s_dr2 = al3("r_dr2", [16, 128], F32R)
    s_Sr = [al3("r_Sr0", [128, 512], F32), al3("r_Sr1", [128, 512], F32)]
    s_zr = [al3("r_zr0", [128, 8], F32), al3("r_zr1", [128, 8], F32)]
    s_ln1 = al3("r_ln1", [128, TT3], F32)
    s_stm = al3("r_stm", [16, TT3], F32R)
    s_stq = al3("r_stq", [16, TT3], F32R)
    s_str = al3("r_str", [16, TT3], F32R)
    s_z64b = al3("r_z64b", [128, 128], F32)

    p_at = ap_("r_pat", [128, 512], F32)
    p_n1 = ap_("r_pn1", [128, 512], F32)
    p_n2 = ap_("r_pn2", [128, 512], F32)
    p_b1 = ap_("r_pb1", [128, 512], F32)
    p_tr = ap_("r_ptr", [128, 512], F32)
    p_st = ap_("r_pst", [16, 512], F32)
    p_pr = [p_misc, ap_("r_ppr1", [128, 512], F32)]

    ctx.run("ve", lambda E, a=s_z64b[:]: E.memset(a, 0.0), writes=(s_z64b,))
    ctx.run("ve", lambda E, o=s_Sr[0][:], i=s_S0[:]: E.tensor_copy(o, i),
            reads=(s_S0,), writes=(s_Sr[0],))
    ctx.run("ve", lambda E, o=s_zr[0][:], i=s_z0[:]: E.tensor_copy(o, i),
            reads=(s_z0,), writes=(s_zr[0],))

    def fm3(t_sb, cb, lo, n, tt=TT3):
        return t_sb[:, cb * tt + lo: cb * tt + lo + n]

    def emit_ln3(src, dst):
        for cb in range(CB):
            ctx.run("se", lambda E, o=fm3(s_sq, cb, 0, TT3), i=fm3(src, cb, 0, TT3):
                    E.activation(o, i, AF.Square), reads=(src,), writes=(s_sq,))
        for cb in range(CB):
            ctx.run("te", lambda E, o=p_st[0:1, 0:TT3], l=s_ones[:, 0:1],
                    r=fm3(src, cb, 0, TT3), st=(cb == 0), sp=(cb == CB - 1):
                    MM(E, o, l, r, start=st, stop=sp),
                    reads=(s_ones, src), writes=(p_st,), inc=(cb == CB - 1))
        for cb in range(CB):
            ctx.run("te", lambda E, o=p_st[0:1, 256:256 + TT3], l=s_ones[:, 0:1],
                    r=fm3(s_sq, cb, 0, TT3), st=(cb == 0), sp=(cb == CB - 1):
                    MM(E, o, l, r, start=st, stop=sp),
                    reads=(s_ones, s_sq), writes=(p_st,), inc=(cb == CB - 1))
        mrow, qrow, rrow = s_stm[0:1, :], s_stq[0:1, :], s_str[0:1, :]
        ctx.run("ve", lambda E, o=mrow, i=p_st[0:1, 0:TT3]:
                E.tensor_scalar(o, i, 1.0 / D, None, ALU.mult), reads=(p_st,), writes=(s_stm,))
        ctx.run("ve", lambda E, o=qrow, i=p_st[0:1, 256:256 + TT3]:
                E.tensor_scalar(o, i, 1.0 / D, None, ALU.mult), reads=(p_st,), writes=(s_stq,))
        ctx.run("ve", lambda E, o=rrow, a=mrow, b=mrow: E.tensor_tensor(o, a, b, ALU.mult),
                reads=(s_stm,), writes=(s_str,))
        ctx.run("ve", lambda E, o=qrow, a=qrow, b=rrow: E.tensor_tensor(o, a, b, ALU.subtract),
                reads=(s_stq, s_str), writes=(s_stq,))
        ctx.run("se", lambda E, o=rrow, i=qrow, e=s_eps[0:1, 1:2]:
                E.activation(o, i, AF.Ln, bias=e), reads=(s_stq, s_eps), writes=(s_str,))
        ctx.run("se", lambda E, o=rrow, i=rrow: E.activation(o, i, AF.Exp, scale=-0.5),
                reads=(s_str,), writes=(s_str,))
        ctx.run("ve", lambda E, o=mrow, a=mrow, b=rrow: E.tensor_tensor(o, a, b, ALU.mult),
                reads=(s_stm, s_str), writes=(s_stm,))
        ctx.run("te", lambda E, o=p_b1[:, 0:TT3], l=s_ones[0:1, :], r=rrow:
                MM(E, o, l, r, start=True, stop=True), reads=(s_ones, s_str), writes=(p_b1,))
        ctx.run("te", lambda E, o=p_b1[:, 256:256 + TT3], l=s_ones[0:1, :], r=mrow:
                MM(E, o, l, r, start=True, stop=True), reads=(s_ones, s_stm), writes=(p_b1,))
        for cb in range(CB):
            ctx.run("ve", lambda E, o=s_ln1[:], a=fm3(src, cb, 0, TT3), b=p_b1[:, 0:TT3]:
                    E.tensor_tensor(o, a, b, ALU.mult), reads=(src, p_b1), writes=(s_ln1,))
            ctx.run("ve", lambda E, o=fm3(dst, cb, 0, TT3), a=s_ln1[:], b=p_b1[:, 256:256 + TT3]:
                    E.tensor_tensor(o, a, b, ALU.subtract), reads=(s_ln1, p_b1), writes=(dst,))

    dma_in(s_wbuf[:].rearrange("p (cb co) -> p cb co", co=1024),
           io["Wo1_t"].rearrange("(cb p) co -> p cb co", p=128), sem="wl")
    nt3 = p_tokens // TT3
    for t in range(nt3):
        t0 = t * TT3
        dma_in(s_qf[:].rearrange("p (cb tt) -> p cb tt", tt=TT3),
               io["qfs"].rearrange("(cb p) tt -> p cb tt", p=128)[:, :, t0:t0 + TT3])
        dma_in(s_kf[:].rearrange("p (cb tt) -> p cb tt", tt=TT3),
               io["kfs"].rearrange("(cb p) tt -> p cb tt", p=128)[:, :, t0:t0 + TT3])
        dma_in(s_x2[:].rearrange("p (cb tt) -> p cb tt", tt=TT3),
               io["x2s"].rearrange("(cb p) tt -> p cb tt", p=128)[:, :, t0:t0 + TT3])
        dma_in(s_vtm[:].rearrange("p (nt c) -> p nt c", c=1024),
               io["vtm"].rearrange("(nt p) c -> p nt c", p=128)[:, 2 * t:2 * t + 2, :])

        for kc in range(2):
            step = 2 * t + kc
            zA, zB = s_zr[step % 2], s_zr[(step + 1) % 2]
            SA, SB = s_Sr[step % 2], s_Sr[(step + 1) % 2]
            for cb in range(CB):
                ctx.run("ve", lambda E, o=s_kcs[:, 128 * cb:128 * cb + 128],
                        a=fm3(s_kf, cb, 128 * kc, C1), z=s_z64b[:]:
                        E.tensor_tensor_scan(o, a, z, 0.0, ALU.add, ALU.add),
                        reads=(s_kf, s_z64b), writes=(s_kcs,))
                ctx.run("ve", lambda E, o=s_kct[:, 128 * cb:128 * cb + 128],
                        i=s_kcs[:, 128 * cb:128 * cb + 128], sc=zA[:, cb:cb + 1]:
                        E.tensor_scalar(o, i, sc, None, ALU.add),
                        reads=(s_kcs, zA), writes=(s_kct,))
                ctx.run("ve", lambda E, o=zB[:, cb:cb + 1], a=zA[:, cb:cb + 1],
                        b=s_kcs[:, 128 * cb + 127:128 * cb + 128]:
                        E.tensor_tensor(o, a, b, ALU.add), reads=(zA, s_kcs), writes=(zB,))
            qfv = s_qf[:].rearrange("p (cb tt) -> p cb tt", tt=TT3)[:, :, 128 * kc:128 * kc + C1]
            ctx.run("ve", lambda E, o=s_qk[:].rearrange("p (cb c) -> p cb c", c=C1),
                    a=qfv, b=s_kct[:].rearrange("p (cb c) -> p cb c", c=C1):
                    E.tensor_tensor(o, a, b, ALU.mult), reads=(s_qf, s_kct), writes=(s_qk,))
            for cb in range(CB):
                ctx.run("te", lambda E, o=p_st[0:16, 0:C1], l=s_dsel[:, 16 * cb:16 * cb + 16],
                        r=s_qk[:, 128 * cb:128 * cb + 128], st=(cb == 0), sp=(cb == CB - 1):
                        MM(E, o, l, r, start=st, stop=sp),
                        reads=(s_dsel, s_qk), writes=(p_st,), inc=(cb == CB - 1))
            ctx.run("ve", lambda E, o=s_dr[:], i=p_st[0:16, 0:C1]:
                    E.tensor_scalar(o, i, 1e-6, None, ALU.add), reads=(p_st,), writes=(s_dr,))
            ctx.run("ve", lambda E, o=s_dr2[:], i=s_dr[:]: E.reciprocal(o, i),
                    reads=(s_dr,), writes=(s_dr2,))

            for cb in range(CB):
                ctx.run("te", lambda E, o=p_tr[:, 0:128],
                        i=fm3(s_kf, cb, 128 * kc, C1), d=s_id[:]:
                        E.transpose(o, i, d), reads=(s_kf, s_id), writes=(p_tr,))
                ctx.run("ve", lambda E, o=s_kftm[:], i=p_tr[:, 0:128]: E.tensor_copy(o, i),
                        reads=(p_tr,), writes=(s_kftm,))
                ctx.run("te", lambda E, o=p_b1[:, 0:C1], l=s_esel2[:, 128 * cb:128 * cb + 128],
                        r=s_dr2[:]: MM(E, o, l, r, start=True, stop=True),
                        reads=(s_esel2, s_dr2), writes=(p_b1,))
                for half in range(2):
                    h = 2 * cb + half
                    hb = 64 * half
                    ctx.run("te", lambda E, o=p_at[:, 0:C1],
                            l=fm3(s_kf, cb, 128 * kc, C1)[hb:hb + 64, :] if False else s_kf[hb:hb + 64, cb * TT3 + 128 * kc: cb * TT3 + 128 * kc + C1],
                            r=s_qf[hb:hb + 64, cb * TT3 + 128 * kc: cb * TT3 + 128 * kc + C1]:
                            MM(E, o, l, r, start=True, stop=True),
                            reads=(s_kf, s_qf), writes=(p_at,))
                    ctx.run("ve", lambda E, o=s_atm1[:], a=p_at[:, 0:C1], b=s_mki[:]:
                            E.tensor_tensor(o, a, b, ALU.mult),
                            reads=(p_at, s_mki), writes=(s_atm1,))
                    ctx.run("te", lambda E, o=p_n1[hb:hb + 64, 0:C1],
                            l=s_vtm[:, 1024 * kc + 64 * h: 1024 * kc + 64 * h + 64],
                            r=s_atm1[:]: MM(E, o, l, r, start=True, stop=True),
                            reads=(s_vtm, s_atm1), writes=(p_n1,))
                    ctx.run("te", lambda E, o=p_n2[hb:hb + 64, 0:C1],
                            l=SA[hb:hb + 64, 64 * cb:64 * cb + 64],
                            r=s_qf[hb:hb + 64, cb * TT3 + 128 * kc: cb * TT3 + 128 * kc + C1]:
                            MM(E, o, l, r, start=True, stop=True),
                            reads=(SA, s_qf), writes=(p_n2,))
                    ctx.run("te", lambda E, o=p_misc[hb:hb + 64, 256:256 + 64],
                            l=s_kftm[:, hb:hb + 64],
                            r=s_vtm[:, 1024 * kc + 64 * h: 1024 * kc + 64 * h + 64]:
                            MM(E, o, l, r, start=True, stop=True),
                            reads=(s_kftm, s_vtm), writes=(p_misc,))
                    ctx.run("ve", lambda E, o=SB[hb:hb + 64, 64 * cb:64 * cb + 64],
                            a=SA[hb:hb + 64, 64 * cb:64 * cb + 64],
                            b=p_misc[hb:hb + 64, 256:256 + 64]:
                            E.tensor_tensor(o, a, b, ALU.add),
                            reads=(SA, p_misc), writes=(SB,))
                ctx.run("ve", lambda E, o=s_nA[:, 0:C1], i=p_n1[:, 0:C1]: E.tensor_copy(o, i),
                        reads=(p_n1,), writes=(s_nA,))
                ctx.run("ve", lambda E, o=s_nA[:, 0:C1], a=s_nA[:, 0:C1], b=p_n2[:, 0:C1]:
                        E.tensor_tensor(o, a, b, ALU.add), reads=(s_nA, p_n2), writes=(s_nA,))
                ctx.run("ve", lambda E, o=fm3(s_att, cb, 128 * kc, C1), a=s_nA[:, 0:C1],
                        b=p_b1[:, 0:C1]: E.tensor_tensor(o, a, b, ALU.mult),
                        reads=(s_nA, p_b1), writes=(s_att,))

        for ob in range(CB):
            pp = p_pr[ob % 2]
            for cb in range(CB):
                ctx.run("te", lambda E, o=pp[:, 0:TT3],
                        l=s_wbuf[:, 1024 * cb + 128 * ob: 1024 * cb + 128 * ob + 128],
                        r=fm3(s_att, cb, 0, TT3), st=(cb == 0), sp=(cb == CB - 1):
                        MM(E, o, l, r, start=st, stop=sp),
                        reads=(s_wbuf, s_att), writes=(pp,), inc=(cb == CB - 1))
            ctx.run("ve", lambda E, o=fm3(s_res, ob, 0, TT3), a=fm3(s_x2, ob, 0, TT3),
                    b=pp[:, 0:TT3]: E.tensor_tensor(o, a, b, ALU.add),
                    reads=(s_x2, pp), writes=(s_res,))
        emit_ln3(s_res, s_x3)
        for cb in range(CB):
            ctx.run("ve", lambda E, o=fm3(s_x3r, cb, 0, TT3), i=fm3(s_x3, cb, 0, TT3):
                    E.tensor_copy(o, i), reads=(s_x3,), writes=(s_x3r,))
        dma_out(io["x3s"].rearrange("(cb p) tt -> p cb tt", p=128)[:, :, t0:t0 + TT3],
                s_x3[:].rearrange("p (cb tt) -> p cb tt", tt=TT3))
        dma_out(io["x3r"].rearrange("(cb p) tt -> p cb tt", p=128)[:, :, t0:t0 + TT3],
                s_x3r[:].rearrange("p (cb tt) -> p cb tt", tt=TT3))

    ctx.full_barrier()
    es3a.close()

    s_x3m = al("r_x3m", [128, CB * TTM], F32R)
    s_x3g = al("r_x3g", [128, CB * TTM], F32)
    s_ya = [al("r_ya0", [128, CB * TTM], F32), al("r_ya1", [128, CB * TTM], F32)]
    s_h1 = al("r_h1", [128, 2048], F32R)
    s_h3 = al("r_h3", [128, 2048], F32)
    s_zz = al("r_zz", [128, 4096], F32R)
    s_we1 = al("r_we1", [128, 4096], F32R)
    s_we3 = al("r_we3", [128, 4096], F32R)
    s_we2 = al("r_we2", [128, 4096], F32R)
    s_combT = al("r_combT", [16, TTM], F32R)
    s_cb1 = al("r_cb1", [128, 512], F32R)
    s_x4 = al("r_x4", [128, CB * TTM], F32)
    s_sq2 = al("r_sq2", [128, 2048], F32)
    s_onesf = al("r_onesf", [128, 128], F32)
    s_ln1m = al("r_ln1m", [128, 256], F32)
    s_stmm = al("r_stmm", [1, 512], F32)
    s_stqm = al("r_stqm", [1, 512], F32)
    s_strm = al("r_strm", [1, 512], F32)
    s_sc = al("r_sc", [128, 16], F32)
    s_se = al("r_se", [128, 16], F32)
    s_so = al("r_so", [128, 16], F32)
    s_sb = al("r_sb", [128, 16], F32)
    s_msk = al("r_msk", [128, 16], F32)
    s_c1 = al("r_c1", [128, 1], F32)
    s_c2 = al("r_c2", [128, 1], F32)
    s_c3 = al("r_c3", [128, 1], F32)
    s_gpA = al("r_gpA", [128, 4], F32)
    s_gpB = al("r_gpB", [128, 4], F32)
    s_kp = al("r_kp", [128, 4], F32)
    s_pen = al("r_pen", [128, 4], F32)

    def emit_ln_m(src, dst):
        for half in range(2):
            for cb in range(CB):
                ctx.run("te", lambda E, o=p_st[0:1, 0:256], l=s_onesf[:, 0:1],
                        r=src[:, cb * TTM + 256 * half: cb * TTM + 256 * half + 256],
                        st=(cb == 0), sp=(cb == CB - 1):
                        E.matmul(o, l, r, start=st, stop=sp),
                        reads=(s_onesf, src), writes=(p_st,), inc=(cb == CB - 1))
            for cb in range(CB):
                ctx.run("se", lambda E, o=s_sq2[:, 256 * cb:256 * cb + 256],
                        i=src[:, cb * TTM + 256 * half: cb * TTM + 256 * half + 256]:
                        E.activation(o, i, AF.Square), reads=(src,), writes=(s_sq2,))
            for cb in range(CB):
                ctx.run("te", lambda E, o=p_st[0:1, 256:512], l=s_onesf[:, 0:1],
                        r=s_sq2[:, 256 * cb:256 * cb + 256], st=(cb == 0), sp=(cb == CB - 1):
                        E.matmul(o, l, r, start=st, stop=sp),
                        reads=(s_onesf, s_sq2), writes=(p_st,), inc=(cb == CB - 1))
            mrow = s_stmm[0:1, 256 * half:256 * half + 256]
            qrow = s_stqm[0:1, 256 * half:256 * half + 256]
            rrow = s_strm[0:1, 256 * half:256 * half + 256]
            ctx.run("ve", lambda E, o=mrow, i=p_st[0:1, 0:256]:
                    E.tensor_scalar(o, i, 1.0 / D, None, ALU.mult), reads=(p_st,), writes=(s_stmm,))
            ctx.run("ve", lambda E, o=qrow, i=p_st[0:1, 256:512]:
                    E.tensor_scalar(o, i, 1.0 / D, None, ALU.mult), reads=(p_st,), writes=(s_stqm,))
            ctx.run("ve", lambda E, o=rrow, a=mrow, b=mrow: E.tensor_tensor(o, a, b, ALU.mult),
                    reads=(s_stmm,), writes=(s_strm,))
            ctx.run("ve", lambda E, o=qrow, a=qrow, b=rrow: E.tensor_tensor(o, a, b, ALU.subtract),
                    reads=(s_stqm, s_strm), writes=(s_stqm,))
            ctx.run("se", lambda E, o=rrow, i=qrow, e=s_eps[0:1, 1:2]:
                    E.activation(o, i, AF.Ln, bias=e), reads=(s_stqm, s_eps), writes=(s_strm,))
            ctx.run("se", lambda E, o=rrow, i=rrow: E.activation(o, i, AF.Exp, scale=-0.5),
                    reads=(s_strm,), writes=(s_strm,))
            ctx.run("ve", lambda E, o=mrow, a=mrow, b=rrow: E.tensor_tensor(o, a, b, ALU.mult),
                    reads=(s_stmm, s_strm), writes=(s_stmm,))
            ctx.run("te", lambda E, o=p_b1[:, 0:256], l=s_onesf[0:1, :], r=rrow:
                    E.matmul(o, l, r, start=True, stop=True), reads=(s_onesf, s_strm), writes=(p_b1,))
            ctx.run("te", lambda E, o=p_b1[:, 256:512], l=s_onesf[0:1, :], r=mrow:
                    E.matmul(o, l, r, start=True, stop=True), reads=(s_onesf, s_stmm), writes=(p_b1,))
            for cb in range(CB):
                sl = slice(cb * TTM + 256 * half, cb * TTM + 256 * half + 256)
                ctx.run("ve", lambda E, o=s_ln1m[:], a=src[:, sl], b=p_b1[:, 0:256]:
                        E.tensor_tensor(o, a, b, ALU.mult), reads=(src, p_b1), writes=(s_ln1m,))
                ctx.run("ve", lambda E, o=dst[:, sl], a=s_ln1m[:], b=p_b1[:, 256:512]:
                        E.tensor_tensor(o, a, b, ALU.subtract), reads=(s_ln1m, p_b1), writes=(dst,))

    dma_in(s_onesf[:], io["ones128f"])
    ntm = p_tokens // TTM
    for t in range(ntm):
        t0 = t * TTM
        dma_in(s_x3m[:].rearrange("p (cb tt) -> p cb tt", tt=TTM),
               io["x3r"].rearrange("(cb p) tt -> p cb tt", p=128)[:, :, t0:t0 + TTM])
        dma_in(s_x3g[:].rearrange("p (cb tt) -> p cb tt", tt=TTM),
               io["x3s"].rearrange("(cb p) tt -> p cb tt", p=128)[:, :, t0:t0 + TTM])

        for tc in range(4):
            for cb in range(CB):
                ctx.run("te", lambda E, o=p_misc[:, 0:16],
                        l=s_x3g[:, cb * TTM + 128 * tc: cb * TTM + 128 * tc + 128],
                        r=s_gw[:, 16 * cb:16 * cb + 16], st=(cb == 0), sp=(cb == CB - 1):
                        E.matmul(o, l, r, start=st, stop=sp),
                        reads=(s_x3g, s_gw), writes=(p_misc,), inc=(cb == CB - 1))
            ctx.run("ve", lambda E, o=s_sc[:], i=p_misc[:, 0:16]: E.tensor_copy(o, i),
                    reads=(p_misc,), writes=(s_sc,))
            ctx.run("ve", lambda E, o=s_c1[:], i=s_sc[:]:
                    E.reduce_max(o, i, axis=mybir.AxisListType.X), reads=(s_sc,), writes=(s_c1,))
            ctx.run("ve", lambda E, o=s_c2[:], i=s_c1[:]:
                    E.tensor_scalar(o, i, -1.0, None, ALU.mult), reads=(s_c1,), writes=(s_c2,))
            ctx.run("se", lambda E, o=s_se[:], i=s_sc[:], b=s_c2[:]:
                    E.activation(o, i, AF.Exp, bias=b), reads=(s_sc, s_c2), writes=(s_se,))
            ctx.run("ve", lambda E, o=s_c1[:], i=s_se[:]:
                    E.reduce_sum(o, i, axis=mybir.AxisListType.X), reads=(s_se,), writes=(s_c1,))
            ctx.run("ve", lambda E, o=s_c3[:], i=s_c1[:]: E.reciprocal(o, i),
                    reads=(s_c1,), writes=(s_c3,))
            ctx.run("ve", lambda E, o=s_so[:], i=s_se[:], sc=s_c3[:]:
                    E.tensor_scalar(o, i, sc, None, ALU.mult), reads=(s_se, s_c3), writes=(s_so,))
            ctx.run("ve", lambda E, o=s_sb[:], a=s_so[:], b=s_gbb[:]:
                    E.tensor_tensor(o, a, b, ALU.add), reads=(s_so, s_gbb), writes=(s_sb,))
            sgv = s_sb[:].rearrange("p (g e) -> p g e", e=4)
            pairs = [(0, 1), (0, 2), (0, 3), (1, 2), (1, 3), (2, 3)]
            for pi, (i_, j_) in enumerate(pairs):
                if pi == 0:
                    ctx.run("ve", lambda E, o=s_gpA[:], a=sgv[:, :, i_:i_ + 1],
                            b=sgv[:, :, j_:j_ + 1]: E.tensor_tensor(o, a, b, ALU.add),
                            reads=(s_sb,), writes=(s_gpA,))
                else:
                    ctx.run("ve", lambda E, o=s_gpB[:], a=sgv[:, :, i_:i_ + 1],
                            b=sgv[:, :, j_:j_ + 1]: E.tensor_tensor(o, a, b, ALU.add),
                            reads=(s_sb,), writes=(s_gpB,))
                    ctx.run("ve", lambda E, o=s_gpA[:], a=s_gpA[:], b=s_gpB[:]:
                            E.tensor_tensor(o, a, b, ALU.max), reads=(s_gpA, s_gpB), writes=(s_gpA,))
            a0, b0 = s_gpA[:, 0:1], s_gpA[:, 1:2]
            c0, d0 = s_gpA[:, 2:3], s_gpA[:, 3:4]
            ctx.run("ve", lambda E, o=s_c1[:], a=a0, b=b0: E.tensor_tensor(o, a, b, ALU.max),
                    reads=(s_gpA,), writes=(s_c1,))
            ctx.run("ve", lambda E, o=s_c2[:], a=c0, b=d0: E.tensor_tensor(o, a, b, ALU.max),
                    reads=(s_gpA,), writes=(s_c2,))
            ctx.run("ve", lambda E, o=s_c1[:], a=s_c1[:], b=s_c2[:]:
                    E.tensor_tensor(o, a, b, ALU.min), reads=(s_c1, s_c2), writes=(s_c1,))
            ctx.run("ve", lambda E, o=s_c2[:], a=a0, b=b0: E.tensor_tensor(o, a, b, ALU.min),
                    reads=(s_gpA,), writes=(s_c2,))
            ctx.run("ve", lambda E, o=s_c3[:], a=c0, b=d0: E.tensor_tensor(o, a, b, ALU.min),
                    reads=(s_gpA,), writes=(s_c3,))
            ctx.run("ve", lambda E, o=s_c2[:], a=s_c2[:], b=s_c3[:]:
                    E.tensor_tensor(o, a, b, ALU.max), reads=(s_c2, s_c3), writes=(s_c2,))
            ctx.run("ve", lambda E, o=s_c1[:], a=s_c1[:], b=s_c2[:]:
                    E.tensor_tensor(o, a, b, ALU.max), reads=(s_c1, s_c2), writes=(s_c1,))
            ctx.run("ve", lambda E, o=s_kp[:], i=s_gpA[:], sc=s_c1[:]:
                    E.tensor_scalar(o, i, sc, None, ALU.is_ge), reads=(s_gpA, s_c1), writes=(s_kp,))
            ctx.run("ve", lambda E, o=s_pen[:], i=s_kp[:]:
                    E.tensor_scalar(o, i, 1.0, 1e30, ALU.subtract, ALU.mult),
                    reads=(s_kp,), writes=(s_pen,))
            for g in range(4):
                ctx.run("ve", lambda E, o=s_msk[:, 4 * g:4 * g + 4], i=s_sb[:, 4 * g:4 * g + 4],
                        s1=s_kp[:, g:g + 1], s2=s_pen[:, g:g + 1]:
                        E.tensor_scalar(o, i, s1, s2, ALU.mult, ALU.add),
                        reads=(s_sb, s_kp, s_pen), writes=(s_msk,))
            ctx.run("ve", lambda E, o=s_c1[:], i=s_msk[:]:
                    E.reduce_max(o, i, axis=mybir.AxisListType.X), reads=(s_msk,), writes=(s_c1,))
            ctx.run("ve", lambda E, o=s_se[:], i=s_msk[:], sc=s_c1[:]:
                    E.tensor_scalar(o, i, sc, None, ALU.is_equal), reads=(s_msk, s_c1), writes=(s_se,))
            ctx.run("ve", lambda E, o=s_se[:], a=s_se[:], b=s_msk[:]:
                    E.scalar_tensor_tensor(o, a, -1e30, b, ALU.mult, ALU.add),
                    reads=(s_se, s_msk), writes=(s_se,))
            ctx.run("ve", lambda E, o=s_c1[:], i=s_se[:]:
                    E.reduce_max(o, i, axis=mybir.AxisListType.X), reads=(s_se,), writes=(s_c1,))
            ctx.run("ve", lambda E, o=s_msk[:], i=s_msk[:], sc=s_c1[:]:
                    E.tensor_scalar(o, i, sc, None, ALU.is_ge), reads=(s_msk, s_c1), writes=(s_msk,))
            ctx.run("ve", lambda E, o=s_so[:], a=s_so[:], b=s_msk[:]:
                    E.tensor_tensor(o, a, b, ALU.mult), reads=(s_so, s_msk), writes=(s_so,))
            ctx.run("te", lambda E, o=p_misc[0:16, 128:256], i=s_so[:], d=s_idg[:]:
                    E.transpose(o, i, d), reads=(s_so, s_idg), writes=(p_misc,))
            ctx.run("ve", lambda E, o=s_combT[0:16, 128 * tc:128 * tc + 128],
                    i=p_misc[0:16, 128:256]: E.tensor_copy(o, i), reads=(p_misc,), writes=(s_combT,))

        ya0 = s_ya[0]
        for ih in range(2):
            for wname, dst, bj, act in [("ws1", s_h1, 0, True), ("ws3", s_h3, 1, False)]:
                dma_in(s_we1[:].rearrange("p (cb i) -> p cb i", i=512),
                       io[wname].rearrange("(cb p) i -> p cb i", p=128)[:, :, 512 * ih:512 * ih + 512],
                       sem="wl")
                for ic in range(4):
                    pp = p_pr[ic % 2]
                    for cb in range(CB):
                        ctx.run("te", lambda E, o=pp[:, 0:512],
                                l=s_we1[:, 512 * cb + 128 * ic: 512 * cb + 128 * ic + 128],
                                r=s_x3m[:, cb * TTM: cb * TTM + 512],
                                st=(cb == 0), sp=(cb == CB - 1):
                                MM(E, o, l, r, start=st, stop=sp),
                                reads=(s_we1, s_x3m), writes=(pp,), inc=(cb == CB - 1))
                    oc = 4 * ih + ic
                    if act:
                        ctx.run("se", lambda E, o=s_h1[:, 512 * ic:512 * ic + 512],
                                i=pp[:, 0:512], b=bscol(bj, oc):
                                E.activation(o, i, AF.Silu, bias=b),
                                reads=(pp, s_becs), writes=(s_h1,))
                    else:
                        ctx.run("ve", lambda E, o=s_h3[:, 512 * ic:512 * ic + 512],
                                i=pp[:, 0:512], b=bscol(bj, oc):
                                E.tensor_scalar(o, i, b, None, ALU.add),
                                reads=(pp, s_becs), writes=(s_h3,))
            ctx.run("ve", lambda E, o=s_zz[:, 2048 * ih:2048 * ih + 2048],
                    a=s_h1[:], b=s_h3[:]: E.tensor_tensor(o, a, b, ALU.mult),
                    reads=(s_h1, s_h3), writes=(s_zz,))
        for dh in range(2):
            dma_in(s_we2[:].rearrange("p (cb i) -> p cb i", i=512),
                   io["ws2"].rearrange("(cb p) i -> p cb i", p=128)[:, :, 512 * dh:512 * dh + 512],
                   sem="wl")
            for ob in range(4):
                gob = 4 * dh + ob
                pp = p_pr[ob % 2]
                for ic in range(CB):
                    ctx.run("te", lambda E, o=pp[:, 0:512],
                            l=s_we2[:, 512 * ic + 128 * ob: 512 * ic + 128 * ob + 128],
                            r=s_zz[:, 512 * ic:512 * ic + 512],
                            st=(ic == 0), sp=(ic == CB - 1):
                            MM(E, o, l, r, start=st, stop=sp),
                            reads=(s_we2, s_zz), writes=(pp,), inc=(ic == CB - 1))
                ctx.run("ve", lambda E, o=ya0[:, gob * TTM: gob * TTM + 512],
                        i=pp[:, 0:512], b=bscol(2, gob):
                        E.tensor_scalar(o, i, b, None, ALU.add),
                        reads=(pp, s_becs), writes=(ya0,))

        for e in range(16):
            yaA = s_ya[e % 2]
            yaB = s_ya[(e + 1) % 2]
            dma_in(s_we1[:].rearrange("p (cb i) -> p cb i", i=512),
                   io["We1"][e].rearrange("(cb p) i -> p cb i", p=128), sem="wl")
            dma_in(s_we3[:].rearrange("p (cb i) -> p cb i", i=512),
                   io["We3"][e].rearrange("(cb p) i -> p cb i", p=128), sem="wl")
            dma_in(s_we2[:].rearrange("p (cb i) -> p cb i", i=1024),
                   io["We2"][e].rearrange("(cb p) i -> p cb i", p=128), sem="wl")
            dma_in(s_b2row[:], io["be2r"][e:e + 1, :])
            ctx.run("te", lambda E, o=p_b1[:, 0:512],
                    l=s_eselE[:, 128 * e:128 * e + 128],
                    r=s_combT[0:16, 0:512]:
                    MM(E, o, l, r, start=True, stop=True),
                    reads=(s_eselE, s_combT), writes=(p_b1,))
            ctx.run("ve", lambda E, o=s_cb1[:], i=p_b1[:, 0:512]: E.tensor_copy(o, i),
                    reads=(p_b1,), writes=(s_cb1,))
            for ic in range(4):
                pp = p_pr[ic % 2]
                for cb in range(CB):
                    ctx.run("te", lambda E, o=pp[:, 0:512],
                            l=s_we1[:, 512 * cb + 128 * ic: 512 * cb + 128 * ic + 128],
                            r=s_x3m[:, cb * TTM: cb * TTM + 512],
                            st=(cb == 0), sp=(cb == CB - 1):
                            MM(E, o, l, r, start=st, stop=sp),
                            reads=(s_we1, s_x3m), writes=(pp,), inc=(cb == CB - 1))
                ctx.run("se", lambda E, o=s_h1[:, 512 * ic:512 * ic + 512], i=pp[:, 0:512],
                        b=be1col(e, ic): E.activation(o, i, AF.Silu, bias=b),
                        reads=(pp, s_becs), writes=(s_h1,))
            for ic in range(4):
                pp = p_pr[ic % 2]
                for cb in range(CB):
                    ctx.run("te", lambda E, o=pp[:, 0:512],
                            l=s_we3[:, 512 * cb + 128 * ic: 512 * cb + 128 * ic + 128],
                            r=s_x3m[:, cb * TTM: cb * TTM + 512],
                            st=(cb == 0), sp=(cb == CB - 1):
                            MM(E, o, l, r, start=st, stop=sp),
                            reads=(s_we3, s_x3m), writes=(pp,), inc=(cb == CB - 1))
                ctx.run("ve", lambda E, o=s_h3[:, 512 * ic:512 * ic + 512], i=pp[:, 0:512],
                        b=be3col(e, ic): E.tensor_scalar(o, i, b, None, ALU.add),
                        reads=(pp, s_becs), writes=(s_h3,))
            ctx.run("ve", lambda E, o=s_h3[:], a=s_h1[:], b=s_h3[:]:
                    E.tensor_tensor(o, a, b, ALU.mult), reads=(s_h1, s_h3), writes=(s_h3,))
            for ic in range(4):
                ctx.run("ve", lambda E, o=s_h1[:, 512 * ic:512 * ic + 512],
                        a=s_h3[:, 512 * ic:512 * ic + 512], b=s_cb1[:]:
                        E.tensor_tensor(o, a, b, ALU.mult), reads=(s_h3, s_cb1), writes=(s_h1,))
            for ob in range(CB):
                pp = p_pr[ob % 2]
                for ic in range(4):
                    ctx.run("te", lambda E, o=pp[:, 0:512],
                            l=s_we2[:, 1024 * ic + 128 * ob: 1024 * ic + 128 * ob + 128],
                            r=s_h1[:, 512 * ic:512 * ic + 512],
                            st=(ic == 0), sp=False:
                            MM(E, o, l, r, start=st, stop=sp),
                            reads=(s_we2, s_h1), writes=(pp,), inc=False)
                ctx.run("te", lambda E, o=pp[:, 0:512],
                        l=s_b2row[0:1, 128 * ob:128 * ob + 128],
                        r=s_cb1[0:1, 0:512]:
                        MM(E, o, l, r, start=False, stop=True),
                        reads=(s_b2row, s_cb1), writes=(pp,))
                ctx.run("ve", lambda E,
                        o=yaB[:, ob * TTM: ob * TTM + 512],
                        a=yaA[:, ob * TTM: ob * TTM + 512],
                        b=pp[:, 0:512]: E.tensor_tensor(o, a, b, ALU.add),
                        reads=(yaA, pp), writes=(yaB,))

        yfin = s_ya[16 % 2]
        for cb in range(CB):
            ctx.run("ve", lambda E, o=yfin[:, cb * TTM:(cb + 1) * TTM],
                    a=yfin[:, cb * TTM:(cb + 1) * TTM],
                    b=s_x3g[:, cb * TTM:(cb + 1) * TTM]:
                    E.tensor_tensor(o, a, b, ALU.add), reads=(yfin, s_x3g), writes=(yfin,))
        emit_ln_m(yfin, s_x4)
        dma_out(io["outT"].rearrange("(cb p) tt -> p cb tt", p=128)[:, :, t0:t0 + TTM],
                s_x4[:].rearrange("p (cb tt) -> p cb tt", tt=TTM))

    ctx.wait("po", "out", ctx.barrier_token("out"))
    ctx.full_barrier()
    es3h.close()


def build_phase3(p_tokens=P):
    nc = bass.Bass("TRN2", target_bir_lowering=False)
    io = {}
    io["x2s"] = nc.dram_tensor("x2s", [D, p_tokens], F32R, kind="ExternalInput").ap()
    for n in ["qfs", "kfs"]:
        io[n] = nc.dram_tensor(n, [D, p_tokens], F32, kind="ExternalInput").ap()
    io["vtm"] = nc.dram_tensor("vtm", [p_tokens, D], F32, kind="ExternalInput").ap()
    io["gS1"] = nc.dram_tensor("gS1", [128, 8 * 512], F32, kind="ExternalInput").ap()
    io["gZ1"] = nc.dram_tensor("gZ1", [128, 64], F32, kind="ExternalInput").ap()
    io["fmask"] = nc.dram_tensor("fmask", [128, 16], F32, kind="ExternalInput").ap()
    io["Wo1_t"] = nc.dram_tensor("Wo1_t", [D, D], F32R, kind="ExternalInput").ap()
    io["gwT"] = nc.dram_tensor("gwT", [D, 16], F32, kind="ExternalInput").ap()
    io["gb"] = nc.dram_tensor("gb", [1, 16], F32, kind="ExternalInput").ap()
    io["gbb"] = nc.dram_tensor("gbb", [128, 16], F32, kind="ExternalInput").ap()
    io["id128f"] = nc.dram_tensor("id128f", [128, 128], F32, kind="ExternalInput").ap()
    io["ones128f"] = nc.dram_tensor("ones128f", [128, 128], F32, kind="ExternalInput").ap()
    io["We1"] = nc.dram_tensor("We1", [16, D, 512], F32R, kind="ExternalInput").ap()
    io["We3"] = nc.dram_tensor("We3", [16, D, 512], F32R, kind="ExternalInput").ap()
    io["We2"] = nc.dram_tensor("We2", [16, 512, D], F32R, kind="ExternalInput").ap()
    io["be1c"] = nc.dram_tensor("be1c", [128, 64], F32, kind="ExternalInput").ap()
    io["be3c"] = nc.dram_tensor("be3c", [128, 64], F32, kind="ExternalInput").ap()
    io["be2c"] = nc.dram_tensor("be2c", [128, 128], F32, kind="ExternalInput").ap()
    io["be2r"] = nc.dram_tensor("be2r", [16, D], F32R, kind="ExternalInput").ap()
    io["bsc"] = nc.dram_tensor("bsc", [128, 24], F32, kind="ExternalInput").ap()
    io["ws1"] = nc.dram_tensor("ws1", [D, D], F32R, kind="ExternalInput").ap()
    io["ws3"] = nc.dram_tensor("ws3", [D, D], F32R, kind="ExternalInput").ap()
    io["ws2"] = nc.dram_tensor("ws2", [D, D], F32R, kind="ExternalInput").ap()
    io["dsel"] = nc.dram_tensor("dsel", [128, 128], F32, kind="ExternalInput").ap()
    io["esel2"] = nc.dram_tensor("esel2", [16, 1024], F32R, kind="ExternalInput").ap()
    io["eselE"] = nc.dram_tensor("eselE", [16, 2048], F32R, kind="ExternalInput").ap()
    io["ones128"] = nc.dram_tensor("ones128", [128, 128], F32R, kind="ExternalInput").ap()
    io["id128"] = nc.dram_tensor("id128", [128, 128], F32, kind="ExternalInput").ap()
    io["mask128i"] = nc.dram_tensor("mask128i", [128, 128], F32, kind="ExternalInput").ap()
    io["epsc"] = nc.dram_tensor("epsc", [128, 2], F32, kind="ExternalInput").ap()
    io["x3s"] = nc.dram_tensor("x3s", [D, p_tokens], F32, kind="Internal").ap()
    io["x3r"] = nc.dram_tensor("x3r", [D, p_tokens], F32R, kind="Internal").ap()
    io["outT"] = nc.dram_tensor("outT", [D, p_tokens], F32, kind="ExternalOutput").ap()
    with nc.Block() as block:
        ctx = Ctx(nc, block)
        emit_phase3(ctx, io, p_tokens)
        ctx.flush()
    return nc


def pack_col_n(v, nch):
    v = np.asarray(v, np.float32).reshape(nch, 128).T
    return np.ascontiguousarray(v)


def stage_phase3_shared(inputs):
    sh = {}
    sh["Wo1_t"] = np.ascontiguousarray(np.asarray(inputs["L1_attn_Wo"], np.float32).T)
    sh["gwT"] = np.ascontiguousarray(np.asarray(inputs["L1_ffn_gate_w"], np.float32).T)
    sh["gb"] = np.ascontiguousarray(np.asarray(inputs["L1_ffn_gate_b"], np.float32).reshape(1, 16))
    sh["gbb"] = np.ascontiguousarray(np.repeat(sh["gb"], 128, axis=0))
    sh["id128f"] = np.eye(128, dtype=np.float32)
    sh["ones128f"] = np.ones((128, 128), np.float32)
    sh["We1"] = np.ascontiguousarray(np.asarray(inputs["L1_ffn_We1"], np.float32))
    sh["We3"] = np.ascontiguousarray(np.asarray(inputs["L1_ffn_We3"], np.float32))
    sh["We2"] = np.ascontiguousarray(np.asarray(inputs["L1_ffn_We2"], np.float32))
    be1 = np.asarray(inputs["L1_ffn_be1"], np.float32)
    be3 = np.asarray(inputs["L1_ffn_be3"], np.float32)
    be2 = np.asarray(inputs["L1_ffn_be2"], np.float32)
    b1c = np.zeros((128, 64), np.float32)
    b3c = np.zeros((128, 64), np.float32)
    b2c = np.zeros((128, 128), np.float32)
    for e in range(16):
        b1c[:, 4 * e:4 * e + 4] = pack_col_n(be1[e], 4)
        b3c[:, 4 * e:4 * e + 4] = pack_col_n(be3[e], 4)
        b2c[:, 8 * e:8 * e + 8] = pack_col_n(be2[e], 8)
    sh["be1c"] = b1c
    sh["be3c"] = b3c
    sh["be2c"] = b2c
    sh["be2r"] = np.ascontiguousarray(be2)
    bsc = np.zeros((128, 24), np.float32)
    bsc[:, 0:8] = pack_col_n(inputs["L1_ffn_bs1"], 8)
    bsc[:, 8:16] = pack_col_n(inputs["L1_ffn_bs3"], 8)
    bsc[:, 16:24] = pack_col_n(inputs["L1_ffn_bs2"], 8)
    sh["bsc"] = bsc
    sh["ws1"] = np.ascontiguousarray(np.asarray(inputs["L1_ffn_ws1"], np.float32))
    sh["ws3"] = np.ascontiguousarray(np.asarray(inputs["L1_ffn_ws3"], np.float32))
    sh["ws2"] = np.ascontiguousarray(np.asarray(inputs["L1_ffn_ws2"], np.float32))
    dsel = np.zeros((128, 128), np.float32)
    for cb in range(8):
        dsel[0:64, 16 * cb + 2 * cb] = 1.0
        dsel[64:128, 16 * cb + 2 * cb + 1] = 1.0
    sh["dsel"] = dsel
    esel2 = np.zeros((16, 1024), np.float32)
    for cb in range(8):
        esel2[2 * cb, 128 * cb:128 * cb + 64] = 1.0
        esel2[2 * cb + 1, 128 * cb + 64:128 * cb + 128] = 1.0
    sh["esel2"] = esel2
    eselE = np.zeros((16, 2048), np.float32)
    for e in range(16):
        eselE[e, 128 * e:128 * e + 128] = 1.0
    sh["eselE"] = eselE
    sh["ones128"] = np.ones((128, 128), np.float32)
    sh["id128"] = np.eye(128, dtype=np.float32)
    sh["mask128i"] = np.triu(np.ones((128, 128), np.float32), 0)
    epsc = np.zeros((128, 2), np.float32)
    epsc[:, 0] = GN_EPS
    epsc[:, 1] = LN_EPS
    sh["epsc"] = epsc
    return sh



def build_fused(p_tokens=P):
    nc = bass.Bass("TRN2", target_bir_lowering=False)
    io = {}
    io["xT"] = nc.dram_tensor("xT", [D, p_tokens + 1], F32, kind="ExternalInput").ap()
    for n in ["Wk_t", "Wv_t", "Wr_t", "Wg_t"]:
        io[n] = nc.dram_tensor(n, [D, D], F32R, kind="ExternalInput").ap()
    io["A_lora"] = nc.dram_tensor("A_lora", [D, 160], F32R, kind="ExternalInput").ap()
    io["B_lora"] = nc.dram_tensor("B_lora", [5, 32, D], F32R, kind="ExternalInput").ap()
    io["td_A"] = nc.dram_tensor("td_A", [D, 64], F32R, kind="ExternalInput").ap()
    io["td_B"] = nc.dram_tensor("td_B", [64, D], F32R, kind="ExternalInput").ap()
    io["cvec"] = nc.dram_tensor("cvec", [128, 8 * NCV1], F32, kind="ExternalInput").ap()
    io["id128"] = nc.dram_tensor("id128", [128, 128], F32, kind="ExternalInput").ap()
    io["mask64s"] = nc.dram_tensor("mask64s", [64, 1024], F32, kind="ExternalInput").ap()
    io["i64"] = nc.dram_tensor("i64", [64, 64], F32, kind="ExternalInput").ap()
    io["dsel"] = nc.dram_tensor("dsel", [128, 128], F32, kind="ExternalInput").ap()
    io["esel"] = nc.dram_tensor("esel", [16, 1024], F32, kind="ExternalInput").ap()
    for n in ["Wo_t", "Wq_t", "Wk1_t", "Wv1_t", "Wg1_t",
              "wfr", "wfi", "Fr", "Fi", "Fin", "Grs", "Gis"]:
        io[n] = nc.dram_tensor(n, [D, D], F32R, kind="ExternalInput").ap()
    io["cvec2"] = nc.dram_tensor("cvec2", [128, 32], F32, kind="ExternalInput").ap()
    io["dselr"] = nc.dram_tensor("dselr", [128, 128], F32R, kind="ExternalInput").ap()
    io["esel2"] = nc.dram_tensor("esel2", [16, 1024], F32R, kind="ExternalInput").ap()
    io["ones128"] = nc.dram_tensor("ones128", [128, 128], F32R, kind="ExternalInput").ap()
    io["epsc"] = nc.dram_tensor("epsc", [128, 2], F32, kind="ExternalInput").ap()
    io["fmask"] = nc.dram_tensor("fmask", [128, 16], F32, kind="ExternalInput").ap()
    io["Wo1_t"] = nc.dram_tensor("Wo1_t", [D, D], F32R, kind="ExternalInput").ap()
    io["gwT"] = nc.dram_tensor("gwT", [D, 16], F32, kind="ExternalInput").ap()
    io["gb"] = nc.dram_tensor("gb", [1, 16], F32, kind="ExternalInput").ap()
    io["gbb"] = nc.dram_tensor("gbb", [128, 16], F32, kind="ExternalInput").ap()
    io["id128f"] = nc.dram_tensor("id128f", [128, 128], F32, kind="ExternalInput").ap()
    io["ones128f"] = nc.dram_tensor("ones128f", [128, 128], F32, kind="ExternalInput").ap()
    io["We1"] = nc.dram_tensor("We1", [16, D, 512], F32R, kind="ExternalInput").ap()
    io["We3"] = nc.dram_tensor("We3", [16, D, 512], F32R, kind="ExternalInput").ap()
    io["We2"] = nc.dram_tensor("We2", [16, 512, D], F32R, kind="ExternalInput").ap()
    io["be1c"] = nc.dram_tensor("be1c", [128, 64], F32, kind="ExternalInput").ap()
    io["be3c"] = nc.dram_tensor("be3c", [128, 64], F32, kind="ExternalInput").ap()
    io["be2c"] = nc.dram_tensor("be2c", [128, 128], F32, kind="ExternalInput").ap()
    io["be2r"] = nc.dram_tensor("be2r", [16, D], F32R, kind="ExternalInput").ap()
    io["bsc"] = nc.dram_tensor("bsc", [128, 24], F32, kind="ExternalInput").ap()
    io["ws1"] = nc.dram_tensor("ws1", [D, D], F32R, kind="ExternalInput").ap()
    io["ws3"] = nc.dram_tensor("ws3", [D, D], F32R, kind="ExternalInput").ap()
    io["ws2"] = nc.dram_tensor("ws2", [D, D], F32R, kind="ExternalInput").ap()
    io["eselE"] = nc.dram_tensor("eselE", [16, 2048], F32R, kind="ExternalInput").ap()
    io["mask128i"] = nc.dram_tensor("mask128i", [128, 128], F32, kind="ExternalInput").ap()
    for n in ["ylocal", "gsil", "x3s"]:
        io[n] = nc.dram_tensor(n, [D, p_tokens], F32, kind="Internal").ap()
    for n in ["RDs", "qfs", "kfs"]:
        io[n] = nc.dram_tensor(n, [D, p_tokens], F32, kind="Internal").ap()
    for n in ["x2s", "x3r"]:
        io[n] = nc.dram_tensor(n, [D, p_tokens], F32R, kind="Internal").ap()
    io["vtm"] = nc.dram_tensor("vtm", [p_tokens, D], F32, kind="Internal").ap()
    cc0_in = nc.dram_tensor("cc0_in", [128, 520], F32, kind="Internal").ap()
    cc0_out = nc.dram_tensor("cc0_out", [8 * 128, 520], F32, kind="Internal").ap()
    cc1_in = nc.dram_tensor("cc1_in", [128, 520], F32, kind="Internal").ap()
    cc1_out = nc.dram_tensor("cc1_out", [8 * 128, 520], F32, kind="Internal").ap()
    io["summ0"] = cc0_in[:, 0:512]
    io["summD"] = cc0_in[:, 512:520]
    io["summ1"] = cc1_in[:, 0:512]
    io["summZ"] = cc1_in[:, 512:520]
    io["outT"] = nc.dram_tensor("outT", [D, p_tokens], F32, kind="ExternalOutput").ap()

    with nc.Block() as block:
        ctx = Ctx(nc, block)
        emit_phase1(ctx, io, p_tokens)
        ctx.run("po", lambda E, a=cc0_in.opt(), b=cc0_out.opt(): E.collective_compute(
            "AllGather", ALU.bypass, replica_groups=[list(range(NCORE))],
            ins=[a], outs=[b]), reads=(cc0_in,), writes=(cc0_out,), sem="cc", d=1)
        emit_phase2(ctx, io, p_tokens,
                    gS_of=lambda r: cc0_out[128 * r:128 * (r + 1), 0:512],
                    gD_of=lambda r: cc0_out[128 * r:128 * (r + 1), 512:520])
        ctx.run("po", lambda E, a=cc1_in.opt(), b=cc1_out.opt(): E.collective_compute(
            "AllGather", ALU.bypass, replica_groups=[list(range(NCORE))],
            ins=[a], outs=[b]), reads=(cc1_in,), writes=(cc1_out,), sem="cc", d=1)
        emit_phase3(ctx, io, p_tokens,
                    gS_of=lambda r: cc1_out[128 * r:128 * (r + 1), 0:512],
                    gZ_of=lambda r: cc1_out[128 * r:128 * (r + 1), 512:520])
        ctx.flush()
    return nc


def stage_fused_inputs(inputs, p_tokens=P):
    maps1 = stage_phase1_inputs(inputs, p_tokens)
    sh2 = stage_phase2_shared(inputs)
    sh3 = stage_phase3_shared(inputs)
    in_maps = []
    for core in range(NCORE):
        m = {}
        m.update(maps1[core])
        m.update(sh2)
        m.update(sh3)
        m["fmask"] = _fold_masks(core)
        in_maps.append(m)
    return in_maps


def kernel(**inputs):
    nc = build_fused()
    in_maps = stage_fused_inputs(inputs)
    res = run_bass_kernel_spmd(nc, in_maps, core_ids=list(range(NCORE))).results
    out = np.empty((B, T, D), np.float32)
    for core in range(NCORE):
        b, q = core // 4, core % 4
        out[b, q * P:(q + 1) * P] = res[core]["outT"].T
    return out
